# version 01
# baseline (speedup 1.0000x reference)
from contextlib import ExitStack

import numpy as np
import ml_dtypes
import concourse.bass as bass
import concourse.mybir as mybir
from concourse.bass_utils import run_bass_kernel_spmd

F32 = mybir.dt.float32
BF16 = mybir.dt.bfloat16
AF = mybir.ActivationFunctionType
ALU = mybir.AluOpType
AX = mybir.AxisListType

D = 2048
NDC = 16
HEADS = 16
HD = 128
HFF = 5632
NHC = 44
T = 1024
NTC = 8
S = 4096
NE = 8
RK = 16
ER = 128
EPS = 1e-5
RSQ = float(1.0 / np.sqrt(128.0))
NEGBIG = -1e9

DEBUG = ()



class Sem:
    __slots__ = ("name", "n", "h")

    def __init__(self, name):
        self.name = name
        self.n = 0
        self.h = None


class R:

    def __init__(self, name):
        self.name = name
        self.cmds = []
        self.hw = {}

    def wait(self, sem, val=None):
        v = sem.n if val is None else val
        if v <= self.hw.get(sem, -1):
            return
        self.hw[sem] = v
        self.cmds.append(("w", sem, v))

    def wait_tok(self, tok):
        if tok is not None:
            self.wait(tok[0], tok[1])

    def op(self, method, *args, inc=None, inc_k=1, **kw):
        if inc is not None:
            inc.n += inc_k
        self.cmds.append(("o", method, args, kw, inc, inc_k))
        return inc.n if inc is not None else None

    def replay(self, e):
        for c in self.cmds:
            if c[0] == "w":
                e.wait_ge(c[1].h, c[2])
            else:
                _, m, args, kw, inc, k = c
                instr = getattr(e, m)(*args, **kw)
                if inc is not None:
                    instr.then_inc(inc.h, k)


class Q:

    def __init__(self, name, nslots):
        self.slots = [Sem(f"{name}{i}") for i in range(nslots)]
        self.i = 0

    def dma(self, rec, out, in_):
        sm = self.slots[self.i % len(self.slots)]
        self.i += 1
        rec.wait(sm, sm.n)
        rec.op("dma_start", out=out, in_=in_, inc=sm, inc_k=16)
        return (sm, sm.n)



def build(debug=()):
    nc = bass.Bass("TRN2", target_bir_lowering=False)

    def inp(name, shape, dtype=F32):
        return nc.dram_tensor(name, shape, dtype, kind="ExternalInput").ap()

    x_own = inp("x_own", [T, D])
    w_qkv = inp("w_qkv", [D, 3 * D])
    w_out = inp("w_out", [D, D])
    gate_wT = inp("gate_wT", [D, NE])
    w_upT = inp("w_upT", [D, HFF])
    w_gateT = inp("w_gateT", [D, HFF])
    w_downT = inp("w_downT", [HFF, D])
    upA_T = inp("upA_T", [D, ER])
    gateA_T = inp("gateA_T", [D, ER])
    dnA_T = inp("dnA_T", [HFF, ER])
    upB_T = inp("upB_T", [ER, HFF])
    gateB_T = inp("gateB_T", [ER, HFF])
    dnB_T = inp("dnB_T", [ER, D])
    mask512 = inp("mask512", [128, 512])

    out_own = nc.dram_tensor("out_own", [T, D], F32, kind="ExternalOutput").ap()

    kv_in_k = nc.dram_tensor("kv_in_k", [HEADS * HD, T], BF16, kind="Internal").ap()
    kv_in_v = nc.dram_tensor("kv_in_v", [T, HEADS * HD], BF16, kind="Internal").ap()
    kv_out_k = nc.dram_tensor("kv_out_k", [4 * HEADS * HD, T], BF16, kind="Internal").ap()
    kv_out_v = nc.dram_tensor("kv_out_v", [4 * T, HEADS * HD], BF16, kind="Internal").ap()
    lg_in = nc.dram_tensor("lg_in", [T, NE], F32, kind="Internal").ap()
    lg_out = nc.dram_tensor("lg_out", [4 * T, NE], F32, kind="Internal").ap()
    GROUPS = [[0, 1, 2, 3], [4, 5, 6, 7]]

    identb_d = nc.inline_tensor(np.eye(128, dtype=ml_dtypes.bfloat16), name="c_identb").ap()
    identf_d = nc.inline_tensor(np.eye(128, dtype=np.float32), name="c_identf").ap()
    onesf_d = nc.inline_tensor(np.ones((1, 128), dtype=np.float32), name="c_onesf").ap()
    negonesf_d = nc.inline_tensor(np.full((1, 128), -1.0, dtype=np.float32), name="c_negonesf").ap()
    blk_np = np.zeros((NE, ER), dtype=ml_dtypes.bfloat16)
    for e_ in range(NE):
        blk_np[e_, e_ * RK:(e_ + 1) * RK] = 2.0
    blk2_d = nc.inline_tensor(blk_np, name="c_blk2").ap()

    dbg_out = {}
    for name, shape, dt in [
        ("d_lnfm", [128, NDC * T], F32),
        ("d_q", [128, HEADS * T], F32),
        ("d_kvk", [HEADS * HD, T], F32),
        ("d_kvv", [T, HEADS * HD], F32),
        ("d_attnout", [128, HEADS * T], F32),
        ("d_x2", [128, NTC * D], F32),
        ("d_ln2fm", [128, NDC * T], F32),
        ("d_logits", [T, NE], F32),
        ("d_gstats", [128, 64], F32),
        ("d_mixed", [128, NHC * 512], F32),
        ("d_ffntm", [128, 4 * D], F32),
    ]:
        if name in debug:
            dbg_out[name] = nc.dram_tensor(name, shape, dt, kind="ExternalOutput").ap()

    ts = Sem("ts")
    vs = Sem("vs")
    ss = Sem("ss")
    cskv = Sem("cskv")
    cslg = Sem("cslg")
    Qg = Q("gq", 8)
    Qy = Q("sq", 8)
    SEMS = [ts, vs, ss, cskv, cslg] + Qg.slots + Qy.slots

    rt, rv, rs, rg, ry = R("tensor"), R("vector"), R("scalar"), R("gpsimd"), R("sync")
    ALLR = [rt, rv, rs, rg, ry]

    def barrier():
        for r in ALLR:
            for sm in SEMS:
                r.wait(sm)

    st = ExitStack()
    with st:
        def sb(name, shape, dtype):
            return st.enter_context(nc.sbuf_tensor(name, shape, dtype))

        def psum(name, shape, dtype=F32):
            return st.enter_context(nc.psum_tensor(name, shape, dtype))

        for sm in SEMS:
            sm.h = st.enter_context(nc.semaphore(sm.name))

        x_res = sb("x_res", [128, NTC * D], F32)
        fm_buf = sb("fm_buf", [128, NDC * T], BF16)
        statscr = sb("statscr", [128, 48], F32)
        lgtm_sb = sb("lgtm_sb", [128, NTC * NE], F32)
        identb = sb("identb", [128, 128], BF16)
        identf = sb("identf", [128, 128], F32)
        onesf = sb("onesf", [1, 128], F32)
        negonesf = sb("negonesf", [1, 128], F32)
        blk2 = sb("blk2", [NE, ER], BF16)
        mask_sb = sb("mask_sb", [128, 512], F32)

        fm_v = fm_buf[:].rearrange("p (c t) -> p c t", c=NDC)
        stQ = ExitStack()
        q_all = stQ.enter_context(nc.sbuf_tensor("q_all", [128, HEADS * T], BF16))

        tok_x = Qy.dma(ry, x_res[:].rearrange("p (j d) -> p j d", j=NTC),
                       x_own.rearrange("(j p) d -> p j d", p=128))
        tok_c = {}
        for nm, dst, src in [("identb", identb[:], identb_d),
                             ("identf", identf[:], identf_d),
                             ("onesf", onesf[:], onesf_d),
                             ("negonesf", negonesf[:], negonesf_d),
                             ("blk2", blk2[:], blk2_d),
                             ("mask", mask_sb[:], mask512)]:
            tok_c[nm] = Qy.dma(ry, dst, src)

        def ln_stats(xin):
            bnbuf = statscr[:, 16:40]
            mv = statscr[:, 40:42]
            mean, rstd = statscr[:, 40:41], statscr[:, 44:45]
            rv.wait(vs)
            rv.wait(ss)
            for c_ in range(4):
                rv.op("bn_stats", out=bnbuf[:, c_ * 6:(c_ + 1) * 6],
                      in_=xin[:, c_ * 512:(c_ + 1) * 512], inc=vs)
            rv.wait(vs)
            rv.op("bn_aggr", out=mv, in_=bnbuf, inc=vs)
            rv.wait(vs)
            rv.op("tensor_scalar", statscr[:, 42:43], statscr[:, 41:42], EPS, None,
                  ALU.add, inc=vs)
            rs.wait(vs)
            rs.op("activation", statscr[:, 43:44], statscr[:, 42:43], AF.Ln, inc=ss)
            rs.wait(ss)
            rs.op("activation", rstd, statscr[:, 43:44], AF.Exp, scale=-0.5, inc=ss)
            return mean, rstd

        stA = ExitStack()
        sbA = lambda n_, sh, dt: stA.enter_context(nc.sbuf_tensor(n_, sh, dt))
        psA = lambda n_, sh, dt=F32: stA.enter_context(nc.psum_tensor(n_, sh, dt))
        lntm = sbA("lntm", [128, 2 * D], BF16)
        wqkv_sb = sbA("wqkv_sb", [128, 2 * NDC * 512], BF16)
        kvstage = sbA("kvstage", [128, 4 * 512], BF16)
        p_lt = [psA(f"p_lt{i}", [128, 1024], BF16) for i in range(2)]
        p_mm = [psA(f"p_mm{i}", [128, 512], F32) for i in range(4)]

        lt_done_ts = [0, 0]
        for j in range(NTC):
            par = j % 2
            xin = x_res[:, j * D:(j + 1) * D]
            rv.wait_tok(tok_x)
            rs.wait_tok(tok_x)
            mean, rstd = ln_stats(xin)
            rv.wait(ss)
            rv.wait(ts, lt_done_ts[par])
            rv.op("tensor_scalar", lntm[:, par * D:(par + 1) * D], xin,
                  mean, rstd, ALU.subtract, op1=ALU.mult, inc=vs)
            rt.wait(vs)
            rt.wait_tok(tok_c["identb"])
            for k in range(NDC):
                bank = p_lt[k // 8]
                rt.op("transpose", bank[:, (k % 8) * 128:(k % 8 + 1) * 128],
                      lntm[:, par * D + k * 128: par * D + (k + 1) * 128],
                      identb[:], inc=(ts if k % 8 == 7 else None))
            lt_done_ts[par] = ts.n
            rv.wait(ts)
            for half in range(2):
                rv.op("tensor_copy",
                      out=fm_v[:, half * 8:(half + 1) * 8, j * 128:(j + 1) * 128],
                      in_=p_lt[half][:].rearrange("p (c t) -> p c t", c=8),
                      inc=vs)
        lnfm_done_vs = vs.n

        if "d_lnfm" in dbg_out:
            rg.wait(vs)
            Qg.dma(rg, dbg_out["d_lnfm"], fm_buf[:])

        wq_v = wqkv_sb[:].rearrange("p (b c n) -> p b c n", b=2, c=NDC)
        grp_consumed_ts = [0, 0]
        bank_free = [0, 0, 0, 0]
        stage_free = [None, None, None, None]

        def mm_to_bank(bi, lhsT_rhs_list):
            rt.wait(ss, bank_free[bi])
            n = len(lhsT_rhs_list)
            for i, (lt_, rh_) in enumerate(lhsT_rhs_list):
                rt.op("matmul", p_mm[bi][:], lt_, rh_,
                      start=(i == 0), stop=(i == n - 1),
                      inc=(ts if i == n - 1 else None))
            return ts.n

        for g in range(12):
            par = g % 2
            rg.wait(ts, grp_consumed_ts[par])
            tok_g = Qg.dma(rg, wq_v[:, par],
                           w_qkv[:, g * 512:(g + 1) * 512].rearrange("(c p) n -> p c n", p=128))
            rt.wait_tok(tok_g)
            rt.wait(vs, lnfm_done_vs)
            if g < 8:
                for c in range(4):
                    for half in range(2):
                        bi = 2 * (c % 2) + half
                        mm_done = mm_to_bank(bi, [
                            (wq_v[:, par, k, c * 128:(c + 1) * 128],
                             fm_v[:, k, half * 512:(half + 1) * 512])
                            for k in range(NDC)])
                        col = g * 512 + c * 128
                        rs.wait(ts, mm_done)
                        if g < 4:
                            h_ = col // 128
                            rs.op("copy",
                                  out=q_all[:, h_ * T + half * 512:h_ * T + half * 512 + 512],
                                  in_=p_mm[bi][:], inc=ss)
                        else:
                            si = bi
                            stg = kvstage[:, si * 512:(si + 1) * 512]
                            rs.wait_tok(stage_free[si])
                            rs.op("copy", out=stg, in_=p_mm[bi][:], inc=ss)
                            h_ = (col - 2048) // 128
                            ry.wait(ss)
                            stage_free[si] = Qy.dma(
                                ry,
                                kv_in_k[h_ * 128:(h_ + 1) * 128,
                                        half * 512:(half + 1) * 512],
                                stg)
                        bank_free[bi] = ss.n
            else:
                for tc in range(NTC):
                    bi = 2 * (tc % 2)
                    mm_done = mm_to_bank(bi, [
                        (fm_v[:, k, tc * 128:(tc + 1) * 128], wq_v[:, par, k, :])
                        for k in range(NDC)])
                    si = tc % 4
                    stg = kvstage[:, si * 512:(si + 1) * 512]
                    rs.wait(ts, mm_done)
                    rs.wait_tok(stage_free[si])
                    rs.op("copy", out=stg, in_=p_mm[bi][:], inc=ss)
                    bank_free[bi] = ss.n
                    vcol = (g - 8) * 512
                    ry.wait(ss)
                    stage_free[si] = Qy.dma(
                        ry, kv_in_v[tc * 128:(tc + 1) * 128, vcol:vcol + 512], stg)
            grp_consumed_ts[par] = ts.n

        if "d_q" in dbg_out:
            rg.wait(ss)
            Qg.dma(rg, dbg_out["d_q"], q_all[:])

        for sm in Qy.slots:
            rg.wait(sm)
        rg.op("collective_compute", "AllGather", ALU.bypass,
              replica_groups=GROUPS, ins=[kv_in_k], outs=[kv_out_k], inc=cskv)
        rg.op("collective_compute", "AllGather", ALU.bypass,
              replica_groups=GROUPS, ins=[kv_in_v], outs=[kv_out_v], inc=cskv)
        kv_gathered = (cskv, cskv.n)

        if "d_kvk" in dbg_out:
            rg.wait_tok(kv_gathered)
            Qg.dma(rg, dbg_out["d_kvk"], kv_in_k)
        if "d_kvv" in dbg_out:
            rg.wait_tok(kv_gathered)
            Qg.dma(rg, dbg_out["d_kvv"], kv_in_v)

        barrier()
        stA.close()

        stB = ExitStack()
        sbB = lambda n_, sh, dt: stB.enter_context(nc.sbuf_tensor(n_, sh, dt))
        psB = lambda n_, sh, dt=F32: stB.enter_context(nc.psum_tensor(n_, sh, dt))
        kh_sb = sbB("kh_sb", [128, 2 * 8 * 512], BF16)
        vh_sb = sbB("vh_sb", [128, 2 * 8 * 512], BF16)
        scores = sbB("scores", [128, 4096], F32)
        pb = [sbB("pb0", [128, 4096], BF16), sbB("pb1", [128, 4096], BF16)]
        pt_sb = sbB("pt_sb", [128, 2 * 1024], BF16)
        p_s = [psB(f"p_s{i}", [128, 512]) for i in range(2)]
        p_pt = [psB(f"p_pt{i}", [128, 1024], BF16) for i in range(2)]
        p_o = [psB(f"p_o{i}", [128, 256]) for i in range(2)]

        kh_v = kh_sb[:].rearrange("p (b j f t) -> p b j f t", b=2, j=8, f=4)
        vh_v = vh_sb[:].rearrange("p (b j f t) -> p b j f t", b=2, j=8, f=4)
        negmax = statscr[:, 0:1]
        zrow = statscr[:, 1:2]
        rz = statscr[:, 2:3]

        khb_consumed = [0, 0]
        kv_tok = [None, None]
        sbank_free = [0, 0]
        sbank_free_sem = [None, None]
        ptbank_free = [0, 0]
        o_free = [0, 0]
        pb_free = [0, 0]

        for h in range(HEADS):
            hb = h % 2
            ry.wait_tok(kv_gathered)
            ry.wait(ts, khb_consumed[hb])
            toks = []
            for r_ in range(4):
                toks.append(Qy.dma(
                    ry, kh_v[:, hb, :, r_, :],
                    kv_out_k[r_ * 2048 + h * 128:r_ * 2048 + (h + 1) * 128, :]
                    .rearrange("p (j t) -> p j t", j=8)))
                toks.append(Qy.dma(
                    ry, vh_v[:, hb, :, r_, :],
                    kv_out_v[r_ * T:(r_ + 1) * T, h * 128:(h + 1) * 128]
                    .rearrange("(j p) t -> p j t", p=128)))
            kv_tok[hb] = toks

            for pj in (0, 2, 4, 6):
                for jt in (pj, pj + 1):
                    rl = jt + 1
                    pbuf = pb[jt - pj]
                    for lv in range(rl):
                        bi = lv % 2
                        if sbank_free_sem[bi] is not None:
                            rt.wait(sbank_free_sem[bi], sbank_free[bi])
                        for tk in kv_tok[hb]:
                            rt.wait_tok(tk)
                        rt.op("matmul", p_s[bi][:],
                              q_all[:, h * T + jt * 128:h * T + (jt + 1) * 128],
                              kh_sb[:, hb * 4096 + lv * 512:hb * 4096 + (lv + 1) * 512],
                              start=True, stop=True, inc=ts)
                        mm_done = ts.n
                        dst = scores[:, lv * 512:(lv + 1) * 512]
                        if lv < jt:
                            rs.wait(ts, mm_done)
                            rs.wait(ss)
                            rs.op("activation", dst, p_s[bi][:], AF.Copy,
                                  scale=RSQ, inc=ss)
                            sbank_free_sem[bi], sbank_free[bi] = ss, ss.n
                        else:
                            rv.wait(ts, mm_done)
                            rv.wait_tok(tok_c["mask"])
                            rv.wait(ss)
                            rv.op("scalar_tensor_tensor", dst, p_s[bi][:],
                                  RSQ, mask_sb[:], ALU.mult, ALU.add, inc=vs)
                            sbank_free_sem[bi], sbank_free[bi] = vs, vs.n
                    rv.wait(ss)
                    rv.wait(vs)
                    rv.op("tensor_reduce", negmax, scores[:, 0:rl * 512],
                          AX.X, ALU.max, negate=True, inc=vs)
                    if jt == pj:
                        rv.wait(ts, pb_free[0])
                        rv.op("memset", pbuf[:, rl * 512:(rl + 1) * 512], 0.0, inc=vs)
                    rs.wait(vs)
                    rs.wait(ts, pb_free[jt - pj])
                    rs.op("activation", pbuf[:, 0:rl * 512], scores[:, 0:rl * 512],
                          AF.Exp, bias=negmax, accum_out=zrow, inc=ss)
                    rv.wait(ss)
                    rv.op("reciprocal", rz, zrow, inc=vs)
                    rv.wait(vs)
                    rv.op("tensor_scalar", pbuf[:, 0:rl * 512], pbuf[:, 0:rl * 512],
                          rz, None, ALU.mult, inc=vs)
                ptilde_vs = vs.n

                opar = (pj // 2) % 2
                rt.wait(vs, o_free[opar])
                for lv in range(pj + 2):
                    ppar = lv % 2
                    rt.wait(vs, ptilde_vs)
                    rt.wait(vs, ptbank_free[ppar])
                    for q4 in range(4):
                        for ti in range(2):
                            rt.op("transpose",
                                  p_pt[ppar][:, (q4 * 2 + ti) * 128:(q4 * 2 + ti + 1) * 128],
                                  pb[ti][:, (lv * 4 + q4) * 128:(lv * 4 + q4 + 1) * 128],
                                  identb[:],
                                  inc=(ts if (q4 == 3 and ti == 1) else None))
                    tr_done = ts.n
                    rv.wait(ts, tr_done)
                    rv.op("tensor_copy", out=pt_sb[:, ppar * 1024:(ppar + 1) * 1024],
                          in_=p_pt[ppar][:], inc=vs)
                    ptbank_free[ppar] = vs.n
                    rt.wait(vs, ptbank_free[ppar])
                    for q4 in range(4):
                        rt.op("matmul", p_o[opar][:],
                              vh_sb[:, hb * 4096 + (lv * 4 + q4) * 128:
                                    hb * 4096 + (lv * 4 + q4 + 1) * 128],
                              pt_sb[:, ppar * 1024 + q4 * 256:ppar * 1024 + (q4 + 1) * 256],
                              start=(lv == 0 and q4 == 0),
                              stop=(lv == pj + 1 and q4 == 3),
                              inc=(ts if (lv == pj + 1 and q4 == 3) else None))
                pb_free[0] = pb_free[1] = ts.n
                av_done = ts.n
                rv.wait(ts, av_done)
                rv.op("tensor_copy",
                      out=fm_buf[:, h * T + pj * 128:h * T + (pj + 2) * 128],
                      in_=p_o[opar][:], inc=vs)
                o_free[opar] = vs.n
            khb_consumed[hb] = ts.n

        if "d_attnout" in dbg_out:
            rg.wait(vs)
            Qg.dma(rg, dbg_out["d_attnout"], fm_buf[:])

        barrier()
        stB.close()
        stQ.close()

        stC = ExitStack()
        sbC = lambda n_, sh, dt: stC.enter_context(nc.sbuf_tensor(n_, sh, dt))
        psC = lambda n_, sh, dt=F32: stC.enter_context(nc.psum_tensor(n_, sh, dt))
        wout_sb = sbC("wout_sb", [128, NDC * D], BF16)
        rowscr2 = sbC("rowscr2", [128, 2 * D], F32)
        ln2f32 = sbC("ln2f32", [128, 2 * 512], F32)
        lgfm_sb = sbC("lgfm_sb", [NE, T], F32)
        gwT_sb = sbC("gwT_sb", [128, NDC * NE], F32)
        p_o2 = [psC(f"p_o2{i}", [128, 512]) for i in range(4)]
        p_t = [psC(f"p_t{i}", [128, 512]) for i in range(2)]
        p_g = psC("p_g", [NE, 128])
        p_x8 = psC("p_x8", [128, NE])

        wout_v = wout_sb[:].rearrange("p (c n) -> p c n", c=NDC)
        gwT_v = gwT_sb[:].rearrange("p (c e) -> p c e", c=NDC)

        tok_wout = Qg.dma(rg, wout_v, w_out.rearrange("(c p) n -> p c n", p=128))
        tok_gwt = Qy.dma(ry, gwT_v, gate_wT.rearrange("(c p) e -> p c e", p=128))

        attnout_vs = vs.n
        o2bank_free = [0, 0, 0, 0]
        for j in range(NTC):
            par = j % 2
            for q in range(4):
                rt.wait(vs, o2bank_free[q])
                rt.wait_tok(tok_wout)
                for c in range(NDC):
                    rt.op("matmul", p_o2[q][:],
                          fm_buf[:, c * T + j * 128:c * T + (j + 1) * 128],
                          wout_v[:, c, q * 512:(q + 1) * 512],
                          start=(c == 0), stop=(c == NDC - 1),
                          inc=(ts if c == NDC - 1 else None))
            mm_done = ts.n
            rv.wait(ts, mm_done)
            for q in range(4):
                rv.op("tensor_copy", out=rowscr2[:, par * D + q * 512:par * D + (q + 1) * 512],
                      in_=p_o2[q][:], inc=vs)
                o2bank_free[q] = vs.n
            row = rowscr2[:, par * D:(par + 1) * D]
            mean, rstd = ln_stats(row)
            rv.wait(ss)
            rv.op("tensor_scalar", row, row, mean, rstd,
                  ALU.subtract, op1=ALU.mult, inc=vs)
            rv.wait(vs)
            rv.op("tensor_tensor", x_res[:, j * D:(j + 1) * D],
                  x_res[:, j * D:(j + 1) * D], row, ALU.add, inc=vs)
        x2_done_vs = vs.n
        wout_used_ts = ts.n

        if "d_x2" in dbg_out:
            rg.wait(vs)
            Qg.dma(rg, dbg_out["d_x2"], x_res[:])

        pg_free = 0
        ln2f_free = [0, 0]
        for j in range(NTC):
            par = j % 2
            row = rowscr2[:, par * D:(par + 1) * D]
            rv.wait(vs, x2_done_vs)
            mean, rstd = ln_stats(x_res[:, j * D:(j + 1) * D])
            rv.wait(ss)
            rv.wait(ts)
            rv.op("tensor_scalar", row, x_res[:, j * D:(j + 1) * D], mean, rstd,
                  ALU.subtract, op1=ALU.mult, inc=vs)
            row_vs = vs.n
            for k4 in range(4):
                tpar = k4 % 2
                rt.wait(vs, row_vs)
                rt.wait(ss, ln2f_free[tpar])
                rt.wait(ts, wout_used_ts)
                for i in range(4):
                    rt.op("transpose", p_t[tpar][:, i * 128:(i + 1) * 128],
                          rowscr2[:, par * D + (4 * k4 + i) * 128:
                                  par * D + (4 * k4 + i + 1) * 128],
                          identf[:], inc=(ts if i == 3 else None))
                tr_done = ts.n
                rv.wait(ts, tr_done)
                rv.op("tensor_copy", out=ln2f32[:, tpar * 512:(tpar + 1) * 512],
                      in_=p_t[tpar][:], inc=vs)
                l32_vs = vs.n
                rs.wait(vs, l32_vs)
                rs.op("copy",
                      out=fm_v[:, 4 * k4:4 * k4 + 4, j * 128:(j + 1) * 128],
                      in_=ln2f32[:, tpar * 512:(tpar + 1) * 512]
                      .rearrange("p (c t) -> p c t", c=4), inc=ss)
                ln2f_free[tpar] = ss.n
                rt.wait(vs, l32_vs)
                rt.wait_tok(tok_gwt)
                if k4 == 0:
                    rt.wait(vs, pg_free)
                for i in range(4):
                    k = 4 * k4 + i
                    rt.op("matmul", p_g[:],
                          gwT_v[:, k, :],
                          ln2f32[:, tpar * 512 + i * 128:tpar * 512 + (i + 1) * 128],
                          start=(k == 0), stop=(k == NDC - 1),
                          inc=(ts if k == NDC - 1 else None))
            gmm_done = ts.n
            rv.wait(ts, gmm_done)
            rv.op("tensor_copy", out=lgfm_sb[0:NE, j * 128:(j + 1) * 128],
                  in_=p_g[:], inc=vs)
            pg_free = vs.n
        ln2_done_ss = ss.n

        if "d_ln2fm" in dbg_out:
            rg.wait(ss)
            Qg.dma(rg, dbg_out["d_ln2fm"], fm_buf[:])

        for j in range(NTC):
            rt.wait(vs)
            rt.op("transpose", p_x8[:], lgfm_sb[0:NE, j * 128:(j + 1) * 128],
                  identf[0:NE, 0:NE], inc=ts)
            rv.wait(ts)
            rv.op("tensor_copy", out=lgtm_sb[:, j * NE:(j + 1) * NE],
                  in_=p_x8[:], inc=vs)
        rv_lgtm = vs.n
        ry.wait(vs, rv_lgtm)
        tok_lgin = Qy.dma(ry, lg_in.rearrange("(c p) e -> p c e", p=128),
                          lgtm_sb[:].rearrange("p (c e) -> p c e", c=NTC))
        rg.wait_tok(tok_lgin)
        rg.op("collective_compute", "AllGather", ALU.bypass,
              replica_groups=GROUPS, ins=[lg_in], outs=[lg_out], inc=cslg)
        lg_gathered = (cslg, cslg.n)

        if "d_logits" in dbg_out:
            rg.wait_tok(lg_gathered)
            Qg.dma(rg, dbg_out["d_logits"], lg_in)

        barrier()
        stC.close()

        stD = ExitStack()
        sbD = lambda n_, sh, dt: stD.enter_context(nc.sbuf_tensor(n_, sh, dt))
        mixed = sbD("mixed", [128, NHC * 512], BF16)
        dnB_sb = sbD("dnB_sb", [128, D], BF16)
        wkb = sbD("wkb", [128, 2 * 512], F32)
        mt4 = sbD("mt4", [128, 4 * 512], BF16)
        rtld = sbD("rtld", [128, 4 * 512], BF16)
        upstr = sbD("upstr", [128, 2 * 4480], BF16)
        wdnstr = sbD("wdnstr", [128, 2 * D], BF16)
        up_v = upstr[:].rearrange("p (b n) -> p b n", b=2)

        tok_dnB = Qg.dma(rg, dnB_sb[:], dnB_T)

        stG = ExitStack()
        sbG = lambda n_, sh, dt: stG.enter_context(nc.sbuf_tensor(n_, sh, dt))
        psG = lambda n_, sh, dt=F32: stG.enter_context(nc.psum_tensor(n_, sh, dt))
        lgt = sbG("lgt", [128, 32 * NE], F32)
        tscr = sbG("tscr", [128, 16 * 32], F32)
        e_scr = sbG("e_scr", [128, 48], F32)
        ind_tm = sbG("ind_tm", [128, 2 * NTC * NE], BF16)
        indfm = sbG("indfm", [NE, 2 * T], BF16)
        wrow = sbG("wrow", [1, 160], F32)
        p_col = psG("p_col", [128, 16])
        p_row = psG("p_row", [1, 128])
        p_i8 = stG.enter_context(nc.psum_tensor("p_i8", [NE, 128], BF16))
        p_m512 = psG("p_m512", [128, 512])

        ry.wait_tok(lg_gathered)
        tok_lgt = Qy.dma(ry, lgt[:].rearrange("p (c e) -> p c e", e=NE),
                         lg_out.rearrange("(c p) e -> p c e", p=128))

        def tournament(view, n, m1, m2):
            cols = [view[:, :, i_:i_ + 1] for i_ in range(8)]
            scr = lambda i_: tscr[:, i_ * 32:i_ * 32 + n].unsqueeze(2)
            def tt(dst, a_, b_, op):
                rv.wait(vs)
                rv.op("tensor_tensor", dst, a_, b_, op, inc=vs)
            a = [scr(i_) for i_ in range(4)]
            iw = [scr(4 + i_) for i_ in range(4)]
            for i_ in range(4):
                tt(a[i_], cols[2 * i_], cols[2 * i_ + 1], ALU.max)
                tt(iw[i_], cols[2 * i_], cols[2 * i_ + 1], ALU.min)
            b1, j1, b2, j2, k1 = scr(8), scr(9), scr(10), scr(11), scr(12)
            tt(b1, a[0], a[1], ALU.max)
            tt(j1, a[0], a[1], ALU.min)
            tt(b2, a[2], a[3], ALU.max)
            tt(j2, a[2], a[3], ALU.min)
            tt(m1.unsqueeze(2), b1, b2, ALU.max)
            tt(k1, b1, b2, ALU.min)
            t1, t2, t3 = a[0], a[1], a[2]
            tt(t1, j1, j2, ALU.max)
            tt(t2, iw[0], iw[1], ALU.max)
            tt(t3, iw[2], iw[3], ALU.max)
            tt(j1, t1, k1, ALU.max)
            tt(j2, t2, t3, ALU.max)
            tt(m2.unsqueeze(2), j1, j2, ALU.max)

        m1g = tscr[:, 13 * 32:14 * 32]
        m2g = tscr[:, 14 * 32:15 * 32]
        rv.wait_tok(tok_lgt)
        tournament(lgt[:].rearrange("p (c e) -> p c e", e=NE), 32, m1g, m2g)
        m1o = e_scr[:, 16:24]
        m2o = e_scr[:, 24:32]
        tournament(lgtm_sb[:].rearrange("p (c e) -> p c e", e=NE), NE, m1o, m2o)

        for k in range(2):
            mg = m1g if k == 0 else m2g
            mo = m1o if k == 0 else m2o
            gred = statscr[:, 16:17]
            rv.wait(vs)
            rv.wait(ss)
            rv.op("tensor_reduce", gred, mg, AX.X, ALU.max, inc=vs)
            rt.wait(vs)
            rt.op("transpose", p_row[0:1, :], gred, identf[:], inc=ts)
            rv.wait(ts)
            rv.op("tensor_copy", out=wrow[0:1, 0:128], in_=p_row[0:1, :], inc=vs)
            rv.wait(vs)
            rv.op("tensor_reduce", wrow[0:1, 128:129], wrow[0:1, 0:128], AX.X,
                  ALU.max, inc=vs)
            gmax_vs = vs.n
            rt.wait(vs, gmax_vs)
            rt.op("matmul", p_col[:, 0:1], negonesf[0:1, :], wrow[0:1, 128:129],
                  start=True, stop=True, inc=ts)
            ngb = statscr[:, 17:18]
            rv.wait(ts)
            rv.op("tensor_copy", out=ngb, in_=p_col[:, 0:1], inc=vs)
            zc = statscr[:, 18:19]
            rs.wait(vs)
            rs.op("activation", tscr[:, 15 * 32:16 * 32], mg, AF.Exp,
                  bias=ngb, accum_out=zc, inc=ss)
            rt.wait(ss)
            rt.op("transpose", p_row[0:1, :], zc, identf[:], inc=ts)
            rv.wait(ts)
            rv.op("tensor_copy", out=wrow[0:1, 0:128], in_=p_row[0:1, :], inc=vs)
            rv.wait(vs)
            rv.op("tensor_reduce", wrow[0:1, 129:130], wrow[0:1, 0:128], AX.X,
                  ALU.add, inc=vs)
            rv.wait(vs)
            rv.op("reciprocal", wrow[0:1, 130:131], wrow[0:1, 129:130], inc=vs)
            rt.wait(vs)
            rt.op("matmul", p_col[:, 1:2], onesf[0:1, :], wrow[0:1, 130:131],
                  start=True, stop=True, inc=ts)
            rzb = statscr[:, 19:20]
            rv.wait(ts)
            rv.op("tensor_copy", out=rzb, in_=p_col[:, 1:2], inc=vs)
            eo = e_scr[:, 32:40]
            rs.wait(vs)
            rs.op("activation", eo, mo, AF.Exp, bias=ngb, inc=ss)
            rv.wait(ss)
            rv.op("tensor_scalar", statscr[:, k * 8:(k + 1) * 8], eo, rzb, None,
                  ALU.mult, inc=vs)
            for j in range(NTC):
                rv.wait(vs)
                rv.op("tensor_scalar",
                      ind_tm[:, (k * NTC + j) * NE:(k * NTC + j + 1) * NE],
                      lgtm_sb[:, j * NE:(j + 1) * NE],
                      mo[:, j:j + 1], None, ALU.is_equal, inc=vs)
                rt.wait(vs)
                rt.op("transpose", p_i8[:],
                      ind_tm[:, (k * NTC + j) * NE:(k * NTC + j + 1) * NE],
                      identb[:], inc=ts)
                rv.wait(ts)
                rv.op("tensor_copy",
                      out=indfm[0:NE, k * T + j * 128:k * T + (j + 1) * 128],
                      in_=p_i8[:], inc=vs)
        for half in range(2):
            for k in range(2):
                rt.wait(vs)
                rt.wait_tok(tok_c["blk2"])
                rt.op("matmul", p_m512[:], blk2[:],
                      indfm[0:NE, k * T + half * 512:k * T + (half + 1) * 512],
                      start=True, stop=True, inc=ts)
                rv.wait(ts)
                rv.op("tensor_copy",
                      out=mt4[:, (half * 2 + k) * 512:(half * 2 + k + 1) * 512],
                      in_=p_m512[:], inc=vs)
        barrier()
        stG.close()
        hscr = sbD("hscr", [128, 4 * 512], F32)
        chb = sbD("chb", [128, 2 * 512], BF16)
        rowscrD = sbD("rowscrD", [128, D], F32)
        ub_s = hscr[:, 0 * 512:1 * 512]
        gb_s = hscr[:, 1 * 512:2 * 512]
        su0_s = hscr[:, 2 * 512:3 * 512]
        su1_s = hscr[:, 3 * 512:4 * 512]

        wdn_v = wdnstr[:].rearrange("p (b d) -> p b d", b=2)
        bundle_used_ts = [0, 0]
        dn_used_ts = [0, 0]
        for half in range(2):
            hs = half * 512
            stP = ExitStack()
            p_r = stP.enter_context(nc.psum_tensor(f"p_r{half}", [128, 512], F32))
            p_wk = stP.enter_context(nc.psum_tensor(f"p_wk{half}", [128, 128], F32))
            p_wr = stP.enter_context(nc.psum_tensor(f"p_wr{half}", [1, 128], F32))
            wrow2 = stP.enter_context(nc.sbuf_tensor(f"wrow2{half}", [1, 128], F32))
            rg.wait(ts, dn_used_ts[0])
            rg.wait(ts, dn_used_ts[1])
            tok_upA = Qg.dma(rg, wdn_v[:, 0].rearrange("p (c e) -> p c e", c=NDC),
                             upA_T.rearrange("(c p) e -> p c e", p=128))
            tok_gtA = Qg.dma(rg, wdn_v[:, 1].rearrange("p (c e) -> p c e", c=NDC),
                             gateA_T.rearrange("(c p) e -> p c e", p=128))
            for k in range(2):
                for c4 in range(4):
                    tch = half * 4 + c4
                    rt.wait(vs)
                    rt.op("transpose", p_wr[0:1, :],
                          statscr[:, k * 8 + tch:k * 8 + tch + 1], identf[:], inc=ts)
                    rv.wait(ts)
                    rv.op("tensor_copy", out=wrow2[0:1, :], in_=p_wr[0:1, :], inc=vs)
                    rv_w = vs.n
                    rt.wait(vs, rv_w)
                    rt.op("matmul", p_wk[:], onesf[0:1, :], wrow2[0:1, :],
                          start=True, stop=True, inc=ts)
                    rv.wait(ts)
                    rv.op("tensor_copy", out=wkb[:, k * 512 + c4 * 128:
                                                 k * 512 + (c4 + 1) * 128],
                          in_=p_wk[:], inc=vs)
            for side in range(2):
                rt.wait_tok(tok_upA if side == 0 else tok_gtA)
                rt.wait(vs)
                for kd in range(NDC):
                    rt.op("matmul", p_r[:],
                          wdn_v[:, side, kd * 128:(kd + 1) * 128],
                          fm_v[:, kd, hs:hs + 512],
                          start=(kd == 0), stop=(kd == NDC - 1),
                          inc=(ts if kd == NDC - 1 else None))
                r_done = ts.n
                for k in range(2):
                    rv.wait(ts, r_done)
                    rv.op("tensor_tensor", rtld[:, (side * 2 + k) * 512:
                                                 (side * 2 + k + 1) * 512],
                          p_r[:], mt4[:, (half * 2 + k) * 512:(half * 2 + k + 1) * 512],
                          ALU.mult, inc=vs)
            rtld_vs = vs.n
            dn_used_ts[0] = dn_used_ts[1] = ts.n
            stP.close()

            stH = ExitStack()
            p_u0 = [stH.enter_context(nc.psum_tensor(f"p_u0{half}{i}", [128, 512], F32))
                    for i in range(2)]
            p_g0 = [stH.enter_context(nc.psum_tensor(f"p_g0{half}{i}", [128, 512], F32))
                    for i in range(2)]
            p_loU = stH.enter_context(nc.psum_tensor(f"p_loU{half}", [128, 512], F32))
            p_loG = stH.enter_context(nc.psum_tensor(f"p_loG{half}", [128, 512], F32))
            p_sraw = [stH.enter_context(nc.psum_tensor(f"p_sraw{half}{i}", [128, 512], F32))
                      for i in range(2)]
            u0_free = [None, None]
            g0_free = [None, None]
            loU_free = 0
            loG_free = 0
            ch_vs = 0
            sraw_ts = 0
            for hh in range(NHC):
                par = hh % 2
                rg.wait(ts, bundle_used_ts[par])
                bt = []
                bt.append(Qg.dma(rg, up_v[:, par, 0:2048].rearrange("p (c h) -> p c h", c=NDC),
                                 w_upT[:, hh * 128:(hh + 1) * 128]
                                 .rearrange("(c p) h -> p c h", p=128)))
                bt.append(Qg.dma(rg, up_v[:, par, 2048:4096].rearrange("p (c h) -> p c h", c=NDC),
                                 w_gateT[:, hh * 128:(hh + 1) * 128]
                                 .rearrange("(c p) h -> p c h", p=128)))
                bt.append(Qg.dma(rg, up_v[:, par, 4096:4224],
                                 upB_T[:, hh * 128:(hh + 1) * 128]))
                bt.append(Qg.dma(rg, up_v[:, par, 4224:4352],
                                 gateB_T[:, hh * 128:(hh + 1) * 128]))
                bt.append(Qg.dma(rg, up_v[:, par, 4352:4480],
                                 dnA_T[hh * 128:(hh + 1) * 128, :]))
                if hh > 0:
                    rt.wait(vs, ch_vs)
                    ppar = (hh - 1) % 2
                    for k in range(2):
                        rt.op("matmul", p_sraw[k][:],
                              up_v[:, ppar, 4352:4480],
                              chb[:, k * 512:(k + 1) * 512],
                              start=(hh - 1 == 0), stop=False,
                              inc=(ts if k == 1 else None))
                    sraw_ts = ts.n
                    bundle_used_ts[ppar] = ts.n
                for tk in bt:
                    rt.wait_tok(tk)
                rt.wait_tok(u0_free[par])
                rt.wait(vs, rtld_vs)
                for kd in range(NDC):
                    rt.op("matmul", p_u0[par][:],
                          up_v[:, par, kd * 128:(kd + 1) * 128],
                          fm_v[:, kd, hs:hs + 512],
                          start=(kd == 0), stop=(kd == NDC - 1),
                          inc=(ts if kd == NDC - 1 else None))
                baseu_ts = ts.n
                rt.wait_tok(g0_free[par])
                for kd in range(NDC):
                    rt.op("matmul", p_g0[par][:],
                          up_v[:, par, 2048 + kd * 128:2048 + (kd + 1) * 128],
                          fm_v[:, kd, hs:hs + 512],
                          start=(kd == 0), stop=(kd == NDC - 1),
                          inc=(ts if kd == NDC - 1 else None))
                baseg_ts = ts.n
                rs.wait(ts, baseu_ts)
                rs.wait(vs)
                rs.wait(ss)
                rs.op("copy", out=ub_s, in_=p_u0[par][:], inc=ss)
                ubcopy_ss = ss.n
                rs.wait(ts, baseg_ts)
                rs.op("copy", out=gb_s, in_=p_g0[par][:], inc=ss)
                gbcopy_ss = ss.n
                rt.wait(ss, ubcopy_ss)
                rt.op("matmul", p_u0[par][:], up_v[:, par, 4096:4224],
                      rtld[:, 0:512], start=False, stop=True,
                      skip_group_check=True, inc=ts)
                lo0u_ts = ts.n
                rt.wait(vs, loU_free)
                rt.op("matmul", p_loU[:], up_v[:, par, 4096:4224],
                      rtld[:, 512:1024], start=True, stop=True, inc=ts)
                lo1u_ts = ts.n
                rt.wait(ss, gbcopy_ss)
                rt.op("matmul", p_g0[par][:], up_v[:, par, 4224:4352],
                      rtld[:, 1024:1536], start=False, stop=True,
                      skip_group_check=True, inc=ts)
                lo0g_ts = ts.n
                rt.wait(vs, loG_free)
                rt.op("matmul", p_loG[:], up_v[:, par, 4224:4352],
                      rtld[:, 1536:2048], start=True, stop=True, inc=ts)
                lo1g_ts = ts.n
                rs.wait(ts, lo0u_ts)
                rs.op("activation", su0_s, p_u0[par][:], AF.Silu, inc=ss)
                silu0_ss = ss.n
                u0_free[par] = (ss, silu0_ss)
                rv.wait(ts, lo1u_ts)
                rv.wait(ss, ubcopy_ss)
                rv.op("scalar_tensor_tensor", ub_s, p_loU[:], 1.0, ub_s,
                      ALU.mult, ALU.add, inc=vs)
                loU_free = vs.n
                u1_vs = vs.n
                rs.wait(vs, u1_vs)
                rs.op("activation", su1_s, ub_s, AF.Silu, inc=ss)
                silu1_ss = ss.n
                rv.wait(ss, silu0_ss)
                rv.wait(ts, lo0g_ts)
                rv.wait(ts, sraw_ts)
                rv.op("tensor_tensor", su0_s, su0_s, p_g0[par][:], ALU.mult, inc=vs)
                h0_vs = vs.n
                g0_free[par] = (vs, h0_vs)
                rv.wait(vs, h0_vs)
                rv.op("tensor_tensor", chb[:, 0:512], su0_s, wkb[:, 0:512],
                      ALU.mult, inc=vs)
                rv.wait(ts, lo1g_ts)
                rv.wait(ss, gbcopy_ss)
                rv.op("scalar_tensor_tensor", gb_s, p_loG[:], 1.0, gb_s,
                      ALU.mult, ALU.add, inc=vs)
                loG_free = vs.n
                rv.wait(ss, silu1_ss)
                rv.wait(vs)
                rv.op("tensor_tensor", su1_s, su1_s, gb_s, ALU.mult, inc=vs)
                rv.wait(vs)
                rv.op("tensor_tensor", chb[:, 512:1024], su1_s, wkb[:, 512:1024],
                      ALU.mult, inc=vs)
                rv.wait(vs)
                rv.op("tensor_tensor", mixed[:, hh * 512:(hh + 1) * 512],
                      chb[:, 0:512], chb[:, 512:1024], ALU.add, inc=vs)
                ch_vs = vs.n
            rt.wait(vs, ch_vs)
            ppar = (NHC - 1) % 2
            for k in range(2):
                rt.op("matmul", p_sraw[k][:], up_v[:, ppar, 4352:4480],
                      chb[:, k * 512:(k + 1) * 512],
                      start=False, stop=True, inc=(ts if k == 1 else None))
            sraw_done = ts.n
            bundle_used_ts[ppar] = ts.n
            for k in range(2):
                rv.wait(ts, sraw_done)
                rv.op("tensor_tensor", rtld[:, k * 512:(k + 1) * 512],
                      p_sraw[k][:], mt4[:, (half * 2 + k) * 512:(half * 2 + k + 1) * 512],
                      ALU.mult, inc=vs)
            stld_vs = vs.n
            barrier()
            stH.close()

            stE = ExitStack()
            p_d = [stE.enter_context(nc.psum_tensor(f"p_d{half}{i}", [128, 512], F32))
                   for i in range(8)]
            dbank_free = [0] * 8
            for tpair in range(2):
                for hh in range(NHC):
                    par = hh % 2
                    rg.wait(ts, dn_used_ts[par])
                    tok_dn = Qg.dma(rg, wdn_v[:, par],
                                    w_downT[hh * 128:(hh + 1) * 128, :])
                    rt.wait_tok(tok_dn)
                    if hh == 0:
                        for bi in range(8):
                            rt.wait(vs, dbank_free[bi])
                    for tq2 in range(2):
                        tloc = tpair * 2 + tq2
                        for qb in range(4):
                            rt.op("matmul", p_d[tq2 * 4 + qb][:],
                                  mixed[:, hh * 512 + tloc * 128:
                                        hh * 512 + (tloc + 1) * 128],
                                  wdn_v[:, par, qb * 512:(qb + 1) * 512],
                                  start=(hh == 0), stop=False,
                                  inc=(ts if (tq2 == 1 and qb == 3) else None))
                    dn_used_ts[par] = ts.n
                rt.wait(vs, stld_vs)
                rt.wait_tok(tok_dnB)
                for tq2 in range(2):
                    tloc = tpair * 2 + tq2
                    for qb in range(4):
                        for k in range(2):
                            rt.op("matmul", p_d[tq2 * 4 + qb][:],
                                  rtld[:, k * 512 + tloc * 128:
                                       k * 512 + (tloc + 1) * 128],
                                  dnB_sb[:, qb * 512:(qb + 1) * 512],
                                  start=False, stop=(k == 1),
                                  inc=(ts if (tq2 == 1 and qb == 3 and k == 1)
                                       else None))
                dn_done = ts.n
                for tq2 in range(2):
                    tch = half * 4 + tpair * 2 + tq2
                    rv.wait(ts, dn_done)
                    rv.wait(vs)
                    for qb in range(4):
                        rv.op("tensor_copy",
                              out=rowscrD[:, qb * 512:(qb + 1) * 512],
                              in_=p_d[tq2 * 4 + qb][:], inc=vs)
                        dbank_free[tq2 * 4 + qb] = vs.n
                    mean, rstd = ln_stats(rowscrD[:])
                    rv.wait(ss)
                    rv.op("tensor_scalar", rowscrD[:], rowscrD[:], mean, rstd,
                          ALU.subtract, op1=ALU.mult, inc=vs)
                    rv.wait(vs)
                    rv.op("tensor_tensor", x_res[:, tch * D:(tch + 1) * D],
                          x_res[:, tch * D:(tch + 1) * D], rowscrD[:],
                          ALU.add, inc=vs)
            stE.close()

        if "d_mixed" in dbg_out:
            rg.wait(vs)
            Qg.dma(rg, dbg_out["d_mixed"], mixed[:])

        barrier()
        stD.close()

        rv_final = vs.n
        ry.wait(vs, rv_final)
        tok_out = Qy.dma(ry, out_own.rearrange("(j p) d -> p j d", p=128),
                         x_res[:].rearrange("p (j d) -> p j d", j=NTC))
        ry.wait_tok(tok_out)

        with nc.Block() as block:
            block.tensor(lambda e: rt.replay(e))
            block.vector(lambda e: rv.replay(e))
            block.scalar(lambda e: rs.replay(e))
            block.gpsimd(lambda e: rg.replay(e))
            block.sync(lambda e: ry.replay(e))

    return nc



def host_prep(inputs):
    x = np.asarray(inputs["L0_attn_x"])
    w_qkv = np.ascontiguousarray(inputs["L0_attn_w_qkv"], dtype=np.float32)
    w_out = np.ascontiguousarray(inputs["L0_attn_w_out"], dtype=np.float32)
    gate_w = np.asarray(inputs["L0_ffn_gate_w"])
    w_up = np.asarray(inputs["L0_ffn_w_up"])
    w_gate = np.asarray(inputs["L0_ffn_w_gate"])
    w_down = np.asarray(inputs["L0_ffn_w_down"])
    up_A = np.asarray(inputs["L0_ffn_up_A"])
    up_B = np.asarray(inputs["L0_ffn_up_B"])
    gate_A = np.asarray(inputs["L0_ffn_gate_A"])
    gate_B = np.asarray(inputs["L0_ffn_gate_B"])
    down_A = np.asarray(inputs["L0_ffn_down_A"])
    down_B = np.asarray(inputs["L0_ffn_down_B"])

    shared = {
        "w_qkv": w_qkv,
        "w_out": w_out,
        "gate_wT": np.ascontiguousarray(gate_w.T),
        "w_upT": np.ascontiguousarray(w_up.T),
        "w_gateT": np.ascontiguousarray(w_gate.T),
        "w_downT": np.ascontiguousarray(w_down.T),
        "upA_T": np.ascontiguousarray(up_A.reshape(ER, D).T),
        "gateA_T": np.ascontiguousarray(gate_A.reshape(ER, D).T),
        "dnA_T": np.ascontiguousarray(down_A.reshape(ER, HFF).T),
        "upB_T": np.ascontiguousarray(up_B.transpose(0, 2, 1).reshape(ER, HFF)),
        "gateB_T": np.ascontiguousarray(gate_B.transpose(0, 2, 1).reshape(ER, HFF)),
        "dnB_T": np.ascontiguousarray(down_B.transpose(0, 2, 1).reshape(ER, D)),
    }
    in_maps = []
    for core in range(8):
        b, r = core // 4, core % 4
        xb = x[b].astype(np.float32, copy=False)
        x_ownc = np.ascontiguousarray(
            xb.reshape(32, 128, D)[r::4].reshape(T, D))
        m = np.zeros((128, 512), dtype=np.float32)
        for rp in range(4):
            if rp > r:
                m[:, rp * 128:(rp + 1) * 128] = NEGBIG
            elif rp == r:
                tri = np.triu(np.full((128, 128), NEGBIG, dtype=np.float32), k=1)
                m[:, rp * 128:(rp + 1) * 128] = tri
        d = dict(shared)
        d["x_own"] = x_ownc
        d["mask512"] = m
        in_maps.append(d)
    return in_maps


def kernel(**inputs):
    in_maps = host_prep(inputs)
    nc = build(debug=DEBUG)
    res = run_bass_kernel_spmd(nc, in_maps, core_ids=list(range(8)))
    out = np.empty((2, S, D), dtype=np.float32)
    for core in range(8):
        b, r = core // 4, core % 4
        o = res.results[core]["out_own"].reshape(NTC, 128, D)
        out[b].reshape(32, 128, D)[r::4] = o
    return out


# version 03
# speedup vs baseline: 1.0522x; 1.0522x over previous
from contextlib import ExitStack

import numpy as np
import ml_dtypes
import concourse.bass as bass
import concourse.mybir as mybir
from concourse.bass_utils import run_bass_kernel_spmd

F32 = mybir.dt.float32
BF16 = mybir.dt.bfloat16
AF = mybir.ActivationFunctionType
ALU = mybir.AluOpType
AX = mybir.AxisListType

D = 2048
NDC = 16
HEADS = 16
HD = 128
HFF = 5632
NHC = 44
T = 1024
NTC = 8
S = 4096
NE = 8
RK = 16
ER = 128
EPS = 1e-5
RSQ = float(1.0 / np.sqrt(128.0))
NEGBIG = -1e9

DEBUG = ()



class Sem:
    __slots__ = ("name", "n", "h")

    def __init__(self, name):
        self.name = name
        self.n = 0
        self.h = None


class R:

    def __init__(self, name):
        self.name = name
        self.cmds = []
        self.hw = {}

    def wait(self, sem, val=None):
        v = sem.n if val is None else val
        if v <= self.hw.get(sem, -1):
            return
        self.hw[sem] = v
        self.cmds.append(("w", sem, v))

    def wait_tok(self, tok):
        if tok is not None:
            self.wait(tok[0], tok[1])

    def op(self, method, *args, inc=None, inc_k=1, **kw):
        if inc is not None:
            inc.n += inc_k
        self.cmds.append(("o", method, args, kw, inc, inc_k))
        return inc.n if inc is not None else None

    def replay(self, e):
        for c in self.cmds:
            if c[0] == "w":
                e.wait_ge(c[1].h, c[2])
            else:
                _, m, args, kw, inc, k = c
                instr = getattr(e, m)(*args, **kw)
                if inc is not None:
                    instr.then_inc(inc.h, k)


class Q:

    def __init__(self, name, nslots):
        self.slots = [Sem(f"{name}{i}") for i in range(nslots)]
        self.i = 0

    def dma(self, rec, out, in_):
        sm = self.slots[self.i % len(self.slots)]
        self.i += 1
        rec.wait(sm, sm.n)
        rec.op("dma_start", out=out, in_=in_, inc=sm, inc_k=16)
        return (sm, sm.n)



def build(debug=()):
    nc = bass.Bass("TRN2", target_bir_lowering=False)

    def inp(name, shape, dtype=F32):
        return nc.dram_tensor(name, shape, dtype, kind="ExternalInput").ap()

    x_own = inp("x_own", [T, D])
    w_qkv = inp("w_qkv", [D, 3 * D])
    w_out = inp("w_out", [D, D])
    gate_wT = inp("gate_wT", [D, NE])
    w_upT = inp("w_upT", [D, HFF])
    w_gateT = inp("w_gateT", [D, HFF])
    w_downT = inp("w_downT", [HFF, D])
    upA_T = inp("upA_T", [D, ER])
    gateA_T = inp("gateA_T", [D, ER])
    dnA_T = inp("dnA_T", [HFF, ER])
    upB_T = inp("upB_T", [ER, HFF])
    gateB_T = inp("gateB_T", [ER, HFF])
    dnB_T = inp("dnB_T", [ER, D])
    mask512 = inp("mask512", [128, 512])

    out_own = nc.dram_tensor("out_own", [T, D], F32, kind="ExternalOutput").ap()

    kv_in_k = nc.dram_tensor("kv_in_k", [HEADS * HD, T], BF16, kind="Internal").ap()
    kv_in_v = nc.dram_tensor("kv_in_v", [T, HEADS * HD], BF16, kind="Internal").ap()
    kv_out_k = nc.dram_tensor("kv_out_k", [4 * HEADS * HD, T], BF16, kind="Internal").ap()
    kv_out_v = nc.dram_tensor("kv_out_v", [4 * T, HEADS * HD], BF16, kind="Internal").ap()
    lg_in = nc.dram_tensor("lg_in", [T, NE], F32, kind="Internal").ap()
    lg_out = nc.dram_tensor("lg_out", [4 * T, NE], F32, kind="Internal").ap()
    GROUPS = [[0, 1, 2, 3], [4, 5, 6, 7]]

    identb_d = nc.inline_tensor(np.eye(128, dtype=ml_dtypes.bfloat16), name="c_identb").ap()
    identf_d = nc.inline_tensor(np.eye(128, dtype=np.float32), name="c_identf").ap()
    onesf_d = nc.inline_tensor(np.ones((1, 128), dtype=np.float32), name="c_onesf").ap()
    negonesf_d = nc.inline_tensor(np.full((1, 128), -1.0, dtype=np.float32), name="c_negonesf").ap()
    blk_np = np.zeros((NE, ER), dtype=ml_dtypes.bfloat16)
    for e_ in range(NE):
        blk_np[e_, e_ * RK:(e_ + 1) * RK] = 2.0
    blk2_d = nc.inline_tensor(blk_np, name="c_blk2").ap()

    dbg_out = {}
    for name, shape, dt in [
        ("d_lnfm", [128, NDC * T], F32),
        ("d_q", [128, HEADS * T], F32),
        ("d_kvk", [HEADS * HD, T], F32),
        ("d_kvv", [T, HEADS * HD], F32),
        ("d_attnout", [128, HEADS * T], F32),
        ("d_x2", [128, NTC * D], F32),
        ("d_ln2fm", [128, NDC * T], F32),
        ("d_logits", [T, NE], F32),
        ("d_gstats", [128, 64], F32),
        ("d_mixed", [128, NHC * 512], F32),
        ("d_ffntm", [128, 4 * D], F32),
    ]:
        if name in debug:
            dbg_out[name] = nc.dram_tensor(name, shape, dt, kind="ExternalOutput").ap()

    ts = Sem("ts")
    vs = Sem("vs")
    ss = Sem("ss")
    cskv = Sem("cskv")
    cslg = Sem("cslg")
    Qg = Q("gq", 8)
    Qy = Q("sq", 8)
    SEMS = [ts, vs, ss, cskv, cslg] + Qg.slots + Qy.slots

    rt, rv, rs, rg, ry = R("tensor"), R("vector"), R("scalar"), R("gpsimd"), R("sync")
    ALLR = [rt, rv, rs, rg, ry]

    def barrier():
        for r in ALLR:
            for sm in SEMS:
                r.wait(sm)

    st = ExitStack()
    with st:
        def sb(name, shape, dtype):
            return st.enter_context(nc.sbuf_tensor(name, shape, dtype))

        def psum(name, shape, dtype=F32):
            return st.enter_context(nc.psum_tensor(name, shape, dtype))

        for sm in SEMS:
            sm.h = st.enter_context(nc.semaphore(sm.name))

        x_res = sb("x_res", [128, NTC * D], F32)
        fm_buf = sb("fm_buf", [128, NDC * T], BF16)
        statscr = sb("statscr", [128, 48], F32)
        lgtm_sb = sb("lgtm_sb", [128, NTC * NE], F32)
        identb = sb("identb", [128, 128], BF16)
        identf = sb("identf", [128, 128], F32)
        onesf = sb("onesf", [1, 128], F32)
        negonesf = sb("negonesf", [1, 128], F32)
        blk2 = sb("blk2", [NE, ER], BF16)
        mask_sb = sb("mask_sb", [128, 512], F32)

        fm_v = fm_buf[:].rearrange("p (c t) -> p c t", c=NDC)
        stQ = ExitStack()
        q_all = stQ.enter_context(nc.sbuf_tensor("q_all", [128, HEADS * T], BF16))

        tok_x = Qy.dma(ry, x_res[:].rearrange("p (j d) -> p j d", j=NTC),
                       x_own.rearrange("(j p) d -> p j d", p=128))
        tok_c = {}
        for nm, dst, src in [("identb", identb[:], identb_d),
                             ("identf", identf[:], identf_d),
                             ("onesf", onesf[:], onesf_d),
                             ("negonesf", negonesf[:], negonesf_d),
                             ("blk2", blk2[:], blk2_d),
                             ("mask", mask_sb[:], mask512)]:
            tok_c[nm] = Qy.dma(ry, dst, src)

        def ln_stats(xin):
            bnbuf = statscr[:, 16:40]
            mv = statscr[:, 40:42]
            mean, rstd = statscr[:, 40:41], statscr[:, 44:45]
            rv.wait(vs)
            rv.wait(ss)
            for c_ in range(4):
                rv.op("bn_stats", out=bnbuf[:, c_ * 6:(c_ + 1) * 6],
                      in_=xin[:, c_ * 512:(c_ + 1) * 512], inc=vs)
            rv.wait(vs)
            rv.op("bn_aggr", out=mv, in_=bnbuf, inc=vs)
            rv.wait(vs)
            rv.op("tensor_scalar", statscr[:, 42:43], statscr[:, 41:42], EPS, None,
                  ALU.add, inc=vs)
            rs.wait(vs)
            rs.op("activation", statscr[:, 43:44], statscr[:, 42:43], AF.Ln, inc=ss)
            rs.wait(ss)
            rs.op("activation", rstd, statscr[:, 43:44], AF.Exp, scale=-0.5, inc=ss)
            return mean, rstd

        stA = ExitStack()
        sbA = lambda n_, sh, dt: stA.enter_context(nc.sbuf_tensor(n_, sh, dt))
        psA = lambda n_, sh, dt=F32: stA.enter_context(nc.psum_tensor(n_, sh, dt))
        lntm = sbA("lntm", [128, 2 * D], BF16)
        wqkv_sb = sbA("wqkv_sb", [128, 2 * NDC * 512], BF16)
        kvstage = sbA("kvstage", [128, 4 * 512], BF16)
        p_lt = [psA(f"p_lt{i}", [128, 1024], BF16) for i in range(2)]
        p_mm = [psA(f"p_mm{i}", [128, 512], F32) for i in range(4)]

        lt_done_ts = [0, 0]
        for j in range(NTC):
            par = j % 2
            xin = x_res[:, j * D:(j + 1) * D]
            rv.wait_tok(tok_x)
            rs.wait_tok(tok_x)
            mean, rstd = ln_stats(xin)
            rv.wait(ss)
            rv.wait(ts, lt_done_ts[par])
            rv.op("tensor_scalar", lntm[:, par * D:(par + 1) * D], xin,
                  mean, rstd, ALU.subtract, op1=ALU.mult, inc=vs)
            rt.wait(vs)
            rt.wait_tok(tok_c["identb"])
            for k in range(NDC):
                bank = p_lt[k // 8]
                rt.op("transpose", bank[:, (k % 8) * 128:(k % 8 + 1) * 128],
                      lntm[:, par * D + k * 128: par * D + (k + 1) * 128],
                      identb[:], inc=(ts if k % 8 == 7 else None))
            lt_done_ts[par] = ts.n
            rv.wait(ts)
            for half in range(2):
                rv.op("tensor_copy",
                      out=fm_v[:, half * 8:(half + 1) * 8, j * 128:(j + 1) * 128],
                      in_=p_lt[half][:].rearrange("p (c t) -> p c t", c=8),
                      inc=vs)
        lnfm_done_vs = vs.n

        if "d_lnfm" in dbg_out:
            rg.wait(vs)
            Qg.dma(rg, dbg_out["d_lnfm"], fm_buf[:])

        wq_v = wqkv_sb[:].rearrange("p (b c n) -> p b c n", b=2, c=NDC)
        grp_consumed_ts = [0, 0]
        bank_free = [0, 0, 0, 0]
        stage_free = [None, None, None, None]

        def mm_to_bank(bi, lhsT_rhs_list):
            rt.wait(ss, bank_free[bi])
            n = len(lhsT_rhs_list)
            for i, (lt_, rh_) in enumerate(lhsT_rhs_list):
                rt.op("matmul", p_mm[bi][:], lt_, rh_,
                      start=(i == 0), stop=(i == n - 1),
                      inc=(ts if i == n - 1 else None))
            return ts.n

        ag_launched = {}
        for gi, g in enumerate([4, 5, 6, 7, 8, 9, 10, 11, 0, 1, 2, 3]):
            par = gi % 2
            rg.wait(ts, grp_consumed_ts[par])
            tok_g = Qg.dma(rg, wq_v[:, par],
                           w_qkv[:, g * 512:(g + 1) * 512].rearrange("(c p) n -> p c n", p=128))
            rt.wait_tok(tok_g)
            rt.wait(vs, lnfm_done_vs)
            if g < 8:
                for c in range(4):
                    for half in range(2):
                        bi = 2 * (c % 2) + half
                        mm_done = mm_to_bank(bi, [
                            (wq_v[:, par, k, c * 128:(c + 1) * 128],
                             fm_v[:, k, half * 512:(half + 1) * 512])
                            for k in range(NDC)])
                        col = g * 512 + c * 128
                        rs.wait(ts, mm_done)
                        if g < 4:
                            h_ = col // 128
                            rs.op("copy",
                                  out=q_all[:, h_ * T + half * 512:h_ * T + half * 512 + 512],
                                  in_=p_mm[bi][:], inc=ss)
                        else:
                            si = bi
                            stg = kvstage[:, si * 512:(si + 1) * 512]
                            rs.wait_tok(stage_free[si])
                            rs.op("copy", out=stg, in_=p_mm[bi][:], inc=ss)
                            h_ = (col - 2048) // 128
                            ry.wait(ss)
                            stage_free[si] = Qy.dma(
                                ry,
                                kv_in_k[h_ * 128:(h_ + 1) * 128,
                                        half * 512:(half + 1) * 512],
                                stg)
                        bank_free[bi] = ss.n
            else:
                for tc in range(NTC):
                    bi = 2 * (tc % 2)
                    mm_done = mm_to_bank(bi, [
                        (fm_v[:, k, tc * 128:(tc + 1) * 128], wq_v[:, par, k, :])
                        for k in range(NDC)])
                    si = tc % 4
                    stg = kvstage[:, si * 512:(si + 1) * 512]
                    rs.wait(ts, mm_done)
                    rs.wait_tok(stage_free[si])
                    rs.op("copy", out=stg, in_=p_mm[bi][:], inc=ss)
                    bank_free[bi] = ss.n
                    vcol = (g - 8) * 512
                    ry.wait(ss)
                    stage_free[si] = Qy.dma(
                        ry, kv_in_v[tc * 128:(tc + 1) * 128, vcol:vcol + 512], stg)
            grp_consumed_ts[par] = ts.n
            if g == 7:
                for sm in Qy.slots:
                    rg.wait(sm)
                rg.op("collective_compute", "AllGather", ALU.bypass,
                      replica_groups=GROUPS, ins=[kv_in_k], outs=[kv_out_k],
                      inc=cskv)
                ag_launched["k"] = (cskv, cskv.n)
            elif g == 11:
                for sm in Qy.slots:
                    rg.wait(sm)
                rg.op("collective_compute", "AllGather", ALU.bypass,
                      replica_groups=GROUPS, ins=[kv_in_v], outs=[kv_out_v],
                      inc=cskv)
                ag_launched["v"] = (cskv, cskv.n)

        if "d_q" in dbg_out:
            rg.wait(ss)
            Qg.dma(rg, dbg_out["d_q"], q_all[:])

        kv_gathered = (cskv, cskv.n)

        if "d_kvk" in dbg_out:
            rg.wait_tok(kv_gathered)
            Qg.dma(rg, dbg_out["d_kvk"], kv_in_k)
        if "d_kvv" in dbg_out:
            rg.wait_tok(kv_gathered)
            Qg.dma(rg, dbg_out["d_kvv"], kv_in_v)

        barrier()
        stA.close()

        stB = ExitStack()
        sbB = lambda n_, sh, dt: stB.enter_context(nc.sbuf_tensor(n_, sh, dt))
        psB = lambda n_, sh, dt=F32: stB.enter_context(nc.psum_tensor(n_, sh, dt))
        kh_sb = sbB("kh_sb", [128, 2 * 8 * 512], BF16)
        vh_sb = sbB("vh_sb", [128, 2 * 8 * 512], BF16)
        scores = sbB("scores", [128, 4096], F32)
        pb = [sbB("pb0", [128, 4096], BF16), sbB("pb1", [128, 4096], BF16)]
        pt_sb = sbB("pt_sb", [128, 2 * 1024], BF16)
        rzrow_sb = sbB("rzrow_sb", [1, 256], F32)
        rzb_sb = sbB("rzb_sb", [128, 256], F32)
        p_s = [psB(f"p_s{i}", [128, 512]) for i in range(3)]
        p_pt = [psB(f"p_pt{i}", [128, 1024], BF16) for i in range(2)]
        p_o = [psB(f"p_o{i}", [128, 256]) for i in range(2)]
        p_misc = psB("p_misc", [128, 256])

        kh_v = kh_sb[:].rearrange("p (b j f t) -> p b j f t", b=2, j=8, f=4)
        vh_v = vh_sb[:].rearrange("p (b j f t) -> p b j f t", b=2, j=8, f=4)
        bmax = statscr[:, 0:8]
        maxu = statscr[:, 8:9]
        negb = statscr[:, 9:11]
        ztile = statscr[:, 11:13]
        rzt = statscr[:, 13:15]

        khb_consumed = [0, 0]
        kv_tok = [None, None]
        ptbank_free_s = [0, 0]
        sbank_free = [None, None, None]
        sbank_i = 0
        ptbank_free = [0, 0]
        o_free = [0, 0]
        pb_free = [0, 0]
        exp_ss = 0

        for h in range(HEADS):
            hb = h % 2
            ry.wait_tok(ag_launched["k"])
            ry.wait_tok(ag_launched["v"])
            ry.wait(ts, khb_consumed[hb])
            toks = []
            for r_ in range(4):
                toks.append(Qy.dma(
                    ry, kh_v[:, hb, :, r_, :],
                    kv_out_k[r_ * 2048 + h * 128:r_ * 2048 + (h + 1) * 128, :]
                    .rearrange("p (j t) -> p j t", j=8)))
                toks.append(Qy.dma(
                    ry, vh_v[:, hb, :, r_, :],
                    kv_out_v[r_ * T:(r_ + 1) * T, h * 128:(h + 1) * 128]
                    .rearrange("(j p) t -> p j t", p=128)))
            kv_tok[hb] = toks

            for pj in (0, 2, 4, 6):
                for ti, jt in enumerate((pj, pj + 1)):
                    rl = jt + 1
                    pbuf = pb[ti]
                    for lv in range(rl):
                        bi = sbank_i % 3
                        sbank_i += 1
                        rt.wait_tok(sbank_free[bi])
                        for tk in kv_tok[hb]:
                            rt.wait_tok(tk)
                        rt.op("matmul", p_s[bi][:],
                              q_all[:, h * T + jt * 128:h * T + (jt + 1) * 128],
                              kh_sb[:, hb * 4096 + lv * 512:hb * 4096 + (lv + 1) * 512],
                              start=True, stop=True, inc=ts)
                        mm_done = ts.n
                        dst = scores[:, lv * 512:(lv + 1) * 512]
                        if lv == jt:
                            rv.wait(ts, mm_done)
                            rv.wait(ss, exp_ss)
                            rv.wait_tok(tok_c["mask"])
                            rv.op("tensor_tensor", dst, p_s[bi][:], mask_sb[:],
                                  ALU.add, inc=vs)
                            sbank_free[bi] = (vs, vs.n)
                        else:
                            rs.wait(ts, mm_done)
                            rs.wait(ss, exp_ss)
                            rs.op("copy", out=dst, in_=p_s[bi][:], inc=ss)
                            sbank_free[bi] = (ss, ss.n)
                            rv.wait(ss)
                        rv.wait(vs)
                        rv.op("tensor_reduce", bmax[:, lv:lv + 1], dst, AX.X,
                              ALU.max, inc=vs)
                    rv.wait(vs)
                    rv.op("tensor_reduce", maxu, bmax[:, 0:rl], AX.X, ALU.max,
                          inc=vs)
                    rv.wait(vs)
                    rv.op("tensor_scalar", negb[:, ti:ti + 1], maxu, -RSQ, None,
                          ALU.mult, inc=vs)
                    if ti == 0:
                        rv.wait(ts, pb_free[0])
                        rv.op("memset", pbuf[:, rl * 512:(rl + 1) * 512], 0.0,
                              inc=vs)
                    rs.wait(vs)
                    rs.wait(ts, pb_free[ti])
                    rs.op("activation", pbuf[:, 0:rl * 512], scores[:, 0:rl * 512],
                          AF.Exp, bias=negb[:, ti:ti + 1], scale=RSQ,
                          accum_out=ztile[:, ti:ti + 1], inc=ss)
                    exp_ss = ss.n
                    rv.wait(ss)
                    rv.op("reciprocal", rzt[:, ti:ti + 1], ztile[:, ti:ti + 1],
                          inc=vs)
                ptilde_vs = vs.n

                rv_rz = vs.n
                rt.wait(vs, rv_rz)
                for ti in range(2):
                    rt.op("transpose", p_misc[0:1, ti * 128:(ti + 1) * 128],
                          rzt[:, ti:ti + 1], identf[:],
                          inc=(ts if ti == 1 else None))
                rv.wait(ts)
                rv.op("tensor_copy", out=rzrow_sb[0:1, :], in_=p_misc[0:1, 0:256],
                      inc=vs)
                rzrow_vs = vs.n
                rt.wait(vs, rzrow_vs)
                rt.op("matmul", p_misc[:], onesf[0:1, :], rzrow_sb[0:1, :],
                      start=True, stop=True, inc=ts)
                rv.wait(ts)
                rv.op("tensor_copy", out=rzb_sb[:], in_=p_misc[:], inc=vs)
                rzb_vs = vs.n

                opar = (pj // 2) % 2
                rt.wait(vs, o_free[opar])
                tr_last = 0
                for lv in range(pj + 2):
                    ppar = lv % 2
                    rt.wait(vs, ptilde_vs)
                    rt.wait(vs, ptbank_free[ppar])
                    rt.wait(ss, ptbank_free_s[ppar])
                    for q4 in range(4):
                        for ti in range(2):
                            rt.op("transpose",
                                  p_pt[ppar][:, (q4 * 2 + ti) * 128:(q4 * 2 + ti + 1) * 128],
                                  pb[ti][:, (lv * 4 + q4) * 128:(lv * 4 + q4 + 1) * 128],
                                  identb[:],
                                  inc=(ts if (q4 == 3 and ti == 1) else None))
                    tr_done = ts.n
                    tr_last = tr_done
                    if lv % 2 == 0:
                        rv.wait(ts, tr_done)
                        rv.op("tensor_copy", out=pt_sb[:, ppar * 1024:(ppar + 1) * 1024],
                              in_=p_pt[ppar][:], inc=vs)
                        ptbank_free[ppar] = vs.n
                        rt.wait(vs, ptbank_free[ppar])
                    else:
                        rs.wait(ts, tr_done)
                        rs.op("copy", out=pt_sb[:, ppar * 1024:(ppar + 1) * 1024],
                              in_=p_pt[ppar][:], inc=ss)
                        ptbank_free_s[ppar] = ss.n
                        rt.wait(ss, ptbank_free_s[ppar])
                    for q4 in range(4):
                        rt.op("matmul", p_o[opar][:],
                              vh_sb[:, hb * 4096 + (lv * 4 + q4) * 128:
                                    hb * 4096 + (lv * 4 + q4 + 1) * 128],
                              pt_sb[:, ppar * 1024 + q4 * 256:ppar * 1024 + (q4 + 1) * 256],
                              start=(lv == 0 and q4 == 0),
                              stop=(lv == pj + 1 and q4 == 3),
                              inc=(ts if (lv == pj + 1 and q4 == 3) else None))
                pb_free[0] = pb_free[1] = tr_last
                av_done = ts.n
                rv.wait(ts, av_done)
                rv.wait(vs, rzb_vs)
                rv.op("tensor_tensor",
                      fm_buf[:, h * T + pj * 128:h * T + (pj + 2) * 128],
                      p_o[opar][:], rzb_sb[:], ALU.mult, inc=vs)
                o_free[opar] = vs.n
            khb_consumed[hb] = ts.n

        if "d_attnout" in dbg_out:
            rg.wait(vs)
            Qg.dma(rg, dbg_out["d_attnout"], fm_buf[:])

        barrier()
        stB.close()
        stQ.close()

        stC = ExitStack()
        sbC = lambda n_, sh, dt: stC.enter_context(nc.sbuf_tensor(n_, sh, dt))
        psC = lambda n_, sh, dt=F32: stC.enter_context(nc.psum_tensor(n_, sh, dt))
        wout_sb = sbC("wout_sb", [128, NDC * D], BF16)
        rowscr2 = sbC("rowscr2", [128, 2 * D], F32)
        ln2f32 = sbC("ln2f32", [128, 2 * 512], F32)
        lgfm_sb = sbC("lgfm_sb", [NE, T], F32)
        gwT_sb = sbC("gwT_sb", [128, NDC * NE], F32)
        p_o2 = [psC(f"p_o2{i}", [128, 512]) for i in range(4)]
        p_t = [psC(f"p_t{i}", [128, 512]) for i in range(2)]
        p_g = psC("p_g", [NE, 128])
        p_x8 = psC("p_x8", [128, NE])

        wout_v = wout_sb[:].rearrange("p (c n) -> p c n", c=NDC)
        gwT_v = gwT_sb[:].rearrange("p (c e) -> p c e", c=NDC)

        tok_wout = Qg.dma(rg, wout_v, w_out.rearrange("(c p) n -> p c n", p=128))
        tok_gwt = Qy.dma(ry, gwT_v, gate_wT.rearrange("(c p) e -> p c e", p=128))

        attnout_vs = vs.n
        o2bank_free = [0, 0, 0, 0]
        for j in range(NTC):
            par = j % 2
            for q in range(4):
                rt.wait(vs, o2bank_free[q])
                rt.wait_tok(tok_wout)
                for c in range(NDC):
                    rt.op("matmul", p_o2[q][:],
                          fm_buf[:, c * T + j * 128:c * T + (j + 1) * 128],
                          wout_v[:, c, q * 512:(q + 1) * 512],
                          start=(c == 0), stop=(c == NDC - 1),
                          inc=(ts if c == NDC - 1 else None))
            mm_done = ts.n
            rv.wait(ts, mm_done)
            for q in range(4):
                rv.op("tensor_copy", out=rowscr2[:, par * D + q * 512:par * D + (q + 1) * 512],
                      in_=p_o2[q][:], inc=vs)
                o2bank_free[q] = vs.n
            row = rowscr2[:, par * D:(par + 1) * D]
            mean, rstd = ln_stats(row)
            rv.wait(ss)
            rv.op("tensor_scalar", row, row, mean, rstd,
                  ALU.subtract, op1=ALU.mult, inc=vs)
            rv.wait(vs)
            rv.op("tensor_tensor", x_res[:, j * D:(j + 1) * D],
                  x_res[:, j * D:(j + 1) * D], row, ALU.add, inc=vs)
        x2_done_vs = vs.n
        wout_used_ts = ts.n

        if "d_x2" in dbg_out:
            rg.wait(vs)
            Qg.dma(rg, dbg_out["d_x2"], x_res[:])

        pg_free = 0
        ln2f_free = [0, 0]
        for j in range(NTC):
            par = j % 2
            row = rowscr2[:, par * D:(par + 1) * D]
            rv.wait(vs, x2_done_vs)
            mean, rstd = ln_stats(x_res[:, j * D:(j + 1) * D])
            rv.wait(ss)
            rv.wait(ts)
            rv.op("tensor_scalar", row, x_res[:, j * D:(j + 1) * D], mean, rstd,
                  ALU.subtract, op1=ALU.mult, inc=vs)
            row_vs = vs.n
            for k4 in range(4):
                tpar = k4 % 2
                rt.wait(vs, row_vs)
                rt.wait(ss, ln2f_free[tpar])
                rt.wait(ts, wout_used_ts)
                for i in range(4):
                    rt.op("transpose", p_t[tpar][:, i * 128:(i + 1) * 128],
                          rowscr2[:, par * D + (4 * k4 + i) * 128:
                                  par * D + (4 * k4 + i + 1) * 128],
                          identf[:], inc=(ts if i == 3 else None))
                tr_done = ts.n
                rv.wait(ts, tr_done)
                rv.op("tensor_copy", out=ln2f32[:, tpar * 512:(tpar + 1) * 512],
                      in_=p_t[tpar][:], inc=vs)
                l32_vs = vs.n
                rs.wait(vs, l32_vs)
                rs.op("copy",
                      out=fm_v[:, 4 * k4:4 * k4 + 4, j * 128:(j + 1) * 128],
                      in_=ln2f32[:, tpar * 512:(tpar + 1) * 512]
                      .rearrange("p (c t) -> p c t", c=4), inc=ss)
                ln2f_free[tpar] = ss.n
                rt.wait(vs, l32_vs)
                rt.wait_tok(tok_gwt)
                if k4 == 0:
                    rt.wait(vs, pg_free)
                for i in range(4):
                    k = 4 * k4 + i
                    rt.op("matmul", p_g[:],
                          gwT_v[:, k, :],
                          ln2f32[:, tpar * 512 + i * 128:tpar * 512 + (i + 1) * 128],
                          start=(k == 0), stop=(k == NDC - 1),
                          inc=(ts if k == NDC - 1 else None))
            gmm_done = ts.n
            rv.wait(ts, gmm_done)
            rv.op("tensor_copy", out=lgfm_sb[0:NE, j * 128:(j + 1) * 128],
                  in_=p_g[:], inc=vs)
            pg_free = vs.n
        ln2_done_ss = ss.n

        if "d_ln2fm" in dbg_out:
            rg.wait(ss)
            Qg.dma(rg, dbg_out["d_ln2fm"], fm_buf[:])

        for j in range(NTC):
            rt.wait(vs)
            rt.op("transpose", p_x8[:], lgfm_sb[0:NE, j * 128:(j + 1) * 128],
                  identf[0:NE, 0:NE], inc=ts)
            rv.wait(ts)
            rv.op("tensor_copy", out=lgtm_sb[:, j * NE:(j + 1) * NE],
                  in_=p_x8[:], inc=vs)
        rv_lgtm = vs.n
        ry.wait(vs, rv_lgtm)
        tok_lgin = Qy.dma(ry, lg_in.rearrange("(c p) e -> p c e", p=128),
                          lgtm_sb[:].rearrange("p (c e) -> p c e", c=NTC))
        rg.wait_tok(tok_lgin)
        rg.op("collective_compute", "AllGather", ALU.bypass,
              replica_groups=GROUPS, ins=[lg_in], outs=[lg_out], inc=cslg)
        lg_gathered = (cslg, cslg.n)

        if "d_logits" in dbg_out:
            rg.wait_tok(lg_gathered)
            Qg.dma(rg, dbg_out["d_logits"], lg_in)

        barrier()
        stC.close()

        stD = ExitStack()
        sbD = lambda n_, sh, dt: stD.enter_context(nc.sbuf_tensor(n_, sh, dt))
        mixed = sbD("mixed", [128, NHC * 512], BF16)
        dnB_sb = sbD("dnB_sb", [128, D], BF16)
        wkb = sbD("wkb", [128, 2 * 512], F32)
        mt4 = sbD("mt4", [128, 4 * 512], BF16)
        rtld = sbD("rtld", [128, 4 * 512], BF16)
        upstr = sbD("upstr", [128, 2 * 4480], BF16)
        wdnstr = sbD("wdnstr", [128, 2 * D], BF16)
        up_v = upstr[:].rearrange("p (b n) -> p b n", b=2)

        tok_dnB = Qg.dma(rg, dnB_sb[:], dnB_T)

        stG = ExitStack()
        sbG = lambda n_, sh, dt: stG.enter_context(nc.sbuf_tensor(n_, sh, dt))
        psG = lambda n_, sh, dt=F32: stG.enter_context(nc.psum_tensor(n_, sh, dt))
        lgt = sbG("lgt", [128, 32 * NE], F32)
        tscr = sbG("tscr", [128, 16 * 32], F32)
        e_scr = sbG("e_scr", [128, 48], F32)
        ind_tm = sbG("ind_tm", [128, 2 * NTC * NE], BF16)
        indfm = sbG("indfm", [NE, 2 * T], BF16)
        wrow = sbG("wrow", [1, 160], F32)
        p_col = psG("p_col", [128, 16])
        p_row = psG("p_row", [1, 128])
        p_i8 = stG.enter_context(nc.psum_tensor("p_i8", [NE, 128], BF16))
        p_m512 = psG("p_m512", [128, 512])

        ry.wait_tok(lg_gathered)
        tok_lgt = Qy.dma(ry, lgt[:].rearrange("p (c e) -> p c e", e=NE),
                         lg_out.rearrange("(c p) e -> p c e", p=128))

        def tournament(view, n, m1, m2):
            cols = [view[:, :, i_:i_ + 1] for i_ in range(8)]
            scr = lambda i_: tscr[:, i_ * 32:i_ * 32 + n].unsqueeze(2)
            def tt(dst, a_, b_, op):
                rv.wait(vs)
                rv.op("tensor_tensor", dst, a_, b_, op, inc=vs)
            a = [scr(i_) for i_ in range(4)]
            iw = [scr(4 + i_) for i_ in range(4)]
            for i_ in range(4):
                tt(a[i_], cols[2 * i_], cols[2 * i_ + 1], ALU.max)
                tt(iw[i_], cols[2 * i_], cols[2 * i_ + 1], ALU.min)
            b1, j1, b2, j2, k1 = scr(8), scr(9), scr(10), scr(11), scr(12)
            tt(b1, a[0], a[1], ALU.max)
            tt(j1, a[0], a[1], ALU.min)
            tt(b2, a[2], a[3], ALU.max)
            tt(j2, a[2], a[3], ALU.min)
            tt(m1.unsqueeze(2), b1, b2, ALU.max)
            tt(k1, b1, b2, ALU.min)
            t1, t2, t3 = a[0], a[1], a[2]
            tt(t1, j1, j2, ALU.max)
            tt(t2, iw[0], iw[1], ALU.max)
            tt(t3, iw[2], iw[3], ALU.max)
            tt(j1, t1, k1, ALU.max)
            tt(j2, t2, t3, ALU.max)
            tt(m2.unsqueeze(2), j1, j2, ALU.max)

        m1g = tscr[:, 13 * 32:14 * 32]
        m2g = tscr[:, 14 * 32:15 * 32]
        rv.wait_tok(tok_lgt)
        tournament(lgt[:].rearrange("p (c e) -> p c e", e=NE), 32, m1g, m2g)
        m1o = e_scr[:, 16:24]
        m2o = e_scr[:, 24:32]
        tournament(lgtm_sb[:].rearrange("p (c e) -> p c e", e=NE), NE, m1o, m2o)

        for k in range(2):
            mg = m1g if k == 0 else m2g
            mo = m1o if k == 0 else m2o
            gred = statscr[:, 16:17]
            rv.wait(vs)
            rv.wait(ss)
            rv.op("tensor_reduce", gred, mg, AX.X, ALU.max, inc=vs)
            rt.wait(vs)
            rt.op("transpose", p_row[0:1, :], gred, identf[:], inc=ts)
            rv.wait(ts)
            rv.op("tensor_copy", out=wrow[0:1, 0:128], in_=p_row[0:1, :], inc=vs)
            rv.wait(vs)
            rv.op("tensor_reduce", wrow[0:1, 128:129], wrow[0:1, 0:128], AX.X,
                  ALU.max, inc=vs)
            gmax_vs = vs.n
            rt.wait(vs, gmax_vs)
            rt.op("matmul", p_col[:, 0:1], negonesf[0:1, :], wrow[0:1, 128:129],
                  start=True, stop=True, inc=ts)
            ngb = statscr[:, 17:18]
            rv.wait(ts)
            rv.op("tensor_copy", out=ngb, in_=p_col[:, 0:1], inc=vs)
            zc = statscr[:, 18:19]
            rs.wait(vs)
            rs.op("activation", tscr[:, 15 * 32:16 * 32], mg, AF.Exp,
                  bias=ngb, accum_out=zc, inc=ss)
            rt.wait(ss)
            rt.op("transpose", p_row[0:1, :], zc, identf[:], inc=ts)
            rv.wait(ts)
            rv.op("tensor_copy", out=wrow[0:1, 0:128], in_=p_row[0:1, :], inc=vs)
            rv.wait(vs)
            rv.op("tensor_reduce", wrow[0:1, 129:130], wrow[0:1, 0:128], AX.X,
                  ALU.add, inc=vs)
            rv.wait(vs)
            rv.op("reciprocal", wrow[0:1, 130:131], wrow[0:1, 129:130], inc=vs)
            rt.wait(vs)
            rt.op("matmul", p_col[:, 1:2], onesf[0:1, :], wrow[0:1, 130:131],
                  start=True, stop=True, inc=ts)
            rzb = statscr[:, 19:20]
            rv.wait(ts)
            rv.op("tensor_copy", out=rzb, in_=p_col[:, 1:2], inc=vs)
            eo = e_scr[:, 32:40]
            rs.wait(vs)
            rs.op("activation", eo, mo, AF.Exp, bias=ngb, inc=ss)
            rv.wait(ss)
            rv.op("tensor_scalar", statscr[:, k * 8:(k + 1) * 8], eo, rzb, None,
                  ALU.mult, inc=vs)
            for j in range(NTC):
                rv.wait(vs)
                rv.op("tensor_scalar",
                      ind_tm[:, (k * NTC + j) * NE:(k * NTC + j + 1) * NE],
                      lgtm_sb[:, j * NE:(j + 1) * NE],
                      mo[:, j:j + 1], None, ALU.is_equal, inc=vs)
                rt.wait(vs)
                rt.op("transpose", p_i8[:],
                      ind_tm[:, (k * NTC + j) * NE:(k * NTC + j + 1) * NE],
                      identb[:], inc=ts)
                rv.wait(ts)
                rv.op("tensor_copy",
                      out=indfm[0:NE, k * T + j * 128:k * T + (j + 1) * 128],
                      in_=p_i8[:], inc=vs)
        for half in range(2):
            for k in range(2):
                rt.wait(vs)
                rt.wait_tok(tok_c["blk2"])
                rt.op("matmul", p_m512[:], blk2[:],
                      indfm[0:NE, k * T + half * 512:k * T + (half + 1) * 512],
                      start=True, stop=True, inc=ts)
                rv.wait(ts)
                rv.op("tensor_copy",
                      out=mt4[:, (half * 2 + k) * 512:(half * 2 + k + 1) * 512],
                      in_=p_m512[:], inc=vs)
        barrier()
        stG.close()
        hscr = sbD("hscr", [128, 4 * 512], F32)
        chb = sbD("chb", [128, 2 * 512], BF16)
        rowscrD = sbD("rowscrD", [128, D], F32)
        ub_s = hscr[:, 0 * 512:1 * 512]
        gb_s = hscr[:, 1 * 512:2 * 512]
        su0_s = hscr[:, 2 * 512:3 * 512]
        su1_s = hscr[:, 3 * 512:4 * 512]

        wdn_v = wdnstr[:].rearrange("p (b d) -> p b d", b=2)
        bundle_used_ts = [0, 0]
        dn_used_ts = [0, 0]
        for half in range(2):
            hs = half * 512
            stP = ExitStack()
            p_r = stP.enter_context(nc.psum_tensor(f"p_r{half}", [128, 512], F32))
            p_wk = stP.enter_context(nc.psum_tensor(f"p_wk{half}", [128, 128], F32))
            p_wr = stP.enter_context(nc.psum_tensor(f"p_wr{half}", [1, 128], F32))
            wrow2 = stP.enter_context(nc.sbuf_tensor(f"wrow2{half}", [1, 128], F32))
            rg.wait(ts, dn_used_ts[0])
            rg.wait(ts, dn_used_ts[1])
            tok_upA = Qg.dma(rg, wdn_v[:, 0].rearrange("p (c e) -> p c e", c=NDC),
                             upA_T.rearrange("(c p) e -> p c e", p=128))
            tok_gtA = Qg.dma(rg, wdn_v[:, 1].rearrange("p (c e) -> p c e", c=NDC),
                             gateA_T.rearrange("(c p) e -> p c e", p=128))
            for k in range(2):
                for c4 in range(4):
                    tch = half * 4 + c4
                    rt.wait(vs)
                    rt.op("transpose", p_wr[0:1, :],
                          statscr[:, k * 8 + tch:k * 8 + tch + 1], identf[:], inc=ts)
                    rv.wait(ts)
                    rv.op("tensor_copy", out=wrow2[0:1, :], in_=p_wr[0:1, :], inc=vs)
                    rv_w = vs.n
                    rt.wait(vs, rv_w)
                    rt.op("matmul", p_wk[:], onesf[0:1, :], wrow2[0:1, :],
                          start=True, stop=True, inc=ts)
                    rv.wait(ts)
                    rv.op("tensor_copy", out=wkb[:, k * 512 + c4 * 128:
                                                 k * 512 + (c4 + 1) * 128],
                          in_=p_wk[:], inc=vs)
            for side in range(2):
                rt.wait_tok(tok_upA if side == 0 else tok_gtA)
                rt.wait(vs)
                for kd in range(NDC):
                    rt.op("matmul", p_r[:],
                          wdn_v[:, side, kd * 128:(kd + 1) * 128],
                          fm_v[:, kd, hs:hs + 512],
                          start=(kd == 0), stop=(kd == NDC - 1),
                          inc=(ts if kd == NDC - 1 else None))
                r_done = ts.n
                for k in range(2):
                    rv.wait(ts, r_done)
                    rv.op("tensor_tensor", rtld[:, (side * 2 + k) * 512:
                                                 (side * 2 + k + 1) * 512],
                          p_r[:], mt4[:, (half * 2 + k) * 512:(half * 2 + k + 1) * 512],
                          ALU.mult, inc=vs)
            rtld_vs = vs.n
            dn_used_ts[0] = dn_used_ts[1] = ts.n
            stP.close()

            stH = ExitStack()
            p_u0 = [stH.enter_context(nc.psum_tensor(f"p_u0{half}{i}", [128, 512], F32))
                    for i in range(2)]
            p_g0 = [stH.enter_context(nc.psum_tensor(f"p_g0{half}{i}", [128, 512], F32))
                    for i in range(2)]
            p_loU = stH.enter_context(nc.psum_tensor(f"p_loU{half}", [128, 512], F32))
            p_loG = stH.enter_context(nc.psum_tensor(f"p_loG{half}", [128, 512], F32))
            p_sraw = [stH.enter_context(nc.psum_tensor(f"p_sraw{half}{i}", [128, 512], F32))
                      for i in range(2)]
            u0_free = [None, None]
            g0_free = [None, None]
            loU_free = 0
            loG_free = 0
            ch_vs = 0
            sraw_ts = 0
            for hh in range(NHC):
                par = hh % 2
                rg.wait(ts, bundle_used_ts[par])
                bt = []
                bt.append(Qg.dma(rg, up_v[:, par, 0:2048].rearrange("p (c h) -> p c h", c=NDC),
                                 w_upT[:, hh * 128:(hh + 1) * 128]
                                 .rearrange("(c p) h -> p c h", p=128)))
                bt.append(Qg.dma(rg, up_v[:, par, 2048:4096].rearrange("p (c h) -> p c h", c=NDC),
                                 w_gateT[:, hh * 128:(hh + 1) * 128]
                                 .rearrange("(c p) h -> p c h", p=128)))
                bt.append(Qg.dma(rg, up_v[:, par, 4096:4224],
                                 upB_T[:, hh * 128:(hh + 1) * 128]))
                bt.append(Qg.dma(rg, up_v[:, par, 4224:4352],
                                 gateB_T[:, hh * 128:(hh + 1) * 128]))
                bt.append(Qg.dma(rg, up_v[:, par, 4352:4480],
                                 dnA_T[hh * 128:(hh + 1) * 128, :]))
                for tk in bt:
                    rt.wait_tok(tk)
                rt.wait_tok(u0_free[par])
                rt.wait(vs, rtld_vs)
                for kd in range(NDC):
                    rt.op("matmul", p_u0[par][:],
                          up_v[:, par, kd * 128:(kd + 1) * 128],
                          fm_v[:, kd, hs:hs + 512],
                          start=(kd == 0), stop=(kd == NDC - 1),
                          inc=(ts if kd == NDC - 1 else None))
                baseu_ts = ts.n
                rt.wait_tok(g0_free[par])
                for kd in range(NDC):
                    rt.op("matmul", p_g0[par][:],
                          up_v[:, par, 2048 + kd * 128:2048 + (kd + 1) * 128],
                          fm_v[:, kd, hs:hs + 512],
                          start=(kd == 0), stop=(kd == NDC - 1),
                          inc=(ts if kd == NDC - 1 else None))
                baseg_ts = ts.n
                if hh > 0:
                    rt.wait(vs, ch_vs)
                    ppar = (hh - 1) % 2
                    for k in range(2):
                        rt.op("matmul", p_sraw[k][:],
                              up_v[:, ppar, 4352:4480],
                              chb[:, k * 512:(k + 1) * 512],
                              start=(hh - 1 == 0), stop=False,
                              inc=(ts if k == 1 else None))
                    sraw_ts = ts.n
                    bundle_used_ts[ppar] = ts.n
                rs.wait(ts, baseu_ts)
                rs.wait(vs)
                rs.wait(ss)
                rs.op("copy", out=ub_s, in_=p_u0[par][:], inc=ss)
                ubcopy_ss = ss.n
                rs.wait(ts, baseg_ts)
                rs.op("copy", out=gb_s, in_=p_g0[par][:], inc=ss)
                gbcopy_ss = ss.n
                rt.wait(ss, ubcopy_ss)
                rt.op("matmul", p_u0[par][:], up_v[:, par, 4096:4224],
                      rtld[:, 0:512], start=False, stop=True,
                      skip_group_check=True, inc=ts)
                lo0u_ts = ts.n
                rt.wait(vs, loU_free)
                rt.op("matmul", p_loU[:], up_v[:, par, 4096:4224],
                      rtld[:, 512:1024], start=True, stop=True, inc=ts)
                lo1u_ts = ts.n
                rt.wait(ss, gbcopy_ss)
                rt.op("matmul", p_g0[par][:], up_v[:, par, 4224:4352],
                      rtld[:, 1024:1536], start=False, stop=True,
                      skip_group_check=True, inc=ts)
                lo0g_ts = ts.n
                rt.wait(vs, loG_free)
                rt.op("matmul", p_loG[:], up_v[:, par, 4224:4352],
                      rtld[:, 1536:2048], start=True, stop=True, inc=ts)
                lo1g_ts = ts.n
                rs.wait(ts, lo0u_ts)
                rs.op("activation", su0_s, p_u0[par][:], AF.Silu, inc=ss)
                silu0_ss = ss.n
                u0_free[par] = (ss, silu0_ss)
                rv.wait(ts, lo1u_ts)
                rv.wait(ss, ubcopy_ss)
                rv.op("scalar_tensor_tensor", ub_s, p_loU[:], 1.0, ub_s,
                      ALU.mult, ALU.add, inc=vs)
                loU_free = vs.n
                u1_vs = vs.n
                rs.wait(vs, u1_vs)
                rs.op("activation", su1_s, ub_s, AF.Silu, inc=ss)
                silu1_ss = ss.n
                rv.wait(ss, silu0_ss)
                rv.wait(ts, lo0g_ts)
                rv.wait(ts, sraw_ts)
                rv.op("tensor_tensor", su0_s, su0_s, p_g0[par][:], ALU.mult, inc=vs)
                h0_vs = vs.n
                g0_free[par] = (vs, h0_vs)
                rv.wait(vs, h0_vs)
                rv.op("tensor_tensor", chb[:, 0:512], su0_s, wkb[:, 0:512],
                      ALU.mult, inc=vs)
                rv.wait(ts, lo1g_ts)
                rv.wait(ss, gbcopy_ss)
                rv.op("scalar_tensor_tensor", gb_s, p_loG[:], 1.0, gb_s,
                      ALU.mult, ALU.add, inc=vs)
                loG_free = vs.n
                rv.wait(ss, silu1_ss)
                rv.wait(vs)
                rv.op("tensor_tensor", su1_s, su1_s, gb_s, ALU.mult, inc=vs)
                rv.wait(vs)
                rv.op("tensor_tensor", chb[:, 512:1024], su1_s, wkb[:, 512:1024],
                      ALU.mult, inc=vs)
                rv.wait(vs)
                rv.op("tensor_tensor", mixed[:, hh * 512:(hh + 1) * 512],
                      chb[:, 0:512], chb[:, 512:1024], ALU.add, inc=vs)
                ch_vs = vs.n
            rt.wait(vs, ch_vs)
            ppar = (NHC - 1) % 2
            for k in range(2):
                rt.op("matmul", p_sraw[k][:], up_v[:, ppar, 4352:4480],
                      chb[:, k * 512:(k + 1) * 512],
                      start=False, stop=True, inc=(ts if k == 1 else None))
            sraw_done = ts.n
            bundle_used_ts[ppar] = ts.n
            for k in range(2):
                rv.wait(ts, sraw_done)
                rv.op("tensor_tensor", rtld[:, k * 512:(k + 1) * 512],
                      p_sraw[k][:], mt4[:, (half * 2 + k) * 512:(half * 2 + k + 1) * 512],
                      ALU.mult, inc=vs)
            stld_vs = vs.n
            barrier()
            stH.close()

            stE = ExitStack()
            p_d = [stE.enter_context(nc.psum_tensor(f"p_d{half}{i}", [128, 512], F32))
                   for i in range(8)]
            dbank_free = [0] * 8
            for tpair in range(2):
                for hh in range(NHC):
                    par = hh % 2
                    rg.wait(ts, dn_used_ts[par])
                    tok_dn = Qg.dma(rg, wdn_v[:, par],
                                    w_downT[hh * 128:(hh + 1) * 128, :])
                    rt.wait_tok(tok_dn)
                    if hh == 0:
                        for bi in range(8):
                            rt.wait(vs, dbank_free[bi])
                    for tq2 in range(2):
                        tloc = tpair * 2 + tq2
                        for qb in range(4):
                            rt.op("matmul", p_d[tq2 * 4 + qb][:],
                                  mixed[:, hh * 512 + tloc * 128:
                                        hh * 512 + (tloc + 1) * 128],
                                  wdn_v[:, par, qb * 512:(qb + 1) * 512],
                                  start=(hh == 0), stop=False,
                                  inc=(ts if (tq2 == 1 and qb == 3) else None))
                    dn_used_ts[par] = ts.n
                rt.wait(vs, stld_vs)
                rt.wait_tok(tok_dnB)
                for tq2 in range(2):
                    tloc = tpair * 2 + tq2
                    for qb in range(4):
                        for k in range(2):
                            rt.op("matmul", p_d[tq2 * 4 + qb][:],
                                  rtld[:, k * 512 + tloc * 128:
                                       k * 512 + (tloc + 1) * 128],
                                  dnB_sb[:, qb * 512:(qb + 1) * 512],
                                  start=False, stop=(k == 1),
                                  inc=(ts if (tq2 == 1 and qb == 3 and k == 1)
                                       else None))
                dn_done = ts.n
                for tq2 in range(2):
                    tch = half * 4 + tpair * 2 + tq2
                    rv.wait(ts, dn_done)
                    rv.wait(vs)
                    for qb in range(4):
                        rv.op("tensor_copy",
                              out=rowscrD[:, qb * 512:(qb + 1) * 512],
                              in_=p_d[tq2 * 4 + qb][:], inc=vs)
                        dbank_free[tq2 * 4 + qb] = vs.n
                    mean, rstd = ln_stats(rowscrD[:])
                    rv.wait(ss)
                    rv.op("tensor_scalar", rowscrD[:], rowscrD[:], mean, rstd,
                          ALU.subtract, op1=ALU.mult, inc=vs)
                    rv.wait(vs)
                    rv.op("tensor_tensor", x_res[:, tch * D:(tch + 1) * D],
                          x_res[:, tch * D:(tch + 1) * D], rowscrD[:],
                          ALU.add, inc=vs)
            stE.close()

        if "d_mixed" in dbg_out:
            rg.wait(vs)
            Qg.dma(rg, dbg_out["d_mixed"], mixed[:])

        barrier()
        stD.close()

        rv_final = vs.n
        ry.wait(vs, rv_final)
        tok_out = Qy.dma(ry, out_own.rearrange("(j p) d -> p j d", p=128),
                         x_res[:].rearrange("p (j d) -> p j d", j=NTC))
        ry.wait_tok(tok_out)

        with nc.Block() as block:
            block.tensor(lambda e: rt.replay(e))
            block.vector(lambda e: rv.replay(e))
            block.scalar(lambda e: rs.replay(e))
            block.gpsimd(lambda e: rg.replay(e))
            block.sync(lambda e: ry.replay(e))

    return nc



def host_prep(inputs):
    x = np.asarray(inputs["L0_attn_x"])
    w_qkv = np.ascontiguousarray(inputs["L0_attn_w_qkv"], dtype=np.float32)
    w_out = np.ascontiguousarray(inputs["L0_attn_w_out"], dtype=np.float32)
    gate_w = np.asarray(inputs["L0_ffn_gate_w"])
    w_up = np.asarray(inputs["L0_ffn_w_up"])
    w_gate = np.asarray(inputs["L0_ffn_w_gate"])
    w_down = np.asarray(inputs["L0_ffn_w_down"])
    up_A = np.asarray(inputs["L0_ffn_up_A"])
    up_B = np.asarray(inputs["L0_ffn_up_B"])
    gate_A = np.asarray(inputs["L0_ffn_gate_A"])
    gate_B = np.asarray(inputs["L0_ffn_gate_B"])
    down_A = np.asarray(inputs["L0_ffn_down_A"])
    down_B = np.asarray(inputs["L0_ffn_down_B"])

    shared = {
        "w_qkv": w_qkv,
        "w_out": w_out,
        "gate_wT": np.ascontiguousarray(gate_w.T),
        "w_upT": np.ascontiguousarray(w_up.T),
        "w_gateT": np.ascontiguousarray(w_gate.T),
        "w_downT": np.ascontiguousarray(w_down.T),
        "upA_T": np.ascontiguousarray(up_A.reshape(ER, D).T),
        "gateA_T": np.ascontiguousarray(gate_A.reshape(ER, D).T),
        "dnA_T": np.ascontiguousarray(down_A.reshape(ER, HFF).T),
        "upB_T": np.ascontiguousarray(up_B.transpose(0, 2, 1).reshape(ER, HFF)),
        "gateB_T": np.ascontiguousarray(gate_B.transpose(0, 2, 1).reshape(ER, HFF)),
        "dnB_T": np.ascontiguousarray(down_B.transpose(0, 2, 1).reshape(ER, D)),
    }
    in_maps = []
    for core in range(8):
        b, r = core // 4, core % 4
        xb = x[b].astype(np.float32, copy=False)
        x_ownc = np.ascontiguousarray(
            xb.reshape(32, 128, D)[r::4].reshape(T, D))
        mval = NEGBIG / RSQ
        m = np.zeros((128, 512), dtype=np.float32)
        for rp in range(4):
            if rp > r:
                m[:, rp * 128:(rp + 1) * 128] = mval
            elif rp == r:
                tri = np.triu(np.full((128, 128), mval, dtype=np.float32), k=1)
                m[:, rp * 128:(rp + 1) * 128] = tri
        d = dict(shared)
        d["x_own"] = x_ownc
        d["mask512"] = m
        in_maps.append(d)
    return in_maps


def kernel(**inputs):
    in_maps = host_prep(inputs)
    nc = build(debug=DEBUG)
    res = run_bass_kernel_spmd(nc, in_maps, core_ids=list(range(8)))
    out = np.empty((2, S, D), dtype=np.float32)
    for core in range(8):
        b, r = core // 4, core % 4
        o = res.results[core]["out_own"].reshape(NTC, 128, D)
        out[b].reshape(32, 128, D)[r::4] = o
    return out


# version 04
# speedup vs baseline: 1.2658x; 1.2030x over previous
from contextlib import ExitStack

import numpy as np
import ml_dtypes
import concourse.bass as bass
import concourse.mybir as mybir
from concourse.bass_utils import run_bass_kernel_spmd

F32 = mybir.dt.float32
BF16 = mybir.dt.bfloat16
AF = mybir.ActivationFunctionType
ALU = mybir.AluOpType
AX = mybir.AxisListType

D = 2048
NDC = 16
HEADS = 16
HD = 128
HFF = 5632
NHC = 44
T = 1024
NTC = 8
S = 4096
NE = 8
RK = 16
ER = 128
EPS = 1e-5
RSQ = float(1.0 / np.sqrt(128.0))
NEGBIG = -1e9

DEBUG = ()



class Sem:
    __slots__ = ("name", "n", "h")

    def __init__(self, name):
        self.name = name
        self.n = 0
        self.h = None


class R:

    def __init__(self, name):
        self.name = name
        self.cmds = []
        self.hw = {}

    def wait(self, sem, val=None):
        v = sem.n if val is None else val
        if v <= self.hw.get(sem, -1):
            return
        self.hw[sem] = v
        self.cmds.append(("w", sem, v))

    def wait_tok(self, tok):
        if tok is not None:
            self.wait(tok[0], tok[1])

    def op(self, method, *args, inc=None, inc_k=1, **kw):
        if inc is not None:
            inc.n += inc_k
        self.cmds.append(("o", method, args, kw, inc, inc_k))
        return inc.n if inc is not None else None

    def replay(self, e):
        for c in self.cmds:
            if c[0] == "w":
                e.wait_ge(c[1].h, c[2])
            else:
                _, m, args, kw, inc, k = c
                instr = getattr(e, m)(*args, **kw)
                if inc is not None:
                    instr.then_inc(inc.h, k)


class Q:

    def __init__(self, name, nslots):
        self.slots = [Sem(f"{name}{i}") for i in range(nslots)]
        self.i = 0

    def dma(self, rec, out, in_):
        sm = self.slots[self.i % len(self.slots)]
        self.i += 1
        rec.wait(sm, sm.n)
        rec.op("dma_start", out=out, in_=in_, inc=sm, inc_k=16)
        return (sm, sm.n)



def build(debug=()):
    nc = bass.Bass("TRN2", target_bir_lowering=False)

    def inp(name, shape, dtype=F32):
        return nc.dram_tensor(name, shape, dtype, kind="ExternalInput").ap()

    x_own = inp("x_own", [T, D])
    w_qkv = inp("w_qkv", [D, 3 * D])
    w_out = inp("w_out", [D, D])
    gate_wT = inp("gate_wT", [D, NE])
    w_upT = inp("w_upT", [D, HFF])
    w_gateT = inp("w_gateT", [D, HFF])
    w_downT = inp("w_downT", [HFF, D])
    upA_T = inp("upA_T", [D, ER])
    gateA_T = inp("gateA_T", [D, ER])
    dnA_T = inp("dnA_T", [HFF, ER])
    upB_T = inp("upB_T", [ER, HFF])
    gateB_T = inp("gateB_T", [ER, HFF])
    dnB_T = inp("dnB_T", [ER, D])
    mask512 = inp("mask512", [128, 512])

    out_own = nc.dram_tensor("out_own", [T, D], F32, kind="ExternalOutput").ap()

    kv_in_k = nc.dram_tensor("kv_in_k", [HEADS * HD, T], BF16, kind="Internal").ap()
    kv_in_v = nc.dram_tensor("kv_in_v", [T, HEADS * HD], BF16, kind="Internal").ap()
    kv_out_k = nc.dram_tensor("kv_out_k", [4 * HEADS * HD, T], BF16, kind="Internal").ap()
    kv_out_v = nc.dram_tensor("kv_out_v", [4 * T, HEADS * HD], BF16, kind="Internal").ap()
    kn_in = nc.dram_tensor("kn_in", [HEADS, T], F32, kind="Internal").ap()
    kn_out = nc.dram_tensor("kn_out", [4 * HEADS, T], F32, kind="Internal").ap()
    lg_in = nc.dram_tensor("lg_in", [T, NE], F32, kind="Internal").ap()
    lg_out = nc.dram_tensor("lg_out", [4 * T, NE], F32, kind="Internal").ap()
    GROUPS = [[0, 1, 2, 3], [4, 5, 6, 7]]

    identb_d = nc.inline_tensor(np.eye(128, dtype=ml_dtypes.bfloat16), name="c_identb").ap()
    identf_d = nc.inline_tensor(np.eye(128, dtype=np.float32), name="c_identf").ap()
    onesf_d = nc.inline_tensor(np.ones((1, 128), dtype=np.float32), name="c_onesf").ap()
    onescol_d = nc.inline_tensor(np.ones((128, 1), dtype=ml_dtypes.bfloat16), name="c_onescol").ap()
    negonesf_d = nc.inline_tensor(np.full((1, 128), -1.0, dtype=np.float32), name="c_negonesf").ap()
    blk_np = np.zeros((NE, ER), dtype=ml_dtypes.bfloat16)
    for e_ in range(NE):
        blk_np[e_, e_ * RK:(e_ + 1) * RK] = 2.0
    blk2_d = nc.inline_tensor(blk_np, name="c_blk2").ap()

    dbg_out = {}
    for name, shape, dt in [
        ("d_lnfm", [128, NDC * T], F32),
        ("d_q", [128, HEADS * T], F32),
        ("d_kvk", [HEADS * HD, T], F32),
        ("d_kvv", [T, HEADS * HD], F32),
        ("d_attnout", [128, HEADS * T], F32),
        ("d_x2", [128, NTC * D], F32),
        ("d_ln2fm", [128, NDC * T], F32),
        ("d_logits", [T, NE], F32),
        ("d_gstats", [128, 64], F32),
        ("d_mixed", [128, NHC * 512], F32),
        ("d_ffntm", [128, 4 * D], F32),
    ]:
        if name in debug:
            dbg_out[name] = nc.dram_tensor(name, shape, dt, kind="ExternalOutput").ap()

    ts = Sem("ts")
    vs = Sem("vs")
    ss = Sem("ss")
    cskv = Sem("cskv")
    cslg = Sem("cslg")
    csn = Sem("csn")
    Qg = Q("gq", 8)
    Qy = Q("sq", 8)
    SEMS = [ts, vs, ss, cskv, cslg, csn] + Qg.slots + Qy.slots

    rt, rv, rs, rg, ry = R("tensor"), R("vector"), R("scalar"), R("gpsimd"), R("sync")
    ALLR = [rt, rv, rs, rg, ry]

    def barrier():
        for r in ALLR:
            for sm in SEMS:
                r.wait(sm)

    st = ExitStack()
    with st:
        def sb(name, shape, dtype):
            return st.enter_context(nc.sbuf_tensor(name, shape, dtype))

        def psum(name, shape, dtype=F32):
            return st.enter_context(nc.psum_tensor(name, shape, dtype))

        for sm in SEMS:
            sm.h = st.enter_context(nc.semaphore(sm.name))

        x_res = sb("x_res", [128, NTC * D], F32)
        fm_buf = sb("fm_buf", [128, NDC * T], BF16)
        statscr = sb("statscr", [128, 48], F32)
        lgtm_sb = sb("lgtm_sb", [128, NTC * NE], F32)
        identb = sb("identb", [128, 128], BF16)
        identf = sb("identf", [128, 128], F32)
        onesf = sb("onesf", [1, 128], F32)
        onescol = sb("onescol", [128, 1], BF16)
        negonesf = sb("negonesf", [1, 128], F32)
        blk2 = sb("blk2", [NE, ER], BF16)
        mask_sb = sb("mask_sb", [128, 512], F32)

        fm_v = fm_buf[:].rearrange("p (c t) -> p c t", c=NDC)
        stQ = ExitStack()
        q_all = stQ.enter_context(nc.sbuf_tensor("q_all", [128, HEADS * T], BF16))
        qn_sb = stQ.enter_context(nc.sbuf_tensor("qn_sb", [128, 128], F32))

        tok_x = Qy.dma(ry, x_res[:].rearrange("p (j d) -> p j d", j=NTC),
                       x_own.rearrange("(j p) d -> p j d", p=128))
        tok_c = {}
        for nm, dst, src in [("identb", identb[:], identb_d),
                             ("identf", identf[:], identf_d),
                             ("onesf", onesf[:], onesf_d),
                             ("onescol", onescol[:], onescol_d),
                             ("negonesf", negonesf[:], negonesf_d),
                             ("blk2", blk2[:], blk2_d),
                             ("mask", mask_sb[:], mask512)]:
            tok_c[nm] = Qy.dma(ry, dst, src)

        def ln_stats(xin):
            bnbuf = statscr[:, 16:40]
            mv = statscr[:, 40:42]
            mean, rstd = statscr[:, 40:41], statscr[:, 44:45]
            rv.wait(vs)
            rv.wait(ss)
            for c_ in range(4):
                rv.op("bn_stats", out=bnbuf[:, c_ * 6:(c_ + 1) * 6],
                      in_=xin[:, c_ * 512:(c_ + 1) * 512], inc=vs)
            rv.wait(vs)
            rv.op("bn_aggr", out=mv, in_=bnbuf, inc=vs)
            rv.wait(vs)
            rv.op("tensor_scalar", statscr[:, 42:43], statscr[:, 41:42], EPS, None,
                  ALU.add, inc=vs)
            rs.wait(vs)
            rs.op("activation", statscr[:, 43:44], statscr[:, 42:43], AF.Ln, inc=ss)
            rs.wait(ss)
            rs.op("activation", rstd, statscr[:, 43:44], AF.Exp, scale=-0.5, inc=ss)
            return mean, rstd

        stA = ExitStack()
        sbA = lambda n_, sh, dt: stA.enter_context(nc.sbuf_tensor(n_, sh, dt))
        psA = lambda n_, sh, dt=F32: stA.enter_context(nc.psum_tensor(n_, sh, dt))
        lntm = sbA("lntm", [128, 2 * D], BF16)
        wqkv_sb = sbA("wqkv_sb", [128, 3 * NDC * 512], BF16)
        kvstage = sbA("kvstage", [128, 4 * 512], BF16)
        sqst = sbA("sqst", [128, 2 * 512], BF16)
        kn_row = sbA("kn_row", [1, 512], F32)
        qn_row = sbA("qn_row", [1, 512], F32)
        p_lt = [psA(f"p_lt{i}", [128, 1024], BF16) for i in range(2)]
        p_mm = [psA(f"p_mm{i}", [128, 512], F32) for i in range(4)]
        p_nr = psA("p_nr", [1, 512])
        p_qt = psA("p_qt", [128, 4])
        nstate = {"sq": [0, 0], "nr": None, "qt": 0, "knrow": None, "qnrow": 0}

        def norms_from(src_bf, h_, half_, is_q):
            p2 = (h_ * 2 + half_) % 2
            rs.wait(ts, nstate["sq"][p2])
            rs.wait(ss)
            rs.op("activation", sqst[:, p2 * 512:(p2 + 1) * 512], src_bf,
                  AF.Square, inc=ss)
            sq_ss = ss.n
            rt.wait(ss, sq_ss)
            rt.wait_tok(nstate["nr"])
            rt.wait_tok(tok_c["onescol"])
            rt.op("matmul", p_nr[0:1, :], onescol[:],
                  sqst[:, p2 * 512:(p2 + 1) * 512], start=True, stop=True, inc=ts)
            nstate["sq"][p2] = ts.n
            nr_ts = ts.n
            if is_q:
                rv.wait(ts, nr_ts)
                rv.wait(ts, nstate["qnrow"])
                rv.op("tensor_copy", out=qn_row[0:1, :], in_=p_nr[0:1, :], inc=vs)
                nstate["nr"] = (vs, vs.n)
                rt.wait(vs, vs.n)
                rt.wait(vs, nstate["qt"])
                for i4 in range(4):
                    rt.op("transpose", p_qt[:, i4:i4 + 1],
                          qn_row[0:1, i4 * 128:(i4 + 1) * 128],
                          identf[0:1, 0:1], inc=(ts if i4 == 3 else None))
                nstate["qnrow"] = ts.n
                rv.wait(ts, ts.n)
                rv.op("tensor_copy",
                      out=qn_sb[:, h_ * 8 + half_ * 4:h_ * 8 + half_ * 4 + 4],
                      in_=p_qt[:], inc=vs)
                nstate["qt"] = vs.n
            else:
                rv.wait(ts, nr_ts)
                rv.wait_tok(nstate["knrow"])
                rv.op("tensor_copy", out=kn_row[0:1, :], in_=p_nr[0:1, :], inc=vs)
                nstate["nr"] = (vs, vs.n)
                ry.wait(vs)
                nstate["knrow"] = Qy.dma(
                    ry, kn_in[h_:h_ + 1, half_ * 512:(half_ + 1) * 512],
                    kn_row[0:1, :])


        lt_done_ts = [0, 0]
        for j in range(NTC):
            par = j % 2
            xin = x_res[:, j * D:(j + 1) * D]
            rv.wait_tok(tok_x)
            rs.wait_tok(tok_x)
            mean, rstd = ln_stats(xin)
            rv.wait(ss)
            rv.wait(ts, lt_done_ts[par])
            rv.op("tensor_scalar", lntm[:, par * D:(par + 1) * D], xin,
                  mean, rstd, ALU.subtract, op1=ALU.mult, inc=vs)
            rt.wait(vs)
            rt.wait_tok(tok_c["identb"])
            for k in range(NDC):
                bank = p_lt[k // 8]
                rt.op("transpose", bank[:, (k % 8) * 128:(k % 8 + 1) * 128],
                      lntm[:, par * D + k * 128: par * D + (k + 1) * 128],
                      identb[:], inc=(ts if k % 8 == 7 else None))
            lt_done_ts[par] = ts.n
            rv.wait(ts)
            for half in range(2):
                rv.op("tensor_copy",
                      out=fm_v[:, half * 8:(half + 1) * 8, j * 128:(j + 1) * 128],
                      in_=p_lt[half][:].rearrange("p (c t) -> p c t", c=8),
                      inc=vs)
        lnfm_done_vs = vs.n

        if "d_lnfm" in dbg_out:
            rg.wait(vs)
            Qg.dma(rg, dbg_out["d_lnfm"], fm_buf[:])

        wq_v = wqkv_sb[:].rearrange("p (b c n) -> p b c n", b=3, c=NDC)
        grp_consumed_ts = [0, 0, 0]
        bank_free = [0, 0, 0, 0]
        stage_free = [None, None, None, None]

        def mm_to_bank(bi, lhsT_rhs_list):
            rt.wait(ss, bank_free[bi])
            n = len(lhsT_rhs_list)
            for i, (lt_, rh_) in enumerate(lhsT_rhs_list):
                rt.op("matmul", p_mm[bi][:], lt_, rh_,
                      start=(i == 0), stop=(i == n - 1),
                      inc=(ts if i == n - 1 else None))
            return ts.n

        ag_launched = {}
        for gi, g in enumerate([4, 5, 6, 7, 8, 9, 10, 11, 0, 1, 2, 3]):
            par = gi % 3
            rg.wait(ts, grp_consumed_ts[par])
            tok_g = Qg.dma(rg, wq_v[:, par],
                           w_qkv[:, g * 512:(g + 1) * 512].rearrange("(c p) n -> p c n", p=128))
            rt.wait_tok(tok_g)
            rt.wait(vs, lnfm_done_vs)
            if g < 8:
                for c in range(4):
                    for half in range(2):
                        bi = 2 * (c % 2) + half
                        mm_done = mm_to_bank(bi, [
                            (wq_v[:, par, k, c * 128:(c + 1) * 128],
                             fm_v[:, k, half * 512:(half + 1) * 512])
                            for k in range(NDC)])
                        col = g * 512 + c * 128
                        rs.wait(ts, mm_done)
                        if g < 4:
                            h_ = col // 128
                            qa = q_all[:, h_ * T + half * 512:h_ * T + half * 512 + 512]
                            rs.op("copy", out=qa, in_=p_mm[bi][:], inc=ss)
                            norms_from(qa, h_, half, True)
                        else:
                            si = bi
                            stg = kvstage[:, si * 512:(si + 1) * 512]
                            rs.wait_tok(stage_free[si])
                            rs.op("copy", out=stg, in_=p_mm[bi][:], inc=ss)
                            h_ = (col - 2048) // 128
                            ry.wait(ss)
                            stage_free[si] = Qy.dma(
                                ry,
                                kv_in_k[h_ * 128:(h_ + 1) * 128,
                                        half * 512:(half + 1) * 512],
                                stg)
                            norms_from(stg, h_, half, False)
                        bank_free[bi] = ss.n
            else:
                for tc in range(NTC):
                    bi = 2 * (tc % 2)
                    mm_done = mm_to_bank(bi, [
                        (fm_v[:, k, tc * 128:(tc + 1) * 128], wq_v[:, par, k, :])
                        for k in range(NDC)])
                    si = tc % 4
                    stg = kvstage[:, si * 512:(si + 1) * 512]
                    rs.wait(ts, mm_done)
                    rs.wait_tok(stage_free[si])
                    rs.op("copy", out=stg, in_=p_mm[bi][:], inc=ss)
                    bank_free[bi] = ss.n
                    vcol = (g - 8) * 512
                    ry.wait(ss)
                    stage_free[si] = Qy.dma(
                        ry, kv_in_v[tc * 128:(tc + 1) * 128, vcol:vcol + 512], stg)
            grp_consumed_ts[par] = ts.n
            if g == 7:
                for sm in Qy.slots:
                    rg.wait(sm)
                rg.op("collective_compute", "AllGather", ALU.bypass,
                      replica_groups=GROUPS, ins=[kv_in_k], outs=[kv_out_k],
                      inc=cskv)
                ag_launched["k"] = (cskv, cskv.n)
                rg.op("collective_compute", "AllGather", ALU.bypass,
                      replica_groups=GROUPS, ins=[kn_in], outs=[kn_out],
                      inc=csn)
                ag_launched["kn"] = (csn, csn.n)
            elif g == 11:
                for sm in Qy.slots:
                    rg.wait(sm)
                rg.op("collective_compute", "AllGather", ALU.bypass,
                      replica_groups=GROUPS, ins=[kv_in_v], outs=[kv_out_v],
                      inc=cskv)
                ag_launched["v"] = (cskv, cskv.n)

        if "d_q" in dbg_out:
            rg.wait(ss)
            Qg.dma(rg, dbg_out["d_q"], q_all[:])

        kv_gathered = (cskv, cskv.n)

        if "d_kvk" in dbg_out:
            rg.wait_tok(kv_gathered)
            Qg.dma(rg, dbg_out["d_kvk"], kv_in_k)
        if "d_kvv" in dbg_out:
            rg.wait_tok(kv_gathered)
            Qg.dma(rg, dbg_out["d_kvv"], kv_in_v)

        barrier()
        stA.close()

        stB = ExitStack()
        sbB = lambda n_, sh, dt: stB.enter_context(nc.sbuf_tensor(n_, sh, dt))
        psB = lambda n_, sh, dt=F32: stB.enter_context(nc.psum_tensor(n_, sh, dt))
        kh_sb = sbB("kh_sb", [128, 2 * 8 * 512], BF16)
        vh_sb = sbB("vh_sb", [128, 2 * 8 * 512], BF16)
        pb = [sbB("pb0", [128, 4096], BF16), sbB("pb1", [128, 4096], BF16)]
        pt_sb = sbB("pt_sb", [128, 2 * 1024], BF16)
        dscr = sbB("dscr", [128, 512], F32)
        kn_all = sbB("kn_all", [64, 1024], F32)
        kncol = sbB("kncol", [64, 1], F32)
        kmrow = sbB("kmrow", [1, 80], F32)
        rzrow_sb = sbB("rzrow_sb", [1, 256], F32)
        rzb_sb = sbB("rzb_sb", [128, 256], F32)
        p_s = [psB(f"p_s{i}", [128, 512]) for i in range(3)]
        p_pt = [psB(f"p_pt{i}", [128, 1024], BF16) for i in range(2)]
        p_o = [psB(f"p_o{i}", [128, 256]) for i in range(2)]
        p_misc = psB("p_misc", [128, 256])

        kh_v = kh_sb[:].rearrange("p (b j f t) -> p b j f t", b=2, j=8, f=4)
        vh_v = vh_sb[:].rearrange("p (b j f t) -> p b j f t", b=2, j=8, f=4)
        km2col = statscr[:, 0:1]
        ztile = statscr[:, 11:13]
        rzt = statscr[:, 13:15]
        negb8 = statscr[:, 16:24]
        zp = statscr[:, 24:40]

        khb_consumed = [0, 0]
        kv_tok = [None, None]
        knh_free = [0, 0]
        ptbank_free_s = [0, 0]
        sbank_free = [None, None, None]
        sbank_i = 0
        ptbank_free = [0, 0]
        o_free = [0, 0]
        pb_free = [0, 0]
        exp_ss = 0

        ry.wait_tok(ag_launched["kn"])
        tok_kn = Qy.dma(ry, kn_all[0:64, :], kn_out)
        rv.wait_tok(tok_kn)
        rv.op("tensor_reduce", kncol[0:64, :], kn_all[0:64, :], AX.X, ALU.max,
              inc=vs)
        rt.wait(vs)
        rt.op("transpose", p_misc[0:1, 0:64], kncol[0:64, :],
              identf[0:64, 0:64], inc=ts)
        rv.wait(ts)
        rv.op("tensor_copy", out=kmrow[0:1, 0:64], in_=p_misc[0:1, 0:64], inc=vs)
        rv.wait(vs)
        rv.op("tensor_tensor", kmrow[0:1, 64:80], kmrow[0:1, 0:16],
              kmrow[0:1, 16:32], ALU.max, inc=vs)
        rv.wait(vs)
        rv.op("tensor_tensor", kmrow[0:1, 64:80], kmrow[0:1, 64:80],
              kmrow[0:1, 32:48], ALU.max, inc=vs)
        rv.wait(vs)
        rv.op("tensor_tensor", kmrow[0:1, 64:80], kmrow[0:1, 64:80],
              kmrow[0:1, 48:64], ALU.max, inc=vs)
        km_vs = vs.n
        for h in range(HEADS):
            hb = h % 2
            ry.wait_tok(ag_launched["k"])
            ry.wait_tok(ag_launched["v"])
            ry.wait(ts, khb_consumed[hb])
            toks = []
            for r_ in range(4):
                toks.append(Qy.dma(
                    ry, kh_v[:, hb, :, r_, :],
                    kv_out_k[r_ * 2048 + h * 128:r_ * 2048 + (h + 1) * 128, :]
                    .rearrange("p (j t) -> p j t", j=8)))
                toks.append(Qy.dma(
                    ry, vh_v[:, hb, :, r_, :],
                    kv_out_v[r_ * T:(r_ + 1) * T, h * 128:(h + 1) * 128]
                    .rearrange("(j p) t -> p j t", p=128)))
            kv_tok[hb] = toks
            rv.wait(ss)
            rt.wait(vs, km_vs)
            rt.op("matmul", p_misc[:, 0:1], onesf[0:1, :],
                  kmrow[0:1, 64 + h:65 + h], start=True, stop=True, inc=ts)
            rv.wait(ts)
            rv.op("tensor_copy", out=km2col, in_=p_misc[:, 0:1], inc=vs)
            rv.wait(vs)
            rv.op("tensor_scalar", negb8, qn_sb[:, h * 8:(h + 1) * 8],
                  km2col, -RSQ / 2.0, ALU.add, op1=ALU.mult, inc=vs)
            negb_vs = vs.n

            for pj in (0, 2, 4, 6):
                for ti, jt in enumerate((pj, pj + 1)):
                    rl = jt + 1
                    pbuf = pb[ti]
                    if ti == 0:
                        rv.wait(ts, pb_free[0])
                        rv.op("memset", pbuf[:, rl * 512:(rl + 1) * 512], 0.0,
                              inc=vs)
                    for lv in range(rl):
                        bi = sbank_i % 3
                        sbank_i += 1
                        rt.wait_tok(sbank_free[bi])
                        for tk in kv_tok[hb]:
                            rt.wait_tok(tk)
                        rt.op("matmul", p_s[bi][:],
                              q_all[:, h * T + jt * 128:h * T + (jt + 1) * 128],
                              kh_sb[:, hb * 4096 + lv * 512:hb * 4096 + (lv + 1) * 512],
                              start=True, stop=True, inc=ts)
                        mm_done = ts.n
                        dst = pbuf[:, lv * 512:(lv + 1) * 512]
                        zcol = zp[:, ti * 8 + lv:ti * 8 + lv + 1]
                        if lv == jt:
                            rv.wait(ts, mm_done)
                            rv.wait_tok(tok_c["mask"])
                            rv.wait(ss)
                            rv.op("tensor_tensor", dscr[:], p_s[bi][:],
                                  mask_sb[:], ALU.add, inc=vs)
                            sbank_free[bi] = (vs, vs.n)
                            rs.wait(vs)
                            rs.wait(ts, pb_free[ti])
                            rs.op("activation", dst, dscr[:], AF.Exp,
                                  bias=negb8[:, jt:jt + 1], scale=RSQ,
                                  accum_out=zcol, inc=ss)
                        else:
                            rs.wait(ts, mm_done)
                            rs.wait(vs, negb_vs)
                            rs.wait(ts, pb_free[ti])
                            rs.op("activation", dst, p_s[bi][:], AF.Exp,
                                  bias=negb8[:, jt:jt + 1], scale=RSQ,
                                  accum_out=zcol, inc=ss)
                            sbank_free[bi] = (ss, ss.n)
                    rv.wait(ss)
                    rv.op("tensor_reduce", ztile[:, ti:ti + 1],
                          zp[:, ti * 8:ti * 8 + rl], AX.X, ALU.add, inc=vs)
                    rv.wait(vs)
                    rv.op("reciprocal", rzt[:, ti:ti + 1], ztile[:, ti:ti + 1],
                          inc=vs)
                ptilde_vs = vs.n
                ptilde_vs = vs.n

                rv_rz = vs.n
                rt.wait(vs, rv_rz)
                for ti in range(2):
                    rt.op("transpose", p_misc[0:1, ti * 128:(ti + 1) * 128],
                          rzt[:, ti:ti + 1], identf[:],
                          inc=(ts if ti == 1 else None))
                rv.wait(ts)
                rv.op("tensor_copy", out=rzrow_sb[0:1, :], in_=p_misc[0:1, 0:256],
                      inc=vs)
                rzrow_vs = vs.n
                rt.wait(vs, rzrow_vs)
                rt.op("matmul", p_misc[:], onesf[0:1, :], rzrow_sb[0:1, :],
                      start=True, stop=True, inc=ts)
                rv.wait(ts)
                rv.op("tensor_copy", out=rzb_sb[:], in_=p_misc[:], inc=vs)
                rzb_vs = vs.n

                opar = (pj // 2) % 2
                rt.wait(vs, o_free[opar])
                tr_last = 0
                for lv in range(pj + 2):
                    ppar = lv % 2
                    rt.wait(vs, ptilde_vs)
                    rt.wait(vs, ptbank_free[ppar])
                    rt.wait(ss, ptbank_free_s[ppar])
                    for q4 in range(4):
                        for ti in range(2):
                            rt.op("transpose",
                                  p_pt[ppar][:, (q4 * 2 + ti) * 128:(q4 * 2 + ti + 1) * 128],
                                  pb[ti][:, (lv * 4 + q4) * 128:(lv * 4 + q4 + 1) * 128],
                                  identb[:],
                                  inc=(ts if (q4 == 3 and ti == 1) else None))
                    tr_done = ts.n
                    tr_last = tr_done
                    if lv % 2 == 0:
                        rv.wait(ts, tr_done)
                        rv.op("tensor_copy", out=pt_sb[:, ppar * 1024:(ppar + 1) * 1024],
                              in_=p_pt[ppar][:], inc=vs)
                        ptbank_free[ppar] = vs.n
                        rt.wait(vs, ptbank_free[ppar])
                    else:
                        rs.wait(ts, tr_done)
                        rs.op("copy", out=pt_sb[:, ppar * 1024:(ppar + 1) * 1024],
                              in_=p_pt[ppar][:], inc=ss)
                        ptbank_free_s[ppar] = ss.n
                        rt.wait(ss, ptbank_free_s[ppar])
                    for q4 in range(4):
                        rt.op("matmul", p_o[opar][:],
                              vh_sb[:, hb * 4096 + (lv * 4 + q4) * 128:
                                    hb * 4096 + (lv * 4 + q4 + 1) * 128],
                              pt_sb[:, ppar * 1024 + q4 * 256:ppar * 1024 + (q4 + 1) * 256],
                              start=(lv == 0 and q4 == 0),
                              stop=(lv == pj + 1 and q4 == 3),
                              inc=(ts if (lv == pj + 1 and q4 == 3) else None))
                pb_free[0] = pb_free[1] = tr_last
                av_done = ts.n
                rv.wait(ts, av_done)
                rv.wait(vs, rzb_vs)
                rv.op("tensor_tensor",
                      fm_buf[:, h * T + pj * 128:h * T + (pj + 2) * 128],
                      p_o[opar][:], rzb_sb[:], ALU.mult, inc=vs)
                o_free[opar] = vs.n
            khb_consumed[hb] = ts.n

        if "d_attnout" in dbg_out:
            rg.wait(vs)
            Qg.dma(rg, dbg_out["d_attnout"], fm_buf[:])

        barrier()
        stB.close()
        stQ.close()

        stC = ExitStack()
        sbC = lambda n_, sh, dt: stC.enter_context(nc.sbuf_tensor(n_, sh, dt))
        psC = lambda n_, sh, dt=F32: stC.enter_context(nc.psum_tensor(n_, sh, dt))
        wout_sb = sbC("wout_sb", [128, NDC * D], BF16)
        rowscr2 = sbC("rowscr2", [128, 2 * D], F32)
        ln2f32 = sbC("ln2f32", [128, 2 * 512], F32)
        lgfm_sb = sbC("lgfm_sb", [NE, T], F32)
        gwT_sb = sbC("gwT_sb", [128, NDC * NE], F32)
        p_o2 = [psC(f"p_o2{i}", [128, 512]) for i in range(4)]
        p_t = [psC(f"p_t{i}", [128, 512]) for i in range(2)]
        p_g = psC("p_g", [NE, 128])
        p_x8 = psC("p_x8", [128, NE])

        wout_v = wout_sb[:].rearrange("p (c n) -> p c n", c=NDC)
        gwT_v = gwT_sb[:].rearrange("p (c e) -> p c e", c=NDC)

        tok_wout = Qg.dma(rg, wout_v, w_out.rearrange("(c p) n -> p c n", p=128))
        tok_gwt = Qy.dma(ry, gwT_v, gate_wT.rearrange("(c p) e -> p c e", p=128))

        attnout_vs = vs.n
        o2bank_free = [0, 0, 0, 0]
        for j in range(NTC):
            par = j % 2
            for q in range(4):
                rt.wait(vs, o2bank_free[q])
                rt.wait_tok(tok_wout)
                for c in range(NDC):
                    rt.op("matmul", p_o2[q][:],
                          fm_buf[:, c * T + j * 128:c * T + (j + 1) * 128],
                          wout_v[:, c, q * 512:(q + 1) * 512],
                          start=(c == 0), stop=(c == NDC - 1),
                          inc=(ts if c == NDC - 1 else None))
            mm_done = ts.n
            rv.wait(ts, mm_done)
            for q in range(4):
                rv.op("tensor_copy", out=rowscr2[:, par * D + q * 512:par * D + (q + 1) * 512],
                      in_=p_o2[q][:], inc=vs)
                o2bank_free[q] = vs.n
            row = rowscr2[:, par * D:(par + 1) * D]
            mean, rstd = ln_stats(row)
            rv.wait(ss)
            rv.op("tensor_scalar", row, row, mean, rstd,
                  ALU.subtract, op1=ALU.mult, inc=vs)
            rv.wait(vs)
            rv.op("tensor_tensor", x_res[:, j * D:(j + 1) * D],
                  x_res[:, j * D:(j + 1) * D], row, ALU.add, inc=vs)
        x2_done_vs = vs.n
        wout_used_ts = ts.n

        if "d_x2" in dbg_out:
            rg.wait(vs)
            Qg.dma(rg, dbg_out["d_x2"], x_res[:])

        pg_free = 0
        ln2f_free = [0, 0]
        for j in range(NTC):
            par = j % 2
            row = rowscr2[:, par * D:(par + 1) * D]
            rv.wait(vs, x2_done_vs)
            mean, rstd = ln_stats(x_res[:, j * D:(j + 1) * D])
            rv.wait(ss)
            rv.wait(ts)
            rv.op("tensor_scalar", row, x_res[:, j * D:(j + 1) * D], mean, rstd,
                  ALU.subtract, op1=ALU.mult, inc=vs)
            row_vs = vs.n
            for k4 in range(4):
                tpar = k4 % 2
                rt.wait(vs, row_vs)
                rt.wait(ss, ln2f_free[tpar])
                rt.wait(ts, wout_used_ts)
                for i in range(4):
                    rt.op("transpose", p_t[tpar][:, i * 128:(i + 1) * 128],
                          rowscr2[:, par * D + (4 * k4 + i) * 128:
                                  par * D + (4 * k4 + i + 1) * 128],
                          identf[:], inc=(ts if i == 3 else None))
                tr_done = ts.n
                rv.wait(ts, tr_done)
                rv.op("tensor_copy", out=ln2f32[:, tpar * 512:(tpar + 1) * 512],
                      in_=p_t[tpar][:], inc=vs)
                l32_vs = vs.n
                rs.wait(vs, l32_vs)
                rs.op("copy",
                      out=fm_v[:, 4 * k4:4 * k4 + 4, j * 128:(j + 1) * 128],
                      in_=ln2f32[:, tpar * 512:(tpar + 1) * 512]
                      .rearrange("p (c t) -> p c t", c=4), inc=ss)
                ln2f_free[tpar] = ss.n
                rt.wait(vs, l32_vs)
                rt.wait_tok(tok_gwt)
                if k4 == 0:
                    rt.wait(vs, pg_free)
                for i in range(4):
                    k = 4 * k4 + i
                    rt.op("matmul", p_g[:],
                          gwT_v[:, k, :],
                          ln2f32[:, tpar * 512 + i * 128:tpar * 512 + (i + 1) * 128],
                          start=(k == 0), stop=(k == NDC - 1),
                          inc=(ts if k == NDC - 1 else None))
            gmm_done = ts.n
            rv.wait(ts, gmm_done)
            rv.op("tensor_copy", out=lgfm_sb[0:NE, j * 128:(j + 1) * 128],
                  in_=p_g[:], inc=vs)
            pg_free = vs.n
        ln2_done_ss = ss.n

        if "d_ln2fm" in dbg_out:
            rg.wait(ss)
            Qg.dma(rg, dbg_out["d_ln2fm"], fm_buf[:])

        for j in range(NTC):
            rt.wait(vs)
            rt.op("transpose", p_x8[:], lgfm_sb[0:NE, j * 128:(j + 1) * 128],
                  identf[0:NE, 0:NE], inc=ts)
            rv.wait(ts)
            rv.op("tensor_copy", out=lgtm_sb[:, j * NE:(j + 1) * NE],
                  in_=p_x8[:], inc=vs)
        rv_lgtm = vs.n
        ry.wait(vs, rv_lgtm)
        tok_lgin = Qy.dma(ry, lg_in.rearrange("(c p) e -> p c e", p=128),
                          lgtm_sb[:].rearrange("p (c e) -> p c e", c=NTC))
        rg.wait_tok(tok_lgin)
        rg.op("collective_compute", "AllGather", ALU.bypass,
              replica_groups=GROUPS, ins=[lg_in], outs=[lg_out], inc=cslg)
        lg_gathered = (cslg, cslg.n)

        if "d_logits" in dbg_out:
            rg.wait_tok(lg_gathered)
            Qg.dma(rg, dbg_out["d_logits"], lg_in)

        barrier()
        stC.close()

        stD = ExitStack()
        sbD = lambda n_, sh, dt: stD.enter_context(nc.sbuf_tensor(n_, sh, dt))
        mixed = sbD("mixed", [128, NHC * 512], BF16)
        dnB_sb = sbD("dnB_sb", [128, D], BF16)
        wkb = sbD("wkb", [128, 2 * 512], F32)
        mt4 = sbD("mt4", [128, 4 * 512], BF16)
        rtld = sbD("rtld", [128, 4 * 512], BF16)
        upstr = sbD("upstr", [128, 2 * 4352], BF16)
        dnA_sb3 = sbD("dnA_sb3", [128, 3 * 128], BF16)
        wdnstr = sbD("wdnstr", [128, 2 * D], BF16)
        up_v = upstr[:].rearrange("p (b n) -> p b n", b=2)

        tok_dnB = Qg.dma(rg, dnB_sb[:], dnB_T)

        stG = ExitStack()
        sbG = lambda n_, sh, dt: stG.enter_context(nc.sbuf_tensor(n_, sh, dt))
        psG = lambda n_, sh, dt=F32: stG.enter_context(nc.psum_tensor(n_, sh, dt))
        lgt = sbG("lgt", [128, 32 * NE], F32)
        tscr = sbG("tscr", [128, 16 * 32], F32)
        e_scr = sbG("e_scr", [128, 48], F32)
        ind_tm = sbG("ind_tm", [128, 2 * NTC * NE], BF16)
        indfm = sbG("indfm", [NE, 2 * T], BF16)
        wrow = sbG("wrow", [1, 160], F32)
        p_col = psG("p_col", [128, 16])
        p_row = psG("p_row", [1, 128])
        p_i8 = stG.enter_context(nc.psum_tensor("p_i8", [NE, 128], BF16))
        p_m512 = psG("p_m512", [128, 512])

        ry.wait_tok(lg_gathered)
        tok_lgt = Qy.dma(ry, lgt[:].rearrange("p (c e) -> p c e", e=NE),
                         lg_out.rearrange("(c p) e -> p c e", p=128))

        def tournament(view, n, m1, m2):
            cols = [view[:, :, i_:i_ + 1] for i_ in range(8)]
            scr = lambda i_: tscr[:, i_ * 32:i_ * 32 + n].unsqueeze(2)
            def tt(dst, a_, b_, op):
                rv.wait(vs)
                rv.op("tensor_tensor", dst, a_, b_, op, inc=vs)
            a = [scr(i_) for i_ in range(4)]
            iw = [scr(4 + i_) for i_ in range(4)]
            for i_ in range(4):
                tt(a[i_], cols[2 * i_], cols[2 * i_ + 1], ALU.max)
                tt(iw[i_], cols[2 * i_], cols[2 * i_ + 1], ALU.min)
            b1, j1, b2, j2, k1 = scr(8), scr(9), scr(10), scr(11), scr(12)
            tt(b1, a[0], a[1], ALU.max)
            tt(j1, a[0], a[1], ALU.min)
            tt(b2, a[2], a[3], ALU.max)
            tt(j2, a[2], a[3], ALU.min)
            tt(m1.unsqueeze(2), b1, b2, ALU.max)
            tt(k1, b1, b2, ALU.min)
            t1, t2, t3 = a[0], a[1], a[2]
            tt(t1, j1, j2, ALU.max)
            tt(t2, iw[0], iw[1], ALU.max)
            tt(t3, iw[2], iw[3], ALU.max)
            tt(j1, t1, k1, ALU.max)
            tt(j2, t2, t3, ALU.max)
            tt(m2.unsqueeze(2), j1, j2, ALU.max)

        m1g = tscr[:, 13 * 32:14 * 32]
        m2g = tscr[:, 14 * 32:15 * 32]
        rv.wait_tok(tok_lgt)
        tournament(lgt[:].rearrange("p (c e) -> p c e", e=NE), 32, m1g, m2g)
        m1o = e_scr[:, 16:24]
        m2o = e_scr[:, 24:32]
        tournament(lgtm_sb[:].rearrange("p (c e) -> p c e", e=NE), NE, m1o, m2o)

        for k in range(2):
            mg = m1g if k == 0 else m2g
            mo = m1o if k == 0 else m2o
            gred = statscr[:, 16:17]
            rv.wait(vs)
            rv.wait(ss)
            rv.op("tensor_reduce", gred, mg, AX.X, ALU.max, inc=vs)
            rt.wait(vs)
            rt.op("transpose", p_row[0:1, :], gred, identf[:], inc=ts)
            rv.wait(ts)
            rv.op("tensor_copy", out=wrow[0:1, 0:128], in_=p_row[0:1, :], inc=vs)
            rv.wait(vs)
            rv.op("tensor_reduce", wrow[0:1, 128:129], wrow[0:1, 0:128], AX.X,
                  ALU.max, inc=vs)
            gmax_vs = vs.n
            rt.wait(vs, gmax_vs)
            rt.op("matmul", p_col[:, 0:1], negonesf[0:1, :], wrow[0:1, 128:129],
                  start=True, stop=True, inc=ts)
            ngb = statscr[:, 17:18]
            rv.wait(ts)
            rv.op("tensor_copy", out=ngb, in_=p_col[:, 0:1], inc=vs)
            zc = statscr[:, 18:19]
            rs.wait(vs)
            rs.op("activation", tscr[:, 15 * 32:16 * 32], mg, AF.Exp,
                  bias=ngb, accum_out=zc, inc=ss)
            rt.wait(ss)
            rt.op("transpose", p_row[0:1, :], zc, identf[:], inc=ts)
            rv.wait(ts)
            rv.op("tensor_copy", out=wrow[0:1, 0:128], in_=p_row[0:1, :], inc=vs)
            rv.wait(vs)
            rv.op("tensor_reduce", wrow[0:1, 129:130], wrow[0:1, 0:128], AX.X,
                  ALU.add, inc=vs)
            rv.wait(vs)
            rv.op("reciprocal", wrow[0:1, 130:131], wrow[0:1, 129:130], inc=vs)
            rt.wait(vs)
            rt.op("matmul", p_col[:, 1:2], onesf[0:1, :], wrow[0:1, 130:131],
                  start=True, stop=True, inc=ts)
            rzb = statscr[:, 19:20]
            rv.wait(ts)
            rv.op("tensor_copy", out=rzb, in_=p_col[:, 1:2], inc=vs)
            eo = e_scr[:, 32:40]
            rs.wait(vs)
            rs.op("activation", eo, mo, AF.Exp, bias=ngb, inc=ss)
            rv.wait(ss)
            rv.op("tensor_scalar", statscr[:, k * 8:(k + 1) * 8], eo, rzb, None,
                  ALU.mult, inc=vs)
            for j in range(NTC):
                rv.wait(vs)
                rv.op("tensor_scalar",
                      ind_tm[:, (k * NTC + j) * NE:(k * NTC + j + 1) * NE],
                      lgtm_sb[:, j * NE:(j + 1) * NE],
                      mo[:, j:j + 1], None, ALU.is_equal, inc=vs)
                rt.wait(vs)
                rt.op("transpose", p_i8[:],
                      ind_tm[:, (k * NTC + j) * NE:(k * NTC + j + 1) * NE],
                      identb[:], inc=ts)
                rv.wait(ts)
                rv.op("tensor_copy",
                      out=indfm[0:NE, k * T + j * 128:k * T + (j + 1) * 128],
                      in_=p_i8[:], inc=vs)
        for half in range(2):
            for k in range(2):
                rt.wait(vs)
                rt.wait_tok(tok_c["blk2"])
                rt.op("matmul", p_m512[:], blk2[:],
                      indfm[0:NE, k * T + half * 512:k * T + (half + 1) * 512],
                      start=True, stop=True, inc=ts)
                rv.wait(ts)
                rv.op("tensor_copy",
                      out=mt4[:, (half * 2 + k) * 512:(half * 2 + k + 1) * 512],
                      in_=p_m512[:], inc=vs)
        barrier()
        stG.close()
        hscr = sbD("hscr", [128, 4 * 512], F32)
        chb = sbD("chb", [128, 2 * 512], BF16)
        rowscrD = sbD("rowscrD", [128, D], F32)
        ub_s = hscr[:, 0 * 512:1 * 512]
        gb_s = hscr[:, 1 * 512:2 * 512]
        su0_s = hscr[:, 2 * 512:3 * 512]
        su1_s = hscr[:, 3 * 512:4 * 512]

        wdn_v = wdnstr[:].rearrange("p (b d) -> p b d", b=2)
        bundle_used_ts = [0, 0]
        dn_used_ts = [0, 0]
        for half in range(2):
            hs = half * 512
            stP = ExitStack()
            p_r = stP.enter_context(nc.psum_tensor(f"p_r{half}", [128, 512], F32))
            p_wk = stP.enter_context(nc.psum_tensor(f"p_wk{half}", [128, 128], F32))
            p_wr = stP.enter_context(nc.psum_tensor(f"p_wr{half}", [1, 128], F32))
            wrow2 = stP.enter_context(nc.sbuf_tensor(f"wrow2{half}", [1, 128], F32))
            rg.wait(ts, dn_used_ts[0])
            rg.wait(ts, dn_used_ts[1])
            tok_upA = Qg.dma(rg, wdn_v[:, 0].rearrange("p (c e) -> p c e", c=NDC),
                             upA_T.rearrange("(c p) e -> p c e", p=128))
            tok_gtA = Qg.dma(rg, wdn_v[:, 1].rearrange("p (c e) -> p c e", c=NDC),
                             gateA_T.rearrange("(c p) e -> p c e", p=128))
            for k in range(2):
                for c4 in range(4):
                    tch = half * 4 + c4
                    rt.wait(vs)
                    rt.op("transpose", p_wr[0:1, :],
                          statscr[:, k * 8 + tch:k * 8 + tch + 1], identf[:], inc=ts)
                    rv.wait(ts)
                    rv.op("tensor_copy", out=wrow2[0:1, :], in_=p_wr[0:1, :], inc=vs)
                    rv_w = vs.n
                    rt.wait(vs, rv_w)
                    rt.op("matmul", p_wk[:], onesf[0:1, :], wrow2[0:1, :],
                          start=True, stop=True, inc=ts)
                    rv.wait(ts)
                    rv.op("tensor_copy", out=wkb[:, k * 512 + c4 * 128:
                                                 k * 512 + (c4 + 1) * 128],
                          in_=p_wk[:], inc=vs)
            for side in range(2):
                rt.wait_tok(tok_upA if side == 0 else tok_gtA)
                rt.wait(vs)
                for kd in range(NDC):
                    rt.op("matmul", p_r[:],
                          wdn_v[:, side, kd * 128:(kd + 1) * 128],
                          fm_v[:, kd, hs:hs + 512],
                          start=(kd == 0), stop=(kd == NDC - 1),
                          inc=(ts if kd == NDC - 1 else None))
                r_done = ts.n
                for k in range(2):
                    rv.wait(ts, r_done)
                    rv.op("tensor_tensor", rtld[:, (side * 2 + k) * 512:
                                                 (side * 2 + k + 1) * 512],
                          p_r[:], mt4[:, (half * 2 + k) * 512:(half * 2 + k + 1) * 512],
                          ALU.mult, inc=vs)
            rtld_vs = vs.n
            dn_used_ts[0] = dn_used_ts[1] = ts.n
            stP.close()

            stH = ExitStack()
            p_u0 = [stH.enter_context(nc.psum_tensor(f"p_u0{half}{i}", [128, 512], F32))
                    for i in range(2)]
            p_g0 = [stH.enter_context(nc.psum_tensor(f"p_g0{half}{i}", [128, 512], F32))
                    for i in range(2)]
            p_loU = stH.enter_context(nc.psum_tensor(f"p_loU{half}", [128, 512], F32))
            p_loG = stH.enter_context(nc.psum_tensor(f"p_loG{half}", [128, 512], F32))
            p_sraw = [stH.enter_context(nc.psum_tensor(f"p_sraw{half}{i}", [128, 512], F32))
                      for i in range(2)]
            u0_free = [None, None]
            g0_free = [None, None]
            dnA_used = [0, 0, 0]
            dnA_tok = [None, None, None]
            loU_free = 0
            loG_free = 0
            ch_vs = 0
            sraw_ts = 0
            for hh in range(NHC):
                par = hh % 2
                rg.wait(ts, bundle_used_ts[par])
                bt = []
                bt.append(Qg.dma(rg, up_v[:, par, 0:2048].rearrange("p (c h) -> p c h", c=NDC),
                                 w_upT[:, hh * 128:(hh + 1) * 128]
                                 .rearrange("(c p) h -> p c h", p=128)))
                bt.append(Qg.dma(rg, up_v[:, par, 2048:4096].rearrange("p (c h) -> p c h", c=NDC),
                                 w_gateT[:, hh * 128:(hh + 1) * 128]
                                 .rearrange("(c p) h -> p c h", p=128)))
                bt.append(Qg.dma(rg, up_v[:, par, 4096:4224],
                                 upB_T[:, hh * 128:(hh + 1) * 128]))
                bt.append(Qg.dma(rg, up_v[:, par, 4224:4352],
                                 gateB_T[:, hh * 128:(hh + 1) * 128]))
                rg.wait(ts, dnA_used[hh % 3])
                dnA_tok[hh % 3] = Qg.dma(
                    rg, dnA_sb3[:, (hh % 3) * 128:(hh % 3 + 1) * 128],
                    dnA_T[hh * 128:(hh + 1) * 128, :])
                for tk in bt:
                    rt.wait_tok(tk)
                rt.wait_tok(u0_free[par])
                rt.wait(vs, rtld_vs)
                for kd in range(NDC):
                    rt.op("matmul", p_u0[par][:],
                          up_v[:, par, kd * 128:(kd + 1) * 128],
                          fm_v[:, kd, hs:hs + 512],
                          start=(kd == 0), stop=(kd == NDC - 1),
                          inc=(ts if kd == NDC - 1 else None))
                baseu_ts = ts.n
                rt.wait_tok(g0_free[par])
                for kd in range(NDC):
                    rt.op("matmul", p_g0[par][:],
                          up_v[:, par, 2048 + kd * 128:2048 + (kd + 1) * 128],
                          fm_v[:, kd, hs:hs + 512],
                          start=(kd == 0), stop=(kd == NDC - 1),
                          inc=(ts if kd == NDC - 1 else None))
                baseg_ts = ts.n
                if hh > 0:
                    rt.wait(vs, ch_vs)
                    rt.wait_tok(dnA_tok[(hh - 1) % 3])
                    for k in range(2):
                        rt.op("matmul", p_sraw[k][:],
                              dnA_sb3[:, ((hh - 1) % 3) * 128:((hh - 1) % 3 + 1) * 128],
                              chb[:, k * 512:(k + 1) * 512],
                              start=(hh - 1 == 0), stop=False,
                              inc=(ts if k == 1 else None))
                    sraw_ts = ts.n
                    dnA_used[(hh - 1) % 3] = ts.n
                rs.wait(ts, baseu_ts)
                rs.wait(vs)
                rs.wait(ss)
                rs.op("copy", out=ub_s, in_=p_u0[par][:], inc=ss)
                ubcopy_ss = ss.n
                rs.wait(ts, baseg_ts)
                rs.op("copy", out=gb_s, in_=p_g0[par][:], inc=ss)
                gbcopy_ss = ss.n
                rt.wait(ss, ubcopy_ss)
                rt.op("matmul", p_u0[par][:], up_v[:, par, 4096:4224],
                      rtld[:, 0:512], start=False, stop=True,
                      skip_group_check=True, inc=ts)
                lo0u_ts = ts.n
                rt.wait(vs, loU_free)
                rt.op("matmul", p_loU[:], up_v[:, par, 4096:4224],
                      rtld[:, 512:1024], start=True, stop=True, inc=ts)
                lo1u_ts = ts.n
                rt.wait(ss, gbcopy_ss)
                rt.op("matmul", p_g0[par][:], up_v[:, par, 4224:4352],
                      rtld[:, 1024:1536], start=False, stop=True,
                      skip_group_check=True, inc=ts)
                lo0g_ts = ts.n
                rt.wait(vs, loG_free)
                rt.op("matmul", p_loG[:], up_v[:, par, 4224:4352],
                      rtld[:, 1536:2048], start=True, stop=True, inc=ts)
                lo1g_ts = ts.n
                bundle_used_ts[par] = ts.n
                rs.wait(ts, lo0u_ts)
                rs.op("activation", su0_s, p_u0[par][:], AF.Silu, inc=ss)
                silu0_ss = ss.n
                u0_free[par] = (ss, silu0_ss)
                rv.wait(ts, lo1u_ts)
                rv.wait(ss, ubcopy_ss)
                rv.op("scalar_tensor_tensor", ub_s, p_loU[:], 1.0, ub_s,
                      ALU.mult, ALU.add, inc=vs)
                loU_free = vs.n
                u1_vs = vs.n
                rs.wait(vs, u1_vs)
                rs.op("activation", su1_s, ub_s, AF.Silu, inc=ss)
                silu1_ss = ss.n
                rv.wait(ss, silu0_ss)
                rv.wait(ts, lo0g_ts)
                rv.wait(ts, sraw_ts)
                rv.op("tensor_tensor", su0_s, su0_s, p_g0[par][:], ALU.mult, inc=vs)
                h0_vs = vs.n
                g0_free[par] = (vs, h0_vs)
                rv.wait(vs, h0_vs)
                rv.op("tensor_tensor", chb[:, 0:512], su0_s, wkb[:, 0:512],
                      ALU.mult, inc=vs)
                rv.wait(ts, lo1g_ts)
                rv.wait(ss, gbcopy_ss)
                rv.op("scalar_tensor_tensor", gb_s, p_loG[:], 1.0, gb_s,
                      ALU.mult, ALU.add, inc=vs)
                loG_free = vs.n
                rv.wait(ss, silu1_ss)
                rv.wait(vs)
                rv.op("tensor_tensor", su1_s, su1_s, gb_s, ALU.mult, inc=vs)
                rv.wait(vs)
                rv.op("tensor_tensor", chb[:, 512:1024], su1_s, wkb[:, 512:1024],
                      ALU.mult, inc=vs)
                rv.wait(vs)
                rv.op("tensor_tensor", mixed[:, hh * 512:(hh + 1) * 512],
                      chb[:, 0:512], chb[:, 512:1024], ALU.add, inc=vs)
                ch_vs = vs.n
            rt.wait(vs, ch_vs)
            rt.wait_tok(dnA_tok[(NHC - 1) % 3])
            for k in range(2):
                rt.op("matmul", p_sraw[k][:],
                      dnA_sb3[:, ((NHC - 1) % 3) * 128:((NHC - 1) % 3 + 1) * 128],
                      chb[:, k * 512:(k + 1) * 512],
                      start=False, stop=True, inc=(ts if k == 1 else None))
            sraw_done = ts.n
            for k in range(2):
                rv.wait(ts, sraw_done)
                rv.op("tensor_tensor", rtld[:, k * 512:(k + 1) * 512],
                      p_sraw[k][:], mt4[:, (half * 2 + k) * 512:(half * 2 + k + 1) * 512],
                      ALU.mult, inc=vs)
            stld_vs = vs.n
            barrier()
            stH.close()

            stE = ExitStack()
            p_d = [stE.enter_context(nc.psum_tensor(f"p_d{half}{i}", [128, 512], F32))
                   for i in range(8)]
            dbank_free = [0] * 8
            for tpair in range(2):
                for hh in range(NHC):
                    par = hh % 2
                    rg.wait(ts, dn_used_ts[par])
                    tok_dn = Qg.dma(rg, wdn_v[:, par],
                                    w_downT[hh * 128:(hh + 1) * 128, :])
                    rt.wait_tok(tok_dn)
                    if hh == 0:
                        for bi in range(8):
                            rt.wait(vs, dbank_free[bi])
                    for tq2 in range(2):
                        tloc = tpair * 2 + tq2
                        for qb in range(4):
                            rt.op("matmul", p_d[tq2 * 4 + qb][:],
                                  mixed[:, hh * 512 + tloc * 128:
                                        hh * 512 + (tloc + 1) * 128],
                                  wdn_v[:, par, qb * 512:(qb + 1) * 512],
                                  start=(hh == 0), stop=False,
                                  inc=(ts if (tq2 == 1 and qb == 3) else None))
                    dn_used_ts[par] = ts.n
                rt.wait(vs, stld_vs)
                rt.wait_tok(tok_dnB)
                for tq2 in range(2):
                    tloc = tpair * 2 + tq2
                    for qb in range(4):
                        for k in range(2):
                            rt.op("matmul", p_d[tq2 * 4 + qb][:],
                                  rtld[:, k * 512 + tloc * 128:
                                       k * 512 + (tloc + 1) * 128],
                                  dnB_sb[:, qb * 512:(qb + 1) * 512],
                                  start=False, stop=(k == 1),
                                  inc=(ts if (tq2 == 1 and qb == 3 and k == 1)
                                       else None))
                dn_done = ts.n
                for tq2 in range(2):
                    tch = half * 4 + tpair * 2 + tq2
                    rv.wait(ts, dn_done)
                    rv.wait(vs)
                    for qb in range(4):
                        rv.op("tensor_copy",
                              out=rowscrD[:, qb * 512:(qb + 1) * 512],
                              in_=p_d[tq2 * 4 + qb][:], inc=vs)
                        dbank_free[tq2 * 4 + qb] = vs.n
                    mean, rstd = ln_stats(rowscrD[:])
                    rv.wait(ss)
                    rv.op("tensor_scalar", rowscrD[:], rowscrD[:], mean, rstd,
                          ALU.subtract, op1=ALU.mult, inc=vs)
                    rv.wait(vs)
                    rv.op("tensor_tensor", x_res[:, tch * D:(tch + 1) * D],
                          x_res[:, tch * D:(tch + 1) * D], rowscrD[:],
                          ALU.add, inc=vs)
            stE.close()

        if "d_mixed" in dbg_out:
            rg.wait(vs)
            Qg.dma(rg, dbg_out["d_mixed"], mixed[:])

        barrier()
        stD.close()

        rv_final = vs.n
        ry.wait(vs, rv_final)
        tok_out = Qy.dma(ry, out_own.rearrange("(j p) d -> p j d", p=128),
                         x_res[:].rearrange("p (j d) -> p j d", j=NTC))
        ry.wait_tok(tok_out)

        with nc.Block() as block:
            block.tensor(lambda e: rt.replay(e))
            block.vector(lambda e: rv.replay(e))
            block.scalar(lambda e: rs.replay(e))
            block.gpsimd(lambda e: rg.replay(e))
            block.sync(lambda e: ry.replay(e))

    return nc



def host_prep(inputs):
    x = np.asarray(inputs["L0_attn_x"])
    w_qkv = np.ascontiguousarray(inputs["L0_attn_w_qkv"], dtype=np.float32)
    w_out = np.ascontiguousarray(inputs["L0_attn_w_out"], dtype=np.float32)
    gate_w = np.asarray(inputs["L0_ffn_gate_w"])
    w_up = np.asarray(inputs["L0_ffn_w_up"])
    w_gate = np.asarray(inputs["L0_ffn_w_gate"])
    w_down = np.asarray(inputs["L0_ffn_w_down"])
    up_A = np.asarray(inputs["L0_ffn_up_A"])
    up_B = np.asarray(inputs["L0_ffn_up_B"])
    gate_A = np.asarray(inputs["L0_ffn_gate_A"])
    gate_B = np.asarray(inputs["L0_ffn_gate_B"])
    down_A = np.asarray(inputs["L0_ffn_down_A"])
    down_B = np.asarray(inputs["L0_ffn_down_B"])

    shared = {
        "w_qkv": w_qkv,
        "w_out": w_out,
        "gate_wT": np.ascontiguousarray(gate_w.T),
        "w_upT": np.ascontiguousarray(w_up.T),
        "w_gateT": np.ascontiguousarray(w_gate.T),
        "w_downT": np.ascontiguousarray(w_down.T),
        "upA_T": np.ascontiguousarray(up_A.reshape(ER, D).T),
        "gateA_T": np.ascontiguousarray(gate_A.reshape(ER, D).T),
        "dnA_T": np.ascontiguousarray(down_A.reshape(ER, HFF).T),
        "upB_T": np.ascontiguousarray(up_B.transpose(0, 2, 1).reshape(ER, HFF)),
        "gateB_T": np.ascontiguousarray(gate_B.transpose(0, 2, 1).reshape(ER, HFF)),
        "dnB_T": np.ascontiguousarray(down_B.transpose(0, 2, 1).reshape(ER, D)),
    }
    in_maps = []
    for core in range(8):
        b, r = core // 4, core % 4
        xb = x[b].astype(np.float32, copy=False)
        x_ownc = np.ascontiguousarray(
            xb.reshape(32, 128, D)[r::4].reshape(T, D))
        mval = NEGBIG / RSQ
        m = np.zeros((128, 512), dtype=np.float32)
        for rp in range(4):
            if rp > r:
                m[:, rp * 128:(rp + 1) * 128] = mval
            elif rp == r:
                tri = np.triu(np.full((128, 128), mval, dtype=np.float32), k=1)
                m[:, rp * 128:(rp + 1) * 128] = tri
        d = dict(shared)
        d["x_own"] = x_ownc
        d["mask512"] = m
        in_maps.append(d)
    return in_maps


def kernel(**inputs):
    in_maps = host_prep(inputs)
    nc = build(debug=DEBUG)
    res = run_bass_kernel_spmd(nc, in_maps, core_ids=list(range(8)))
    out = np.empty((2, S, D), dtype=np.float32)
    for core in range(8):
        b, r = core // 4, core % 4
        o = res.results[core]["out_own"].reshape(NTC, 128, D)
        out[b].reshape(32, 128, D)[r::4] = o
    return out


# version 05
# speedup vs baseline: 1.2736x; 1.0061x over previous
from contextlib import ExitStack

import numpy as np
import ml_dtypes
import concourse.bass as bass
import concourse.mybir as mybir
from concourse.bass_utils import run_bass_kernel_spmd

F32 = mybir.dt.float32
BF16 = mybir.dt.bfloat16
AF = mybir.ActivationFunctionType
ALU = mybir.AluOpType
AX = mybir.AxisListType

D = 2048
NDC = 16
HEADS = 16
HD = 128
HFF = 5632
NHC = 44
T = 1024
NTC = 8
S = 4096
NE = 8
RK = 16
ER = 128
EPS = 1e-5
RSQ = float(1.0 / np.sqrt(128.0))
NEGBIG = -1e9

DEBUG = ()



class Sem:
    __slots__ = ("name", "n", "h")

    def __init__(self, name):
        self.name = name
        self.n = 0
        self.h = None


class R:

    def __init__(self, name):
        self.name = name
        self.cmds = []
        self.hw = {}

    def wait(self, sem, val=None):
        v = sem.n if val is None else val
        if v <= self.hw.get(sem, -1):
            return
        self.hw[sem] = v
        self.cmds.append(("w", sem, v))

    def wait_tok(self, tok):
        if tok is not None:
            self.wait(tok[0], tok[1])

    def op(self, method, *args, inc=None, inc_k=1, **kw):
        if inc is not None:
            inc.n += inc_k
        self.cmds.append(("o", method, args, kw, inc, inc_k))
        return inc.n if inc is not None else None

    def replay(self, e):
        for c in self.cmds:
            if c[0] == "w":
                e.wait_ge(c[1].h, c[2])
            else:
                _, m, args, kw, inc, k = c
                instr = getattr(e, m)(*args, **kw)
                if inc is not None:
                    instr.then_inc(inc.h, k)


class Q:

    def __init__(self, name, nslots):
        self.slots = [Sem(f"{name}{i}") for i in range(nslots)]
        self.i = 0

    def dma(self, rec, out, in_):
        sm = self.slots[self.i % len(self.slots)]
        self.i += 1
        rec.wait(sm, sm.n)
        rec.op("dma_start", out=out, in_=in_, inc=sm, inc_k=16)
        return (sm, sm.n)



def build(debug=()):
    nc = bass.Bass("TRN2", target_bir_lowering=False)

    def inp(name, shape, dtype=F32):
        return nc.dram_tensor(name, shape, dtype, kind="ExternalInput").ap()

    x_own = inp("x_own", [T, D])
    w_qkv = inp("w_qkv", [D, 3 * D])
    w_out = inp("w_out", [D, D])
    gate_wT = inp("gate_wT", [D, NE])
    w_upT = inp("w_upT", [D, HFF])
    w_gateT = inp("w_gateT", [D, HFF])
    w_downT = inp("w_downT", [HFF, D])
    upA_T = inp("upA_T", [D, ER])
    gateA_T = inp("gateA_T", [D, ER])
    dnA_T = inp("dnA_T", [HFF, ER])
    upB_T = inp("upB_T", [ER, HFF])
    gateB_T = inp("gateB_T", [ER, HFF])
    dnB_T = inp("dnB_T", [ER, D])
    mask512 = inp("mask512", [128, 512])

    out_own = nc.dram_tensor("out_own", [T, D], F32, kind="ExternalOutput").ap()

    kv_in_k = nc.dram_tensor("kv_in_k", [HEADS * HD, T], BF16, kind="Internal").ap()
    kv_in_v = nc.dram_tensor("kv_in_v", [T, HEADS * HD], BF16, kind="Internal").ap()
    kv_out_k = nc.dram_tensor("kv_out_k", [4 * HEADS * HD, T], BF16, kind="Internal").ap()
    kv_out_v = nc.dram_tensor("kv_out_v", [4 * T, HEADS * HD], BF16, kind="Internal").ap()
    kn_in = nc.dram_tensor("kn_in", [HEADS, T], F32, kind="Internal").ap()
    kn_out = nc.dram_tensor("kn_out", [4 * HEADS, T], F32, kind="Internal").ap()
    lg_in = nc.dram_tensor("lg_in", [T, NE], F32, kind="Internal").ap()
    lg_out = nc.dram_tensor("lg_out", [4 * T, NE], F32, kind="Internal").ap()
    GROUPS = [[0, 1, 2, 3], [4, 5, 6, 7]]

    identb_d = nc.inline_tensor(np.eye(128, dtype=ml_dtypes.bfloat16), name="c_identb").ap()
    identf_d = nc.inline_tensor(np.eye(128, dtype=np.float32), name="c_identf").ap()
    onesf_d = nc.inline_tensor(np.ones((1, 128), dtype=np.float32), name="c_onesf").ap()
    onescol_d = nc.inline_tensor(np.ones((128, 1), dtype=ml_dtypes.bfloat16), name="c_onescol").ap()
    negonesf_d = nc.inline_tensor(np.full((1, 128), -1.0, dtype=np.float32), name="c_negonesf").ap()
    blk_np = np.zeros((NE, ER), dtype=ml_dtypes.bfloat16)
    for e_ in range(NE):
        blk_np[e_, e_ * RK:(e_ + 1) * RK] = 2.0
    blk2_d = nc.inline_tensor(blk_np, name="c_blk2").ap()

    dbg_out = {}
    for name, shape, dt in [
        ("d_lnfm", [128, NDC * T], F32),
        ("d_q", [128, HEADS * T], F32),
        ("d_kvk", [HEADS * HD, T], F32),
        ("d_kvv", [T, HEADS * HD], F32),
        ("d_attnout", [128, HEADS * T], F32),
        ("d_x2", [128, NTC * D], F32),
        ("d_ln2fm", [128, NDC * T], F32),
        ("d_logits", [T, NE], F32),
        ("d_gstats", [128, 64], F32),
        ("d_mixed", [128, NHC * 512], F32),
        ("d_ffntm", [128, 4 * D], F32),
    ]:
        if name in debug:
            dbg_out[name] = nc.dram_tensor(name, shape, dt, kind="ExternalOutput").ap()

    ts = Sem("ts")
    vs = Sem("vs")
    ss = Sem("ss")
    cskv = Sem("cskv")
    cslg = Sem("cslg")
    csn = Sem("csn")
    Qg = Q("gq", 8)
    Qy = Q("sq", 8)
    SEMS = [ts, vs, ss, cskv, cslg, csn] + Qg.slots + Qy.slots

    rt, rv, rs, rg, ry = R("tensor"), R("vector"), R("scalar"), R("gpsimd"), R("sync")
    ALLR = [rt, rv, rs, rg, ry]

    def barrier():
        for r in ALLR:
            for sm in SEMS:
                r.wait(sm)

    st = ExitStack()
    with st:
        def sb(name, shape, dtype):
            return st.enter_context(nc.sbuf_tensor(name, shape, dtype))

        def psum(name, shape, dtype=F32):
            return st.enter_context(nc.psum_tensor(name, shape, dtype))

        for sm in SEMS:
            sm.h = st.enter_context(nc.semaphore(sm.name))

        x_res = sb("x_res", [128, NTC * D], F32)
        fm_buf = sb("fm_buf", [128, NDC * T], BF16)
        statscr = sb("statscr", [128, 48], F32)
        lgtm_sb = sb("lgtm_sb", [128, NTC * NE], F32)
        identb = sb("identb", [128, 128], BF16)
        identf = sb("identf", [128, 128], F32)
        onesf = sb("onesf", [1, 128], F32)
        onescol = sb("onescol", [128, 1], BF16)
        negonesf = sb("negonesf", [1, 128], F32)
        blk2 = sb("blk2", [NE, ER], BF16)

        fm_v = fm_buf[:].rearrange("p (c t) -> p c t", c=NDC)
        stQ = ExitStack()
        q_all = stQ.enter_context(nc.sbuf_tensor("q_all", [128, HEADS * T], BF16))
        qn_sb = stQ.enter_context(nc.sbuf_tensor("qn_sb", [128, 128], F32))
        mask_sb = stQ.enter_context(nc.sbuf_tensor("mask_sb", [128, 512], F32))

        tok_x = Qy.dma(ry, x_res[:].rearrange("p (j d) -> p j d", j=NTC),
                       x_own.rearrange("(j p) d -> p j d", p=128))
        tok_c = {}
        for nm, dst, src in [("identb", identb[:], identb_d),
                             ("identf", identf[:], identf_d),
                             ("onesf", onesf[:], onesf_d),
                             ("onescol", onescol[:], onescol_d),
                             ("negonesf", negonesf[:], negonesf_d),
                             ("blk2", blk2[:], blk2_d),
                             ("mask", mask_sb[:], mask512)]:
            tok_c[nm] = Qy.dma(ry, dst, src)

        def ln_stats(xin):
            bnbuf = statscr[:, 16:40]
            mv = statscr[:, 40:42]
            mean, rstd = statscr[:, 40:41], statscr[:, 44:45]
            rv.wait(vs)
            rv.wait(ss)
            for c_ in range(4):
                rv.op("bn_stats", out=bnbuf[:, c_ * 6:(c_ + 1) * 6],
                      in_=xin[:, c_ * 512:(c_ + 1) * 512], inc=vs)
            rv.wait(vs)
            rv.op("bn_aggr", out=mv, in_=bnbuf, inc=vs)
            rv.wait(vs)
            rv.op("tensor_scalar", statscr[:, 42:43], statscr[:, 41:42], EPS, None,
                  ALU.add, inc=vs)
            rs.wait(vs)
            rs.op("activation", statscr[:, 43:44], statscr[:, 42:43], AF.Ln, inc=ss)
            rs.wait(ss)
            rs.op("activation", rstd, statscr[:, 43:44], AF.Exp, scale=-0.5, inc=ss)
            return mean, rstd

        stA = ExitStack()
        sbA = lambda n_, sh, dt: stA.enter_context(nc.sbuf_tensor(n_, sh, dt))
        psA = lambda n_, sh, dt=F32: stA.enter_context(nc.psum_tensor(n_, sh, dt))
        lntm = sbA("lntm", [128, 2 * D], BF16)
        wqkv_sb = sbA("wqkv_sb", [128, 3 * NDC * 512], BF16)
        kvstage = sbA("kvstage", [128, 4 * 512], BF16)
        sqst = sbA("sqst", [128, 2 * 512], BF16)
        kn_row = sbA("kn_row", [1, 512], F32)
        qn_row = sbA("qn_row", [1, 512], F32)
        p_lt = [psA(f"p_lt{i}", [128, 1024], BF16) for i in range(2)]
        p_mm = [psA(f"p_mm{i}", [128, 512], F32) for i in range(4)]
        p_nr = psA("p_nr", [1, 512])
        p_qt = psA("p_qt", [128, 4])
        nstate = {"sq": [0, 0], "nr": None, "qt": 0, "knrow": None, "qnrow": 0}

        def norms_from(src_bf, h_, half_, is_q):
            p2 = (h_ * 2 + half_) % 2
            rs.wait(ts, nstate["sq"][p2])
            rs.wait(ss)
            rs.op("activation", sqst[:, p2 * 512:(p2 + 1) * 512], src_bf,
                  AF.Square, inc=ss)
            sq_ss = ss.n
            rt.wait(ss, sq_ss)
            rt.wait_tok(nstate["nr"])
            rt.wait_tok(tok_c["onescol"])
            rt.op("matmul", p_nr[0:1, :], onescol[:],
                  sqst[:, p2 * 512:(p2 + 1) * 512], start=True, stop=True, inc=ts)
            nstate["sq"][p2] = ts.n
            nr_ts = ts.n
            if is_q:
                rv.wait(ts, nr_ts)
                rv.wait(ts, nstate["qnrow"])
                rv.op("tensor_copy", out=qn_row[0:1, :], in_=p_nr[0:1, :], inc=vs)
                nstate["nr"] = (vs, vs.n)
                rt.wait(vs, vs.n)
                rt.wait(vs, nstate["qt"])
                for i4 in range(4):
                    rt.op("transpose", p_qt[:, i4:i4 + 1],
                          qn_row[0:1, i4 * 128:(i4 + 1) * 128],
                          identf[0:1, 0:1], inc=(ts if i4 == 3 else None))
                nstate["qnrow"] = ts.n
                rv.wait(ts, ts.n)
                rv.op("tensor_copy",
                      out=qn_sb[:, h_ * 8 + half_ * 4:h_ * 8 + half_ * 4 + 4],
                      in_=p_qt[:], inc=vs)
                nstate["qt"] = vs.n
            else:
                rv.wait(ts, nr_ts)
                rv.wait_tok(nstate["knrow"])
                rv.op("tensor_copy", out=kn_row[0:1, :], in_=p_nr[0:1, :], inc=vs)
                nstate["nr"] = (vs, vs.n)
                ry.wait(vs)
                nstate["knrow"] = Qy.dma(
                    ry, kn_in[h_:h_ + 1, half_ * 512:(half_ + 1) * 512],
                    kn_row[0:1, :])


        lt_done_ts = [0, 0]
        for j in range(NTC):
            par = j % 2
            xin = x_res[:, j * D:(j + 1) * D]
            rv.wait_tok(tok_x)
            rs.wait_tok(tok_x)
            mean, rstd = ln_stats(xin)
            rv.wait(ss)
            rv.wait(ts, lt_done_ts[par])
            rv.op("tensor_scalar", lntm[:, par * D:(par + 1) * D], xin,
                  mean, rstd, ALU.subtract, op1=ALU.mult, inc=vs)
            rt.wait(vs)
            rt.wait_tok(tok_c["identb"])
            for k in range(NDC):
                bank = p_lt[k // 8]
                rt.op("transpose", bank[:, (k % 8) * 128:(k % 8 + 1) * 128],
                      lntm[:, par * D + k * 128: par * D + (k + 1) * 128],
                      identb[:], inc=(ts if k % 8 == 7 else None))
            lt_done_ts[par] = ts.n
            rv.wait(ts)
            for half in range(2):
                rv.op("tensor_copy",
                      out=fm_v[:, half * 8:(half + 1) * 8, j * 128:(j + 1) * 128],
                      in_=p_lt[half][:].rearrange("p (c t) -> p c t", c=8),
                      inc=vs)
            if j == 3:
                lnfm_half0_vs = vs.n
        lnfm_done_vs = vs.n

        if "d_lnfm" in dbg_out:
            rg.wait(vs)
            Qg.dma(rg, dbg_out["d_lnfm"], fm_buf[:])

        wq_v = wqkv_sb[:].rearrange("p (b c n) -> p b c n", b=3, c=NDC)
        grp_consumed_ts = [0, 0, 0]
        bank_free = [0, 0, 0, 0]
        stage_free = [None, None, None, None]

        def mm_to_bank(bi, lhsT_rhs_list):
            rt.wait(ss, bank_free[bi])
            n = len(lhsT_rhs_list)
            for i, (lt_, rh_) in enumerate(lhsT_rhs_list):
                rt.op("matmul", p_mm[bi][:], lt_, rh_,
                      start=(i == 0), stop=(i == n - 1),
                      inc=(ts if i == n - 1 else None))
            return ts.n

        ag_launched = {}
        for gi, g in enumerate([4, 5, 6, 7, 8, 9, 10, 11, 0, 1, 2, 3]):
            par = gi % 3
            rg.wait(ts, grp_consumed_ts[par])
            tok_g = Qg.dma(rg, wq_v[:, par],
                           w_qkv[:, g * 512:(g + 1) * 512].rearrange("(c p) n -> p c n", p=128))
            rt.wait_tok(tok_g)
            rt.wait(vs, lnfm_half0_vs)
            if g < 8:
                for c in range(4):
                    for half in range(2):
                        if half == 1:
                            rt.wait(vs, lnfm_done_vs)
                        bi = 2 * (c % 2) + half
                        mm_done = mm_to_bank(bi, [
                            (wq_v[:, par, k, c * 128:(c + 1) * 128],
                             fm_v[:, k, half * 512:(half + 1) * 512])
                            for k in range(NDC)])
                        col = g * 512 + c * 128
                        rs.wait(ts, mm_done)
                        if g < 4:
                            h_ = col // 128
                            qa = q_all[:, h_ * T + half * 512:h_ * T + half * 512 + 512]
                            rs.op("copy", out=qa, in_=p_mm[bi][:], inc=ss)
                            norms_from(qa, h_, half, True)
                        else:
                            si = bi
                            stg = kvstage[:, si * 512:(si + 1) * 512]
                            rs.wait_tok(stage_free[si])
                            rs.op("copy", out=stg, in_=p_mm[bi][:], inc=ss)
                            h_ = (col - 2048) // 128
                            ry.wait(ss)
                            stage_free[si] = Qy.dma(
                                ry,
                                kv_in_k[h_ * 128:(h_ + 1) * 128,
                                        half * 512:(half + 1) * 512],
                                stg)
                            norms_from(stg, h_, half, False)
                        bank_free[bi] = ss.n
            else:
                for tc in range(NTC):
                    if tc >= 4:
                        rt.wait(vs, lnfm_done_vs)
                    bi = 2 * (tc % 2)
                    mm_done = mm_to_bank(bi, [
                        (fm_v[:, k, tc * 128:(tc + 1) * 128], wq_v[:, par, k, :])
                        for k in range(NDC)])
                    si = tc % 4
                    stg = kvstage[:, si * 512:(si + 1) * 512]
                    rs.wait(ts, mm_done)
                    rs.wait_tok(stage_free[si])
                    rs.op("copy", out=stg, in_=p_mm[bi][:], inc=ss)
                    bank_free[bi] = ss.n
                    vcol = (g - 8) * 512
                    ry.wait(ss)
                    stage_free[si] = Qy.dma(
                        ry, kv_in_v[tc * 128:(tc + 1) * 128, vcol:vcol + 512], stg)
            grp_consumed_ts[par] = ts.n
            if g == 7:
                for sm in Qy.slots:
                    rg.wait(sm)
                rg.op("collective_compute", "AllGather", ALU.bypass,
                      replica_groups=GROUPS, ins=[kv_in_k], outs=[kv_out_k],
                      inc=cskv)
                ag_launched["k"] = (cskv, cskv.n)
                rg.op("collective_compute", "AllGather", ALU.bypass,
                      replica_groups=GROUPS, ins=[kn_in], outs=[kn_out],
                      inc=csn)
                ag_launched["kn"] = (csn, csn.n)
            elif g == 11:
                for sm in Qy.slots:
                    rg.wait(sm)
                rg.op("collective_compute", "AllGather", ALU.bypass,
                      replica_groups=GROUPS, ins=[kv_in_v], outs=[kv_out_v],
                      inc=cskv)
                ag_launched["v"] = (cskv, cskv.n)

        if "d_q" in dbg_out:
            rg.wait(ss)
            Qg.dma(rg, dbg_out["d_q"], q_all[:])

        kv_gathered = (cskv, cskv.n)

        if "d_kvk" in dbg_out:
            rg.wait_tok(kv_gathered)
            Qg.dma(rg, dbg_out["d_kvk"], kv_in_k)
        if "d_kvv" in dbg_out:
            rg.wait_tok(kv_gathered)
            Qg.dma(rg, dbg_out["d_kvv"], kv_in_v)

        barrier()
        stA.close()

        stB = ExitStack()
        sbB = lambda n_, sh, dt: stB.enter_context(nc.sbuf_tensor(n_, sh, dt))
        psB = lambda n_, sh, dt=F32: stB.enter_context(nc.psum_tensor(n_, sh, dt))
        kh_sb = sbB("kh_sb", [128, 2 * 8 * 512], BF16)
        vh_sb = sbB("vh_sb", [128, 2 * 8 * 512], BF16)
        pb = [sbB("pb0", [128, 4096], BF16), sbB("pb1", [128, 4096], BF16)]
        pt_sb = sbB("pt_sb", [128, 2 * 1024], BF16)
        dscr = sbB("dscr", [128, 512], F32)
        kn_all = sbB("kn_all", [64, 1024], F32)
        kncol = sbB("kncol", [64, 1], F32)
        kmrow = sbB("kmrow", [1, 80], F32)
        rzrow_sb = sbB("rzrow_sb", [1, 256], F32)
        rzb_sb = sbB("rzb_sb", [128, 256], F32)
        p_s = [psB(f"p_s{i}", [128, 512]) for i in range(3)]
        p_pt = [psB(f"p_pt{i}", [128, 1024], BF16) for i in range(2)]
        p_o = [psB(f"p_o{i}", [128, 256]) for i in range(2)]
        p_misc = psB("p_misc", [128, 256])

        kh_v = kh_sb[:].rearrange("p (b j f t) -> p b j f t", b=2, j=8, f=4)
        vh_v = vh_sb[:].rearrange("p (b j f t) -> p b j f t", b=2, j=8, f=4)
        km2col = statscr[:, 0:1]
        ztile = statscr[:, 11:13]
        rzt = statscr[:, 13:15]
        negb8 = statscr[:, 16:24]
        zp = statscr[:, 24:40]

        khb_consumed = [0, 0]
        kv_tok = [None, None]
        knh_free = [0, 0]
        ptbank_free_s = [0, 0]
        sbank_free = [None, None, None]
        sbank_i = 0
        ptbank_free = [0, 0]
        o_free = [0, 0]
        pb_free = [0, 0]
        exp_ss = 0

        ry.wait_tok(ag_launched["kn"])
        tok_kn = Qy.dma(ry, kn_all[0:64, :], kn_out)
        rv.wait_tok(tok_kn)
        rv.op("tensor_reduce", kncol[0:64, :], kn_all[0:64, :], AX.X, ALU.max,
              inc=vs)
        rt.wait(vs)
        rt.op("transpose", p_misc[0:1, 0:64], kncol[0:64, :],
              identf[0:64, 0:64], inc=ts)
        rv.wait(ts)
        rv.op("tensor_copy", out=kmrow[0:1, 0:64], in_=p_misc[0:1, 0:64], inc=vs)
        rv.wait(vs)
        rv.op("tensor_tensor", kmrow[0:1, 64:80], kmrow[0:1, 0:16],
              kmrow[0:1, 16:32], ALU.max, inc=vs)
        rv.wait(vs)
        rv.op("tensor_tensor", kmrow[0:1, 64:80], kmrow[0:1, 64:80],
              kmrow[0:1, 32:48], ALU.max, inc=vs)
        rv.wait(vs)
        rv.op("tensor_tensor", kmrow[0:1, 64:80], kmrow[0:1, 64:80],
              kmrow[0:1, 48:64], ALU.max, inc=vs)
        km_vs = vs.n
        for h in range(HEADS):
            hb = h % 2
            ry.wait_tok(ag_launched["k"])
            ry.wait_tok(ag_launched["v"])
            ry.wait(ts, khb_consumed[hb])
            toks = []
            for r_ in range(4):
                toks.append(Qy.dma(
                    ry, kh_v[:, hb, :, r_, :],
                    kv_out_k[r_ * 2048 + h * 128:r_ * 2048 + (h + 1) * 128, :]
                    .rearrange("p (j t) -> p j t", j=8)))
                toks.append(Qy.dma(
                    ry, vh_v[:, hb, :, r_, :],
                    kv_out_v[r_ * T:(r_ + 1) * T, h * 128:(h + 1) * 128]
                    .rearrange("(j p) t -> p j t", p=128)))
            kv_tok[hb] = toks
            rv.wait(ss)
            rt.wait(vs, km_vs)
            rt.op("matmul", p_misc[:, 0:1], onesf[0:1, :],
                  kmrow[0:1, 64 + h:65 + h], start=True, stop=True, inc=ts)
            rv.wait(ts)
            rv.op("tensor_copy", out=km2col, in_=p_misc[:, 0:1], inc=vs)
            rv.wait(vs)
            rv.op("tensor_scalar", negb8, qn_sb[:, h * 8:(h + 1) * 8],
                  km2col, -RSQ / 2.0, ALU.add, op1=ALU.mult, inc=vs)
            negb_vs = vs.n

            for pj in (0, 2, 4, 6):
                for ti, jt in enumerate((pj, pj + 1)):
                    rl = jt + 1
                    pbuf = pb[ti]
                    if ti == 0:
                        rv.wait(ts, pb_free[0])
                        rv.op("memset", pbuf[:, rl * 512:(rl + 1) * 512], 0.0,
                              inc=vs)
                    for lv in range(rl):
                        bi = sbank_i % 3
                        sbank_i += 1
                        rt.wait_tok(sbank_free[bi])
                        for tk in kv_tok[hb]:
                            rt.wait_tok(tk)
                        rt.op("matmul", p_s[bi][:],
                              q_all[:, h * T + jt * 128:h * T + (jt + 1) * 128],
                              kh_sb[:, hb * 4096 + lv * 512:hb * 4096 + (lv + 1) * 512],
                              start=True, stop=True, inc=ts)
                        mm_done = ts.n
                        dst = pbuf[:, lv * 512:(lv + 1) * 512]
                        zcol = zp[:, ti * 8 + lv:ti * 8 + lv + 1]
                        if lv == jt:
                            rv.wait(ts, mm_done)
                            rv.wait_tok(tok_c["mask"])
                            rv.wait(ss)
                            rv.op("tensor_tensor", dscr[:], p_s[bi][:],
                                  mask_sb[:], ALU.add, inc=vs)
                            sbank_free[bi] = (vs, vs.n)
                            rs.wait(vs)
                            rs.wait(ts, pb_free[ti])
                            rs.op("activation", dst, dscr[:], AF.Exp,
                                  bias=negb8[:, jt:jt + 1], scale=RSQ,
                                  accum_out=zcol, inc=ss)
                        else:
                            rs.wait(ts, mm_done)
                            rs.wait(vs, negb_vs)
                            rs.wait(ts, pb_free[ti])
                            rs.op("activation", dst, p_s[bi][:], AF.Exp,
                                  bias=negb8[:, jt:jt + 1], scale=RSQ,
                                  accum_out=zcol, inc=ss)
                            sbank_free[bi] = (ss, ss.n)
                    rv.wait(ss)
                    rv.op("tensor_reduce", ztile[:, ti:ti + 1],
                          zp[:, ti * 8:ti * 8 + rl], AX.X, ALU.add, inc=vs)
                    rv.wait(vs)
                    rv.op("reciprocal", rzt[:, ti:ti + 1], ztile[:, ti:ti + 1],
                          inc=vs)
                ptilde_vs = vs.n
                ptilde_vs = vs.n

                rv_rz = vs.n
                rt.wait(vs, rv_rz)
                for ti in range(2):
                    rt.op("transpose", p_misc[0:1, ti * 128:(ti + 1) * 128],
                          rzt[:, ti:ti + 1], identf[:],
                          inc=(ts if ti == 1 else None))
                rv.wait(ts)
                rv.op("tensor_copy", out=rzrow_sb[0:1, :], in_=p_misc[0:1, 0:256],
                      inc=vs)
                rzrow_vs = vs.n
                rt.wait(vs, rzrow_vs)
                rt.op("matmul", p_misc[:], onesf[0:1, :], rzrow_sb[0:1, :],
                      start=True, stop=True, inc=ts)
                rv.wait(ts)
                rv.op("tensor_copy", out=rzb_sb[:], in_=p_misc[:], inc=vs)
                rzb_vs = vs.n

                opar = (pj // 2) % 2
                rt.wait(vs, o_free[opar])
                tr_last = 0
                for lv in range(pj + 2):
                    ppar = lv % 2
                    rt.wait(vs, ptilde_vs)
                    rt.wait(vs, ptbank_free[ppar])
                    rt.wait(ss, ptbank_free_s[ppar])
                    for q4 in range(4):
                        for ti in range(2):
                            rt.op("transpose",
                                  p_pt[ppar][:, (q4 * 2 + ti) * 128:(q4 * 2 + ti + 1) * 128],
                                  pb[ti][:, (lv * 4 + q4) * 128:(lv * 4 + q4 + 1) * 128],
                                  identb[:],
                                  inc=(ts if (q4 == 3 and ti == 1) else None))
                    tr_done = ts.n
                    tr_last = tr_done
                    if lv % 2 == 0:
                        rv.wait(ts, tr_done)
                        rv.op("tensor_copy", out=pt_sb[:, ppar * 1024:(ppar + 1) * 1024],
                              in_=p_pt[ppar][:], inc=vs)
                        ptbank_free[ppar] = vs.n
                        rt.wait(vs, ptbank_free[ppar])
                    else:
                        rs.wait(ts, tr_done)
                        rs.op("copy", out=pt_sb[:, ppar * 1024:(ppar + 1) * 1024],
                              in_=p_pt[ppar][:], inc=ss)
                        ptbank_free_s[ppar] = ss.n
                        rt.wait(ss, ptbank_free_s[ppar])
                    for q4 in range(4):
                        rt.op("matmul", p_o[opar][:],
                              vh_sb[:, hb * 4096 + (lv * 4 + q4) * 128:
                                    hb * 4096 + (lv * 4 + q4 + 1) * 128],
                              pt_sb[:, ppar * 1024 + q4 * 256:ppar * 1024 + (q4 + 1) * 256],
                              start=(lv == 0 and q4 == 0),
                              stop=(lv == pj + 1 and q4 == 3),
                              inc=(ts if (lv == pj + 1 and q4 == 3) else None))
                pb_free[0] = pb_free[1] = tr_last
                av_done = ts.n
                rv.wait(ts, av_done)
                rv.wait(vs, rzb_vs)
                rv.op("tensor_tensor",
                      fm_buf[:, h * T + pj * 128:h * T + (pj + 2) * 128],
                      p_o[opar][:], rzb_sb[:], ALU.mult, inc=vs)
                o_free[opar] = vs.n
            khb_consumed[hb] = ts.n

        if "d_attnout" in dbg_out:
            rg.wait(vs)
            Qg.dma(rg, dbg_out["d_attnout"], fm_buf[:])

        barrier()
        stB.close()
        stQ.close()
        stL = ExitStack()
        ln2_buf = stL.enter_context(nc.sbuf_tensor("ln2_buf", [128, NDC * T], BF16))
        ln2_v = ln2_buf[:].rearrange("p (c t) -> p c t", c=NDC)

        stC = ExitStack()
        sbC = lambda n_, sh, dt: stC.enter_context(nc.sbuf_tensor(n_, sh, dt))
        psC = lambda n_, sh, dt=F32: stC.enter_context(nc.psum_tensor(n_, sh, dt))
        wout_sb = sbC("wout_sb", [128, NDC * D], BF16)
        rowscr2 = sbC("rowscr2", [128, D], F32)
        ln2f32 = sbC("ln2f32", [128, 512], F32)
        lgj_sb = sbC("lgj_sb", [NE, 128], F32)
        gwT_sb = sbC("gwT_sb", [128, NDC * NE], F32)
        p_o2 = [psC(f"p_o2{i}", [128, 512]) for i in range(4)]
        p_t = [psC(f"p_t{i}", [128, 512]) for i in range(2)]
        p_g = psC("p_g", [NE, 128])
        p_x8 = psC("p_x8", [128, NE])

        wout_v = wout_sb[:].rearrange("p (c n) -> p c n", c=NDC)
        gwT_v = gwT_sb[:].rearrange("p (c e) -> p c e", c=NDC)

        tok_wout = Qg.dma(rg, wout_v, w_out.rearrange("(c p) n -> p c n", p=128))
        tok_gwt = Qy.dma(ry, gwT_v, gate_wT.rearrange("(c p) e -> p c e", p=128))

        attnout_vs = vs.n
        o2bank_free = [0, 0, 0, 0]
        pg_free = 0
        ln2f_free = [0, 0]
        for j in range(NTC):
            for q in range(4):
                rt.wait(vs, o2bank_free[q])
                rt.wait_tok(tok_wout)
                for c in range(NDC):
                    rt.op("matmul", p_o2[q][:],
                          fm_buf[:, c * T + j * 128:c * T + (j + 1) * 128],
                          wout_v[:, c, q * 512:(q + 1) * 512],
                          start=(c == 0), stop=(c == NDC - 1),
                          inc=(ts if c == NDC - 1 else None))
            mm_done = ts.n
            rv.wait(ts, mm_done)
            rv.wait(ts)
            for q in range(4):
                rv.op("tensor_copy", out=rowscr2[:, q * 512:(q + 1) * 512],
                      in_=p_o2[q][:], inc=vs)
                o2bank_free[q] = vs.n
            row = rowscr2[:]
            mean, rstd = ln_stats(row)
            rv.wait(ss)
            rv.op("tensor_scalar", row, row, mean, rstd,
                  ALU.subtract, op1=ALU.mult, inc=vs)
            rv.wait(vs)
            rv.op("tensor_tensor", x_res[:, j * D:(j + 1) * D],
                  x_res[:, j * D:(j + 1) * D], row, ALU.add, inc=vs)
            mean, rstd = ln_stats(x_res[:, j * D:(j + 1) * D])
            rv.wait(ss)
            rv.wait(vs)
            rv.op("tensor_scalar", row, x_res[:, j * D:(j + 1) * D], mean, rstd,
                  ALU.subtract, op1=ALU.mult, inc=vs)
            row_vs = vs.n
            for k4 in range(4):
                tpar = k4 % 2
                rt.wait(vs, row_vs)
                rt.wait(ss, ln2f_free[0])
                for i in range(4):
                    rt.op("transpose", p_t[tpar][:, i * 128:(i + 1) * 128],
                          rowscr2[:, (4 * k4 + i) * 128:(4 * k4 + i + 1) * 128],
                          identf[:], inc=(ts if i == 3 else None))
                tr_done = ts.n
                rv.wait(ts, tr_done)
                rv.wait(ss, ln2f_free[0])
                rv.op("tensor_copy", out=ln2f32[:],
                      in_=p_t[tpar][:], inc=vs)
                l32_vs = vs.n
                rs.wait(vs, l32_vs)
                rs.op("copy",
                      out=ln2_v[:, 4 * k4:4 * k4 + 4, j * 128:(j + 1) * 128],
                      in_=ln2f32[:].rearrange("p (c t) -> p c t", c=4), inc=ss)
                ln2f_free[0] = ss.n
                rt.wait(vs, l32_vs)
                rt.wait_tok(tok_gwt)
                if k4 == 0:
                    rt.wait(vs, pg_free)
                for i in range(4):
                    k = 4 * k4 + i
                    rt.op("matmul", p_g[:],
                          gwT_v[:, k, :],
                          ln2f32[:, i * 128:(i + 1) * 128],
                          start=(k == 0), stop=(k == NDC - 1),
                          inc=(ts if k == NDC - 1 else None))
            gmm_done = ts.n
            rv.wait(ts, gmm_done)
            rv.wait(ts)
            rv.op("tensor_copy", out=lgj_sb[:], in_=p_g[:], inc=vs)
            pg_free = vs.n
            rt.wait(vs, pg_free)
            rt.op("transpose", p_x8[:], lgj_sb[:], identf[0:NE, 0:NE], inc=ts)
            rv.wait(ts)
            rv.op("tensor_copy", out=lgtm_sb[:, j * NE:(j + 1) * NE],
                  in_=p_x8[:], inc=vs)
        x2_done_vs = vs.n
        ln2_done_ss = ss.n

        if "d_x2" in dbg_out:
            rg.wait(vs)
            Qg.dma(rg, dbg_out["d_x2"], x_res[:])
        if "d_ln2fm" in dbg_out:
            rg.wait(ss)
            Qg.dma(rg, dbg_out["d_ln2fm"], ln2_buf[:])

        rv_lgtm = vs.n
        ry.wait(vs, rv_lgtm)
        tok_lgin = Qy.dma(ry, lg_in.rearrange("(c p) e -> p c e", p=128),
                          lgtm_sb[:].rearrange("p (c e) -> p c e", c=NTC))
        rg.wait_tok(tok_lgin)
        rg.op("collective_compute", "AllGather", ALU.bypass,
              replica_groups=GROUPS, ins=[lg_in], outs=[lg_out], inc=cslg)
        lg_gathered = (cslg, cslg.n)

        if "d_logits" in dbg_out:
            rg.wait_tok(lg_gathered)
            Qg.dma(rg, dbg_out["d_logits"], lg_in)

        barrier()
        stC.close()

        stD = ExitStack()
        sbD = lambda n_, sh, dt: stD.enter_context(nc.sbuf_tensor(n_, sh, dt))
        mixed = sbD("mixed", [128, NHC * 512], BF16)
        wkb = sbD("wkb", [128, 2 * 512], F32)
        mt4 = sbD("mt4", [128, 4 * 512], BF16)
        rtld = sbD("rtld", [128, 4 * 512], BF16)
        upstr = fm_buf[:, 0:8704]
        wdnstr = fm_buf[:, 8704:12800]
        dnA_sb3 = fm_buf[:, 12800:13184]
        chb = fm_buf[:, 13184:14208]
        dnB_sb = fm_buf[:, 14208:16256]
        up_v = upstr.rearrange("p (b n) -> p b n", b=2)

        tok_dnB = Qg.dma(rg, dnB_sb, dnB_T)

        stG = ExitStack()
        sbG = lambda n_, sh, dt: stG.enter_context(nc.sbuf_tensor(n_, sh, dt))
        psG = lambda n_, sh, dt=F32: stG.enter_context(nc.psum_tensor(n_, sh, dt))
        lgt = sbG("lgt", [128, 32 * NE], F32)
        tscr = sbG("tscr", [128, 16 * 32], F32)
        e_scr = sbG("e_scr", [128, 48], F32)
        ind_tm = sbG("ind_tm", [128, 2 * NTC * NE], BF16)
        indfm = sbG("indfm", [NE, 2 * T], BF16)
        wrow = sbG("wrow", [1, 160], F32)
        p_col = psG("p_col", [128, 16])
        p_row = psG("p_row", [1, 128])
        p_i8 = stG.enter_context(nc.psum_tensor("p_i8", [NE, 128], BF16))
        p_m512 = psG("p_m512", [128, 512])

        ry.wait_tok(lg_gathered)
        tok_lgt = Qy.dma(ry, lgt[:].rearrange("p (c e) -> p c e", e=NE),
                         lg_out.rearrange("(c p) e -> p c e", p=128))

        def tournament(view, n, m1, m2):
            cols = [view[:, :, i_:i_ + 1] for i_ in range(8)]
            scr = lambda i_: tscr[:, i_ * 32:i_ * 32 + n].unsqueeze(2)
            def tt(dst, a_, b_, op):
                rv.wait(vs)
                rv.op("tensor_tensor", dst, a_, b_, op, inc=vs)
            a = [scr(i_) for i_ in range(4)]
            iw = [scr(4 + i_) for i_ in range(4)]
            for i_ in range(4):
                tt(a[i_], cols[2 * i_], cols[2 * i_ + 1], ALU.max)
                tt(iw[i_], cols[2 * i_], cols[2 * i_ + 1], ALU.min)
            b1, j1, b2, j2, k1 = scr(8), scr(9), scr(10), scr(11), scr(12)
            tt(b1, a[0], a[1], ALU.max)
            tt(j1, a[0], a[1], ALU.min)
            tt(b2, a[2], a[3], ALU.max)
            tt(j2, a[2], a[3], ALU.min)
            tt(m1.unsqueeze(2), b1, b2, ALU.max)
            tt(k1, b1, b2, ALU.min)
            t1, t2, t3 = a[0], a[1], a[2]
            tt(t1, j1, j2, ALU.max)
            tt(t2, iw[0], iw[1], ALU.max)
            tt(t3, iw[2], iw[3], ALU.max)
            tt(j1, t1, k1, ALU.max)
            tt(j2, t2, t3, ALU.max)
            tt(m2.unsqueeze(2), j1, j2, ALU.max)

        m1g = tscr[:, 13 * 32:14 * 32]
        m2g = tscr[:, 14 * 32:15 * 32]
        rv.wait_tok(tok_lgt)
        tournament(lgt[:].rearrange("p (c e) -> p c e", e=NE), 32, m1g, m2g)
        m1o = e_scr[:, 16:24]
        m2o = e_scr[:, 24:32]
        tournament(lgtm_sb[:].rearrange("p (c e) -> p c e", e=NE), NE, m1o, m2o)

        for k in range(2):
            mg = m1g if k == 0 else m2g
            mo = m1o if k == 0 else m2o
            gred = statscr[:, 16:17]
            rv.wait(vs)
            rv.wait(ss)
            rv.op("tensor_reduce", gred, mg, AX.X, ALU.max, inc=vs)
            rt.wait(vs)
            rt.op("transpose", p_row[0:1, :], gred, identf[:], inc=ts)
            rv.wait(ts)
            rv.op("tensor_copy", out=wrow[0:1, 0:128], in_=p_row[0:1, :], inc=vs)
            rv.wait(vs)
            rv.op("tensor_reduce", wrow[0:1, 128:129], wrow[0:1, 0:128], AX.X,
                  ALU.max, inc=vs)
            gmax_vs = vs.n
            rt.wait(vs, gmax_vs)
            rt.op("matmul", p_col[:, 0:1], negonesf[0:1, :], wrow[0:1, 128:129],
                  start=True, stop=True, inc=ts)
            ngb = statscr[:, 17:18]
            rv.wait(ts)
            rv.op("tensor_copy", out=ngb, in_=p_col[:, 0:1], inc=vs)
            zc = statscr[:, 18:19]
            rs.wait(vs)
            rs.op("activation", tscr[:, 15 * 32:16 * 32], mg, AF.Exp,
                  bias=ngb, accum_out=zc, inc=ss)
            rt.wait(ss)
            rt.op("transpose", p_row[0:1, :], zc, identf[:], inc=ts)
            rv.wait(ts)
            rv.op("tensor_copy", out=wrow[0:1, 0:128], in_=p_row[0:1, :], inc=vs)
            rv.wait(vs)
            rv.op("tensor_reduce", wrow[0:1, 129:130], wrow[0:1, 0:128], AX.X,
                  ALU.add, inc=vs)
            rv.wait(vs)
            rv.op("reciprocal", wrow[0:1, 130:131], wrow[0:1, 129:130], inc=vs)
            rt.wait(vs)
            rt.op("matmul", p_col[:, 1:2], onesf[0:1, :], wrow[0:1, 130:131],
                  start=True, stop=True, inc=ts)
            rzb = statscr[:, 19:20]
            rv.wait(ts)
            rv.op("tensor_copy", out=rzb, in_=p_col[:, 1:2], inc=vs)
            eo = e_scr[:, 32:40]
            rs.wait(vs)
            rs.op("activation", eo, mo, AF.Exp, bias=ngb, inc=ss)
            rv.wait(ss)
            rv.op("tensor_scalar", statscr[:, k * 8:(k + 1) * 8], eo, rzb, None,
                  ALU.mult, inc=vs)
            for j in range(NTC):
                rv.wait(vs)
                rv.op("tensor_scalar",
                      ind_tm[:, (k * NTC + j) * NE:(k * NTC + j + 1) * NE],
                      lgtm_sb[:, j * NE:(j + 1) * NE],
                      mo[:, j:j + 1], None, ALU.is_equal, inc=vs)
                rt.wait(vs)
                rt.op("transpose", p_i8[:],
                      ind_tm[:, (k * NTC + j) * NE:(k * NTC + j + 1) * NE],
                      identb[:], inc=ts)
                rv.wait(ts)
                rv.op("tensor_copy",
                      out=indfm[0:NE, k * T + j * 128:k * T + (j + 1) * 128],
                      in_=p_i8[:], inc=vs)
        for half in range(2):
            for k in range(2):
                rt.wait(vs)
                rt.wait_tok(tok_c["blk2"])
                rt.op("matmul", p_m512[:], blk2[:],
                      indfm[0:NE, k * T + half * 512:k * T + (half + 1) * 512],
                      start=True, stop=True, inc=ts)
                rv.wait(ts)
                rv.op("tensor_copy",
                      out=mt4[:, (half * 2 + k) * 512:(half * 2 + k + 1) * 512],
                      in_=p_m512[:], inc=vs)
        barrier()
        stG.close()
        hscr = sbD("hscr", [128, 4 * 512], F32)
        rowscrD = sbD("rowscrD", [128, D], F32)
        ub_s = hscr[:, 0 * 512:1 * 512]
        gb_s = hscr[:, 1 * 512:2 * 512]
        su0_s = hscr[:, 2 * 512:3 * 512]
        su1_s = hscr[:, 3 * 512:4 * 512]

        wdn_v = wdnstr.rearrange("p (b d) -> p b d", b=2)
        bundle_used_ts = [0, 0]
        dn_used_ts = [0, 0]
        for half in range(2):
            hs = half * 512
            stP = ExitStack()
            p_r = stP.enter_context(nc.psum_tensor(f"p_r{half}", [128, 512], F32))
            p_wk = stP.enter_context(nc.psum_tensor(f"p_wk{half}", [128, 128], F32))
            p_wr = stP.enter_context(nc.psum_tensor(f"p_wr{half}", [1, 128], F32))
            wrow2 = stP.enter_context(nc.sbuf_tensor(f"wrow2{half}", [1, 128], F32))
            rg.wait(ts, dn_used_ts[0])
            rg.wait(ts, dn_used_ts[1])
            tok_upA = Qg.dma(rg, wdn_v[:, 0].rearrange("p (c e) -> p c e", c=NDC),
                             upA_T.rearrange("(c p) e -> p c e", p=128))
            tok_gtA = Qg.dma(rg, wdn_v[:, 1].rearrange("p (c e) -> p c e", c=NDC),
                             gateA_T.rearrange("(c p) e -> p c e", p=128))
            for k in range(2):
                for c4 in range(4):
                    tch = half * 4 + c4
                    rt.wait(vs)
                    rt.op("transpose", p_wr[0:1, :],
                          statscr[:, k * 8 + tch:k * 8 + tch + 1], identf[:], inc=ts)
                    rv.wait(ts)
                    rv.op("tensor_copy", out=wrow2[0:1, :], in_=p_wr[0:1, :], inc=vs)
                    rv_w = vs.n
                    rt.wait(vs, rv_w)
                    rt.op("matmul", p_wk[:], onesf[0:1, :], wrow2[0:1, :],
                          start=True, stop=True, inc=ts)
                    rv.wait(ts)
                    rv.op("tensor_copy", out=wkb[:, k * 512 + c4 * 128:
                                                 k * 512 + (c4 + 1) * 128],
                          in_=p_wk[:], inc=vs)
            for side in range(2):
                rt.wait_tok(tok_upA if side == 0 else tok_gtA)
                rt.wait(vs)
                for kd in range(NDC):
                    rt.op("matmul", p_r[:],
                          wdn_v[:, side, kd * 128:(kd + 1) * 128],
                          ln2_v[:, kd, hs:hs + 512],
                          start=(kd == 0), stop=(kd == NDC - 1),
                          inc=(ts if kd == NDC - 1 else None))
                r_done = ts.n
                for k in range(2):
                    rv.wait(ts, r_done)
                    rv.op("tensor_tensor", rtld[:, (side * 2 + k) * 512:
                                                 (side * 2 + k + 1) * 512],
                          p_r[:], mt4[:, (half * 2 + k) * 512:(half * 2 + k + 1) * 512],
                          ALU.mult, inc=vs)
            rtld_vs = vs.n
            dn_used_ts[0] = dn_used_ts[1] = ts.n
            stP.close()

            stH = ExitStack()
            p_u0 = [stH.enter_context(nc.psum_tensor(f"p_u0{half}{i}", [128, 512], F32))
                    for i in range(2)]
            p_g0 = [stH.enter_context(nc.psum_tensor(f"p_g0{half}{i}", [128, 512], F32))
                    for i in range(2)]
            p_loU = stH.enter_context(nc.psum_tensor(f"p_loU{half}", [128, 512], F32))
            p_loG = stH.enter_context(nc.psum_tensor(f"p_loG{half}", [128, 512], F32))
            p_sraw = [stH.enter_context(nc.psum_tensor(f"p_sraw{half}{i}", [128, 512], F32))
                      for i in range(2)]
            u0_free = [None, None]
            g0_free = [None, None]
            dnA_used = [0, 0, 0]
            dnA_tok = [None, None, None]
            loU_free = 0
            loG_free = 0
            ch_vs = 0
            sraw_ts = 0
            for hh in range(NHC):
                par = hh % 2
                rg.wait(ts, bundle_used_ts[par])
                bt = []
                bt.append(Qg.dma(rg, up_v[:, par, 0:2048].rearrange("p (c h) -> p c h", c=NDC),
                                 w_upT[:, hh * 128:(hh + 1) * 128]
                                 .rearrange("(c p) h -> p c h", p=128)))
                bt.append(Qg.dma(rg, up_v[:, par, 2048:4096].rearrange("p (c h) -> p c h", c=NDC),
                                 w_gateT[:, hh * 128:(hh + 1) * 128]
                                 .rearrange("(c p) h -> p c h", p=128)))
                bt.append(Qg.dma(rg, up_v[:, par, 4096:4224],
                                 upB_T[:, hh * 128:(hh + 1) * 128]))
                bt.append(Qg.dma(rg, up_v[:, par, 4224:4352],
                                 gateB_T[:, hh * 128:(hh + 1) * 128]))
                rg.wait(ts, dnA_used[hh % 3])
                dnA_tok[hh % 3] = Qg.dma(
                    rg, dnA_sb3[:, (hh % 3) * 128:(hh % 3 + 1) * 128],
                    dnA_T[hh * 128:(hh + 1) * 128, :])
                for tk in bt:
                    rt.wait_tok(tk)
                rt.wait_tok(u0_free[par])
                rt.wait(vs, rtld_vs)
                for kd in range(NDC):
                    rt.op("matmul", p_u0[par][:],
                          up_v[:, par, kd * 128:(kd + 1) * 128],
                          ln2_v[:, kd, hs:hs + 512],
                          start=(kd == 0), stop=(kd == NDC - 1),
                          inc=(ts if kd == NDC - 1 else None))
                baseu_ts = ts.n
                rt.wait_tok(g0_free[par])
                for kd in range(NDC):
                    rt.op("matmul", p_g0[par][:],
                          up_v[:, par, 2048 + kd * 128:2048 + (kd + 1) * 128],
                          ln2_v[:, kd, hs:hs + 512],
                          start=(kd == 0), stop=(kd == NDC - 1),
                          inc=(ts if kd == NDC - 1 else None))
                baseg_ts = ts.n
                if hh > 0:
                    rt.wait(vs, ch_vs)
                    rt.wait_tok(dnA_tok[(hh - 1) % 3])
                    for k in range(2):
                        rt.op("matmul", p_sraw[k][:],
                              dnA_sb3[:, ((hh - 1) % 3) * 128:((hh - 1) % 3 + 1) * 128],
                              chb[:, k * 512:(k + 1) * 512],
                              start=(hh - 1 == 0), stop=False,
                              inc=(ts if k == 1 else None))
                    sraw_ts = ts.n
                    dnA_used[(hh - 1) % 3] = ts.n
                rs.wait(ts, baseu_ts)
                rs.wait(vs)
                rs.wait(ss)
                rs.op("copy", out=ub_s, in_=p_u0[par][:], inc=ss)
                ubcopy_ss = ss.n
                rs.wait(ts, baseg_ts)
                rs.op("copy", out=gb_s, in_=p_g0[par][:], inc=ss)
                gbcopy_ss = ss.n
                rt.wait(ss, ubcopy_ss)
                rt.op("matmul", p_u0[par][:], up_v[:, par, 4096:4224],
                      rtld[:, 0:512], start=False, stop=True,
                      skip_group_check=True, inc=ts)
                lo0u_ts = ts.n
                rt.wait(vs, loU_free)
                rt.op("matmul", p_loU[:], up_v[:, par, 4096:4224],
                      rtld[:, 512:1024], start=True, stop=True, inc=ts)
                lo1u_ts = ts.n
                rt.wait(ss, gbcopy_ss)
                rt.op("matmul", p_g0[par][:], up_v[:, par, 4224:4352],
                      rtld[:, 1024:1536], start=False, stop=True,
                      skip_group_check=True, inc=ts)
                lo0g_ts = ts.n
                rt.wait(vs, loG_free)
                rt.op("matmul", p_loG[:], up_v[:, par, 4224:4352],
                      rtld[:, 1536:2048], start=True, stop=True, inc=ts)
                lo1g_ts = ts.n
                bundle_used_ts[par] = ts.n
                rs.wait(ts, lo0u_ts)
                rs.op("activation", su0_s, p_u0[par][:], AF.Silu, inc=ss)
                silu0_ss = ss.n
                u0_free[par] = (ss, silu0_ss)
                rv.wait(ts, lo1u_ts)
                rv.wait(ss, ubcopy_ss)
                rv.op("scalar_tensor_tensor", ub_s, p_loU[:], 1.0, ub_s,
                      ALU.mult, ALU.add, inc=vs)
                loU_free = vs.n
                u1_vs = vs.n
                rs.wait(vs, u1_vs)
                rs.op("activation", su1_s, ub_s, AF.Silu, inc=ss)
                silu1_ss = ss.n
                rv.wait(ss, silu0_ss)
                rv.wait(ts, lo0g_ts)
                rv.wait(ts, sraw_ts)
                rv.op("tensor_tensor", su0_s, su0_s, p_g0[par][:], ALU.mult, inc=vs)
                h0_vs = vs.n
                g0_free[par] = (vs, h0_vs)
                rv.wait(vs, h0_vs)
                rv.op("tensor_tensor", chb[:, 0:512], su0_s, wkb[:, 0:512],
                      ALU.mult, inc=vs)
                rv.wait(ts, lo1g_ts)
                rv.wait(ss, gbcopy_ss)
                rv.op("scalar_tensor_tensor", gb_s, p_loG[:], 1.0, gb_s,
                      ALU.mult, ALU.add, inc=vs)
                loG_free = vs.n
                rv.wait(ss, silu1_ss)
                rv.wait(vs)
                rv.op("tensor_tensor", su1_s, su1_s, gb_s, ALU.mult, inc=vs)
                rv.wait(vs)
                rv.op("tensor_tensor", chb[:, 512:1024], su1_s, wkb[:, 512:1024],
                      ALU.mult, inc=vs)
                rv.wait(vs)
                rv.op("tensor_tensor", mixed[:, hh * 512:(hh + 1) * 512],
                      chb[:, 0:512], chb[:, 512:1024], ALU.add, inc=vs)
                ch_vs = vs.n
            rt.wait(vs, ch_vs)
            rt.wait_tok(dnA_tok[(NHC - 1) % 3])
            for k in range(2):
                rt.op("matmul", p_sraw[k][:],
                      dnA_sb3[:, ((NHC - 1) % 3) * 128:((NHC - 1) % 3 + 1) * 128],
                      chb[:, k * 512:(k + 1) * 512],
                      start=False, stop=True, inc=(ts if k == 1 else None))
            sraw_done = ts.n
            for k in range(2):
                rv.wait(ts, sraw_done)
                rv.op("tensor_tensor", rtld[:, k * 512:(k + 1) * 512],
                      p_sraw[k][:], mt4[:, (half * 2 + k) * 512:(half * 2 + k + 1) * 512],
                      ALU.mult, inc=vs)
            stld_vs = vs.n
            barrier()
            stH.close()

            stE = ExitStack()
            p_d = [stE.enter_context(nc.psum_tensor(f"p_d{half}{i}", [128, 512], F32))
                   for i in range(8)]
            dbank_free = [0] * 8
            for tpair in range(2):
                for hh in range(NHC):
                    par = hh % 2
                    rg.wait(ts, dn_used_ts[par])
                    tok_dn = Qg.dma(rg, wdn_v[:, par],
                                    w_downT[hh * 128:(hh + 1) * 128, :])
                    rt.wait_tok(tok_dn)
                    if hh == 0:
                        for bi in range(8):
                            rt.wait(vs, dbank_free[bi])
                    for tq2 in range(2):
                        tloc = tpair * 2 + tq2
                        for qb in range(4):
                            rt.op("matmul", p_d[tq2 * 4 + qb][:],
                                  mixed[:, hh * 512 + tloc * 128:
                                        hh * 512 + (tloc + 1) * 128],
                                  wdn_v[:, par, qb * 512:(qb + 1) * 512],
                                  start=(hh == 0), stop=False,
                                  inc=(ts if (tq2 == 1 and qb == 3) else None))
                    dn_used_ts[par] = ts.n
                rt.wait(vs, stld_vs)
                rt.wait_tok(tok_dnB)
                for tq2 in range(2):
                    tloc = tpair * 2 + tq2
                    for qb in range(4):
                        for k in range(2):
                            rt.op("matmul", p_d[tq2 * 4 + qb][:],
                                  rtld[:, k * 512 + tloc * 128:
                                       k * 512 + (tloc + 1) * 128],
                                  dnB_sb[:, qb * 512:(qb + 1) * 512],
                                  start=False, stop=(k == 1),
                                  inc=(ts if (tq2 == 1 and qb == 3 and k == 1)
                                       else None))
                dn_done = ts.n
                for tq2 in range(2):
                    tch = half * 4 + tpair * 2 + tq2
                    rv.wait(ts, dn_done)
                    rv.wait(vs)
                    for qb in range(4):
                        rv.op("tensor_copy",
                              out=rowscrD[:, qb * 512:(qb + 1) * 512],
                              in_=p_d[tq2 * 4 + qb][:], inc=vs)
                        dbank_free[tq2 * 4 + qb] = vs.n
                    mean, rstd = ln_stats(rowscrD[:])
                    rv.wait(ss)
                    rv.op("tensor_scalar", rowscrD[:], rowscrD[:], mean, rstd,
                          ALU.subtract, op1=ALU.mult, inc=vs)
                    rv.wait(vs)
                    rv.op("tensor_tensor", x_res[:, tch * D:(tch + 1) * D],
                          x_res[:, tch * D:(tch + 1) * D], rowscrD[:],
                          ALU.add, inc=vs)
            stE.close()

        if "d_mixed" in dbg_out:
            rg.wait(vs)
            Qg.dma(rg, dbg_out["d_mixed"], mixed[:])

        barrier()
        stD.close()
        stL.close()

        rv_final = vs.n
        ry.wait(vs, rv_final)
        tok_out = Qy.dma(ry, out_own.rearrange("(j p) d -> p j d", p=128),
                         x_res[:].rearrange("p (j d) -> p j d", j=NTC))
        ry.wait_tok(tok_out)

        with nc.Block() as block:
            block.tensor(lambda e: rt.replay(e))
            block.vector(lambda e: rv.replay(e))
            block.scalar(lambda e: rs.replay(e))
            block.gpsimd(lambda e: rg.replay(e))
            block.sync(lambda e: ry.replay(e))

    return nc



def host_prep(inputs):
    x = np.asarray(inputs["L0_attn_x"])
    w_qkv = np.ascontiguousarray(inputs["L0_attn_w_qkv"], dtype=np.float32)
    w_out = np.ascontiguousarray(inputs["L0_attn_w_out"], dtype=np.float32)
    gate_w = np.asarray(inputs["L0_ffn_gate_w"])
    w_up = np.asarray(inputs["L0_ffn_w_up"])
    w_gate = np.asarray(inputs["L0_ffn_w_gate"])
    w_down = np.asarray(inputs["L0_ffn_w_down"])
    up_A = np.asarray(inputs["L0_ffn_up_A"])
    up_B = np.asarray(inputs["L0_ffn_up_B"])
    gate_A = np.asarray(inputs["L0_ffn_gate_A"])
    gate_B = np.asarray(inputs["L0_ffn_gate_B"])
    down_A = np.asarray(inputs["L0_ffn_down_A"])
    down_B = np.asarray(inputs["L0_ffn_down_B"])

    shared = {
        "w_qkv": w_qkv,
        "w_out": w_out,
        "gate_wT": np.ascontiguousarray(gate_w.T),
        "w_upT": np.ascontiguousarray(w_up.T),
        "w_gateT": np.ascontiguousarray(w_gate.T),
        "w_downT": np.ascontiguousarray(w_down.T),
        "upA_T": np.ascontiguousarray(up_A.reshape(ER, D).T),
        "gateA_T": np.ascontiguousarray(gate_A.reshape(ER, D).T),
        "dnA_T": np.ascontiguousarray(down_A.reshape(ER, HFF).T),
        "upB_T": np.ascontiguousarray(up_B.transpose(0, 2, 1).reshape(ER, HFF)),
        "gateB_T": np.ascontiguousarray(gate_B.transpose(0, 2, 1).reshape(ER, HFF)),
        "dnB_T": np.ascontiguousarray(down_B.transpose(0, 2, 1).reshape(ER, D)),
    }
    in_maps = []
    for core in range(8):
        b, r = core // 4, core % 4
        xb = x[b].astype(np.float32, copy=False)
        x_ownc = np.ascontiguousarray(
            xb.reshape(32, 128, D)[r::4].reshape(T, D))
        mval = NEGBIG / RSQ
        m = np.zeros((128, 512), dtype=np.float32)
        for rp in range(4):
            if rp > r:
                m[:, rp * 128:(rp + 1) * 128] = mval
            elif rp == r:
                tri = np.triu(np.full((128, 128), mval, dtype=np.float32), k=1)
                m[:, rp * 128:(rp + 1) * 128] = tri
        d = dict(shared)
        d["x_own"] = x_ownc
        d["mask512"] = m
        in_maps.append(d)
    return in_maps


def kernel(**inputs):
    in_maps = host_prep(inputs)
    nc = build(debug=DEBUG)
    res = run_bass_kernel_spmd(nc, in_maps, core_ids=list(range(8)))
    out = np.empty((2, S, D), dtype=np.float32)
    for core in range(8):
        b, r = core // 4, core % 4
        o = res.results[core]["out_own"].reshape(NTC, 128, D)
        out[b].reshape(32, 128, D)[r::4] = o
    return out


# version 06
# speedup vs baseline: 1.3738x; 1.0787x over previous
from contextlib import ExitStack

import numpy as np
import ml_dtypes
import concourse.bass as bass
import concourse.mybir as mybir
from concourse.bass_utils import run_bass_kernel_spmd

F32 = mybir.dt.float32
BF16 = mybir.dt.bfloat16
AF = mybir.ActivationFunctionType
ALU = mybir.AluOpType
AX = mybir.AxisListType

D = 2048
NDC = 16
HEADS = 16
HD = 128
HFF = 5632
NHC = 44
T = 1024
NTC = 8
S = 4096
NE = 8
RK = 16
ER = 128
EPS = 1e-5
RSQ = float(1.0 / np.sqrt(128.0))
NEGBIG = -1e9

DEBUG = ()



class Sem:
    __slots__ = ("name", "n", "h")

    def __init__(self, name):
        self.name = name
        self.n = 0
        self.h = None


class R:

    def __init__(self, name):
        self.name = name
        self.cmds = []
        self.hw = {}

    def wait(self, sem, val=None):
        v = sem.n if val is None else val
        if v <= self.hw.get(sem, -1):
            return
        self.hw[sem] = v
        self.cmds.append(("w", sem, v))

    def wait_tok(self, tok):
        if tok is not None:
            self.wait(tok[0], tok[1])

    def op(self, method, *args, inc=None, inc_k=1, **kw):
        if inc is not None:
            inc.n += inc_k
        self.cmds.append(("o", method, args, kw, inc, inc_k))
        return inc.n if inc is not None else None

    def replay(self, e):
        for c in self.cmds:
            if c[0] == "w":
                e.wait_ge(c[1].h, c[2])
            else:
                _, m, args, kw, inc, k = c
                instr = getattr(e, m)(*args, **kw)
                if inc is not None:
                    instr.then_inc(inc.h, k)


class Q:

    def __init__(self, name, nslots):
        self.slots = [Sem(f"{name}{i}") for i in range(nslots)]
        self.i = 0

    def dma(self, rec, out, in_):
        sm = self.slots[self.i % len(self.slots)]
        self.i += 1
        rec.wait(sm, sm.n)
        rec.op("dma_start", out=out, in_=in_, inc=sm, inc_k=16)
        return (sm, sm.n)



def build(debug=()):
    nc = bass.Bass("TRN2", target_bir_lowering=False)

    def inp(name, shape, dtype=F32):
        return nc.dram_tensor(name, shape, dtype, kind="ExternalInput").ap()

    x_own = inp("x_own", [T, D])
    w_qkv = inp("w_qkv", [D, 3 * D])
    w_out = inp("w_out", [D, D])
    gate_wT = inp("gate_wT", [D, NE])
    w_upT = inp("w_upT", [D, HFF])
    w_gateT = inp("w_gateT", [D, HFF])
    w_downT = inp("w_downT", [HFF, D])
    upA_T = inp("upA_T", [D, ER])
    gateA_T = inp("gateA_T", [D, ER])
    dnA_T = inp("dnA_T", [HFF, ER])
    upB_T = inp("upB_T", [ER, HFF])
    gateB_T = inp("gateB_T", [ER, HFF])
    dnB_T = inp("dnB_T", [ER, D])
    mask512 = inp("mask512", [128, 512])

    out_own = nc.dram_tensor("out_own", [T, D], F32, kind="ExternalOutput").ap()

    kv_in_k = nc.dram_tensor("kv_in_k", [HEADS * HD, T], BF16, kind="Internal").ap()
    kv_in_v = nc.dram_tensor("kv_in_v", [T, HEADS * HD], BF16, kind="Internal").ap()
    kv_out_k = nc.dram_tensor("kv_out_k", [4 * HEADS * HD, T], BF16, kind="Internal").ap()
    kv_out_v = nc.dram_tensor("kv_out_v", [4 * T, HEADS * HD], BF16, kind="Internal").ap()
    kn_in = nc.dram_tensor("kn_in", [HEADS, T], F32, kind="Internal").ap()
    w_upT_bf = nc.dram_tensor("w_upT_bf", [D, HFF], BF16, kind="Internal").ap()
    w_gateT_bf = nc.dram_tensor("w_gateT_bf", [D, HFF], BF16, kind="Internal").ap()
    w_downT_bf = nc.dram_tensor("w_downT_bf", [HFF, D], BF16, kind="Internal").ap()
    kn_out = nc.dram_tensor("kn_out", [4 * HEADS, T], F32, kind="Internal").ap()
    lg_in = nc.dram_tensor("lg_in", [T, NE], F32, kind="Internal").ap()
    lg_out = nc.dram_tensor("lg_out", [4 * T, NE], F32, kind="Internal").ap()
    GROUPS = [[0, 1, 2, 3], [4, 5, 6, 7]]

    identb_d = nc.inline_tensor(np.eye(128, dtype=ml_dtypes.bfloat16), name="c_identb").ap()
    identf_d = nc.inline_tensor(np.eye(128, dtype=np.float32), name="c_identf").ap()
    onesf_d = nc.inline_tensor(np.ones((1, 128), dtype=np.float32), name="c_onesf").ap()
    onescol_d = nc.inline_tensor(np.ones((128, 1), dtype=ml_dtypes.bfloat16), name="c_onescol").ap()
    negonesf_d = nc.inline_tensor(np.full((1, 128), -1.0, dtype=np.float32), name="c_negonesf").ap()
    blk_np = np.zeros((NE, ER), dtype=ml_dtypes.bfloat16)
    for e_ in range(NE):
        blk_np[e_, e_ * RK:(e_ + 1) * RK] = 2.0
    blk2_d = nc.inline_tensor(blk_np, name="c_blk2").ap()

    dbg_out = {}
    for name, shape, dt in [
        ("d_lnfm", [128, NDC * T], F32),
        ("d_q", [128, HEADS * T], F32),
        ("d_kvk", [HEADS * HD, T], F32),
        ("d_kvv", [T, HEADS * HD], F32),
        ("d_attnout", [128, HEADS * T], F32),
        ("d_x2", [128, NTC * D], F32),
        ("d_ln2fm", [128, NDC * T], F32),
        ("d_logits", [T, NE], F32),
        ("d_gstats", [128, 64], F32),
        ("d_mixed", [128, NHC * 512], F32),
        ("d_ffntm", [128, 4 * D], F32),
    ]:
        if name in debug:
            dbg_out[name] = nc.dram_tensor(name, shape, dt, kind="ExternalOutput").ap()

    ts = Sem("ts")
    vs = Sem("vs")
    ss = Sem("ss")
    cskv = Sem("cskv")
    cslg = Sem("cslg")
    csn = Sem("csn")
    Qg = Q("gq", 8)
    Qy = Q("sq", 8)
    SEMS = [ts, vs, ss, cskv, cslg, csn] + Qg.slots + Qy.slots

    rt, rv, rs, rg, ry = R("tensor"), R("vector"), R("scalar"), R("gpsimd"), R("sync")
    ALLR = [rt, rv, rs, rg, ry]

    def barrier():
        for r in ALLR:
            for sm in SEMS:
                r.wait(sm)

    st = ExitStack()
    with st:
        def sb(name, shape, dtype):
            return st.enter_context(nc.sbuf_tensor(name, shape, dtype))

        def psum(name, shape, dtype=F32):
            return st.enter_context(nc.psum_tensor(name, shape, dtype))

        for sm in SEMS:
            sm.h = st.enter_context(nc.semaphore(sm.name))

        x_res = sb("x_res", [128, NTC * D], F32)
        fm_buf = sb("fm_buf", [128, NDC * T], BF16)
        statscr = sb("statscr", [128, 48], F32)
        lgtm_sb = sb("lgtm_sb", [128, NTC * NE], F32)
        identb = sb("identb", [128, 128], BF16)
        identf = sb("identf", [128, 128], F32)
        onesf = sb("onesf", [1, 128], F32)
        onescol = sb("onescol", [128, 1], BF16)
        negonesf = sb("negonesf", [1, 128], F32)
        blk2 = sb("blk2", [NE, ER], BF16)

        fm_v = fm_buf[:].rearrange("p (c t) -> p c t", c=NDC)
        stQ = ExitStack()
        q_all = stQ.enter_context(nc.sbuf_tensor("q_all", [128, HEADS * T], BF16))
        qn_sb = stQ.enter_context(nc.sbuf_tensor("qn_sb", [128, 128], F32))
        mask_sb = stQ.enter_context(nc.sbuf_tensor("mask_sb", [128, 512], F32))

        tok_x = Qy.dma(ry, x_res[:].rearrange("p (j d) -> p j d", j=NTC),
                       x_own.rearrange("(j p) d -> p j d", p=128))
        tok_c = {}
        for nm, dst, src in [("identb", identb[:], identb_d),
                             ("identf", identf[:], identf_d),
                             ("onesf", onesf[:], onesf_d),
                             ("onescol", onescol[:], onescol_d),
                             ("negonesf", negonesf[:], negonesf_d),
                             ("blk2", blk2[:], blk2_d),
                             ("mask", mask_sb[:], mask512)]:
            tok_c[nm] = Qy.dma(ry, dst, src)

        def ln_stats(xin):
            bnbuf = statscr[:, 16:40]
            mv = statscr[:, 40:42]
            mean, rstd = statscr[:, 40:41], statscr[:, 44:45]
            rv.wait(vs)
            rv.wait(ss)
            for c_ in range(4):
                rv.op("bn_stats", out=bnbuf[:, c_ * 6:(c_ + 1) * 6],
                      in_=xin[:, c_ * 512:(c_ + 1) * 512], inc=vs)
            rv.wait(vs)
            rv.op("bn_aggr", out=mv, in_=bnbuf, inc=vs)
            rv.wait(vs)
            rv.op("tensor_scalar", statscr[:, 42:43], statscr[:, 41:42], EPS, None,
                  ALU.add, inc=vs)
            rs.wait(vs)
            rs.op("activation", statscr[:, 43:44], statscr[:, 42:43], AF.Ln, inc=ss)
            rs.wait(ss)
            rs.op("activation", rstd, statscr[:, 43:44], AF.Exp, scale=-0.5, inc=ss)
            return mean, rstd

        stA = ExitStack()
        sbA = lambda n_, sh, dt: stA.enter_context(nc.sbuf_tensor(n_, sh, dt))
        psA = lambda n_, sh, dt=F32: stA.enter_context(nc.psum_tensor(n_, sh, dt))
        lntm = sbA("lntm", [128, 2 * D], BF16)
        wqkv_sb = sbA("wqkv_sb", [128, 3 * NDC * 512], BF16)
        kvstage = sbA("kvstage", [128, 4 * 512], BF16)
        sqst = sbA("sqst", [128, 2 * 512], BF16)
        kn_row = sbA("kn_row", [1, 512], F32)
        qn_row = sbA("qn_row", [1, 512], F32)
        p_lt = [psA(f"p_lt{i}", [128, 1024], BF16) for i in range(2)]
        p_mm = [psA(f"p_mm{i}", [128, 512], F32) for i in range(4)]
        p_nr = psA("p_nr", [1, 512])
        p_qt = psA("p_qt", [128, 4])
        nstate = {"sq": [0, 0], "nr": None, "qt": 0, "knrow": None, "qnrow": 0}

        def norms_from(src_bf, h_, half_, is_q):
            p2 = (h_ * 2 + half_) % 2
            rs.wait(ts, nstate["sq"][p2])
            rs.wait(ss)
            rs.op("activation", sqst[:, p2 * 512:(p2 + 1) * 512], src_bf,
                  AF.Square, inc=ss)
            sq_ss = ss.n
            rt.wait(ss, sq_ss)
            rt.wait_tok(nstate["nr"])
            rt.wait_tok(tok_c["onescol"])
            rt.op("matmul", p_nr[0:1, :], onescol[:],
                  sqst[:, p2 * 512:(p2 + 1) * 512], start=True, stop=True, inc=ts)
            nstate["sq"][p2] = ts.n
            nr_ts = ts.n
            if is_q:
                rv.wait(ts, nr_ts)
                rv.wait(ts, nstate["qnrow"])
                rv.op("tensor_copy", out=qn_row[0:1, :], in_=p_nr[0:1, :], inc=vs)
                nstate["nr"] = (vs, vs.n)
                rt.wait(vs, vs.n)
                rt.wait(vs, nstate["qt"])
                for i4 in range(4):
                    rt.op("transpose", p_qt[:, i4:i4 + 1],
                          qn_row[0:1, i4 * 128:(i4 + 1) * 128],
                          identf[0:1, 0:1], inc=(ts if i4 == 3 else None))
                nstate["qnrow"] = ts.n
                rv.wait(ts, ts.n)
                rv.op("tensor_copy",
                      out=qn_sb[:, h_ * 8 + half_ * 4:h_ * 8 + half_ * 4 + 4],
                      in_=p_qt[:], inc=vs)
                nstate["qt"] = vs.n
            else:
                rv.wait(ts, nr_ts)
                rv.wait_tok(nstate["knrow"])
                rv.op("tensor_copy", out=kn_row[0:1, :], in_=p_nr[0:1, :], inc=vs)
                nstate["nr"] = (vs, vs.n)
                ry.wait(vs)
                nstate["knrow"] = Qy.dma(
                    ry, kn_in[h_:h_ + 1, half_ * 512:(half_ + 1) * 512],
                    kn_row[0:1, :])


        lt_done_ts = [0, 0]
        for j in range(NTC):
            par = j % 2
            xin = x_res[:, j * D:(j + 1) * D]
            rv.wait_tok(tok_x)
            rs.wait_tok(tok_x)
            mean, rstd = ln_stats(xin)
            rv.wait(ss)
            rv.wait(ts, lt_done_ts[par])
            rv.op("tensor_scalar", lntm[:, par * D:(par + 1) * D], xin,
                  mean, rstd, ALU.subtract, op1=ALU.mult, inc=vs)
            rt.wait(vs)
            rt.wait_tok(tok_c["identb"])
            for k in range(NDC):
                bank = p_lt[k // 8]
                rt.op("transpose", bank[:, (k % 8) * 128:(k % 8 + 1) * 128],
                      lntm[:, par * D + k * 128: par * D + (k + 1) * 128],
                      identb[:], inc=(ts if k % 8 == 7 else None))
            lt_done_ts[par] = ts.n
            rv.wait(ts)
            for half in range(2):
                rv.op("tensor_copy",
                      out=fm_v[:, half * 8:(half + 1) * 8, j * 128:(j + 1) * 128],
                      in_=p_lt[half][:].rearrange("p (c t) -> p c t", c=8),
                      inc=vs)
            if j == 3:
                lnfm_half0_vs = vs.n
        lnfm_done_vs = vs.n

        if "d_lnfm" in dbg_out:
            rg.wait(vs)
            Qg.dma(rg, dbg_out["d_lnfm"], fm_buf[:])

        wq_v = wqkv_sb[:].rearrange("p (b c n) -> p b c n", b=3, c=NDC)
        grp_consumed_ts = [0, 0, 0]
        bank_free = [0, 0, 0, 0]
        stage_free = [None, None, None, None]

        def mm_to_bank(bi, lhsT_rhs_list):
            rt.wait(ss, bank_free[bi])
            n = len(lhsT_rhs_list)
            for i, (lt_, rh_) in enumerate(lhsT_rhs_list):
                rt.op("matmul", p_mm[bi][:], lt_, rh_,
                      start=(i == 0), stop=(i == n - 1),
                      inc=(ts if i == n - 1 else None))
            return ts.n

        ag_launched = {}
        for gi, g in enumerate([4, 5, 6, 7, 8, 9, 10, 11, 0, 1, 2, 3]):
            par = gi % 3
            rg.wait(ts, grp_consumed_ts[par])
            tok_g = Qg.dma(rg, wq_v[:, par],
                           w_qkv[:, g * 512:(g + 1) * 512].rearrange("(c p) n -> p c n", p=128))
            rt.wait_tok(tok_g)
            rt.wait(vs, lnfm_half0_vs)
            if g < 8:
                for c in range(4):
                    for half in range(2):
                        if half == 1:
                            rt.wait(vs, lnfm_done_vs)
                        bi = 2 * (c % 2) + half
                        mm_done = mm_to_bank(bi, [
                            (wq_v[:, par, k, c * 128:(c + 1) * 128],
                             fm_v[:, k, half * 512:(half + 1) * 512])
                            for k in range(NDC)])
                        col = g * 512 + c * 128
                        rs.wait(ts, mm_done)
                        if g < 4:
                            h_ = col // 128
                            qa = q_all[:, h_ * T + half * 512:h_ * T + half * 512 + 512]
                            rs.op("copy", out=qa, in_=p_mm[bi][:], inc=ss)
                            norms_from(qa, h_, half, True)
                        else:
                            si = bi
                            stg = kvstage[:, si * 512:(si + 1) * 512]
                            rs.wait_tok(stage_free[si])
                            rs.op("copy", out=stg, in_=p_mm[bi][:], inc=ss)
                            h_ = (col - 2048) // 128
                            ry.wait(ss)
                            stage_free[si] = Qy.dma(
                                ry,
                                kv_in_k[h_ * 128:(h_ + 1) * 128,
                                        half * 512:(half + 1) * 512],
                                stg)
                            norms_from(stg, h_, half, False)
                        bank_free[bi] = ss.n
            else:
                for tc in range(NTC):
                    if tc >= 4:
                        rt.wait(vs, lnfm_done_vs)
                    bi = 2 * (tc % 2)
                    mm_done = mm_to_bank(bi, [
                        (fm_v[:, k, tc * 128:(tc + 1) * 128], wq_v[:, par, k, :])
                        for k in range(NDC)])
                    si = tc % 4
                    stg = kvstage[:, si * 512:(si + 1) * 512]
                    rs.wait(ts, mm_done)
                    rs.wait_tok(stage_free[si])
                    rs.op("copy", out=stg, in_=p_mm[bi][:], inc=ss)
                    bank_free[bi] = ss.n
                    vcol = (g - 8) * 512
                    ry.wait(ss)
                    stage_free[si] = Qy.dma(
                        ry, kv_in_v[tc * 128:(tc + 1) * 128, vcol:vcol + 512], stg)
            grp_consumed_ts[par] = ts.n
            if g == 7:
                for sm in Qy.slots:
                    rg.wait(sm)
                rg.op("collective_compute", "AllGather", ALU.bypass,
                      replica_groups=GROUPS, ins=[kv_in_k], outs=[kv_out_k],
                      inc=cskv)
                ag_launched["k"] = (cskv, cskv.n)
                rg.op("collective_compute", "AllGather", ALU.bypass,
                      replica_groups=GROUPS, ins=[kn_in], outs=[kn_out],
                      inc=csn)
                ag_launched["kn"] = (csn, csn.n)
            elif g == 11:
                for sm in Qy.slots:
                    rg.wait(sm)
                rg.op("collective_compute", "AllGather", ALU.bypass,
                      replica_groups=GROUPS, ins=[kv_in_v], outs=[kv_out_v],
                      inc=cskv)
                ag_launched["v"] = (cskv, cskv.n)

        wconv_toks = []
        for src, dst, rows in [(w_upT, w_upT_bf, D), (w_gateT, w_gateT_bf, D),
                               (w_downT, w_downT_bf, HFF)]:
            q4 = rows // 4
            for ci in range(4):
                wconv_toks.append(Qg.dma(rg, dst[ci * q4:(ci + 1) * q4, :],
                                         src[ci * q4:(ci + 1) * q4, :]))

        if "d_q" in dbg_out:
            rg.wait(ss)
            Qg.dma(rg, dbg_out["d_q"], q_all[:])

        kv_gathered = (cskv, cskv.n)

        if "d_kvk" in dbg_out:
            rg.wait_tok(kv_gathered)
            Qg.dma(rg, dbg_out["d_kvk"], kv_in_k)
        if "d_kvv" in dbg_out:
            rg.wait_tok(kv_gathered)
            Qg.dma(rg, dbg_out["d_kvv"], kv_in_v)

        barrier()
        stA.close()

        stB = ExitStack()
        sbB = lambda n_, sh, dt: stB.enter_context(nc.sbuf_tensor(n_, sh, dt))
        psB = lambda n_, sh, dt=F32: stB.enter_context(nc.psum_tensor(n_, sh, dt))
        kh_sb = sbB("kh_sb", [128, 2 * 8 * 512], BF16)
        vh_sb = sbB("vh_sb", [128, 2 * 8 * 512], BF16)
        pb = [sbB("pb0", [128, 4096], BF16), sbB("pb1", [128, 4096], BF16)]
        pt_sb = sbB("pt_sb", [128, 2 * 1024], BF16)
        dscr = sbB("dscr", [128, 512], F32)
        kn_all2 = sbB("kn_all2", [64, 1024], F32)
        kn_all = sbB("kn_all", [128, 132], F32)
        kncol = sbB("kncol", [64, 1], F32)
        kmrow = sbB("kmrow", [1, 80], F32)
        qmrow = sbB("qmrow", [1, 128], F32)
        negbs_all = sbB("negbs_all", [128, 128], F32)
        zrow_sb = sbB("zrow_sb", [1, 512], F32)
        rzb_sb = sbB("rzb_sb", [128, 256], F32)
        p_s = [psB(f"p_s{i}", [128, 512]) for i in range(3)]
        p_o = [psB(f"p_o{i}", [128, 256]) for i in range(2)]
        p_misc = psB("p_misc", [128, 256])
        p_z = psB("p_z", [1, 256])

        kh_v = kh_sb[:].rearrange("p (b j f t) -> p b j f t", b=2, j=8, f=4)
        vh_v = vh_sb[:].rearrange("p (b j f t) -> p b j f t", b=2, j=8, f=4)


        khb_consumed = [0, 0]
        kv_tok = [None, None]
        sbank_free = [None, None, None]
        sbank_i = 0
        o_free = [0, 0]
        pb_free = [0, 0]
        zrow_free = 0

        ry.wait_tok(ag_launched["kn"])
        tok_kn = Qy.dma(ry, kn_all2[0:64, :], kn_out)
        rv.wait_tok(tok_kn)
        rv.op("tensor_reduce", kncol[0:64, :], kn_all2[0:64, :], AX.X, ALU.max,
              inc=vs)
        rt.wait(vs)
        rt.op("transpose", p_misc[0:1, 0:64], kncol[0:64, :],
              identf[0:64, 0:64], inc=ts)
        rv.wait(ts)
        rv.op("tensor_copy", out=kmrow[0:1, 0:64], in_=p_misc[0:1, 0:64], inc=vs)
        rv.wait(vs)
        rv.op("tensor_tensor", kmrow[0:1, 64:80], kmrow[0:1, 0:16],
              kmrow[0:1, 16:32], ALU.max, inc=vs)
        rv.wait(vs)
        rv.op("tensor_tensor", kmrow[0:1, 64:80], kmrow[0:1, 64:80],
              kmrow[0:1, 32:48], ALU.max, inc=vs)
        rv.wait(vs)
        rv.op("tensor_tensor", kmrow[0:1, 64:80], kmrow[0:1, 64:80],
              kmrow[0:1, 48:64], ALU.max, inc=vs)
        rt.wait(vs)
        rt.op("transpose", p_misc[:, 128:256], qn_sb[:], identf[:], inc=ts)
        rv.wait(ts)
        rv.op("tensor_copy", out=kn_all[0:128, 0:128], in_=p_misc[:, 128:256],
              inc=vs)
        rv.wait(vs)
        rv.op("tensor_reduce", kncol[0:64, 0:1] if False else kn_all[0:128, 128:129],
              kn_all[0:128, 0:128], AX.X, ALU.max, inc=vs)
        rt.wait(vs)
        rt.op("transpose", p_misc[0:1, 0:128], kn_all[0:128, 128:129],
              identf[:], inc=ts)
        rv.wait(ts)
        rv.op("tensor_copy", out=kmrow[0:1, 0:64] if False else qmrow[0:1, 0:128],
              in_=p_misc[0:1, 0:128], inc=vs)
        for j8 in range(8):
            rv.wait(vs)
            rv.op("tensor_tensor", qmrow[0:1, j8:128:8], qmrow[0:1, j8:128:8],
                  kmrow[0:1, 64:80], ALU.add, inc=vs)
        rv.wait(vs)
        rv.op("tensor_scalar", qmrow[0:1, 0:128], qmrow[0:1, 0:128],
              -RSQ / 2.0, None, ALU.mult, inc=vs)
        rt.wait(vs)
        rt.op("matmul", p_misc[:, 128:256], onesf[0:1, :], qmrow[0:1, 0:128],
              start=True, stop=True, inc=ts)
        rv.wait(ts)
        rv.op("tensor_copy", out=negbs_all[:], in_=p_misc[:, 128:256], inc=vs)
        negbs_vs = vs.n

        for h in range(HEADS):
            hb = h % 2
            ry.wait_tok(ag_launched["k"])
            ry.wait_tok(ag_launched["v"])
            ry.wait(ts, khb_consumed[hb])
            toks = []
            for r_ in range(4):
                toks.append(Qy.dma(
                    ry, kh_v[:, hb, :, r_, :],
                    kv_out_k[r_ * 2048 + h * 128:r_ * 2048 + (h + 1) * 128, :]
                    .rearrange("p (j t) -> p j t", j=8)))
                toks.append(Qy.dma(
                    ry, vh_v[:, hb, :, r_, :],
                    kv_out_v[r_ * T:(r_ + 1) * T, h * 128:(h + 1) * 128]
                    .rearrange("(j p) t -> p j t", p=128)))
            kv_tok[hb] = toks

            for pj in (0, 2, 4, 6):
                opar = (pj // 2) % 2
                rt.wait(vs, o_free[opar])
                for ti, jt in enumerate((pj, pj + 1)):
                    rl = jt + 1
                    pbuf = pb[ti]
                    bias_ap = negbs_all[:, h * 8 + jt:h * 8 + jt + 1]
                    for lv in range(rl):
                        bi = sbank_i % 3
                        sbank_i += 1
                        rt.wait_tok(sbank_free[bi])
                        for tk in kv_tok[hb]:
                            rt.wait_tok(tk)
                        for r_ in range(4):
                            rt.op("matmul",
                                  p_s[bi][:, r_ * 128:(r_ + 1) * 128],
                                  kh_sb[:, hb * 4096 + lv * 512 + r_ * 128:
                                        hb * 4096 + lv * 512 + (r_ + 1) * 128],
                                  q_all[:, h * T + jt * 128:h * T + (jt + 1) * 128],
                                  start=True, stop=True,
                                  inc=(ts if r_ == 3 else None))
                        mm_done = ts.n
                        dst = pbuf[:, lv * 512:(lv + 1) * 512]
                        if lv == jt:
                            rv.wait(ts, mm_done)
                            rv.wait_tok(tok_c["mask"])
                            rv.wait(ss)
                            rv.op("tensor_tensor", dscr[:], p_s[bi][:],
                                  mask_sb[:], ALU.add, inc=vs)
                            sbank_free[bi] = (vs, vs.n)
                            rs.wait(vs)
                            rs.wait(ts, pb_free[ti])
                            rs.op("activation", dst, dscr[:], AF.Exp,
                                  bias=bias_ap, scale=RSQ, inc=ss)
                        else:
                            rs.wait(ts, mm_done)
                            rs.wait(vs, negbs_vs)
                            rs.wait(ts, pb_free[ti])
                            rs.op("activation", dst, p_s[bi][:], AF.Exp,
                                  bias=bias_ap, scale=RSQ, inc=ss)
                            sbank_free[bi] = (ss, ss.n)
                    exp_done = ss.n
                    rt.wait(ss, exp_done)
                    rt.wait_tok(tok_c["onescol"])
                    rt.wait(vs, zrow_free)
                    for m_ in range(rl * 4):
                        rt.op("matmul", p_z[0:1, ti * 128:(ti + 1) * 128],
                              onescol[:], pbuf[:, m_ * 128:(m_ + 1) * 128],
                              start=(m_ == 0), stop=(m_ == rl * 4 - 1),
                              inc=(ts if m_ == rl * 4 - 1 else None))
                    rv.wait(ts)
                    rv.op("tensor_copy", out=zrow_sb[0:1, ti * 128:(ti + 1) * 128],
                          in_=p_z[0:1, ti * 128:(ti + 1) * 128], inc=vs)
                    rv.wait(vs)
                    rv.op("reciprocal", zrow_sb[0:1, 256 + ti * 128:256 + (ti + 1) * 128],
                          zrow_sb[0:1, ti * 128:(ti + 1) * 128], inc=vs)
                    rz_vs = vs.n
                    rt.wait(vs, rz_vs)
                    rt.op("matmul", p_misc[:, ti * 128:(ti + 1) * 128],
                          onesf[0:1, :],
                          zrow_sb[0:1, 256 + ti * 128:256 + (ti + 1) * 128],
                          start=True, stop=True, inc=ts)
                    rv.wait(ts)
                    rv.op("tensor_copy", out=rzb_sb[:, ti * 128:(ti + 1) * 128],
                          in_=p_misc[:, ti * 128:(ti + 1) * 128], inc=vs)
                    zrow_free = vs.n
                    if ti == 1:
                        rt.wait(vs)
                    for m_ in range(rl * 4):
                        rt.op("matmul", p_o[opar][:, ti * 128:(ti + 1) * 128],
                              vh_sb[:, hb * 4096 + m_ * 128:hb * 4096 + (m_ + 1) * 128],
                              pbuf[:, m_ * 128:(m_ + 1) * 128],
                              start=(m_ == 0), stop=(m_ == rl * 4 - 1),
                              inc=(ts if m_ == rl * 4 - 1 else None))
                    pb_free[ti] = ts.n
                    av_done = ts.n
                    rv.wait(ts, av_done)
                    rv.wait(vs)
                    rv.op("tensor_tensor",
                          fm_buf[:, h * T + jt * 128:h * T + (jt + 1) * 128],
                          p_o[opar][:, ti * 128:(ti + 1) * 128],
                          rzb_sb[:, ti * 128:(ti + 1) * 128], ALU.mult, inc=vs)
                o_free[opar] = vs.n
            khb_consumed[hb] = ts.n

        if "d_attnout" in dbg_out:
            rg.wait(vs)
            Qg.dma(rg, dbg_out["d_attnout"], fm_buf[:])

        barrier()
        stB.close()
        stQ.close()
        stL = ExitStack()
        ln2_buf = stL.enter_context(nc.sbuf_tensor("ln2_buf", [128, NDC * T], BF16))
        ln2_v = ln2_buf[:].rearrange("p (c t) -> p c t", c=NDC)

        stC = ExitStack()
        sbC = lambda n_, sh, dt: stC.enter_context(nc.sbuf_tensor(n_, sh, dt))
        psC = lambda n_, sh, dt=F32: stC.enter_context(nc.psum_tensor(n_, sh, dt))
        wout_sb = sbC("wout_sb", [128, NDC * D], BF16)
        rowscr2 = sbC("rowscr2", [128, D], F32)
        ln2f32 = sbC("ln2f32", [128, 512], F32)
        lgj_sb = sbC("lgj_sb", [NE, 128], F32)
        gwT_sb = sbC("gwT_sb", [128, NDC * NE], F32)
        p_o2 = [psC(f"p_o2{i}", [128, 512]) for i in range(4)]
        p_t = [psC(f"p_t{i}", [128, 512]) for i in range(2)]
        p_g = psC("p_g", [NE, 128])
        p_x8 = psC("p_x8", [128, NE])

        wout_v = wout_sb[:].rearrange("p (c n) -> p c n", c=NDC)
        gwT_v = gwT_sb[:].rearrange("p (c e) -> p c e", c=NDC)

        tok_wout = Qg.dma(rg, wout_v, w_out.rearrange("(c p) n -> p c n", p=128))
        tok_gwt = Qy.dma(ry, gwT_v, gate_wT.rearrange("(c p) e -> p c e", p=128))

        attnout_vs = vs.n
        o2bank_free = [0, 0, 0, 0]
        pg_free = 0
        ln2f_free = [0, 0]
        for j in range(NTC):
            for q in range(4):
                rt.wait(vs, o2bank_free[q])
                rt.wait_tok(tok_wout)
                for c in range(NDC):
                    rt.op("matmul", p_o2[q][:],
                          fm_buf[:, c * T + j * 128:c * T + (j + 1) * 128],
                          wout_v[:, c, q * 512:(q + 1) * 512],
                          start=(c == 0), stop=(c == NDC - 1),
                          inc=(ts if c == NDC - 1 else None))
            mm_done = ts.n
            rv.wait(ts, mm_done)
            rv.wait(ts)
            for q in range(4):
                rv.op("tensor_copy", out=rowscr2[:, q * 512:(q + 1) * 512],
                      in_=p_o2[q][:], inc=vs)
                o2bank_free[q] = vs.n
            row = rowscr2[:]
            mean, rstd = ln_stats(row)
            rv.wait(ss)
            rv.op("tensor_scalar", row, row, mean, rstd,
                  ALU.subtract, op1=ALU.mult, inc=vs)
            rv.wait(vs)
            rv.op("tensor_tensor", x_res[:, j * D:(j + 1) * D],
                  x_res[:, j * D:(j + 1) * D], row, ALU.add, inc=vs)
            mean, rstd = ln_stats(x_res[:, j * D:(j + 1) * D])
            rv.wait(ss)
            rv.wait(vs)
            rv.op("tensor_scalar", row, x_res[:, j * D:(j + 1) * D], mean, rstd,
                  ALU.subtract, op1=ALU.mult, inc=vs)
            row_vs = vs.n
            for k4 in range(4):
                tpar = k4 % 2
                rt.wait(vs, row_vs)
                rt.wait(ss, ln2f_free[0])
                for i in range(4):
                    rt.op("transpose", p_t[tpar][:, i * 128:(i + 1) * 128],
                          rowscr2[:, (4 * k4 + i) * 128:(4 * k4 + i + 1) * 128],
                          identf[:], inc=(ts if i == 3 else None))
                tr_done = ts.n
                rv.wait(ts, tr_done)
                rv.wait(ss, ln2f_free[0])
                rv.op("tensor_copy", out=ln2f32[:],
                      in_=p_t[tpar][:], inc=vs)
                l32_vs = vs.n
                rs.wait(vs, l32_vs)
                rs.op("copy",
                      out=ln2_v[:, 4 * k4:4 * k4 + 4, j * 128:(j + 1) * 128],
                      in_=ln2f32[:].rearrange("p (c t) -> p c t", c=4), inc=ss)
                ln2f_free[0] = ss.n
                rt.wait(vs, l32_vs)
                rt.wait_tok(tok_gwt)
                if k4 == 0:
                    rt.wait(vs, pg_free)
                for i in range(4):
                    k = 4 * k4 + i
                    rt.op("matmul", p_g[:],
                          gwT_v[:, k, :],
                          ln2f32[:, i * 128:(i + 1) * 128],
                          start=(k == 0), stop=(k == NDC - 1),
                          inc=(ts if k == NDC - 1 else None))
            gmm_done = ts.n
            rv.wait(ts, gmm_done)
            rv.wait(ts)
            rv.op("tensor_copy", out=lgj_sb[:], in_=p_g[:], inc=vs)
            pg_free = vs.n
            rt.wait(vs, pg_free)
            rt.op("transpose", p_x8[:], lgj_sb[:], identf[0:NE, 0:NE], inc=ts)
            rv.wait(ts)
            rv.op("tensor_copy", out=lgtm_sb[:, j * NE:(j + 1) * NE],
                  in_=p_x8[:], inc=vs)
        x2_done_vs = vs.n
        ln2_done_ss = ss.n

        if "d_x2" in dbg_out:
            rg.wait(vs)
            Qg.dma(rg, dbg_out["d_x2"], x_res[:])
        if "d_ln2fm" in dbg_out:
            rg.wait(ss)
            Qg.dma(rg, dbg_out["d_ln2fm"], ln2_buf[:])

        rv_lgtm = vs.n
        ry.wait(vs, rv_lgtm)
        tok_lgin = Qy.dma(ry, lg_in.rearrange("(c p) e -> p c e", p=128),
                          lgtm_sb[:].rearrange("p (c e) -> p c e", c=NTC))
        rg.wait_tok(tok_lgin)
        rg.op("collective_compute", "AllGather", ALU.bypass,
              replica_groups=GROUPS, ins=[lg_in], outs=[lg_out], inc=cslg)
        lg_gathered = (cslg, cslg.n)

        if "d_logits" in dbg_out:
            rg.wait_tok(lg_gathered)
            Qg.dma(rg, dbg_out["d_logits"], lg_in)

        barrier()
        stC.close()

        stD = ExitStack()
        sbD = lambda n_, sh, dt: stD.enter_context(nc.sbuf_tensor(n_, sh, dt))
        mixed = sbD("mixed", [128, NHC * 512], BF16)
        wkb = sbD("wkb", [128, 2 * 512], F32)
        mt4 = sbD("mt4", [128, 4 * 512], BF16)
        rtld = sbD("rtld", [128, 4 * 512], BF16)
        upstr = fm_buf[:, 0:8704]
        wdnstr = fm_buf[:, 8704:12800]
        dnA_sb3 = fm_buf[:, 12800:13184]
        chb = fm_buf[:, 13184:14208]
        dnB_sb = fm_buf[:, 14208:16256]
        up_v = upstr.rearrange("p (b n) -> p b n", b=2)

        tok_dnB = Qg.dma(rg, dnB_sb, dnB_T)

        stG = ExitStack()
        sbG = lambda n_, sh, dt: stG.enter_context(nc.sbuf_tensor(n_, sh, dt))
        psG = lambda n_, sh, dt=F32: stG.enter_context(nc.psum_tensor(n_, sh, dt))
        lgt = sbG("lgt", [128, 32 * NE], F32)
        tscr = sbG("tscr", [128, 16 * 32], F32)
        e_scr = sbG("e_scr", [128, 48], F32)
        ind_tm = sbG("ind_tm", [128, 2 * NTC * NE], BF16)
        indfm = sbG("indfm", [NE, 2 * T], BF16)
        wrow = sbG("wrow", [1, 160], F32)
        p_col = psG("p_col", [128, 16])
        p_row = psG("p_row", [1, 128])
        p_i8 = stG.enter_context(nc.psum_tensor("p_i8", [NE, 128], BF16))
        p_m512 = psG("p_m512", [128, 512])

        ry.wait_tok(lg_gathered)
        tok_lgt = Qy.dma(ry, lgt[:].rearrange("p (c e) -> p c e", e=NE),
                         lg_out.rearrange("(c p) e -> p c e", p=128))

        def tournament(view, n, m1, m2):
            cols = [view[:, :, i_:i_ + 1] for i_ in range(8)]
            scr = lambda i_: tscr[:, i_ * 32:i_ * 32 + n].unsqueeze(2)
            def tt(dst, a_, b_, op):
                rv.wait(vs)
                rv.op("tensor_tensor", dst, a_, b_, op, inc=vs)
            a = [scr(i_) for i_ in range(4)]
            iw = [scr(4 + i_) for i_ in range(4)]
            for i_ in range(4):
                tt(a[i_], cols[2 * i_], cols[2 * i_ + 1], ALU.max)
                tt(iw[i_], cols[2 * i_], cols[2 * i_ + 1], ALU.min)
            b1, j1, b2, j2, k1 = scr(8), scr(9), scr(10), scr(11), scr(12)
            tt(b1, a[0], a[1], ALU.max)
            tt(j1, a[0], a[1], ALU.min)
            tt(b2, a[2], a[3], ALU.max)
            tt(j2, a[2], a[3], ALU.min)
            tt(m1.unsqueeze(2), b1, b2, ALU.max)
            tt(k1, b1, b2, ALU.min)
            t1, t2, t3 = a[0], a[1], a[2]
            tt(t1, j1, j2, ALU.max)
            tt(t2, iw[0], iw[1], ALU.max)
            tt(t3, iw[2], iw[3], ALU.max)
            tt(j1, t1, k1, ALU.max)
            tt(j2, t2, t3, ALU.max)
            tt(m2.unsqueeze(2), j1, j2, ALU.max)

        m1g = tscr[:, 13 * 32:14 * 32]
        m2g = tscr[:, 14 * 32:15 * 32]
        rv.wait_tok(tok_lgt)
        tournament(lgt[:].rearrange("p (c e) -> p c e", e=NE), 32, m1g, m2g)
        m1o = e_scr[:, 16:24]
        m2o = e_scr[:, 24:32]
        tournament(lgtm_sb[:].rearrange("p (c e) -> p c e", e=NE), NE, m1o, m2o)

        for k in range(2):
            mg = m1g if k == 0 else m2g
            mo = m1o if k == 0 else m2o
            gred = statscr[:, 16:17]
            rv.wait(vs)
            rv.wait(ss)
            rv.op("tensor_reduce", gred, mg, AX.X, ALU.max, inc=vs)
            rt.wait(vs)
            rt.op("transpose", p_row[0:1, :], gred, identf[:], inc=ts)
            rv.wait(ts)
            rv.op("tensor_copy", out=wrow[0:1, 0:128], in_=p_row[0:1, :], inc=vs)
            rv.wait(vs)
            rv.op("tensor_reduce", wrow[0:1, 128:129], wrow[0:1, 0:128], AX.X,
                  ALU.max, inc=vs)
            gmax_vs = vs.n
            rt.wait(vs, gmax_vs)
            rt.op("matmul", p_col[:, 0:1], negonesf[0:1, :], wrow[0:1, 128:129],
                  start=True, stop=True, inc=ts)
            ngb = statscr[:, 17:18]
            rv.wait(ts)
            rv.op("tensor_copy", out=ngb, in_=p_col[:, 0:1], inc=vs)
            zc = statscr[:, 18:19]
            rs.wait(vs)
            rs.op("activation", tscr[:, 15 * 32:16 * 32], mg, AF.Exp,
                  bias=ngb, accum_out=zc, inc=ss)
            rt.wait(ss)
            rt.op("transpose", p_row[0:1, :], zc, identf[:], inc=ts)
            rv.wait(ts)
            rv.op("tensor_copy", out=wrow[0:1, 0:128], in_=p_row[0:1, :], inc=vs)
            rv.wait(vs)
            rv.op("tensor_reduce", wrow[0:1, 129:130], wrow[0:1, 0:128], AX.X,
                  ALU.add, inc=vs)
            rv.wait(vs)
            rv.op("reciprocal", wrow[0:1, 130:131], wrow[0:1, 129:130], inc=vs)
            rt.wait(vs)
            rt.op("matmul", p_col[:, 1:2], onesf[0:1, :], wrow[0:1, 130:131],
                  start=True, stop=True, inc=ts)
            rzb = statscr[:, 19:20]
            rv.wait(ts)
            rv.op("tensor_copy", out=rzb, in_=p_col[:, 1:2], inc=vs)
            eo = e_scr[:, 32:40]
            rs.wait(vs)
            rs.op("activation", eo, mo, AF.Exp, bias=ngb, inc=ss)
            rv.wait(ss)
            rv.op("tensor_scalar", statscr[:, k * 8:(k + 1) * 8], eo, rzb, None,
                  ALU.mult, inc=vs)
            for j in range(NTC):
                rv.wait(vs)
                rv.op("tensor_scalar",
                      ind_tm[:, (k * NTC + j) * NE:(k * NTC + j + 1) * NE],
                      lgtm_sb[:, j * NE:(j + 1) * NE],
                      mo[:, j:j + 1], None, ALU.is_equal, inc=vs)
                rt.wait(vs)
                rt.op("transpose", p_i8[:],
                      ind_tm[:, (k * NTC + j) * NE:(k * NTC + j + 1) * NE],
                      identb[:], inc=ts)
                rv.wait(ts)
                rv.op("tensor_copy",
                      out=indfm[0:NE, k * T + j * 128:k * T + (j + 1) * 128],
                      in_=p_i8[:], inc=vs)
        for half in range(2):
            for k in range(2):
                rt.wait(vs)
                rt.wait_tok(tok_c["blk2"])
                rt.op("matmul", p_m512[:], blk2[:],
                      indfm[0:NE, k * T + half * 512:k * T + (half + 1) * 512],
                      start=True, stop=True, inc=ts)
                rv.wait(ts)
                rv.op("tensor_copy",
                      out=mt4[:, (half * 2 + k) * 512:(half * 2 + k + 1) * 512],
                      in_=p_m512[:], inc=vs)
        barrier()
        stG.close()
        hscr = sbD("hscr", [128, 4 * 512], F32)
        rowscrD = sbD("rowscrD", [128, D], F32)
        ub_s = hscr[:, 0 * 512:1 * 512]
        gb_s = hscr[:, 1 * 512:2 * 512]
        su0_s = hscr[:, 2 * 512:3 * 512]
        su1_s = hscr[:, 3 * 512:4 * 512]

        wdn_v = wdnstr.rearrange("p (b d) -> p b d", b=2)
        bundle_used_ts = [0, 0]
        dn_used_ts = [0, 0]
        for half in range(2):
            hs = half * 512
            stP = ExitStack()
            p_r = stP.enter_context(nc.psum_tensor(f"p_r{half}", [128, 512], F32))
            p_wk = stP.enter_context(nc.psum_tensor(f"p_wk{half}", [128, 128], F32))
            p_wr = stP.enter_context(nc.psum_tensor(f"p_wr{half}", [1, 128], F32))
            wrow2 = stP.enter_context(nc.sbuf_tensor(f"wrow2{half}", [1, 128], F32))
            rg.wait(ts, dn_used_ts[0])
            rg.wait(ts, dn_used_ts[1])
            tok_upA = Qg.dma(rg, wdn_v[:, 0].rearrange("p (c e) -> p c e", c=NDC),
                             upA_T.rearrange("(c p) e -> p c e", p=128))
            tok_gtA = Qg.dma(rg, wdn_v[:, 1].rearrange("p (c e) -> p c e", c=NDC),
                             gateA_T.rearrange("(c p) e -> p c e", p=128))
            for k in range(2):
                for c4 in range(4):
                    tch = half * 4 + c4
                    rt.wait(vs)
                    rt.op("transpose", p_wr[0:1, :],
                          statscr[:, k * 8 + tch:k * 8 + tch + 1], identf[:], inc=ts)
                    rv.wait(ts)
                    rv.op("tensor_copy", out=wrow2[0:1, :], in_=p_wr[0:1, :], inc=vs)
                    rv_w = vs.n
                    rt.wait(vs, rv_w)
                    rt.op("matmul", p_wk[:], onesf[0:1, :], wrow2[0:1, :],
                          start=True, stop=True, inc=ts)
                    rv.wait(ts)
                    rv.op("tensor_copy", out=wkb[:, k * 512 + c4 * 128:
                                                 k * 512 + (c4 + 1) * 128],
                          in_=p_wk[:], inc=vs)
            for side in range(2):
                rt.wait_tok(tok_upA if side == 0 else tok_gtA)
                rt.wait(vs)
                for kd in range(NDC):
                    rt.op("matmul", p_r[:],
                          wdn_v[:, side, kd * 128:(kd + 1) * 128],
                          ln2_v[:, kd, hs:hs + 512],
                          start=(kd == 0), stop=(kd == NDC - 1),
                          inc=(ts if kd == NDC - 1 else None))
                r_done = ts.n
                for k in range(2):
                    rv.wait(ts, r_done)
                    rv.op("tensor_tensor", rtld[:, (side * 2 + k) * 512:
                                                 (side * 2 + k + 1) * 512],
                          p_r[:], mt4[:, (half * 2 + k) * 512:(half * 2 + k + 1) * 512],
                          ALU.mult, inc=vs)
            rtld_vs = vs.n
            dn_used_ts[0] = dn_used_ts[1] = ts.n
            stP.close()

            stH = ExitStack()
            p_u0 = [stH.enter_context(nc.psum_tensor(f"p_u0{half}{i}", [128, 512], F32))
                    for i in range(2)]
            p_g0 = [stH.enter_context(nc.psum_tensor(f"p_g0{half}{i}", [128, 512], F32))
                    for i in range(2)]
            p_loU = stH.enter_context(nc.psum_tensor(f"p_loU{half}", [128, 512], F32))
            p_loG = stH.enter_context(nc.psum_tensor(f"p_loG{half}", [128, 512], F32))
            p_sraw = [stH.enter_context(nc.psum_tensor(f"p_sraw{half}{i}", [128, 512], F32))
                      for i in range(2)]
            u0_free = [None, None]
            g0_free = [None, None]
            dnA_used = [0, 0, 0]
            dnA_tok = [None, None, None]
            loU_free = 0
            loG_free = 0
            ch_vs = 0
            sraw_ts = 0
            for hh in range(NHC):
                par = hh % 2
                ry.wait(ts, bundle_used_ts[par])
                for tk in wconv_toks:
                    ry.wait_tok(tk)
                bt = []
                bt.append(Qy.dma(ry, up_v[:, par, 0:2048].rearrange("p (c h) -> p c h", c=NDC),
                                 w_upT_bf[:, hh * 128:(hh + 1) * 128]
                                 .rearrange("(c p) h -> p c h", p=128)))
                bt.append(Qy.dma(ry, up_v[:, par, 2048:4096].rearrange("p (c h) -> p c h", c=NDC),
                                 w_gateT_bf[:, hh * 128:(hh + 1) * 128]
                                 .rearrange("(c p) h -> p c h", p=128)))
                rg.wait(ts, bundle_used_ts[par])
                bt.append(Qg.dma(rg, up_v[:, par, 4096:4224],
                                 upB_T[:, hh * 128:(hh + 1) * 128]))
                bt.append(Qg.dma(rg, up_v[:, par, 4224:4352],
                                 gateB_T[:, hh * 128:(hh + 1) * 128]))
                rg.wait(ts, dnA_used[hh % 3])
                dnA_tok[hh % 3] = Qg.dma(
                    rg, dnA_sb3[:, (hh % 3) * 128:(hh % 3 + 1) * 128],
                    dnA_T[hh * 128:(hh + 1) * 128, :])
                for tk in bt:
                    rt.wait_tok(tk)
                rt.wait_tok(u0_free[par])
                rt.wait(vs, rtld_vs)
                for kd in range(NDC):
                    rt.op("matmul", p_u0[par][:],
                          up_v[:, par, kd * 128:(kd + 1) * 128],
                          ln2_v[:, kd, hs:hs + 512],
                          start=(kd == 0), stop=(kd == NDC - 1),
                          inc=(ts if kd == NDC - 1 else None))
                baseu_ts = ts.n
                rt.wait_tok(g0_free[par])
                for kd in range(NDC):
                    rt.op("matmul", p_g0[par][:],
                          up_v[:, par, 2048 + kd * 128:2048 + (kd + 1) * 128],
                          ln2_v[:, kd, hs:hs + 512],
                          start=(kd == 0), stop=(kd == NDC - 1),
                          inc=(ts if kd == NDC - 1 else None))
                baseg_ts = ts.n
                if hh > 0:
                    rt.wait(vs, ch_vs)
                    rt.wait_tok(dnA_tok[(hh - 1) % 3])
                    for k in range(2):
                        rt.op("matmul", p_sraw[k][:],
                              dnA_sb3[:, ((hh - 1) % 3) * 128:((hh - 1) % 3 + 1) * 128],
                              chb[:, k * 512:(k + 1) * 512],
                              start=(hh - 1 == 0), stop=False,
                              inc=(ts if k == 1 else None))
                    sraw_ts = ts.n
                    dnA_used[(hh - 1) % 3] = ts.n
                rs.wait(ts, baseu_ts)
                rs.wait(vs)
                rs.wait(ss)
                rs.op("copy", out=ub_s, in_=p_u0[par][:], inc=ss)
                ubcopy_ss = ss.n
                rs.wait(ts, baseg_ts)
                rs.op("copy", out=gb_s, in_=p_g0[par][:], inc=ss)
                gbcopy_ss = ss.n
                rt.wait(ss, ubcopy_ss)
                rt.op("matmul", p_u0[par][:], up_v[:, par, 4096:4224],
                      rtld[:, 0:512], start=False, stop=True,
                      skip_group_check=True, inc=ts)
                lo0u_ts = ts.n
                rt.wait(vs, loU_free)
                rt.op("matmul", p_loU[:], up_v[:, par, 4096:4224],
                      rtld[:, 512:1024], start=True, stop=True, inc=ts)
                lo1u_ts = ts.n
                rt.wait(ss, gbcopy_ss)
                rt.op("matmul", p_g0[par][:], up_v[:, par, 4224:4352],
                      rtld[:, 1024:1536], start=False, stop=True,
                      skip_group_check=True, inc=ts)
                lo0g_ts = ts.n
                rt.wait(vs, loG_free)
                rt.op("matmul", p_loG[:], up_v[:, par, 4224:4352],
                      rtld[:, 1536:2048], start=True, stop=True, inc=ts)
                lo1g_ts = ts.n
                bundle_used_ts[par] = ts.n
                rs.wait(ts, lo0u_ts)
                rs.op("activation", su0_s, p_u0[par][:], AF.Silu, inc=ss)
                silu0_ss = ss.n
                u0_free[par] = (ss, silu0_ss)
                rv.wait(ts, lo1u_ts)
                rv.wait(ss, ubcopy_ss)
                rv.op("scalar_tensor_tensor", ub_s, p_loU[:], 1.0, ub_s,
                      ALU.mult, ALU.add, inc=vs)
                loU_free = vs.n
                u1_vs = vs.n
                rs.wait(vs, u1_vs)
                rs.op("activation", su1_s, ub_s, AF.Silu, inc=ss)
                silu1_ss = ss.n
                rv.wait(ss, silu0_ss)
                rv.wait(ts, lo0g_ts)
                rv.wait(ts, sraw_ts)
                rv.op("tensor_tensor", su0_s, su0_s, p_g0[par][:], ALU.mult, inc=vs)
                h0_vs = vs.n
                g0_free[par] = (vs, h0_vs)
                rv.wait(vs, h0_vs)
                rv.op("tensor_tensor", chb[:, 0:512], su0_s, wkb[:, 0:512],
                      ALU.mult, inc=vs)
                rv.wait(ts, lo1g_ts)
                rv.wait(ss, gbcopy_ss)
                rv.op("scalar_tensor_tensor", gb_s, p_loG[:], 1.0, gb_s,
                      ALU.mult, ALU.add, inc=vs)
                loG_free = vs.n
                rv.wait(ss, silu1_ss)
                rv.wait(vs)
                rv.op("tensor_tensor", su1_s, su1_s, gb_s, ALU.mult, inc=vs)
                rv.wait(vs)
                rv.op("tensor_tensor", chb[:, 512:1024], su1_s, wkb[:, 512:1024],
                      ALU.mult, inc=vs)
                rv.wait(vs)
                rv.op("tensor_tensor", mixed[:, hh * 512:(hh + 1) * 512],
                      chb[:, 0:512], chb[:, 512:1024], ALU.add, inc=vs)
                ch_vs = vs.n
            rt.wait(vs, ch_vs)
            rt.wait_tok(dnA_tok[(NHC - 1) % 3])
            for k in range(2):
                rt.op("matmul", p_sraw[k][:],
                      dnA_sb3[:, ((NHC - 1) % 3) * 128:((NHC - 1) % 3 + 1) * 128],
                      chb[:, k * 512:(k + 1) * 512],
                      start=False, stop=True, inc=(ts if k == 1 else None))
            sraw_done = ts.n
            for k in range(2):
                rv.wait(ts, sraw_done)
                rv.op("tensor_tensor", rtld[:, k * 512:(k + 1) * 512],
                      p_sraw[k][:], mt4[:, (half * 2 + k) * 512:(half * 2 + k + 1) * 512],
                      ALU.mult, inc=vs)
            stld_vs = vs.n
            barrier()
            stH.close()

            stE = ExitStack()
            p_d = [stE.enter_context(nc.psum_tensor(f"p_d{half}{i}", [128, 512], F32))
                   for i in range(8)]
            dbank_free = [0] * 8
            for tpair in range(2):
                for hh in range(NHC):
                    par = hh % 2
                    ry.wait(ts, dn_used_ts[par])
                    for tk in wconv_toks:
                        ry.wait_tok(tk)
                    tok_dn = Qy.dma(ry, wdn_v[:, par],
                                    w_downT_bf[hh * 128:(hh + 1) * 128, :])
                    rt.wait_tok(tok_dn)
                    if hh == 0:
                        for bi in range(8):
                            rt.wait(vs, dbank_free[bi])
                    for tq2 in range(2):
                        tloc = tpair * 2 + tq2
                        for qb in range(4):
                            rt.op("matmul", p_d[tq2 * 4 + qb][:],
                                  mixed[:, hh * 512 + tloc * 128:
                                        hh * 512 + (tloc + 1) * 128],
                                  wdn_v[:, par, qb * 512:(qb + 1) * 512],
                                  start=(hh == 0), stop=False,
                                  inc=(ts if (tq2 == 1 and qb == 3) else None))
                    dn_used_ts[par] = ts.n
                rt.wait(vs, stld_vs)
                rt.wait_tok(tok_dnB)
                for tq2 in range(2):
                    tloc = tpair * 2 + tq2
                    for qb in range(4):
                        for k in range(2):
                            rt.op("matmul", p_d[tq2 * 4 + qb][:],
                                  rtld[:, k * 512 + tloc * 128:
                                       k * 512 + (tloc + 1) * 128],
                                  dnB_sb[:, qb * 512:(qb + 1) * 512],
                                  start=False, stop=(k == 1),
                                  inc=(ts if (tq2 == 1 and qb == 3 and k == 1)
                                       else None))
                dn_done = ts.n
                for tq2 in range(2):
                    tch = half * 4 + tpair * 2 + tq2
                    rv.wait(ts, dn_done)
                    rv.wait(vs)
                    for qb in range(4):
                        rv.op("tensor_copy",
                              out=rowscrD[:, qb * 512:(qb + 1) * 512],
                              in_=p_d[tq2 * 4 + qb][:], inc=vs)
                        dbank_free[tq2 * 4 + qb] = vs.n
                    mean, rstd = ln_stats(rowscrD[:])
                    rv.wait(ss)
                    rv.op("tensor_scalar", rowscrD[:], rowscrD[:], mean, rstd,
                          ALU.subtract, op1=ALU.mult, inc=vs)
                    rv.wait(vs)
                    rv.op("tensor_tensor", x_res[:, tch * D:(tch + 1) * D],
                          x_res[:, tch * D:(tch + 1) * D], rowscrD[:],
                          ALU.add, inc=vs)
            stE.close()

        if "d_mixed" in dbg_out:
            rg.wait(vs)
            Qg.dma(rg, dbg_out["d_mixed"], mixed[:])

        barrier()
        stD.close()
        stL.close()

        rv_final = vs.n
        ry.wait(vs, rv_final)
        tok_out = Qy.dma(ry, out_own.rearrange("(j p) d -> p j d", p=128),
                         x_res[:].rearrange("p (j d) -> p j d", j=NTC))
        ry.wait_tok(tok_out)

        with nc.Block() as block:
            block.tensor(lambda e: rt.replay(e))
            block.vector(lambda e: rv.replay(e))
            block.scalar(lambda e: rs.replay(e))
            block.gpsimd(lambda e: rg.replay(e))
            block.sync(lambda e: ry.replay(e))

    return nc



def host_prep(inputs):
    x = np.asarray(inputs["L0_attn_x"])
    w_qkv = np.ascontiguousarray(inputs["L0_attn_w_qkv"], dtype=np.float32)
    w_out = np.ascontiguousarray(inputs["L0_attn_w_out"], dtype=np.float32)
    gate_w = np.asarray(inputs["L0_ffn_gate_w"])
    w_up = np.asarray(inputs["L0_ffn_w_up"])
    w_gate = np.asarray(inputs["L0_ffn_w_gate"])
    w_down = np.asarray(inputs["L0_ffn_w_down"])
    up_A = np.asarray(inputs["L0_ffn_up_A"])
    up_B = np.asarray(inputs["L0_ffn_up_B"])
    gate_A = np.asarray(inputs["L0_ffn_gate_A"])
    gate_B = np.asarray(inputs["L0_ffn_gate_B"])
    down_A = np.asarray(inputs["L0_ffn_down_A"])
    down_B = np.asarray(inputs["L0_ffn_down_B"])

    shared = {
        "w_qkv": w_qkv,
        "w_out": w_out,
        "gate_wT": np.ascontiguousarray(gate_w.T),
        "w_upT": np.ascontiguousarray(w_up.T),
        "w_gateT": np.ascontiguousarray(w_gate.T),
        "w_downT": np.ascontiguousarray(w_down.T),
        "upA_T": np.ascontiguousarray(up_A.reshape(ER, D).T),
        "gateA_T": np.ascontiguousarray(gate_A.reshape(ER, D).T),
        "dnA_T": np.ascontiguousarray(down_A.reshape(ER, HFF).T),
        "upB_T": np.ascontiguousarray(up_B.transpose(0, 2, 1).reshape(ER, HFF)),
        "gateB_T": np.ascontiguousarray(gate_B.transpose(0, 2, 1).reshape(ER, HFF)),
        "dnB_T": np.ascontiguousarray(down_B.transpose(0, 2, 1).reshape(ER, D)),
    }
    in_maps = []
    for core in range(8):
        b, r = core // 4, core % 4
        xb = x[b].astype(np.float32, copy=False)
        x_ownc = np.ascontiguousarray(
            xb.reshape(32, 128, D)[r::4].reshape(T, D))
        mval = NEGBIG / RSQ
        m = np.zeros((128, 512), dtype=np.float32)
        for rp in range(4):
            if rp > r:
                m[:, rp * 128:(rp + 1) * 128] = mval
            elif rp == r:
                tri = np.triu(np.full((128, 128), mval, dtype=np.float32), k=1)
                m[:, rp * 128:(rp + 1) * 128] = tri
        m = np.concatenate([m[:, rp * 128:(rp + 1) * 128].T for rp in range(4)],
                           axis=1)
        m = np.ascontiguousarray(m)
        d = dict(shared)
        d["x_own"] = x_ownc
        d["mask512"] = m
        in_maps.append(d)
    return in_maps


def kernel(**inputs):
    in_maps = host_prep(inputs)
    nc = build(debug=DEBUG)
    res = run_bass_kernel_spmd(nc, in_maps, core_ids=list(range(8)))
    out = np.empty((2, S, D), dtype=np.float32)
    for core in range(8):
        b, r = core // 4, core % 4
        o = res.results[core]["out_own"].reshape(NTC, 128, D)
        out[b].reshape(32, 128, D)[r::4] = o
    return out


# version 07
# speedup vs baseline: 1.3922x; 1.0134x over previous
from contextlib import ExitStack

import numpy as np
import ml_dtypes
import concourse.bass as bass
import concourse.mybir as mybir
from concourse.bass_utils import run_bass_kernel_spmd

F32 = mybir.dt.float32
BF16 = mybir.dt.bfloat16
AF = mybir.ActivationFunctionType
ALU = mybir.AluOpType
AX = mybir.AxisListType

D = 2048
NDC = 16
HEADS = 16
HD = 128
HFF = 5632
NHC = 44
T = 1024
NTC = 8
S = 4096
NE = 8
RK = 16
ER = 128
EPS = 1e-5
RSQ = float(1.0 / np.sqrt(128.0))
NEGBIG = -1e9

DEBUG = ()



class Sem:
    __slots__ = ("name", "n", "h")

    def __init__(self, name):
        self.name = name
        self.n = 0
        self.h = None


class R:

    def __init__(self, name):
        self.name = name
        self.cmds = []
        self.hw = {}

    def wait(self, sem, val=None):
        v = sem.n if val is None else val
        if v <= self.hw.get(sem, -1):
            return
        self.hw[sem] = v
        self.cmds.append(("w", sem, v))

    def wait_tok(self, tok):
        if tok is not None:
            self.wait(tok[0], tok[1])

    def op(self, method, *args, inc=None, inc_k=1, **kw):
        if inc is not None:
            inc.n += inc_k
        self.cmds.append(("o", method, args, kw, inc, inc_k))
        return inc.n if inc is not None else None

    def replay(self, e):
        for c in self.cmds:
            if c[0] == "w":
                e.wait_ge(c[1].h, c[2])
            else:
                _, m, args, kw, inc, k = c
                instr = getattr(e, m)(*args, **kw)
                if inc is not None:
                    instr.then_inc(inc.h, k)


class Q:

    def __init__(self, name, nslots):
        self.slots = [Sem(f"{name}{i}") for i in range(nslots)]
        self.i = 0

    def dma(self, rec, out, in_):
        sm = self.slots[self.i % len(self.slots)]
        self.i += 1
        rec.wait(sm, sm.n)
        rec.op("dma_start", out=out, in_=in_, inc=sm, inc_k=16)
        return (sm, sm.n)



def build(debug=()):
    nc = bass.Bass("TRN2", target_bir_lowering=False)

    def inp(name, shape, dtype=F32):
        return nc.dram_tensor(name, shape, dtype, kind="ExternalInput").ap()

    x_own = inp("x_own", [T, D])
    w_qkv = inp("w_qkv", [D, 3 * D])
    w_out = inp("w_out", [D, D])
    gate_wT = inp("gate_wT", [D, NE])
    w_upT = inp("w_upT", [D, HFF])
    w_gateT = inp("w_gateT", [D, HFF])
    w_downT = inp("w_downT", [HFF, D])
    upA_T = inp("upA_T", [D, ER])
    gateA_T = inp("gateA_T", [D, ER])
    dnA_T = inp("dnA_T", [HFF, ER])
    upB_T = inp("upB_T", [ER, HFF])
    gateB_T = inp("gateB_T", [ER, HFF])
    dnB_T = inp("dnB_T", [ER, D])
    mask512 = inp("mask512", [128, 1536])

    out_own = nc.dram_tensor("out_own", [T, D], F32, kind="ExternalOutput").ap()

    kv_in_k = nc.dram_tensor("kv_in_k", [HEADS * HD, T], BF16, kind="Internal").ap()
    kv_in_v = nc.dram_tensor("kv_in_v", [T, HEADS * HD], BF16, kind="Internal").ap()
    kv_out_k = nc.dram_tensor("kv_out_k", [4 * HEADS * HD, T], BF16, kind="Internal").ap()
    kv_out_v = nc.dram_tensor("kv_out_v", [4 * T, HEADS * HD], BF16, kind="Internal").ap()
    kn_in = nc.dram_tensor("kn_in", [HEADS, T], F32, kind="Internal").ap()
    w_upT_bf = nc.dram_tensor("w_upT_bf", [D, HFF], BF16, kind="Internal").ap()
    w_gateT_bf = nc.dram_tensor("w_gateT_bf", [D, HFF], BF16, kind="Internal").ap()
    w_downT_bf = nc.dram_tensor("w_downT_bf", [HFF, D], BF16, kind="Internal").ap()
    kn_out = nc.dram_tensor("kn_out", [4 * HEADS, T], F32, kind="Internal").ap()
    lg_in = nc.dram_tensor("lg_in", [T, NE], F32, kind="Internal").ap()
    lg_out = nc.dram_tensor("lg_out", [4 * T, NE], F32, kind="Internal").ap()
    GROUPS = [[0, 1, 2, 3], [4, 5, 6, 7]]

    identb_d = nc.inline_tensor(np.eye(128, dtype=ml_dtypes.bfloat16), name="c_identb").ap()
    identf_d = nc.inline_tensor(np.eye(128, dtype=np.float32), name="c_identf").ap()
    onesf_d = nc.inline_tensor(np.ones((1, 128), dtype=np.float32), name="c_onesf").ap()
    onescol_d = nc.inline_tensor(np.ones((128, 1), dtype=ml_dtypes.bfloat16), name="c_onescol").ap()
    negonesf_d = nc.inline_tensor(np.full((1, 128), -1.0, dtype=np.float32), name="c_negonesf").ap()
    blk_np = np.zeros((NE, ER), dtype=ml_dtypes.bfloat16)
    for e_ in range(NE):
        blk_np[e_, e_ * RK:(e_ + 1) * RK] = 2.0
    blk2_d = nc.inline_tensor(blk_np, name="c_blk2").ap()

    dbg_out = {}
    for name, shape, dt in [
        ("d_lnfm", [128, NDC * T], F32),
        ("d_q", [128, HEADS * T], F32),
        ("d_kvk", [HEADS * HD, T], F32),
        ("d_kvv", [T, HEADS * HD], F32),
        ("d_attnout", [128, HEADS * T], F32),
        ("d_x2", [128, NTC * D], F32),
        ("d_ln2fm", [128, NDC * T], F32),
        ("d_logits", [T, NE], F32),
        ("d_gstats", [128, 64], F32),
        ("d_mixed", [128, NHC * 512], F32),
        ("d_ffntm", [128, 4 * D], F32),
    ]:
        if name in debug:
            dbg_out[name] = nc.dram_tensor(name, shape, dt, kind="ExternalOutput").ap()

    ts = Sem("ts")
    vs = Sem("vs")
    ss = Sem("ss")
    cskv = Sem("cskv")
    cslg = Sem("cslg")
    csn = Sem("csn")
    Qg = Q("gq", 8)
    Qy = Q("sq", 8)
    SEMS = [ts, vs, ss, cskv, cslg, csn] + Qg.slots + Qy.slots

    rt, rv, rs, rg, ry = R("tensor"), R("vector"), R("scalar"), R("gpsimd"), R("sync")
    ALLR = [rt, rv, rs, rg, ry]

    def barrier():
        for r in ALLR:
            for sm in SEMS:
                r.wait(sm)

    st = ExitStack()
    with st:
        def sb(name, shape, dtype):
            return st.enter_context(nc.sbuf_tensor(name, shape, dtype))

        def psum(name, shape, dtype=F32):
            return st.enter_context(nc.psum_tensor(name, shape, dtype))

        for sm in SEMS:
            sm.h = st.enter_context(nc.semaphore(sm.name))

        x_res = sb("x_res", [128, NTC * D], F32)
        fm_buf = sb("fm_buf", [128, NDC * T], BF16)
        statscr = sb("statscr", [128, 48], F32)
        lgtm_sb = sb("lgtm_sb", [128, NTC * NE], F32)
        identb = sb("identb", [128, 128], BF16)
        identf = sb("identf", [128, 128], F32)
        onesf = sb("onesf", [1, 128], F32)
        onescol = sb("onescol", [128, 1], BF16)
        negonesf = sb("negonesf", [1, 128], F32)
        blk2 = sb("blk2", [NE, ER], BF16)

        fm_v = fm_buf[:].rearrange("p (c t) -> p c t", c=NDC)
        stQ = ExitStack()
        q_all = stQ.enter_context(nc.sbuf_tensor("q_all", [128, HEADS * T], BF16))
        qn_sb = stQ.enter_context(nc.sbuf_tensor("qn_sb", [128, 128], F32))
        mask_sb = stQ.enter_context(nc.sbuf_tensor("mask_sb", [128, 1536], F32))

        tok_x = Qy.dma(ry, x_res[:].rearrange("p (j d) -> p j d", j=NTC),
                       x_own.rearrange("(j p) d -> p j d", p=128))
        tok_c = {}
        for nm, dst, src in [("identb", identb[:], identb_d),
                             ("identf", identf[:], identf_d),
                             ("onesf", onesf[:], onesf_d),
                             ("onescol", onescol[:], onescol_d),
                             ("negonesf", negonesf[:], negonesf_d),
                             ("blk2", blk2[:], blk2_d),
                             ("mask", mask_sb[:], mask512)]:
            tok_c[nm] = Qy.dma(ry, dst, src)

        def ln_stats(xin):
            bnbuf = statscr[:, 16:40]
            mv = statscr[:, 40:42]
            mean, rstd = statscr[:, 40:41], statscr[:, 44:45]
            rv.wait(vs)
            rv.wait(ss)
            for c_ in range(4):
                rv.op("bn_stats", out=bnbuf[:, c_ * 6:(c_ + 1) * 6],
                      in_=xin[:, c_ * 512:(c_ + 1) * 512], inc=vs)
            rv.wait(vs)
            rv.op("bn_aggr", out=mv, in_=bnbuf, inc=vs)
            rv.wait(vs)
            rv.op("tensor_scalar", statscr[:, 42:43], statscr[:, 41:42], EPS, None,
                  ALU.add, inc=vs)
            rs.wait(vs)
            rs.op("activation", statscr[:, 43:44], statscr[:, 42:43], AF.Ln, inc=ss)
            rs.wait(ss)
            rs.op("activation", rstd, statscr[:, 43:44], AF.Exp, scale=-0.5, inc=ss)
            return mean, rstd

        stA = ExitStack()
        sbA = lambda n_, sh, dt: stA.enter_context(nc.sbuf_tensor(n_, sh, dt))
        psA = lambda n_, sh, dt=F32: stA.enter_context(nc.psum_tensor(n_, sh, dt))
        lntm = sbA("lntm", [128, 2 * D], BF16)
        wqkv_sb = sbA("wqkv_sb", [128, 3 * NDC * 512], BF16)
        kvstage = sbA("kvstage", [128, 4 * 512], BF16)
        sqst = sbA("sqst", [128, 2 * 512], BF16)
        kn_row = sbA("kn_row", [1, 512], F32)
        qn_row = sbA("qn_row", [1, 512], F32)
        p_lt = [psA(f"p_lt{i}", [128, 1024], BF16) for i in range(2)]
        p_mm = [psA(f"p_mm{i}", [128, 512], F32) for i in range(4)]
        p_nr = psA("p_nr", [1, 512])
        p_qt = psA("p_qt", [128, 4])
        nstate = {"sq": [0, 0], "nr": None, "qt": 0, "knrow": None, "qnrow": 0}

        def norms_from(src_bf, h_, half_, is_q):
            p2 = (h_ * 2 + half_) % 2
            rs.wait(ts, nstate["sq"][p2])
            rs.wait(ss)
            rs.op("activation", sqst[:, p2 * 512:(p2 + 1) * 512], src_bf,
                  AF.Square, inc=ss)
            sq_ss = ss.n
            rt.wait(ss, sq_ss)
            rt.wait_tok(nstate["nr"])
            rt.wait_tok(tok_c["onescol"])
            rt.op("matmul", p_nr[0:1, :], onescol[:],
                  sqst[:, p2 * 512:(p2 + 1) * 512], start=True, stop=True, inc=ts)
            nstate["sq"][p2] = ts.n
            nr_ts = ts.n
            if is_q:
                rv.wait(ts, nr_ts)
                rv.wait(ts, nstate["qnrow"])
                rv.op("tensor_copy", out=qn_row[0:1, :], in_=p_nr[0:1, :], inc=vs)
                nstate["nr"] = (vs, vs.n)
                rt.wait(vs, vs.n)
                rt.wait(vs, nstate["qt"])
                for i4 in range(4):
                    rt.op("transpose", p_qt[:, i4:i4 + 1],
                          qn_row[0:1, i4 * 128:(i4 + 1) * 128],
                          identf[0:1, 0:1], inc=(ts if i4 == 3 else None))
                nstate["qnrow"] = ts.n
                rv.wait(ts, ts.n)
                rv.op("tensor_copy",
                      out=qn_sb[:, h_ * 8 + half_ * 4:h_ * 8 + half_ * 4 + 4],
                      in_=p_qt[:], inc=vs)
                nstate["qt"] = vs.n
            else:
                rv.wait(ts, nr_ts)
                rv.wait_tok(nstate["knrow"])
                rv.op("tensor_copy", out=kn_row[0:1, :], in_=p_nr[0:1, :], inc=vs)
                nstate["nr"] = (vs, vs.n)
                ry.wait(vs)
                nstate["knrow"] = Qy.dma(
                    ry, kn_in[h_:h_ + 1, half_ * 512:(half_ + 1) * 512],
                    kn_row[0:1, :])


        lt_done_ts = [0, 0]
        for j in range(NTC):
            par = j % 2
            xin = x_res[:, j * D:(j + 1) * D]
            rv.wait_tok(tok_x)
            rs.wait_tok(tok_x)
            mean, rstd = ln_stats(xin)
            rv.wait(ss)
            rv.wait(ts, lt_done_ts[par])
            rv.op("tensor_scalar", lntm[:, par * D:(par + 1) * D], xin,
                  mean, rstd, ALU.subtract, op1=ALU.mult, inc=vs)
            rt.wait(vs)
            rt.wait_tok(tok_c["identb"])
            for k in range(NDC):
                bank = p_lt[k // 8]
                rt.op("transpose", bank[:, (k % 8) * 128:(k % 8 + 1) * 128],
                      lntm[:, par * D + k * 128: par * D + (k + 1) * 128],
                      identb[:], inc=(ts if k % 8 == 7 else None))
            lt_done_ts[par] = ts.n
            rv.wait(ts)
            for half in range(2):
                rv.op("tensor_copy",
                      out=fm_v[:, half * 8:(half + 1) * 8, j * 128:(j + 1) * 128],
                      in_=p_lt[half][:].rearrange("p (c t) -> p c t", c=8),
                      inc=vs)
            if j == 3:
                lnfm_half0_vs = vs.n
        lnfm_done_vs = vs.n

        if "d_lnfm" in dbg_out:
            rg.wait(vs)
            Qg.dma(rg, dbg_out["d_lnfm"], fm_buf[:])

        wq_v = wqkv_sb[:].rearrange("p (b c n) -> p b c n", b=3, c=NDC)
        grp_consumed_ts = [0, 0, 0]
        bank_free = [0, 0, 0, 0]
        stage_free = [None, None, None, None]

        def mm_to_bank(bi, lhsT_rhs_list):
            rt.wait(ss, bank_free[bi])
            n = len(lhsT_rhs_list)
            for i, (lt_, rh_) in enumerate(lhsT_rhs_list):
                rt.op("matmul", p_mm[bi][:], lt_, rh_,
                      start=(i == 0), stop=(i == n - 1),
                      inc=(ts if i == n - 1 else None))
            return ts.n

        ag_launched = {}
        for gi, g in enumerate([4, 5, 6, 7, 8, 9, 10, 11, 0, 1, 2, 3]):
            par = gi % 3
            rg.wait(ts, grp_consumed_ts[par])
            tok_g = Qg.dma(rg, wq_v[:, par],
                           w_qkv[:, g * 512:(g + 1) * 512].rearrange("(c p) n -> p c n", p=128))
            rt.wait_tok(tok_g)
            rt.wait(vs, lnfm_half0_vs)
            if g < 8:
                for c in range(4):
                    for half in range(2):
                        if half == 1:
                            rt.wait(vs, lnfm_done_vs)
                        bi = 2 * (c % 2) + half
                        mm_done = mm_to_bank(bi, [
                            (wq_v[:, par, k, c * 128:(c + 1) * 128],
                             fm_v[:, k, half * 512:(half + 1) * 512])
                            for k in range(NDC)])
                        col = g * 512 + c * 128
                        rs.wait(ts, mm_done)
                        if g < 4:
                            h_ = col // 128
                            qa = q_all[:, h_ * T + half * 512:h_ * T + half * 512 + 512]
                            rs.op("copy", out=qa, in_=p_mm[bi][:], inc=ss)
                            norms_from(qa, h_, half, True)
                        else:
                            si = bi
                            stg = kvstage[:, si * 512:(si + 1) * 512]
                            rs.wait_tok(stage_free[si])
                            rs.op("copy", out=stg, in_=p_mm[bi][:], inc=ss)
                            h_ = (col - 2048) // 128
                            ry.wait(ss)
                            stage_free[si] = Qy.dma(
                                ry,
                                kv_in_k[h_ * 128:(h_ + 1) * 128,
                                        half * 512:(half + 1) * 512],
                                stg)
                            norms_from(stg, h_, half, False)
                        bank_free[bi] = ss.n
            else:
                for tc in range(NTC):
                    if tc >= 4:
                        rt.wait(vs, lnfm_done_vs)
                    bi = 2 * (tc % 2)
                    mm_done = mm_to_bank(bi, [
                        (fm_v[:, k, tc * 128:(tc + 1) * 128], wq_v[:, par, k, :])
                        for k in range(NDC)])
                    si = tc % 4
                    stg = kvstage[:, si * 512:(si + 1) * 512]
                    rs.wait(ts, mm_done)
                    rs.wait_tok(stage_free[si])
                    rs.op("copy", out=stg, in_=p_mm[bi][:], inc=ss)
                    bank_free[bi] = ss.n
                    vcol = (g - 8) * 512
                    ry.wait(ss)
                    stage_free[si] = Qy.dma(
                        ry, kv_in_v[tc * 128:(tc + 1) * 128, vcol:vcol + 512], stg)
            grp_consumed_ts[par] = ts.n
            if g == 7:
                for sm in Qy.slots:
                    rg.wait(sm)
                rg.op("collective_compute", "AllGather", ALU.bypass,
                      replica_groups=GROUPS, ins=[kv_in_k], outs=[kv_out_k],
                      inc=cskv)
                ag_launched["k"] = (cskv, cskv.n)
                rg.op("collective_compute", "AllGather", ALU.bypass,
                      replica_groups=GROUPS, ins=[kn_in], outs=[kn_out],
                      inc=csn)
                ag_launched["kn"] = (csn, csn.n)
            elif g == 11:
                for sm in Qy.slots:
                    rg.wait(sm)
                rg.op("collective_compute", "AllGather", ALU.bypass,
                      replica_groups=GROUPS, ins=[kv_in_v], outs=[kv_out_v],
                      inc=cskv)
                ag_launched["v"] = (cskv, cskv.n)

        rg.wait(cskv)
        rg.wait(csn)
        wconv_toks = []
        for src, dst, rows in [(w_upT, w_upT_bf, D), (w_gateT, w_gateT_bf, D),
                               (w_downT, w_downT_bf, HFF)]:
            q4 = rows // 4
            for ci in range(4):
                wconv_toks.append(Qg.dma(rg, dst[ci * q4:(ci + 1) * q4, :],
                                         src[ci * q4:(ci + 1) * q4, :]))

        if "d_q" in dbg_out:
            rg.wait(ss)
            Qg.dma(rg, dbg_out["d_q"], q_all[:])

        kv_gathered = (cskv, cskv.n)

        if "d_kvk" in dbg_out:
            rg.wait_tok(kv_gathered)
            Qg.dma(rg, dbg_out["d_kvk"], kv_in_k)
        if "d_kvv" in dbg_out:
            rg.wait_tok(kv_gathered)
            Qg.dma(rg, dbg_out["d_kvv"], kv_in_v)

        barrier()
        stA.close()

        stB = ExitStack()
        sbB = lambda n_, sh, dt: stB.enter_context(nc.sbuf_tensor(n_, sh, dt))
        psB = lambda n_, sh, dt=F32: stB.enter_context(nc.psum_tensor(n_, sh, dt))
        kh_sb = sbB("kh_sb", [128, 2 * 8 * 512], BF16)
        vh_sb = sbB("vh_sb", [128, 2 * 8 * 512], BF16)
        pb = sbB("pb", [128, 8192], BF16)
        pt_sb = sbB("pt_sb", [128, 2 * 1024], BF16)
        dscr = sbB("dscr", [128, 512], F32)
        kn_all2 = sbB("kn_all2", [64, 1024], F32)
        kn_all = sbB("kn_all", [128, 132], F32)
        kncol = sbB("kncol", [64, 1], F32)
        kmrow = sbB("kmrow", [1, 80], F32)
        qmrow = sbB("qmrow", [1, 128], F32)
        negbs_all = sbB("negbs_all", [128, 128], F32)
        negbp = sbB("negbp", [128, 64], F32)
        zrow_sb = sbB("zrow_sb", [1, 512], F32)
        rzb_sb = sbB("rzb_sb", [128, 256], F32)
        p_s = [psB(f"p_s{i}", [128, 512]) for i in range(3)]
        p_o = [psB(f"p_o{i}", [128, 256]) for i in range(2)]
        p_misc = psB("p_misc", [128, 256])
        p_z = psB("p_z", [1, 256])

        kh_v = kh_sb[:].rearrange("p (b j f t) -> p b j f t", b=2, j=8, f=4)
        vh_v = vh_sb[:].rearrange("p (b j f t) -> p b j f t", b=2, j=8, f=4)


        khb_consumed = [0, 0]
        kv_tok = [None, None]
        sbank_free = [None, None, None]
        sbank_i = 0
        o_free = [0, 0]
        pb_free = 0
        zrow_free = 0

        ry.wait_tok(ag_launched["kn"])
        tok_kn = Qy.dma(ry, kn_all2[0:64, :], kn_out)
        rv.wait_tok(tok_kn)
        rv.op("tensor_reduce", kncol[0:64, :], kn_all2[0:64, :], AX.X, ALU.max,
              inc=vs)
        rt.wait(vs)
        rt.op("transpose", p_misc[0:1, 0:64], kncol[0:64, :],
              identf[0:64, 0:64], inc=ts)
        rv.wait(ts)
        rv.op("tensor_copy", out=kmrow[0:1, 0:64], in_=p_misc[0:1, 0:64], inc=vs)
        rv.wait(vs)
        rv.op("tensor_tensor", kmrow[0:1, 64:80], kmrow[0:1, 0:16],
              kmrow[0:1, 16:32], ALU.max, inc=vs)
        rv.wait(vs)
        rv.op("tensor_tensor", kmrow[0:1, 64:80], kmrow[0:1, 64:80],
              kmrow[0:1, 32:48], ALU.max, inc=vs)
        rv.wait(vs)
        rv.op("tensor_tensor", kmrow[0:1, 64:80], kmrow[0:1, 64:80],
              kmrow[0:1, 48:64], ALU.max, inc=vs)
        rt.wait(vs)
        rt.op("transpose", p_misc[:, 128:256], qn_sb[:], identf[:], inc=ts)
        rv.wait(ts)
        rv.op("tensor_copy", out=kn_all[0:128, 0:128], in_=p_misc[:, 128:256],
              inc=vs)
        rv.wait(vs)
        rv.op("tensor_reduce", kncol[0:64, 0:1] if False else kn_all[0:128, 128:129],
              kn_all[0:128, 0:128], AX.X, ALU.max, inc=vs)
        rt.wait(vs)
        rt.op("transpose", p_misc[0:1, 0:128], kn_all[0:128, 128:129],
              identf[:], inc=ts)
        rv.wait(ts)
        rv.op("tensor_copy", out=kmrow[0:1, 0:64] if False else qmrow[0:1, 0:128],
              in_=p_misc[0:1, 0:128], inc=vs)
        for j8 in range(8):
            rv.wait(vs)
            rv.op("tensor_tensor", qmrow[0:1, j8:128:8], qmrow[0:1, j8:128:8],
                  kmrow[0:1, 64:80], ALU.add, inc=vs)
        rv.wait(vs)
        rv.op("tensor_scalar", qmrow[0:1, 0:128], qmrow[0:1, 0:128],
              -RSQ / 2.0, None, ALU.mult, inc=vs)
        rt.wait(vs)
        rt.op("matmul", p_misc[:, 128:256], onesf[0:1, :], qmrow[0:1, 0:128],
              start=True, stop=True, inc=ts)
        rv.wait(ts)
        rv.op("tensor_copy", out=negbs_all[:], in_=p_misc[:, 128:256], inc=vs)
        rv.wait(vs)
        rv.op("tensor_tensor", negbp[:].unsqueeze(2),
              negbs_all[:].rearrange("p (c o) -> p c o", o=2)[:, :, 0:1],
              negbs_all[:].rearrange("p (c o) -> p c o", o=2)[:, :, 1:2],
              ALU.min, inc=vs)
        negbs_vs = vs.n

        for h in range(HEADS):
            hb = h % 2
            ry.wait_tok(ag_launched["k"])
            ry.wait_tok(ag_launched["v"])
            ry.wait(ts, khb_consumed[hb])
            toks = []
            for r_ in range(4):
                toks.append(Qy.dma(
                    ry, kh_v[:, hb, :, r_, :],
                    kv_out_k[r_ * 2048 + h * 128:r_ * 2048 + (h + 1) * 128, :]
                    .rearrange("p (j t) -> p j t", j=8)))
                toks.append(Qy.dma(
                    ry, vh_v[:, hb, :, r_, :],
                    kv_out_v[r_ * T:(r_ + 1) * T, h * 128:(h + 1) * 128]
                    .rearrange("(j p) t -> p j t", p=128)))
            kv_tok[hb] = toks

            for pj in (0, 2, 4, 6):
                opar = (pj // 2) % 2
                lvx = pj + 1
                bias_ap = negbp[:, h * 4 + pj // 2:h * 4 + pj // 2 + 1]
                rt.wait(vs, o_free[opar])
                rv.wait(ts, pb_free)
                rv.op("memset",
                      pb[:, lvx * 1024:(lvx + 1) * 1024]
                      .rearrange("p (f q) -> p f q", f=4)[:, :, 0:128], 0.0,
                      inc=vs)
                pb_pad_vs = vs.n
                for lv in range(pj + 1):
                    for bk in range(2):
                        bi = sbank_i % 3
                        sbank_i += 1
                        rt.wait_tok(sbank_free[bi])
                        for tk in kv_tok[hb]:
                            rt.wait_tok(tk)
                        for rr in range(2):
                            r_ = bk * 2 + rr
                            rt.op("matmul", p_s[bi][:, rr * 256:(rr + 1) * 256],
                                  kh_sb[:, hb * 4096 + lv * 512 + r_ * 128:
                                        hb * 4096 + lv * 512 + (r_ + 1) * 128],
                                  q_all[:, h * T + pj * 128:h * T + (pj + 2) * 128],
                                  start=True, stop=True,
                                  inc=(ts if rr == 1 else None))
                        mm_done = ts.n
                        dst = pb[:, (lv * 4 + bk * 2) * 256:(lv * 4 + bk * 2 + 2) * 256]
                        if lv == pj:
                            rv.wait(ts, mm_done)
                            rv.wait_tok(tok_c["mask"])
                            rv.wait(ss)
                            rv.op("tensor_tensor", dscr[:], p_s[bi][:],
                                  mask_sb[:, bk * 512:(bk + 1) * 512], ALU.add,
                                  inc=vs)
                            sbank_free[bi] = (vs, vs.n)
                            rs.wait(vs)
                            rs.wait(vs, pb_pad_vs)
                            rs.op("activation", dst, dscr[:], AF.Exp,
                                  bias=bias_ap, scale=RSQ, inc=ss)
                        else:
                            rs.wait(ts, mm_done)
                            rs.wait(vs, negbs_vs)
                            rs.wait(vs, pb_pad_vs)
                            rs.op("activation", dst, p_s[bi][:], AF.Exp,
                                  bias=bias_ap, scale=RSQ, inc=ss)
                            sbank_free[bi] = (ss, ss.n)
                bi = sbank_i % 3
                sbank_i += 1
                rt.wait_tok(sbank_free[bi])
                for r_ in range(4):
                    rt.op("matmul", p_s[bi][:, r_ * 128:(r_ + 1) * 128],
                          kh_sb[:, hb * 4096 + lvx * 512 + r_ * 128:
                                hb * 4096 + lvx * 512 + (r_ + 1) * 128],
                          q_all[:, h * T + (pj + 1) * 128:h * T + (pj + 2) * 128],
                          start=True, stop=True, inc=(ts if r_ == 3 else None))
                mm_done = ts.n
                rv.wait(ts, mm_done)
                rv.wait_tok(tok_c["mask"])
                rv.wait(ss)
                rv.op("tensor_tensor", dscr[:], p_s[bi][:],
                      mask_sb[:, 1024:1536], ALU.add, inc=vs)
                sbank_free[bi] = (vs, vs.n)
                rs.wait(vs)
                rs.op("activation",
                      pb[:, lvx * 1024:(lvx + 1) * 1024]
                      .rearrange("p (f q) -> p f q", f=4)[:, :, 128:256],
                      dscr[:].rearrange("p (f q) -> p f q", f=4),
                      AF.Exp, bias=bias_ap, scale=RSQ, inc=ss)
                exp_done = ss.n
                nm = (pj + 1) * 4
                rt.wait(ss, exp_done)
                rt.wait_tok(tok_c["onescol"])
                rt.wait(vs, zrow_free)
                for m_ in range(nm):
                    rt.op("matmul", p_z[0:1, 0:256], onescol[:],
                          pb[:, m_ * 256:(m_ + 1) * 256],
                          start=(m_ == 0), stop=False)
                for r_ in range(4):
                    rt.op("matmul", p_z[0:1, 128:256], onescol[:],
                          pb[:, lvx * 1024 + r_ * 256 + 128:lvx * 1024 + (r_ + 1) * 256],
                          start=False, stop=(r_ == 3),
                          inc=(ts if r_ == 3 else None))
                rv.wait(ts)
                rv.op("tensor_copy", out=zrow_sb[0:1, 0:256], in_=p_z[0:1, 0:256],
                      inc=vs)
                rv.wait(vs)
                rv.op("reciprocal", zrow_sb[0:1, 256:512], zrow_sb[0:1, 0:256],
                      inc=vs)
                rz_vs = vs.n
                rt.wait(vs, rz_vs)
                rt.op("matmul", p_misc[:, 0:256], onesf[0:1, :],
                      zrow_sb[0:1, 256:512], start=True, stop=True, inc=ts)
                rv.wait(ts)
                rv.op("tensor_copy", out=rzb_sb[:], in_=p_misc[:, 0:256], inc=vs)
                zrow_free = vs.n
                for m_ in range(nm):
                    rt.op("matmul", p_o[opar][:], 
                          vh_sb[:, hb * 4096 + m_ * 128:hb * 4096 + (m_ + 1) * 128],
                          pb[:, m_ * 256:(m_ + 1) * 256],
                          start=(m_ == 0), stop=False)
                for r_ in range(4):
                    rt.op("matmul", p_o[opar][:, 128:256],
                          vh_sb[:, hb * 4096 + (lvx * 4 + r_) * 128:
                                hb * 4096 + (lvx * 4 + r_ + 1) * 128],
                          pb[:, lvx * 1024 + r_ * 256 + 128:lvx * 1024 + (r_ + 1) * 256],
                          start=False, stop=(r_ == 3),
                          inc=(ts if r_ == 3 else None))
                pb_free = ts.n
                av_done = ts.n
                rv.wait(ts, av_done)
                rv.wait(vs)
                rv.op("tensor_tensor",
                      fm_buf[:, h * T + pj * 128:h * T + (pj + 2) * 128],
                      p_o[opar][:], rzb_sb[:], ALU.mult, inc=vs)
                o_free[opar] = vs.n
            khb_consumed[hb] = ts.n

        if "d_attnout" in dbg_out:
            rg.wait(vs)
            Qg.dma(rg, dbg_out["d_attnout"], fm_buf[:])

        barrier()
        stB.close()
        stQ.close()
        stL = ExitStack()
        ln2_buf = stL.enter_context(nc.sbuf_tensor("ln2_buf", [128, NDC * T], BF16))
        ln2_v = ln2_buf[:].rearrange("p (c t) -> p c t", c=NDC)

        stC = ExitStack()
        sbC = lambda n_, sh, dt: stC.enter_context(nc.sbuf_tensor(n_, sh, dt))
        psC = lambda n_, sh, dt=F32: stC.enter_context(nc.psum_tensor(n_, sh, dt))
        wout_sb = sbC("wout_sb", [128, NDC * D], BF16)
        rowscr2 = sbC("rowscr2", [128, D], F32)
        ln2f32 = sbC("ln2f32", [128, 512], F32)
        lgj_sb = sbC("lgj_sb", [NE, 128], F32)
        gwT_sb = sbC("gwT_sb", [128, NDC * NE], F32)
        p_o2 = [psC(f"p_o2{i}", [128, 512]) for i in range(4)]
        p_t = [psC(f"p_t{i}", [128, 512]) for i in range(2)]
        p_g = psC("p_g", [NE, 128])
        p_x8 = psC("p_x8", [128, NE])

        wout_v = wout_sb[:].rearrange("p (c n) -> p c n", c=NDC)
        gwT_v = gwT_sb[:].rearrange("p (c e) -> p c e", c=NDC)

        tok_wout = Qg.dma(rg, wout_v, w_out.rearrange("(c p) n -> p c n", p=128))
        tok_gwt = Qy.dma(ry, gwT_v, gate_wT.rearrange("(c p) e -> p c e", p=128))

        attnout_vs = vs.n
        o2bank_free = [0, 0, 0, 0]
        pg_free = 0
        ln2f_free = [0, 0]
        for j in range(NTC):
            for q in range(4):
                rt.wait(vs, o2bank_free[q])
                rt.wait_tok(tok_wout)
                for c in range(NDC):
                    rt.op("matmul", p_o2[q][:],
                          fm_buf[:, c * T + j * 128:c * T + (j + 1) * 128],
                          wout_v[:, c, q * 512:(q + 1) * 512],
                          start=(c == 0), stop=(c == NDC - 1),
                          inc=(ts if c == NDC - 1 else None))
            mm_done = ts.n
            rv.wait(ts, mm_done)
            rv.wait(ts)
            for q in range(4):
                rv.op("tensor_copy", out=rowscr2[:, q * 512:(q + 1) * 512],
                      in_=p_o2[q][:], inc=vs)
                o2bank_free[q] = vs.n
            row = rowscr2[:]
            mean, rstd = ln_stats(row)
            rv.wait(ss)
            rv.op("tensor_scalar", row, row, mean, rstd,
                  ALU.subtract, op1=ALU.mult, inc=vs)
            rv.wait(vs)
            rv.op("tensor_tensor", x_res[:, j * D:(j + 1) * D],
                  x_res[:, j * D:(j + 1) * D], row, ALU.add, inc=vs)
            mean, rstd = ln_stats(x_res[:, j * D:(j + 1) * D])
            rv.wait(ss)
            rv.wait(vs)
            rv.op("tensor_scalar", row, x_res[:, j * D:(j + 1) * D], mean, rstd,
                  ALU.subtract, op1=ALU.mult, inc=vs)
            row_vs = vs.n
            for k4 in range(4):
                tpar = k4 % 2
                rt.wait(vs, row_vs)
                rt.wait(ss, ln2f_free[0])
                for i in range(4):
                    rt.op("transpose", p_t[tpar][:, i * 128:(i + 1) * 128],
                          rowscr2[:, (4 * k4 + i) * 128:(4 * k4 + i + 1) * 128],
                          identf[:], inc=(ts if i == 3 else None))
                tr_done = ts.n
                rv.wait(ts, tr_done)
                rv.wait(ss, ln2f_free[0])
                rv.op("tensor_copy", out=ln2f32[:],
                      in_=p_t[tpar][:], inc=vs)
                l32_vs = vs.n
                rs.wait(vs, l32_vs)
                rs.op("copy",
                      out=ln2_v[:, 4 * k4:4 * k4 + 4, j * 128:(j + 1) * 128],
                      in_=ln2f32[:].rearrange("p (c t) -> p c t", c=4), inc=ss)
                ln2f_free[0] = ss.n
                rt.wait(vs, l32_vs)
                rt.wait_tok(tok_gwt)
                if k4 == 0:
                    rt.wait(vs, pg_free)
                for i in range(4):
                    k = 4 * k4 + i
                    rt.op("matmul", p_g[:],
                          gwT_v[:, k, :],
                          ln2f32[:, i * 128:(i + 1) * 128],
                          start=(k == 0), stop=(k == NDC - 1),
                          inc=(ts if k == NDC - 1 else None))
            gmm_done = ts.n
            rv.wait(ts, gmm_done)
            rv.wait(ts)
            rv.op("tensor_copy", out=lgj_sb[:], in_=p_g[:], inc=vs)
            pg_free = vs.n
            rt.wait(vs, pg_free)
            rt.op("transpose", p_x8[:], lgj_sb[:], identf[0:NE, 0:NE], inc=ts)
            rv.wait(ts)
            rv.op("tensor_copy", out=lgtm_sb[:, j * NE:(j + 1) * NE],
                  in_=p_x8[:], inc=vs)
        x2_done_vs = vs.n
        ln2_done_ss = ss.n

        if "d_x2" in dbg_out:
            rg.wait(vs)
            Qg.dma(rg, dbg_out["d_x2"], x_res[:])
        if "d_ln2fm" in dbg_out:
            rg.wait(ss)
            Qg.dma(rg, dbg_out["d_ln2fm"], ln2_buf[:])

        rv_lgtm = vs.n
        ry.wait(vs, rv_lgtm)
        tok_lgin = Qy.dma(ry, lg_in.rearrange("(c p) e -> p c e", p=128),
                          lgtm_sb[:].rearrange("p (c e) -> p c e", c=NTC))
        rg.wait_tok(tok_lgin)
        rg.op("collective_compute", "AllGather", ALU.bypass,
              replica_groups=GROUPS, ins=[lg_in], outs=[lg_out], inc=cslg)
        lg_gathered = (cslg, cslg.n)

        if "d_logits" in dbg_out:
            rg.wait_tok(lg_gathered)
            Qg.dma(rg, dbg_out["d_logits"], lg_in)

        barrier()
        stC.close()

        stD = ExitStack()
        sbD = lambda n_, sh, dt: stD.enter_context(nc.sbuf_tensor(n_, sh, dt))
        mixed = sbD("mixed", [128, NHC * 512], BF16)
        wkb = sbD("wkb", [128, 2 * 512], F32)
        mt4 = sbD("mt4", [128, 4 * 512], BF16)
        rtld = sbD("rtld", [128, 4 * 512], BF16)
        upstr = fm_buf[:, 0:8704]
        wdnstr = fm_buf[:, 8704:12800]
        dnA_sb3 = fm_buf[:, 12800:13184]
        chb = fm_buf[:, 13184:14208]
        dnB_sb = fm_buf[:, 14208:16256]
        up_v = upstr.rearrange("p (b n) -> p b n", b=2)

        tok_dnB = Qg.dma(rg, dnB_sb, dnB_T)

        stG = ExitStack()
        sbG = lambda n_, sh, dt: stG.enter_context(nc.sbuf_tensor(n_, sh, dt))
        psG = lambda n_, sh, dt=F32: stG.enter_context(nc.psum_tensor(n_, sh, dt))
        lgt = sbG("lgt", [128, 32 * NE], F32)
        tscr = sbG("tscr", [128, 16 * 32], F32)
        e_scr = sbG("e_scr", [128, 48], F32)
        ind_tm = sbG("ind_tm", [128, 2 * NTC * NE], BF16)
        indfm = sbG("indfm", [NE, 2 * T], BF16)
        wrow = sbG("wrow", [1, 160], F32)
        p_col = psG("p_col", [128, 16])
        p_row = psG("p_row", [1, 128])
        p_i8 = stG.enter_context(nc.psum_tensor("p_i8", [NE, 128], BF16))
        p_m512 = psG("p_m512", [128, 512])

        ry.wait_tok(lg_gathered)
        tok_lgt = Qy.dma(ry, lgt[:].rearrange("p (c e) -> p c e", e=NE),
                         lg_out.rearrange("(c p) e -> p c e", p=128))

        def tournament(view, n, m1, m2):
            cols = [view[:, :, i_:i_ + 1] for i_ in range(8)]
            scr = lambda i_: tscr[:, i_ * 32:i_ * 32 + n].unsqueeze(2)
            def tt(dst, a_, b_, op):
                rv.wait(vs)
                rv.op("tensor_tensor", dst, a_, b_, op, inc=vs)
            a = [scr(i_) for i_ in range(4)]
            iw = [scr(4 + i_) for i_ in range(4)]
            for i_ in range(4):
                tt(a[i_], cols[2 * i_], cols[2 * i_ + 1], ALU.max)
                tt(iw[i_], cols[2 * i_], cols[2 * i_ + 1], ALU.min)
            b1, j1, b2, j2, k1 = scr(8), scr(9), scr(10), scr(11), scr(12)
            tt(b1, a[0], a[1], ALU.max)
            tt(j1, a[0], a[1], ALU.min)
            tt(b2, a[2], a[3], ALU.max)
            tt(j2, a[2], a[3], ALU.min)
            tt(m1.unsqueeze(2), b1, b2, ALU.max)
            tt(k1, b1, b2, ALU.min)
            t1, t2, t3 = a[0], a[1], a[2]
            tt(t1, j1, j2, ALU.max)
            tt(t2, iw[0], iw[1], ALU.max)
            tt(t3, iw[2], iw[3], ALU.max)
            tt(j1, t1, k1, ALU.max)
            tt(j2, t2, t3, ALU.max)
            tt(m2.unsqueeze(2), j1, j2, ALU.max)

        m1g = tscr[:, 13 * 32:14 * 32]
        m2g = tscr[:, 14 * 32:15 * 32]
        rv.wait_tok(tok_lgt)
        tournament(lgt[:].rearrange("p (c e) -> p c e", e=NE), 32, m1g, m2g)
        m1o = e_scr[:, 16:24]
        m2o = e_scr[:, 24:32]
        tournament(lgtm_sb[:].rearrange("p (c e) -> p c e", e=NE), NE, m1o, m2o)

        for k in range(2):
            mg = m1g if k == 0 else m2g
            mo = m1o if k == 0 else m2o
            gred = statscr[:, 16:17]
            rv.wait(vs)
            rv.wait(ss)
            rv.op("tensor_reduce", gred, mg, AX.X, ALU.max, inc=vs)
            rt.wait(vs)
            rt.op("transpose", p_row[0:1, :], gred, identf[:], inc=ts)
            rv.wait(ts)
            rv.op("tensor_copy", out=wrow[0:1, 0:128], in_=p_row[0:1, :], inc=vs)
            rv.wait(vs)
            rv.op("tensor_reduce", wrow[0:1, 128:129], wrow[0:1, 0:128], AX.X,
                  ALU.max, inc=vs)
            gmax_vs = vs.n
            rt.wait(vs, gmax_vs)
            rt.op("matmul", p_col[:, 0:1], negonesf[0:1, :], wrow[0:1, 128:129],
                  start=True, stop=True, inc=ts)
            ngb = statscr[:, 17:18]
            rv.wait(ts)
            rv.op("tensor_copy", out=ngb, in_=p_col[:, 0:1], inc=vs)
            zc = statscr[:, 18:19]
            rs.wait(vs)
            rs.op("activation", tscr[:, 15 * 32:16 * 32], mg, AF.Exp,
                  bias=ngb, accum_out=zc, inc=ss)
            rt.wait(ss)
            rt.op("transpose", p_row[0:1, :], zc, identf[:], inc=ts)
            rv.wait(ts)
            rv.op("tensor_copy", out=wrow[0:1, 0:128], in_=p_row[0:1, :], inc=vs)
            rv.wait(vs)
            rv.op("tensor_reduce", wrow[0:1, 129:130], wrow[0:1, 0:128], AX.X,
                  ALU.add, inc=vs)
            rv.wait(vs)
            rv.op("reciprocal", wrow[0:1, 130:131], wrow[0:1, 129:130], inc=vs)
            rt.wait(vs)
            rt.op("matmul", p_col[:, 1:2], onesf[0:1, :], wrow[0:1, 130:131],
                  start=True, stop=True, inc=ts)
            rzb = statscr[:, 19:20]
            rv.wait(ts)
            rv.op("tensor_copy", out=rzb, in_=p_col[:, 1:2], inc=vs)
            eo = e_scr[:, 32:40]
            rs.wait(vs)
            rs.op("activation", eo, mo, AF.Exp, bias=ngb, inc=ss)
            rv.wait(ss)
            rv.op("tensor_scalar", statscr[:, k * 8:(k + 1) * 8], eo, rzb, None,
                  ALU.mult, inc=vs)
            for j in range(NTC):
                rv.wait(vs)
                rv.op("tensor_scalar",
                      ind_tm[:, (k * NTC + j) * NE:(k * NTC + j + 1) * NE],
                      lgtm_sb[:, j * NE:(j + 1) * NE],
                      mo[:, j:j + 1], None, ALU.is_equal, inc=vs)
                rt.wait(vs)
                rt.op("transpose", p_i8[:],
                      ind_tm[:, (k * NTC + j) * NE:(k * NTC + j + 1) * NE],
                      identb[:], inc=ts)
                rv.wait(ts)
                rv.op("tensor_copy",
                      out=indfm[0:NE, k * T + j * 128:k * T + (j + 1) * 128],
                      in_=p_i8[:], inc=vs)
        for half in range(2):
            for k in range(2):
                rt.wait(vs)
                rt.wait_tok(tok_c["blk2"])
                rt.op("matmul", p_m512[:], blk2[:],
                      indfm[0:NE, k * T + half * 512:k * T + (half + 1) * 512],
                      start=True, stop=True, inc=ts)
                rv.wait(ts)
                rv.op("tensor_copy",
                      out=mt4[:, (half * 2 + k) * 512:(half * 2 + k + 1) * 512],
                      in_=p_m512[:], inc=vs)
        barrier()
        stG.close()
        hscr = sbD("hscr", [128, 4 * 512], F32)
        rowscrD = sbD("rowscrD", [128, D], F32)
        ub_s = hscr[:, 0 * 512:1 * 512]
        gb_s = hscr[:, 1 * 512:2 * 512]
        su0_s = hscr[:, 2 * 512:3 * 512]
        su1_s = hscr[:, 3 * 512:4 * 512]

        wdn_v = wdnstr.rearrange("p (b d) -> p b d", b=2)
        bundle_used_ts = [0, 0]
        dn_used_ts = [0, 0]
        for half in range(2):
            hs = half * 512
            stP = ExitStack()
            p_r = stP.enter_context(nc.psum_tensor(f"p_r{half}", [128, 512], F32))
            p_wk = stP.enter_context(nc.psum_tensor(f"p_wk{half}", [128, 128], F32))
            p_wr = stP.enter_context(nc.psum_tensor(f"p_wr{half}", [1, 128], F32))
            wrow2 = stP.enter_context(nc.sbuf_tensor(f"wrow2{half}", [1, 128], F32))
            rg.wait(ts, dn_used_ts[0])
            rg.wait(ts, dn_used_ts[1])
            tok_upA = Qg.dma(rg, wdn_v[:, 0].rearrange("p (c e) -> p c e", c=NDC),
                             upA_T.rearrange("(c p) e -> p c e", p=128))
            tok_gtA = Qg.dma(rg, wdn_v[:, 1].rearrange("p (c e) -> p c e", c=NDC),
                             gateA_T.rearrange("(c p) e -> p c e", p=128))
            for k in range(2):
                for c4 in range(4):
                    tch = half * 4 + c4
                    rt.wait(vs)
                    rt.op("transpose", p_wr[0:1, :],
                          statscr[:, k * 8 + tch:k * 8 + tch + 1], identf[:], inc=ts)
                    rv.wait(ts)
                    rv.op("tensor_copy", out=wrow2[0:1, :], in_=p_wr[0:1, :], inc=vs)
                    rv_w = vs.n
                    rt.wait(vs, rv_w)
                    rt.op("matmul", p_wk[:], onesf[0:1, :], wrow2[0:1, :],
                          start=True, stop=True, inc=ts)
                    rv.wait(ts)
                    rv.op("tensor_copy", out=wkb[:, k * 512 + c4 * 128:
                                                 k * 512 + (c4 + 1) * 128],
                          in_=p_wk[:], inc=vs)
            for side in range(2):
                rt.wait_tok(tok_upA if side == 0 else tok_gtA)
                rt.wait(vs)
                for kd in range(NDC):
                    rt.op("matmul", p_r[:],
                          wdn_v[:, side, kd * 128:(kd + 1) * 128],
                          ln2_v[:, kd, hs:hs + 512],
                          start=(kd == 0), stop=(kd == NDC - 1),
                          inc=(ts if kd == NDC - 1 else None))
                r_done = ts.n
                for k in range(2):
                    rv.wait(ts, r_done)
                    rv.op("tensor_tensor", rtld[:, (side * 2 + k) * 512:
                                                 (side * 2 + k + 1) * 512],
                          p_r[:], mt4[:, (half * 2 + k) * 512:(half * 2 + k + 1) * 512],
                          ALU.mult, inc=vs)
            rtld_vs = vs.n
            dn_used_ts[0] = dn_used_ts[1] = ts.n
            stP.close()

            stH = ExitStack()
            p_u0 = [stH.enter_context(nc.psum_tensor(f"p_u0{half}{i}", [128, 512], F32))
                    for i in range(2)]
            p_g0 = [stH.enter_context(nc.psum_tensor(f"p_g0{half}{i}", [128, 512], F32))
                    for i in range(2)]
            p_loU = stH.enter_context(nc.psum_tensor(f"p_loU{half}", [128, 512], F32))
            p_loG = stH.enter_context(nc.psum_tensor(f"p_loG{half}", [128, 512], F32))
            p_sraw = [stH.enter_context(nc.psum_tensor(f"p_sraw{half}{i}", [128, 512], F32))
                      for i in range(2)]
            u0_free = [None, None]
            g0_free = [None, None]
            dnA_used = [0, 0, 0]
            dnA_tok = [None, None, None]
            loU_free = 0
            loG_free = 0
            ch_vs = 0
            sraw_ts = 0
            for hh in range(NHC):
                par = hh % 2
                ry.wait(ts, bundle_used_ts[par])
                for tk in wconv_toks:
                    ry.wait_tok(tk)
                bt = []
                bt.append(Qy.dma(ry, up_v[:, par, 0:2048].rearrange("p (c h) -> p c h", c=NDC),
                                 w_upT_bf[:, hh * 128:(hh + 1) * 128]
                                 .rearrange("(c p) h -> p c h", p=128)))
                bt.append(Qy.dma(ry, up_v[:, par, 2048:4096].rearrange("p (c h) -> p c h", c=NDC),
                                 w_gateT_bf[:, hh * 128:(hh + 1) * 128]
                                 .rearrange("(c p) h -> p c h", p=128)))
                rg.wait(ts, bundle_used_ts[par])
                bt.append(Qg.dma(rg, up_v[:, par, 4096:4224],
                                 upB_T[:, hh * 128:(hh + 1) * 128]))
                bt.append(Qg.dma(rg, up_v[:, par, 4224:4352],
                                 gateB_T[:, hh * 128:(hh + 1) * 128]))
                rg.wait(ts, dnA_used[hh % 3])
                dnA_tok[hh % 3] = Qg.dma(
                    rg, dnA_sb3[:, (hh % 3) * 128:(hh % 3 + 1) * 128],
                    dnA_T[hh * 128:(hh + 1) * 128, :])
                for tk in bt:
                    rt.wait_tok(tk)
                rt.wait_tok(u0_free[par])
                rt.wait(vs, rtld_vs)
                for kd in range(NDC):
                    rt.op("matmul", p_u0[par][:],
                          up_v[:, par, kd * 128:(kd + 1) * 128],
                          ln2_v[:, kd, hs:hs + 512],
                          start=(kd == 0), stop=(kd == NDC - 1),
                          inc=(ts if kd == NDC - 1 else None))
                baseu_ts = ts.n
                rt.wait_tok(g0_free[par])
                for kd in range(NDC):
                    rt.op("matmul", p_g0[par][:],
                          up_v[:, par, 2048 + kd * 128:2048 + (kd + 1) * 128],
                          ln2_v[:, kd, hs:hs + 512],
                          start=(kd == 0), stop=(kd == NDC - 1),
                          inc=(ts if kd == NDC - 1 else None))
                baseg_ts = ts.n
                if hh > 0:
                    rt.wait(vs, ch_vs)
                    rt.wait_tok(dnA_tok[(hh - 1) % 3])
                    for k in range(2):
                        rt.op("matmul", p_sraw[k][:],
                              dnA_sb3[:, ((hh - 1) % 3) * 128:((hh - 1) % 3 + 1) * 128],
                              chb[:, k * 512:(k + 1) * 512],
                              start=(hh - 1 == 0), stop=False,
                              inc=(ts if k == 1 else None))
                    sraw_ts = ts.n
                    dnA_used[(hh - 1) % 3] = ts.n
                rs.wait(ts, baseu_ts)
                rs.wait(vs)
                rs.wait(ss)
                rs.op("copy", out=ub_s, in_=p_u0[par][:], inc=ss)
                ubcopy_ss = ss.n
                rs.wait(ts, baseg_ts)
                rs.op("copy", out=gb_s, in_=p_g0[par][:], inc=ss)
                gbcopy_ss = ss.n
                rt.wait(ss, ubcopy_ss)
                rt.op("matmul", p_u0[par][:], up_v[:, par, 4096:4224],
                      rtld[:, 0:512], start=False, stop=True,
                      skip_group_check=True, inc=ts)
                lo0u_ts = ts.n
                rt.wait(vs, loU_free)
                rt.op("matmul", p_loU[:], up_v[:, par, 4096:4224],
                      rtld[:, 512:1024], start=True, stop=True, inc=ts)
                lo1u_ts = ts.n
                rt.wait(ss, gbcopy_ss)
                rt.op("matmul", p_g0[par][:], up_v[:, par, 4224:4352],
                      rtld[:, 1024:1536], start=False, stop=True,
                      skip_group_check=True, inc=ts)
                lo0g_ts = ts.n
                rt.wait(vs, loG_free)
                rt.op("matmul", p_loG[:], up_v[:, par, 4224:4352],
                      rtld[:, 1536:2048], start=True, stop=True, inc=ts)
                lo1g_ts = ts.n
                bundle_used_ts[par] = ts.n
                rs.wait(ts, lo0u_ts)
                rs.op("activation", su0_s, p_u0[par][:], AF.Silu, inc=ss)
                silu0_ss = ss.n
                u0_free[par] = (ss, silu0_ss)
                rv.wait(ts, lo1u_ts)
                rv.wait(ss, ubcopy_ss)
                rv.op("scalar_tensor_tensor", ub_s, p_loU[:], 1.0, ub_s,
                      ALU.mult, ALU.add, inc=vs)
                loU_free = vs.n
                u1_vs = vs.n
                rs.wait(vs, u1_vs)
                rs.op("activation", su1_s, ub_s, AF.Silu, inc=ss)
                silu1_ss = ss.n
                rv.wait(ss, silu0_ss)
                rv.wait(ts, lo0g_ts)
                rv.wait(ts, sraw_ts)
                rv.op("tensor_tensor", su0_s, su0_s, p_g0[par][:], ALU.mult, inc=vs)
                h0_vs = vs.n
                g0_free[par] = (vs, h0_vs)
                rv.wait(vs, h0_vs)
                rv.op("tensor_tensor", chb[:, 0:512], su0_s, wkb[:, 0:512],
                      ALU.mult, inc=vs)
                rv.wait(ts, lo1g_ts)
                rv.wait(ss, gbcopy_ss)
                rv.op("scalar_tensor_tensor", gb_s, p_loG[:], 1.0, gb_s,
                      ALU.mult, ALU.add, inc=vs)
                loG_free = vs.n
                rv.wait(ss, silu1_ss)
                rv.wait(vs)
                rv.op("tensor_tensor", su1_s, su1_s, gb_s, ALU.mult, inc=vs)
                rv.wait(vs)
                rv.op("tensor_tensor", chb[:, 512:1024], su1_s, wkb[:, 512:1024],
                      ALU.mult, inc=vs)
                rv.wait(vs)
                rv.op("tensor_tensor", mixed[:, hh * 512:(hh + 1) * 512],
                      chb[:, 0:512], chb[:, 512:1024], ALU.add, inc=vs)
                ch_vs = vs.n
            rt.wait(vs, ch_vs)
            rt.wait_tok(dnA_tok[(NHC - 1) % 3])
            for k in range(2):
                rt.op("matmul", p_sraw[k][:],
                      dnA_sb3[:, ((NHC - 1) % 3) * 128:((NHC - 1) % 3 + 1) * 128],
                      chb[:, k * 512:(k + 1) * 512],
                      start=False, stop=True, inc=(ts if k == 1 else None))
            sraw_done = ts.n
            for k in range(2):
                rv.wait(ts, sraw_done)
                rv.op("tensor_tensor", rtld[:, k * 512:(k + 1) * 512],
                      p_sraw[k][:], mt4[:, (half * 2 + k) * 512:(half * 2 + k + 1) * 512],
                      ALU.mult, inc=vs)
            stld_vs = vs.n
            barrier()
            stH.close()

            stE = ExitStack()
            p_d = [stE.enter_context(nc.psum_tensor(f"p_d{half}{i}", [128, 512], F32))
                   for i in range(8)]
            dbank_free = [0] * 8
            for tpair in range(2):
                for hh in range(NHC):
                    par = hh % 2
                    ry.wait(ts, dn_used_ts[par])
                    for tk in wconv_toks:
                        ry.wait_tok(tk)
                    tok_dn = Qy.dma(ry, wdn_v[:, par],
                                    w_downT_bf[hh * 128:(hh + 1) * 128, :])
                    rt.wait_tok(tok_dn)
                    if hh == 0:
                        for bi in range(8):
                            rt.wait(vs, dbank_free[bi])
                    for tq2 in range(2):
                        tloc = tpair * 2 + tq2
                        for qb in range(4):
                            rt.op("matmul", p_d[tq2 * 4 + qb][:],
                                  mixed[:, hh * 512 + tloc * 128:
                                        hh * 512 + (tloc + 1) * 128],
                                  wdn_v[:, par, qb * 512:(qb + 1) * 512],
                                  start=(hh == 0), stop=False,
                                  inc=(ts if (tq2 == 1 and qb == 3) else None))
                    dn_used_ts[par] = ts.n
                rt.wait(vs, stld_vs)
                rt.wait_tok(tok_dnB)
                for tq2 in range(2):
                    tloc = tpair * 2 + tq2
                    for qb in range(4):
                        for k in range(2):
                            rt.op("matmul", p_d[tq2 * 4 + qb][:],
                                  rtld[:, k * 512 + tloc * 128:
                                       k * 512 + (tloc + 1) * 128],
                                  dnB_sb[:, qb * 512:(qb + 1) * 512],
                                  start=False, stop=(k == 1),
                                  inc=(ts if (tq2 == 1 and qb == 3 and k == 1)
                                       else None))
                dn_done = ts.n
                for tq2 in range(2):
                    tch = half * 4 + tpair * 2 + tq2
                    rv.wait(ts, dn_done)
                    rv.wait(vs)
                    for qb in range(4):
                        rv.op("tensor_copy",
                              out=rowscrD[:, qb * 512:(qb + 1) * 512],
                              in_=p_d[tq2 * 4 + qb][:], inc=vs)
                        dbank_free[tq2 * 4 + qb] = vs.n
                    mean, rstd = ln_stats(rowscrD[:])
                    rv.wait(ss)
                    rv.op("tensor_scalar", rowscrD[:], rowscrD[:], mean, rstd,
                          ALU.subtract, op1=ALU.mult, inc=vs)
                    rv.wait(vs)
                    rv.op("tensor_tensor", x_res[:, tch * D:(tch + 1) * D],
                          x_res[:, tch * D:(tch + 1) * D], rowscrD[:],
                          ALU.add, inc=vs)
            stE.close()

        if "d_mixed" in dbg_out:
            rg.wait(vs)
            Qg.dma(rg, dbg_out["d_mixed"], mixed[:])

        barrier()
        stD.close()
        stL.close()

        rv_final = vs.n
        ry.wait(vs, rv_final)
        tok_out = Qy.dma(ry, out_own.rearrange("(j p) d -> p j d", p=128),
                         x_res[:].rearrange("p (j d) -> p j d", j=NTC))
        ry.wait_tok(tok_out)

        with nc.Block() as block:
            block.tensor(lambda e: rt.replay(e))
            block.vector(lambda e: rv.replay(e))
            block.scalar(lambda e: rs.replay(e))
            block.gpsimd(lambda e: rg.replay(e))
            block.sync(lambda e: ry.replay(e))

    return nc



def host_prep(inputs):
    x = np.asarray(inputs["L0_attn_x"])
    w_qkv = np.ascontiguousarray(inputs["L0_attn_w_qkv"], dtype=np.float32)
    w_out = np.ascontiguousarray(inputs["L0_attn_w_out"], dtype=np.float32)
    gate_w = np.asarray(inputs["L0_ffn_gate_w"])
    w_up = np.asarray(inputs["L0_ffn_w_up"])
    w_gate = np.asarray(inputs["L0_ffn_w_gate"])
    w_down = np.asarray(inputs["L0_ffn_w_down"])
    up_A = np.asarray(inputs["L0_ffn_up_A"])
    up_B = np.asarray(inputs["L0_ffn_up_B"])
    gate_A = np.asarray(inputs["L0_ffn_gate_A"])
    gate_B = np.asarray(inputs["L0_ffn_gate_B"])
    down_A = np.asarray(inputs["L0_ffn_down_A"])
    down_B = np.asarray(inputs["L0_ffn_down_B"])

    shared = {
        "w_qkv": w_qkv,
        "w_out": w_out,
        "gate_wT": np.ascontiguousarray(gate_w.T),
        "w_upT": np.ascontiguousarray(w_up.T),
        "w_gateT": np.ascontiguousarray(w_gate.T),
        "w_downT": np.ascontiguousarray(w_down.T),
        "upA_T": np.ascontiguousarray(up_A.reshape(ER, D).T),
        "gateA_T": np.ascontiguousarray(gate_A.reshape(ER, D).T),
        "dnA_T": np.ascontiguousarray(down_A.reshape(ER, HFF).T),
        "upB_T": np.ascontiguousarray(up_B.transpose(0, 2, 1).reshape(ER, HFF)),
        "gateB_T": np.ascontiguousarray(gate_B.transpose(0, 2, 1).reshape(ER, HFF)),
        "dnB_T": np.ascontiguousarray(down_B.transpose(0, 2, 1).reshape(ER, D)),
    }
    in_maps = []
    for core in range(8):
        b, r = core // 4, core % 4
        xb = x[b].astype(np.float32, copy=False)
        x_ownc = np.ascontiguousarray(
            xb.reshape(32, 128, D)[r::4].reshape(T, D))
        mval = NEGBIG / RSQ
        m = np.zeros((128, 512), dtype=np.float32)
        for rp in range(4):
            if rp > r:
                m[:, rp * 128:(rp + 1) * 128] = mval
            elif rp == r:
                tri = np.triu(np.full((128, 128), mval, dtype=np.float32), k=1)
                m[:, rp * 128:(rp + 1) * 128] = tri
        mT = np.concatenate([m[:, rp * 128:(rp + 1) * 128].T for rp in range(4)],
                            axis=1)
        z128 = np.zeros((128, 128), dtype=np.float32)
        mP = np.concatenate([mT[:, 0:128], z128, mT[:, 128:256], z128,
                             mT[:, 256:384], z128, mT[:, 384:512], z128], axis=1)
        m = np.ascontiguousarray(np.concatenate([mP, mT], axis=1))
        d = dict(shared)
        d["x_own"] = x_ownc
        d["mask512"] = m
        in_maps.append(d)
    return in_maps


def kernel(**inputs):
    in_maps = host_prep(inputs)
    nc = build(debug=DEBUG)
    res = run_bass_kernel_spmd(nc, in_maps, core_ids=list(range(8)))
    out = np.empty((2, S, D), dtype=np.float32)
    for core in range(8):
        b, r = core // 4, core % 4
        o = res.results[core]["out_own"].reshape(NTC, 128, D)
        out[b].reshape(32, 128, D)[r::4] = o
    return out


# version 08
# speedup vs baseline: 1.5267x; 1.0966x over previous
from contextlib import ExitStack

import numpy as np
import ml_dtypes
import concourse.bass as bass
import concourse.mybir as mybir
from concourse.bass_utils import run_bass_kernel_spmd

F32 = mybir.dt.float32
BF16 = mybir.dt.bfloat16
AF = mybir.ActivationFunctionType
ALU = mybir.AluOpType
AX = mybir.AxisListType

D = 2048
NDC = 16
HEADS = 16
HD = 128
HFF = 5632
NHC = 44
T = 1024
NTC = 8
S = 4096
NE = 8
RK = 16
ER = 128
EPS = 1e-5
RSQ = float(1.0 / np.sqrt(128.0))
NEGBIG = -1e9

DEBUG = ()



class Sem:
    __slots__ = ("name", "n", "h")

    def __init__(self, name):
        self.name = name
        self.n = 0
        self.h = None


class R:

    def __init__(self, name):
        self.name = name
        self.cmds = []
        self.hw = {}

    def wait(self, sem, val=None):
        v = sem.n if val is None else val
        if v <= self.hw.get(sem, -1):
            return
        self.hw[sem] = v
        self.cmds.append(("w", sem, v))

    def wait_tok(self, tok):
        if tok is not None:
            self.wait(tok[0], tok[1])

    def op(self, method, *args, inc=None, inc_k=1, **kw):
        if inc is not None:
            inc.n += inc_k
        self.cmds.append(("o", method, args, kw, inc, inc_k))
        return inc.n if inc is not None else None

    def replay(self, e):
        for c in self.cmds:
            if c[0] == "w":
                e.wait_ge(c[1].h, c[2])
            else:
                _, m, args, kw, inc, k = c
                instr = getattr(e, m)(*args, **kw)
                if inc is not None:
                    instr.then_inc(inc.h, k)


class Q:

    def __init__(self, name, nslots):
        self.slots = [Sem(f"{name}{i}") for i in range(nslots)]
        self.i = 0

    def dma(self, rec, out, in_):
        sm = self.slots[self.i % len(self.slots)]
        self.i += 1
        rec.wait(sm, sm.n)
        rec.op("dma_start", out=out, in_=in_, inc=sm, inc_k=16)
        return (sm, sm.n)



def build(debug=()):
    nc = bass.Bass("TRN2", target_bir_lowering=False)

    def inp(name, shape, dtype=F32):
        return nc.dram_tensor(name, shape, dtype, kind="ExternalInput").ap()

    x_own = inp("x_own", [T, D])
    w_qkv = inp("w_qkv", [D, 3 * D])
    w_out = inp("w_out", [D, D])
    gate_wT = inp("gate_wT", [D, NE])
    w_upT = inp("w_upT", [D, HFF])
    w_gateT = inp("w_gateT", [D, HFF])
    w_downT = inp("w_downT", [HFF, D])
    upA_T = inp("upA_T", [D, ER])
    gateA_T = inp("gateA_T", [D, ER])
    dnA_T = inp("dnA_T", [HFF, ER])
    upB_T = inp("upB_T", [ER, HFF])
    gateB_T = inp("gateB_T", [ER, HFF])
    dnB_T = inp("dnB_T", [ER, D])
    mask512 = inp("mask512", [128, 1536])

    out_own = nc.dram_tensor("out_own", [T, D], F32, kind="ExternalOutput").ap()

    kv_in_k = nc.dram_tensor("kv_in_k", [HEADS * HD, T], BF16, kind="Internal").ap()
    kv_in_v = nc.dram_tensor("kv_in_v", [T, HEADS * HD], BF16, kind="Internal").ap()
    kv_out_k = nc.dram_tensor("kv_out_k", [4 * HEADS * HD, T], BF16, kind="Internal").ap()
    kv_out_v = nc.dram_tensor("kv_out_v", [4 * T, HEADS * HD], BF16, kind="Internal").ap()
    kn_in = nc.dram_tensor("kn_in", [HEADS, T], F32, kind="Internal").ap()
    w_upT_bf = nc.dram_tensor("w_upT_bf", [D, HFF], BF16, kind="Internal").ap()
    w_gateT_bf = nc.dram_tensor("w_gateT_bf", [D, HFF], BF16, kind="Internal").ap()
    w_downT_bf = nc.dram_tensor("w_downT_bf", [HFF, D], BF16, kind="Internal").ap()
    kn_out = nc.dram_tensor("kn_out", [4 * HEADS, T], F32, kind="Internal").ap()
    lg_in = nc.dram_tensor("lg_in", [T, NE], F32, kind="Internal").ap()
    lg_out = nc.dram_tensor("lg_out", [4 * T, NE], F32, kind="Internal").ap()
    GROUPS = [[0, 1, 2, 3], [4, 5, 6, 7]]

    identb_d = nc.inline_tensor(np.eye(128, dtype=ml_dtypes.bfloat16), name="c_identb").ap()
    identf_d = nc.inline_tensor(np.eye(128, dtype=np.float32), name="c_identf").ap()
    onesf_d = nc.inline_tensor(np.ones((1, 128), dtype=np.float32), name="c_onesf").ap()
    onescol_d = nc.inline_tensor(np.ones((128, 1), dtype=ml_dtypes.bfloat16), name="c_onescol").ap()
    negonesf_d = nc.inline_tensor(np.full((1, 128), -1.0, dtype=np.float32), name="c_negonesf").ap()
    blk_np = np.zeros((NE, ER), dtype=ml_dtypes.bfloat16)
    for e_ in range(NE):
        blk_np[e_, e_ * RK:(e_ + 1) * RK] = 2.0
    blk2_d = nc.inline_tensor(blk_np, name="c_blk2").ap()

    dbg_out = {}
    for name, shape, dt in [
        ("d_lnfm", [128, NDC * T], F32),
        ("d_q", [128, HEADS * T], F32),
        ("d_kvk", [HEADS * HD, T], F32),
        ("d_kvv", [T, HEADS * HD], F32),
        ("d_attnout", [128, HEADS * T], F32),
        ("d_x2", [128, NTC * D], F32),
        ("d_ln2fm", [128, NDC * T], F32),
        ("d_logits", [T, NE], F32),
        ("d_gstats", [128, 64], F32),
        ("d_mixed", [128, NHC * 512], F32),
        ("d_ffntm", [128, 4 * D], F32),
    ]:
        if name in debug:
            dbg_out[name] = nc.dram_tensor(name, shape, dt, kind="ExternalOutput").ap()

    ts = Sem("ts")
    vs = Sem("vs")
    ss = Sem("ss")
    cskv = Sem("cskv")
    cslg = Sem("cslg")
    csn = Sem("csn")
    Qg = Q("gq", 8)
    Qy = Q("sq", 8)
    SEMS = [ts, vs, ss, cskv, cslg, csn] + Qg.slots + Qy.slots

    rt, rv, rs, rg, ry = R("tensor"), R("vector"), R("scalar"), R("gpsimd"), R("sync")
    ALLR = [rt, rv, rs, rg, ry]

    def barrier():
        for r in ALLR:
            for sm in SEMS:
                r.wait(sm)

    st = ExitStack()
    with st:
        def sb(name, shape, dtype):
            return st.enter_context(nc.sbuf_tensor(name, shape, dtype))

        def psum(name, shape, dtype=F32):
            return st.enter_context(nc.psum_tensor(name, shape, dtype))

        for sm in SEMS:
            sm.h = st.enter_context(nc.semaphore(sm.name))

        x_res = sb("x_res", [128, NTC * D], F32)
        fm_buf = sb("fm_buf", [128, NDC * T], BF16)
        statscr = sb("statscr", [128, 48], F32)
        lgtm_sb = sb("lgtm_sb", [128, NTC * NE], F32)
        identb = sb("identb", [128, 128], BF16)
        identf = sb("identf", [128, 128], F32)
        onesf = sb("onesf", [1, 128], F32)
        onescol = sb("onescol", [128, 1], BF16)
        negonesf = sb("negonesf", [1, 128], F32)
        blk2 = sb("blk2", [NE, ER], BF16)

        fm_v = fm_buf[:].rearrange("p (c t) -> p c t", c=NDC)
        stQ = ExitStack()
        q_all = stQ.enter_context(nc.sbuf_tensor("q_all", [128, HEADS * T], BF16))
        qn_sb = stQ.enter_context(nc.sbuf_tensor("qn_sb", [128, 128], F32))
        mask_sb = stQ.enter_context(nc.sbuf_tensor("mask_sb", [128, 1536], F32))

        tok_x = Qy.dma(ry, x_res[:].rearrange("p (j d) -> p j d", j=NTC),
                       x_own.rearrange("(j p) d -> p j d", p=128))
        tok_c = {}
        for nm, dst, src in [("identb", identb[:], identb_d),
                             ("identf", identf[:], identf_d),
                             ("onesf", onesf[:], onesf_d),
                             ("onescol", onescol[:], onescol_d),
                             ("negonesf", negonesf[:], negonesf_d),
                             ("blk2", blk2[:], blk2_d),
                             ("mask", mask_sb[:], mask512)]:
            tok_c[nm] = Qy.dma(ry, dst, src)

        def ln_stats(xin):
            bnbuf = statscr[:, 16:40]
            mv = statscr[:, 40:42]
            mean, rstd = statscr[:, 40:41], statscr[:, 44:45]
            rv.wait(vs)
            rv.wait(ss)
            for c_ in range(4):
                rv.op("bn_stats", out=bnbuf[:, c_ * 6:(c_ + 1) * 6],
                      in_=xin[:, c_ * 512:(c_ + 1) * 512], inc=vs)
            rv.wait(vs)
            rv.op("bn_aggr", out=mv, in_=bnbuf, inc=vs)
            rv.wait(vs)
            rv.op("tensor_scalar", statscr[:, 42:43], statscr[:, 41:42], EPS, None,
                  ALU.add, inc=vs)
            rs.wait(vs)
            rs.op("activation", statscr[:, 43:44], statscr[:, 42:43], AF.Ln, inc=ss)
            rs.wait(ss)
            rs.op("activation", rstd, statscr[:, 43:44], AF.Exp, scale=-0.5, inc=ss)
            return mean, rstd

        stA = ExitStack()
        sbA = lambda n_, sh, dt: stA.enter_context(nc.sbuf_tensor(n_, sh, dt))
        psA = lambda n_, sh, dt=F32: stA.enter_context(nc.psum_tensor(n_, sh, dt))
        lntm = sbA("lntm", [128, 2 * D], BF16)
        wqkv_sb = sbA("wqkv_sb", [128, 3 * NDC * 512], BF16)
        kvstage = sbA("kvstage", [128, 4 * 512], BF16)
        sqst = sbA("sqst", [128, 2 * 512], BF16)
        kn_row = sbA("kn_row", [1, 512], F32)
        qn_row = sbA("qn_row", [1, 512], F32)
        p_lt = [psA(f"p_lt{i}", [128, 1024], BF16) for i in range(2)]
        p_mm = [psA(f"p_mm{i}", [128, 512], F32) for i in range(4)]
        p_nr = psA("p_nr", [1, 512])
        p_qt = psA("p_qt", [128, 4])
        nstate = {"sq": [0, 0], "nr": None, "qt": 0, "knrow": None, "qnrow": 0}

        def norms_from(src_bf, h_, half_, is_q):
            p2 = (h_ * 2 + half_) % 2
            rs.wait(ts, nstate["sq"][p2])
            rs.wait(ss)
            rs.op("activation", sqst[:, p2 * 512:(p2 + 1) * 512], src_bf,
                  AF.Square, inc=ss)
            sq_ss = ss.n
            rt.wait(ss, sq_ss)
            rt.wait_tok(nstate["nr"])
            rt.wait_tok(tok_c["onescol"])
            rt.op("matmul", p_nr[0:1, :], onescol[:],
                  sqst[:, p2 * 512:(p2 + 1) * 512], start=True, stop=True, inc=ts)
            nstate["sq"][p2] = ts.n
            nr_ts = ts.n
            if is_q:
                rv.wait(ts, nr_ts)
                rv.wait(ts, nstate["qnrow"])
                rv.op("tensor_copy", out=qn_row[0:1, :], in_=p_nr[0:1, :], inc=vs)
                nstate["nr"] = (vs, vs.n)
                rt.wait(vs, vs.n)
                rt.wait(vs, nstate["qt"])
                for i4 in range(4):
                    rt.op("transpose", p_qt[:, i4:i4 + 1],
                          qn_row[0:1, i4 * 128:(i4 + 1) * 128],
                          identf[0:1, 0:1], inc=(ts if i4 == 3 else None))
                nstate["qnrow"] = ts.n
                rv.wait(ts, ts.n)
                rv.op("tensor_copy",
                      out=qn_sb[:, h_ * 8 + half_ * 4:h_ * 8 + half_ * 4 + 4],
                      in_=p_qt[:], inc=vs)
                nstate["qt"] = vs.n
            else:
                rv.wait(ts, nr_ts)
                rv.wait_tok(nstate["knrow"])
                rv.op("tensor_copy", out=kn_row[0:1, :], in_=p_nr[0:1, :], inc=vs)
                nstate["nr"] = (vs, vs.n)
                ry.wait(vs)
                nstate["knrow"] = Qy.dma(
                    ry, kn_in[h_:h_ + 1, half_ * 512:(half_ + 1) * 512],
                    kn_row[0:1, :])


        lt_done_ts = [0, 0]
        for j in range(NTC):
            par = j % 2
            xin = x_res[:, j * D:(j + 1) * D]
            rv.wait_tok(tok_x)
            rs.wait_tok(tok_x)
            mean, rstd = ln_stats(xin)
            rv.wait(ss)
            rv.wait(ts, lt_done_ts[par])
            rv.op("tensor_scalar", lntm[:, par * D:(par + 1) * D], xin,
                  mean, rstd, ALU.subtract, op1=ALU.mult, inc=vs)
            rt.wait(vs)
            rt.wait_tok(tok_c["identb"])
            for k in range(NDC):
                bank = p_lt[k // 8]
                rt.op("transpose", bank[:, (k % 8) * 128:(k % 8 + 1) * 128],
                      lntm[:, par * D + k * 128: par * D + (k + 1) * 128],
                      identb[:], inc=(ts if k % 8 == 7 else None))
            lt_done_ts[par] = ts.n
            rv.wait(ts)
            for half in range(2):
                rv.op("tensor_copy",
                      out=fm_v[:, half * 8:(half + 1) * 8, j * 128:(j + 1) * 128],
                      in_=p_lt[half][:].rearrange("p (c t) -> p c t", c=8),
                      inc=vs)
            if j == 3:
                lnfm_half0_vs = vs.n
        lnfm_done_vs = vs.n

        if "d_lnfm" in dbg_out:
            rg.wait(vs)
            Qg.dma(rg, dbg_out["d_lnfm"], fm_buf[:])

        wq_v = wqkv_sb[:].rearrange("p (b c n) -> p b c n", b=3, c=NDC)
        grp_consumed_ts = [0, 0, 0]
        bank_free = [0, 0, 0, 0]
        stage_free = [None, None, None, None]

        def mm_to_bank(bi, lhsT_rhs_list):
            rt.wait(ss, bank_free[bi])
            n = len(lhsT_rhs_list)
            for i, (lt_, rh_) in enumerate(lhsT_rhs_list):
                rt.op("matmul", p_mm[bi][:], lt_, rh_,
                      start=(i == 0), stop=(i == n - 1),
                      inc=(ts if i == n - 1 else None))
            return ts.n

        ag_launched = {}
        for gi, g in enumerate([4, 5, 6, 7, 8, 9, 10, 11, 0, 1, 2, 3]):
            par = gi % 3
            rg.wait(ts, grp_consumed_ts[par])
            tok_g = Qg.dma(rg, wq_v[:, par],
                           w_qkv[:, g * 512:(g + 1) * 512].rearrange("(c p) n -> p c n", p=128))
            rt.wait_tok(tok_g)
            rt.wait(vs, lnfm_half0_vs)
            if g < 8:
                for c in range(4):
                    for half in range(2):
                        if half == 1:
                            rt.wait(vs, lnfm_done_vs)
                        bi = 2 * (c % 2) + half
                        mm_done = mm_to_bank(bi, [
                            (wq_v[:, par, k, c * 128:(c + 1) * 128],
                             fm_v[:, k, half * 512:(half + 1) * 512])
                            for k in range(NDC)])
                        col = g * 512 + c * 128
                        rs.wait(ts, mm_done)
                        if g < 4:
                            h_ = col // 128
                            qa = q_all[:, h_ * T + half * 512:h_ * T + half * 512 + 512]
                            rs.op("copy", out=qa, in_=p_mm[bi][:], inc=ss)
                            norms_from(qa, h_, half, True)
                        else:
                            si = bi
                            stg = kvstage[:, si * 512:(si + 1) * 512]
                            rs.wait_tok(stage_free[si])
                            rs.op("copy", out=stg, in_=p_mm[bi][:], inc=ss)
                            h_ = (col - 2048) // 128
                            ry.wait(ss)
                            stage_free[si] = Qy.dma(
                                ry,
                                kv_in_k[h_ * 128:(h_ + 1) * 128,
                                        half * 512:(half + 1) * 512],
                                stg)
                            norms_from(stg, h_, half, False)
                        bank_free[bi] = ss.n
            else:
                for tc in range(NTC):
                    if tc >= 4:
                        rt.wait(vs, lnfm_done_vs)
                    bi = 2 * (tc % 2)
                    mm_done = mm_to_bank(bi, [
                        (fm_v[:, k, tc * 128:(tc + 1) * 128], wq_v[:, par, k, :])
                        for k in range(NDC)])
                    si = tc % 4
                    stg = kvstage[:, si * 512:(si + 1) * 512]
                    rs.wait(ts, mm_done)
                    rs.wait_tok(stage_free[si])
                    rs.op("copy", out=stg, in_=p_mm[bi][:], inc=ss)
                    bank_free[bi] = ss.n
                    vcol = (g - 8) * 512
                    ry.wait(ss)
                    stage_free[si] = Qy.dma(
                        ry, kv_in_v[tc * 128:(tc + 1) * 128, vcol:vcol + 512], stg)
            grp_consumed_ts[par] = ts.n
            if g == 7:
                for sm in Qy.slots:
                    rg.wait(sm)
                rg.op("collective_compute", "AllGather", ALU.bypass,
                      replica_groups=GROUPS, ins=[kv_in_k], outs=[kv_out_k],
                      inc=cskv)
                ag_launched["k"] = (cskv, cskv.n)
                rg.op("collective_compute", "AllGather", ALU.bypass,
                      replica_groups=GROUPS, ins=[kn_in], outs=[kn_out],
                      inc=csn)
                ag_launched["kn"] = (csn, csn.n)
            elif g == 11:
                for sm in Qy.slots:
                    rg.wait(sm)
                rg.op("collective_compute", "AllGather", ALU.bypass,
                      replica_groups=GROUPS, ins=[kv_in_v], outs=[kv_out_v],
                      inc=cskv)
                ag_launched["v"] = (cskv, cskv.n)

        wconv_parts = []
        for src, dst, rows in [(w_upT, w_upT_bf, D), (w_gateT, w_gateT_bf, D),
                               (w_downT, w_downT_bf, HFF)]:
            nch = 8 if rows == D else 16
            q_ = rows // nch
            for ci in range(nch):
                wconv_parts.append((dst[ci * q_:(ci + 1) * q_, :],
                                    src[ci * q_:(ci + 1) * q_, :]))
        wconv_toks = []
        rg.wait(cskv)
        rg.wait(csn)

        if "d_q" in dbg_out:
            rg.wait(ss)
            Qg.dma(rg, dbg_out["d_q"], q_all[:])

        kv_gathered = (cskv, cskv.n)

        if "d_kvk" in dbg_out:
            rg.wait_tok(kv_gathered)
            Qg.dma(rg, dbg_out["d_kvk"], kv_in_k)
        if "d_kvv" in dbg_out:
            rg.wait_tok(kv_gathered)
            Qg.dma(rg, dbg_out["d_kvv"], kv_in_v)

        barrier()
        stA.close()

        stB = ExitStack()
        sbB = lambda n_, sh, dt: stB.enter_context(nc.sbuf_tensor(n_, sh, dt))
        psB = lambda n_, sh, dt=F32: stB.enter_context(nc.psum_tensor(n_, sh, dt))
        kh_sb = sbB("kh_sb", [128, 2 * 8 * 512], BF16)
        vh_sb = sbB("vh_sb", [128, 2 * 8 * 512], BF16)
        pb = sbB("pb", [128, 8192], BF16)
        pt_sb = sbB("pt_sb", [128, 2 * 1024], BF16)
        dscr = sbB("dscr", [128, 512], F32)
        kn_all2 = sbB("kn_all2", [64, 1024], F32)
        kn_all = sbB("kn_all", [128, 132], F32)
        kncol = sbB("kncol", [64, 1], F32)
        kmrow = sbB("kmrow", [1, 80], F32)
        qmrow = sbB("qmrow", [1, 128], F32)
        negbs_all = sbB("negbs_all", [128, 128], F32)
        negbp = sbB("negbp", [128, 64], F32)
        zrow_sb = sbB("zrow_sb", [1, 512], F32)
        rzb_sb = sbB("rzb_sb", [128, 256], F32)
        p_s = [psB(f"p_s{i}", [128, 512]) for i in range(3)]
        p_o = [psB(f"p_o{i}", [128, 256]) for i in range(2)]
        p_misc = psB("p_misc", [128, 256])
        p_z = psB("p_z", [1, 256])

        kh_v = kh_sb[:].rearrange("p (b j f t) -> p b j f t", b=2, j=8, f=4)
        vh_v = vh_sb[:].rearrange("p (b j f t) -> p b j f t", b=2, j=8, f=4)


        khb_consumed = [0, 0]
        kv_tok = [None, None]
        sbank_free = [None, None, None]
        sbank_i = 0
        o_free = [0, 0]
        pb_free = 0
        zrow_free = 0

        ry.wait_tok(ag_launched["kn"])
        tok_kn = Qy.dma(ry, kn_all2[0:64, :], kn_out)
        rv.wait_tok(tok_kn)
        rv.op("tensor_reduce", kncol[0:64, :], kn_all2[0:64, :], AX.X, ALU.max,
              inc=vs)
        rt.wait(vs)
        rt.op("transpose", p_misc[0:1, 0:64], kncol[0:64, :],
              identf[0:64, 0:64], inc=ts)
        rv.wait(ts)
        rv.op("tensor_copy", out=kmrow[0:1, 0:64], in_=p_misc[0:1, 0:64], inc=vs)
        rv.wait(vs)
        rv.op("tensor_tensor", kmrow[0:1, 64:80], kmrow[0:1, 0:16],
              kmrow[0:1, 16:32], ALU.max, inc=vs)
        rv.wait(vs)
        rv.op("tensor_tensor", kmrow[0:1, 64:80], kmrow[0:1, 64:80],
              kmrow[0:1, 32:48], ALU.max, inc=vs)
        rv.wait(vs)
        rv.op("tensor_tensor", kmrow[0:1, 64:80], kmrow[0:1, 64:80],
              kmrow[0:1, 48:64], ALU.max, inc=vs)
        rt.wait(vs)
        rt.op("transpose", p_misc[:, 128:256], qn_sb[:], identf[:], inc=ts)
        rv.wait(ts)
        rv.op("tensor_copy", out=kn_all[0:128, 0:128], in_=p_misc[:, 128:256],
              inc=vs)
        rv.wait(vs)
        rv.op("tensor_reduce", kncol[0:64, 0:1] if False else kn_all[0:128, 128:129],
              kn_all[0:128, 0:128], AX.X, ALU.max, inc=vs)
        rt.wait(vs)
        rt.op("transpose", p_misc[0:1, 0:128], kn_all[0:128, 128:129],
              identf[:], inc=ts)
        rv.wait(ts)
        rv.op("tensor_copy", out=kmrow[0:1, 0:64] if False else qmrow[0:1, 0:128],
              in_=p_misc[0:1, 0:128], inc=vs)
        for j8 in range(8):
            rv.wait(vs)
            rv.op("tensor_tensor", qmrow[0:1, j8:128:8], qmrow[0:1, j8:128:8],
                  kmrow[0:1, 64:80], ALU.add, inc=vs)
        rv.wait(vs)
        rv.op("tensor_scalar", qmrow[0:1, 0:128], qmrow[0:1, 0:128],
              -RSQ / 2.0, None, ALU.mult, inc=vs)
        rt.wait(vs)
        rt.op("matmul", p_misc[:, 128:256], onesf[0:1, :], qmrow[0:1, 0:128],
              start=True, stop=True, inc=ts)
        rv.wait(ts)
        rv.op("tensor_copy", out=negbs_all[:], in_=p_misc[:, 128:256], inc=vs)
        rv.wait(vs)
        rv.op("tensor_tensor", negbp[:].unsqueeze(2),
              negbs_all[:].rearrange("p (c o) -> p c o", o=2)[:, :, 0:1],
              negbs_all[:].rearrange("p (c o) -> p c o", o=2)[:, :, 1:2],
              ALU.min, inc=vs)
        negbs_vs = vs.n

        for h in range(HEADS):
            hb = h % 2
            ry.wait_tok(ag_launched["k"])
            ry.wait_tok(ag_launched["v"])
            ry.wait(ts, khb_consumed[hb])
            toks = []
            for r_ in range(4):
                toks.append(Qy.dma(
                    ry, kh_v[:, hb, :, r_, :],
                    kv_out_k[r_ * 2048 + h * 128:r_ * 2048 + (h + 1) * 128, :]
                    .rearrange("p (j t) -> p j t", j=8)))
                toks.append(Qy.dma(
                    ry, vh_v[:, hb, :, r_, :],
                    kv_out_v[r_ * T:(r_ + 1) * T, h * 128:(h + 1) * 128]
                    .rearrange("(j p) t -> p j t", p=128)))
            kv_tok[hb] = toks
            for _ in range(2):
                if wconv_parts:
                    rg.wait(ts, khb_consumed[hb])
                    dst_, src_ = wconv_parts.pop(0)
                    wconv_toks.append(Qg.dma(rg, dst_, src_))

            for pj in (0, 2, 4, 6):
                opar = (pj // 2) % 2
                lvx = pj + 1
                bias_ap = negbp[:, h * 4 + pj // 2:h * 4 + pj // 2 + 1]
                rt.wait(vs, o_free[opar])
                rv.wait(ts, pb_free)
                rv.op("memset",
                      pb[:, lvx * 1024:(lvx + 1) * 1024]
                      .rearrange("p (f q) -> p f q", f=4)[:, :, 0:128], 0.0,
                      inc=vs)
                pb_pad_vs = vs.n
                for lv in range(pj + 1):
                    for bk in range(2):
                        bi = sbank_i % 3
                        sbank_i += 1
                        rt.wait_tok(sbank_free[bi])
                        for tk in kv_tok[hb]:
                            rt.wait_tok(tk)
                        for rr in range(2):
                            r_ = bk * 2 + rr
                            rt.op("matmul", p_s[bi][:, rr * 256:(rr + 1) * 256],
                                  kh_sb[:, hb * 4096 + lv * 512 + r_ * 128:
                                        hb * 4096 + lv * 512 + (r_ + 1) * 128],
                                  q_all[:, h * T + pj * 128:h * T + (pj + 2) * 128],
                                  start=True, stop=True,
                                  inc=(ts if rr == 1 else None))
                        mm_done = ts.n
                        dst = pb[:, (lv * 4 + bk * 2) * 256:(lv * 4 + bk * 2 + 2) * 256]
                        if lv == pj:
                            rv.wait(ts, mm_done)
                            rv.wait_tok(tok_c["mask"])
                            rv.wait(ss)
                            rv.op("tensor_tensor", dscr[:], p_s[bi][:],
                                  mask_sb[:, bk * 512:(bk + 1) * 512], ALU.add,
                                  inc=vs)
                            sbank_free[bi] = (vs, vs.n)
                            rs.wait(vs)
                            rs.wait(vs, pb_pad_vs)
                            rs.op("activation", dst, dscr[:], AF.Exp,
                                  bias=bias_ap, scale=RSQ, inc=ss)
                        else:
                            rs.wait(ts, mm_done)
                            rs.wait(vs, negbs_vs)
                            rs.wait(vs, pb_pad_vs)
                            rs.op("activation", dst, p_s[bi][:], AF.Exp,
                                  bias=bias_ap, scale=RSQ, inc=ss)
                            sbank_free[bi] = (ss, ss.n)
                bi = sbank_i % 3
                sbank_i += 1
                rt.wait_tok(sbank_free[bi])
                for r_ in range(4):
                    rt.op("matmul", p_s[bi][:, r_ * 128:(r_ + 1) * 128],
                          kh_sb[:, hb * 4096 + lvx * 512 + r_ * 128:
                                hb * 4096 + lvx * 512 + (r_ + 1) * 128],
                          q_all[:, h * T + (pj + 1) * 128:h * T + (pj + 2) * 128],
                          start=True, stop=True, inc=(ts if r_ == 3 else None))
                mm_done = ts.n
                rv.wait(ts, mm_done)
                rv.wait_tok(tok_c["mask"])
                rv.wait(ss)
                rv.op("tensor_tensor", dscr[:], p_s[bi][:],
                      mask_sb[:, 1024:1536], ALU.add, inc=vs)
                sbank_free[bi] = (vs, vs.n)
                rs.wait(vs)
                rs.op("activation",
                      pb[:, lvx * 1024:(lvx + 1) * 1024]
                      .rearrange("p (f q) -> p f q", f=4)[:, :, 128:256],
                      dscr[:].rearrange("p (f q) -> p f q", f=4),
                      AF.Exp, bias=bias_ap, scale=RSQ, inc=ss)
                exp_done = ss.n
                nm = (pj + 1) * 4
                rt.wait(ss, exp_done)
                rt.wait_tok(tok_c["onescol"])
                rt.wait(vs, zrow_free)
                for m_ in range(nm):
                    rt.op("matmul", p_z[0:1, 0:256], onescol[:],
                          pb[:, m_ * 256:(m_ + 1) * 256],
                          start=(m_ == 0), stop=False)
                for r_ in range(4):
                    rt.op("matmul", p_z[0:1, 128:256], onescol[:],
                          pb[:, lvx * 1024 + r_ * 256 + 128:lvx * 1024 + (r_ + 1) * 256],
                          start=False, stop=(r_ == 3),
                          inc=(ts if r_ == 3 else None))
                rv.wait(ts)
                rv.op("tensor_copy", out=zrow_sb[0:1, 0:256], in_=p_z[0:1, 0:256],
                      inc=vs)
                rv.wait(vs)
                rv.op("reciprocal", zrow_sb[0:1, 256:512], zrow_sb[0:1, 0:256],
                      inc=vs)
                rz_vs = vs.n
                rt.wait(vs, rz_vs)
                rt.op("matmul", p_misc[:, 0:256], onesf[0:1, :],
                      zrow_sb[0:1, 256:512], start=True, stop=True, inc=ts)
                rv.wait(ts)
                rv.op("tensor_copy", out=rzb_sb[:], in_=p_misc[:, 0:256], inc=vs)
                zrow_free = vs.n
                for m_ in range(nm):
                    rt.op("matmul", p_o[opar][:], 
                          vh_sb[:, hb * 4096 + m_ * 128:hb * 4096 + (m_ + 1) * 128],
                          pb[:, m_ * 256:(m_ + 1) * 256],
                          start=(m_ == 0), stop=False)
                for r_ in range(4):
                    rt.op("matmul", p_o[opar][:, 128:256],
                          vh_sb[:, hb * 4096 + (lvx * 4 + r_) * 128:
                                hb * 4096 + (lvx * 4 + r_ + 1) * 128],
                          pb[:, lvx * 1024 + r_ * 256 + 128:lvx * 1024 + (r_ + 1) * 256],
                          start=False, stop=(r_ == 3),
                          inc=(ts if r_ == 3 else None))
                pb_free = ts.n
                av_done = ts.n
                rv.wait(ts, av_done)
                rv.wait(vs)
                rv.op("tensor_tensor",
                      fm_buf[:, h * T + pj * 128:h * T + (pj + 2) * 128],
                      p_o[opar][:], rzb_sb[:], ALU.mult, inc=vs)
                o_free[opar] = vs.n
            khb_consumed[hb] = ts.n

        if "d_attnout" in dbg_out:
            rg.wait(vs)
            Qg.dma(rg, dbg_out["d_attnout"], fm_buf[:])

        barrier()
        stB.close()
        stQ.close()
        stL = ExitStack()
        ln2_buf = stL.enter_context(nc.sbuf_tensor("ln2_buf", [128, NDC * T], BF16))
        ln2_v = ln2_buf[:].rearrange("p (c t) -> p c t", c=NDC)

        stC = ExitStack()
        sbC = lambda n_, sh, dt: stC.enter_context(nc.sbuf_tensor(n_, sh, dt))
        psC = lambda n_, sh, dt=F32: stC.enter_context(nc.psum_tensor(n_, sh, dt))
        wout_sb = sbC("wout_sb", [128, NDC * D], BF16)
        rowscr2 = sbC("rowscr2", [128, D], F32)
        ln2f32 = sbC("ln2f32", [128, 512], F32)
        lgj_sb = sbC("lgj_sb", [NE, 128], F32)
        gwT_sb = sbC("gwT_sb", [128, NDC * NE], F32)
        p_o2 = [psC(f"p_o2{i}", [128, 512]) for i in range(4)]
        p_t = [psC(f"p_t{i}", [128, 512]) for i in range(2)]
        p_g = psC("p_g", [NE, 128])
        p_x8 = psC("p_x8", [128, NE])

        wout_v = wout_sb[:].rearrange("p (c n) -> p c n", c=NDC)
        gwT_v = gwT_sb[:].rearrange("p (c e) -> p c e", c=NDC)

        tok_wout = Qg.dma(rg, wout_v, w_out.rearrange("(c p) n -> p c n", p=128))
        tok_gwt = Qy.dma(ry, gwT_v, gate_wT.rearrange("(c p) e -> p c e", p=128))

        attnout_vs = vs.n
        o2bank_free = [0, 0, 0, 0]
        pg_free = 0
        ln2f_free = [0, 0]
        for j in range(NTC):
            for q in range(4):
                rt.wait(vs, o2bank_free[q])
                rt.wait_tok(tok_wout)
                for c in range(NDC):
                    rt.op("matmul", p_o2[q][:],
                          fm_buf[:, c * T + j * 128:c * T + (j + 1) * 128],
                          wout_v[:, c, q * 512:(q + 1) * 512],
                          start=(c == 0), stop=(c == NDC - 1),
                          inc=(ts if c == NDC - 1 else None))
            mm_done = ts.n
            rv.wait(ts, mm_done)
            rv.wait(ts)
            for q in range(4):
                rv.op("tensor_copy", out=rowscr2[:, q * 512:(q + 1) * 512],
                      in_=p_o2[q][:], inc=vs)
                o2bank_free[q] = vs.n
            row = rowscr2[:]
            mean, rstd = ln_stats(row)
            rv.wait(ss)
            rv.op("tensor_scalar", row, row, mean, rstd,
                  ALU.subtract, op1=ALU.mult, inc=vs)
            rv.wait(vs)
            rv.op("tensor_tensor", x_res[:, j * D:(j + 1) * D],
                  x_res[:, j * D:(j + 1) * D], row, ALU.add, inc=vs)
            mean, rstd = ln_stats(x_res[:, j * D:(j + 1) * D])
            rv.wait(ss)
            rv.wait(vs)
            rv.op("tensor_scalar", row, x_res[:, j * D:(j + 1) * D], mean, rstd,
                  ALU.subtract, op1=ALU.mult, inc=vs)
            row_vs = vs.n
            for k4 in range(4):
                tpar = k4 % 2
                rt.wait(vs, row_vs)
                rt.wait(ss, ln2f_free[0])
                for i in range(4):
                    rt.op("transpose", p_t[tpar][:, i * 128:(i + 1) * 128],
                          rowscr2[:, (4 * k4 + i) * 128:(4 * k4 + i + 1) * 128],
                          identf[:], inc=(ts if i == 3 else None))
                tr_done = ts.n
                rv.wait(ts, tr_done)
                rv.wait(ss, ln2f_free[0])
                rv.op("tensor_copy", out=ln2f32[:],
                      in_=p_t[tpar][:], inc=vs)
                l32_vs = vs.n
                rs.wait(vs, l32_vs)
                rs.op("copy",
                      out=ln2_v[:, 4 * k4:4 * k4 + 4, j * 128:(j + 1) * 128],
                      in_=ln2f32[:].rearrange("p (c t) -> p c t", c=4), inc=ss)
                ln2f_free[0] = ss.n
                rt.wait(vs, l32_vs)
                rt.wait_tok(tok_gwt)
                if k4 == 0:
                    rt.wait(vs, pg_free)
                for i in range(4):
                    k = 4 * k4 + i
                    rt.op("matmul", p_g[:],
                          gwT_v[:, k, :],
                          ln2f32[:, i * 128:(i + 1) * 128],
                          start=(k == 0), stop=(k == NDC - 1),
                          inc=(ts if k == NDC - 1 else None))
            gmm_done = ts.n
            rv.wait(ts, gmm_done)
            rv.wait(ts)
            rv.op("tensor_copy", out=lgj_sb[:], in_=p_g[:], inc=vs)
            pg_free = vs.n
            rt.wait(vs, pg_free)
            rt.op("transpose", p_x8[:], lgj_sb[:], identf[0:NE, 0:NE], inc=ts)
            rv.wait(ts)
            rv.op("tensor_copy", out=lgtm_sb[:, j * NE:(j + 1) * NE],
                  in_=p_x8[:], inc=vs)
        x2_done_vs = vs.n
        ln2_done_ss = ss.n

        if "d_x2" in dbg_out:
            rg.wait(vs)
            Qg.dma(rg, dbg_out["d_x2"], x_res[:])
        if "d_ln2fm" in dbg_out:
            rg.wait(ss)
            Qg.dma(rg, dbg_out["d_ln2fm"], ln2_buf[:])

        rv_lgtm = vs.n
        ry.wait(vs, rv_lgtm)
        tok_lgin = Qy.dma(ry, lg_in.rearrange("(c p) e -> p c e", p=128),
                          lgtm_sb[:].rearrange("p (c e) -> p c e", c=NTC))
        rg.wait_tok(tok_lgin)
        rg.op("collective_compute", "AllGather", ALU.bypass,
              replica_groups=GROUPS, ins=[lg_in], outs=[lg_out], inc=cslg)
        lg_gathered = (cslg, cslg.n)

        if "d_logits" in dbg_out:
            rg.wait_tok(lg_gathered)
            Qg.dma(rg, dbg_out["d_logits"], lg_in)

        barrier()
        stC.close()

        stD = ExitStack()
        sbD = lambda n_, sh, dt: stD.enter_context(nc.sbuf_tensor(n_, sh, dt))
        mixed = sbD("mixed", [128, NHC * 512], BF16)
        wkb = sbD("wkb", [128, 2 * 512], F32)
        mt4 = sbD("mt4", [128, 4 * 512], BF16)
        rtld = sbD("rtld", [128, 4 * 512], BF16)
        upstr = fm_buf[:, 0:8704]
        wdnstr = fm_buf[:, 8704:12800]
        dnA_sb3 = fm_buf[:, 12800:13184]
        chb = fm_buf[:, 13184:14208]
        dnB_sb = fm_buf[:, 14208:16256]
        up_v = upstr.rearrange("p (b n) -> p b n", b=2)

        tok_dnB = Qg.dma(rg, dnB_sb, dnB_T)

        stG = ExitStack()
        sbG = lambda n_, sh, dt: stG.enter_context(nc.sbuf_tensor(n_, sh, dt))
        psG = lambda n_, sh, dt=F32: stG.enter_context(nc.psum_tensor(n_, sh, dt))
        lgt = sbG("lgt", [128, 32 * NE], F32)
        tscr = sbG("tscr", [128, 16 * 32], F32)
        e_scr = sbG("e_scr", [128, 48], F32)
        ind_tm = sbG("ind_tm", [128, 2 * NTC * NE], BF16)
        indfm = sbG("indfm", [NE, 2 * T], BF16)
        wrow = sbG("wrow", [1, 160], F32)
        p_col = psG("p_col", [128, 16])
        p_row = psG("p_row", [1, 128])
        p_i8 = stG.enter_context(nc.psum_tensor("p_i8", [NE, 128], BF16))
        p_m512 = psG("p_m512", [128, 512])

        ry.wait_tok(lg_gathered)
        tok_lgt = Qy.dma(ry, lgt[:].rearrange("p (c e) -> p c e", e=NE),
                         lg_out.rearrange("(c p) e -> p c e", p=128))

        def tournament(view, n, m1, m2):
            cols = [view[:, :, i_:i_ + 1] for i_ in range(8)]
            scr = lambda i_: tscr[:, i_ * 32:i_ * 32 + n].unsqueeze(2)
            def tt(dst, a_, b_, op):
                rv.wait(vs)
                rv.op("tensor_tensor", dst, a_, b_, op, inc=vs)
            a = [scr(i_) for i_ in range(4)]
            iw = [scr(4 + i_) for i_ in range(4)]
            for i_ in range(4):
                tt(a[i_], cols[2 * i_], cols[2 * i_ + 1], ALU.max)
                tt(iw[i_], cols[2 * i_], cols[2 * i_ + 1], ALU.min)
            b1, j1, b2, j2, k1 = scr(8), scr(9), scr(10), scr(11), scr(12)
            tt(b1, a[0], a[1], ALU.max)
            tt(j1, a[0], a[1], ALU.min)
            tt(b2, a[2], a[3], ALU.max)
            tt(j2, a[2], a[3], ALU.min)
            tt(m1.unsqueeze(2), b1, b2, ALU.max)
            tt(k1, b1, b2, ALU.min)
            t1, t2, t3 = a[0], a[1], a[2]
            tt(t1, j1, j2, ALU.max)
            tt(t2, iw[0], iw[1], ALU.max)
            tt(t3, iw[2], iw[3], ALU.max)
            tt(j1, t1, k1, ALU.max)
            tt(j2, t2, t3, ALU.max)
            tt(m2.unsqueeze(2), j1, j2, ALU.max)

        m1g = tscr[:, 13 * 32:14 * 32]
        m2g = tscr[:, 14 * 32:15 * 32]
        rv.wait_tok(tok_lgt)
        tournament(lgt[:].rearrange("p (c e) -> p c e", e=NE), 32, m1g, m2g)
        m1o = e_scr[:, 16:24]
        m2o = e_scr[:, 24:32]
        tournament(lgtm_sb[:].rearrange("p (c e) -> p c e", e=NE), NE, m1o, m2o)

        for k in range(2):
            mg = m1g if k == 0 else m2g
            mo = m1o if k == 0 else m2o
            gred = statscr[:, 16:17]
            rv.wait(vs)
            rv.wait(ss)
            rv.op("tensor_reduce", gred, mg, AX.X, ALU.max, inc=vs)
            rt.wait(vs)
            rt.op("transpose", p_row[0:1, :], gred, identf[:], inc=ts)
            rv.wait(ts)
            rv.op("tensor_copy", out=wrow[0:1, 0:128], in_=p_row[0:1, :], inc=vs)
            rv.wait(vs)
            rv.op("tensor_reduce", wrow[0:1, 128:129], wrow[0:1, 0:128], AX.X,
                  ALU.max, inc=vs)
            gmax_vs = vs.n
            rt.wait(vs, gmax_vs)
            rt.op("matmul", p_col[:, 0:1], negonesf[0:1, :], wrow[0:1, 128:129],
                  start=True, stop=True, inc=ts)
            ngb = statscr[:, 17:18]
            rv.wait(ts)
            rv.op("tensor_copy", out=ngb, in_=p_col[:, 0:1], inc=vs)
            zc = statscr[:, 18:19]
            rs.wait(vs)
            rs.op("activation", tscr[:, 15 * 32:16 * 32], mg, AF.Exp,
                  bias=ngb, accum_out=zc, inc=ss)
            rt.wait(ss)
            rt.op("transpose", p_row[0:1, :], zc, identf[:], inc=ts)
            rv.wait(ts)
            rv.op("tensor_copy", out=wrow[0:1, 0:128], in_=p_row[0:1, :], inc=vs)
            rv.wait(vs)
            rv.op("tensor_reduce", wrow[0:1, 129:130], wrow[0:1, 0:128], AX.X,
                  ALU.add, inc=vs)
            rv.wait(vs)
            rv.op("reciprocal", wrow[0:1, 130:131], wrow[0:1, 129:130], inc=vs)
            rt.wait(vs)
            rt.op("matmul", p_col[:, 1:2], onesf[0:1, :], wrow[0:1, 130:131],
                  start=True, stop=True, inc=ts)
            rzb = statscr[:, 19:20]
            rv.wait(ts)
            rv.op("tensor_copy", out=rzb, in_=p_col[:, 1:2], inc=vs)
            eo = e_scr[:, 32:40]
            rs.wait(vs)
            rs.op("activation", eo, mo, AF.Exp, bias=ngb, inc=ss)
            rv.wait(ss)
            rv.op("tensor_scalar", statscr[:, k * 8:(k + 1) * 8], eo, rzb, None,
                  ALU.mult, inc=vs)
            for j in range(NTC):
                rv.wait(vs)
                rv.op("tensor_scalar",
                      ind_tm[:, (k * NTC + j) * NE:(k * NTC + j + 1) * NE],
                      lgtm_sb[:, j * NE:(j + 1) * NE],
                      mo[:, j:j + 1], None, ALU.is_equal, inc=vs)
                rt.wait(vs)
                rt.op("transpose", p_i8[:],
                      ind_tm[:, (k * NTC + j) * NE:(k * NTC + j + 1) * NE],
                      identb[:], inc=ts)
                rv.wait(ts)
                rv.op("tensor_copy",
                      out=indfm[0:NE, k * T + j * 128:k * T + (j + 1) * 128],
                      in_=p_i8[:], inc=vs)
        for half in range(2):
            for k in range(2):
                rt.wait(vs)
                rt.wait_tok(tok_c["blk2"])
                rt.op("matmul", p_m512[:], blk2[:],
                      indfm[0:NE, k * T + half * 512:k * T + (half + 1) * 512],
                      start=True, stop=True, inc=ts)
                rv.wait(ts)
                rv.op("tensor_copy",
                      out=mt4[:, (half * 2 + k) * 512:(half * 2 + k + 1) * 512],
                      in_=p_m512[:], inc=vs)
        barrier()
        stG.close()
        hscr = sbD("hscr", [128, 4 * 512], F32)
        rowscrD = sbD("rowscrD", [128, D], F32)
        ub_s = hscr[:, 0 * 512:1 * 512]
        gb_s = hscr[:, 1 * 512:2 * 512]
        su0_s = hscr[:, 2 * 512:3 * 512]
        su1_s = hscr[:, 3 * 512:4 * 512]

        wdn_v = wdnstr.rearrange("p (b d) -> p b d", b=2)
        bundle_used_ts = [0, 0]
        dn_used_ts = [0, 0]
        for half in range(2):
            hs = half * 512
            stP = ExitStack()
            p_r = stP.enter_context(nc.psum_tensor(f"p_r{half}", [128, 512], F32))
            p_wk = stP.enter_context(nc.psum_tensor(f"p_wk{half}", [128, 128], F32))
            p_wr = stP.enter_context(nc.psum_tensor(f"p_wr{half}", [1, 128], F32))
            wrow2 = stP.enter_context(nc.sbuf_tensor(f"wrow2{half}", [1, 128], F32))
            rg.wait(ts, dn_used_ts[0])
            rg.wait(ts, dn_used_ts[1])
            tok_upA = Qg.dma(rg, wdn_v[:, 0].rearrange("p (c e) -> p c e", c=NDC),
                             upA_T.rearrange("(c p) e -> p c e", p=128))
            tok_gtA = Qg.dma(rg, wdn_v[:, 1].rearrange("p (c e) -> p c e", c=NDC),
                             gateA_T.rearrange("(c p) e -> p c e", p=128))
            for k in range(2):
                for c4 in range(4):
                    tch = half * 4 + c4
                    rt.wait(vs)
                    rt.op("transpose", p_wr[0:1, :],
                          statscr[:, k * 8 + tch:k * 8 + tch + 1], identf[:], inc=ts)
                    rv.wait(ts)
                    rv.op("tensor_copy", out=wrow2[0:1, :], in_=p_wr[0:1, :], inc=vs)
                    rv_w = vs.n
                    rt.wait(vs, rv_w)
                    rt.op("matmul", p_wk[:], onesf[0:1, :], wrow2[0:1, :],
                          start=True, stop=True, inc=ts)
                    rv.wait(ts)
                    rv.op("tensor_copy", out=wkb[:, k * 512 + c4 * 128:
                                                 k * 512 + (c4 + 1) * 128],
                          in_=p_wk[:], inc=vs)
            for side in range(2):
                rt.wait_tok(tok_upA if side == 0 else tok_gtA)
                rt.wait(vs)
                for kd in range(NDC):
                    rt.op("matmul", p_r[:],
                          wdn_v[:, side, kd * 128:(kd + 1) * 128],
                          ln2_v[:, kd, hs:hs + 512],
                          start=(kd == 0), stop=(kd == NDC - 1),
                          inc=(ts if kd == NDC - 1 else None))
                r_done = ts.n
                for k in range(2):
                    rv.wait(ts, r_done)
                    rv.op("tensor_tensor", rtld[:, (side * 2 + k) * 512:
                                                 (side * 2 + k + 1) * 512],
                          p_r[:], mt4[:, (half * 2 + k) * 512:(half * 2 + k + 1) * 512],
                          ALU.mult, inc=vs)
            rtld_vs = vs.n
            dn_used_ts[0] = dn_used_ts[1] = ts.n
            stP.close()

            stH = ExitStack()
            p_u0 = [stH.enter_context(nc.psum_tensor(f"p_u0{half}{i}", [128, 512], F32))
                    for i in range(2)]
            p_g0 = [stH.enter_context(nc.psum_tensor(f"p_g0{half}{i}", [128, 512], F32))
                    for i in range(2)]
            p_loU = stH.enter_context(nc.psum_tensor(f"p_loU{half}", [128, 512], F32))
            p_loG = stH.enter_context(nc.psum_tensor(f"p_loG{half}", [128, 512], F32))
            p_sraw = [stH.enter_context(nc.psum_tensor(f"p_sraw{half}{i}", [128, 512], F32))
                      for i in range(2)]
            u0_free = [None, None]
            g0_free = [None, None]
            dnA_used = [0, 0, 0]
            dnA_tok = [None, None, None]
            loU_free = 0
            loG_free = 0
            ch_vs = 0
            sraw_ts = 0
            for hh in range(NHC):
                par = hh % 2
                ry.wait(ts, bundle_used_ts[par])
                for tk in wconv_toks:
                    ry.wait_tok(tk)
                bt = []
                bt.append(Qy.dma(ry, up_v[:, par, 0:2048].rearrange("p (c h) -> p c h", c=NDC),
                                 w_upT_bf[:, hh * 128:(hh + 1) * 128]
                                 .rearrange("(c p) h -> p c h", p=128)))
                bt.append(Qy.dma(ry, up_v[:, par, 2048:4096].rearrange("p (c h) -> p c h", c=NDC),
                                 w_gateT_bf[:, hh * 128:(hh + 1) * 128]
                                 .rearrange("(c p) h -> p c h", p=128)))
                rg.wait(ts, bundle_used_ts[par])
                bt.append(Qg.dma(rg, up_v[:, par, 4096:4224],
                                 upB_T[:, hh * 128:(hh + 1) * 128]))
                bt.append(Qg.dma(rg, up_v[:, par, 4224:4352],
                                 gateB_T[:, hh * 128:(hh + 1) * 128]))
                rg.wait(ts, dnA_used[hh % 3])
                dnA_tok[hh % 3] = Qg.dma(
                    rg, dnA_sb3[:, (hh % 3) * 128:(hh % 3 + 1) * 128],
                    dnA_T[hh * 128:(hh + 1) * 128, :])
                for tk in bt:
                    rt.wait_tok(tk)
                rt.wait_tok(u0_free[par])
                rt.wait(vs, rtld_vs)
                for kd in range(NDC):
                    rt.op("matmul", p_u0[par][:],
                          up_v[:, par, kd * 128:(kd + 1) * 128],
                          ln2_v[:, kd, hs:hs + 512],
                          start=(kd == 0), stop=(kd == NDC - 1),
                          inc=(ts if kd == NDC - 1 else None))
                baseu_ts = ts.n
                rt.wait_tok(g0_free[par])
                for kd in range(NDC):
                    rt.op("matmul", p_g0[par][:],
                          up_v[:, par, 2048 + kd * 128:2048 + (kd + 1) * 128],
                          ln2_v[:, kd, hs:hs + 512],
                          start=(kd == 0), stop=(kd == NDC - 1),
                          inc=(ts if kd == NDC - 1 else None))
                baseg_ts = ts.n
                if hh > 0:
                    rt.wait(vs, ch_vs)
                    rt.wait_tok(dnA_tok[(hh - 1) % 3])
                    for k in range(2):
                        rt.op("matmul", p_sraw[k][:],
                              dnA_sb3[:, ((hh - 1) % 3) * 128:((hh - 1) % 3 + 1) * 128],
                              chb[:, k * 512:(k + 1) * 512],
                              start=(hh - 1 == 0), stop=False,
                              inc=(ts if k == 1 else None))
                    sraw_ts = ts.n
                    dnA_used[(hh - 1) % 3] = ts.n
                rs.wait(ts, baseu_ts)
                rs.wait(vs)
                rs.wait(ss)
                rs.op("copy", out=ub_s, in_=p_u0[par][:], inc=ss)
                ubcopy_ss = ss.n
                rs.wait(ts, baseg_ts)
                rs.op("copy", out=gb_s, in_=p_g0[par][:], inc=ss)
                gbcopy_ss = ss.n
                rt.wait(ss, ubcopy_ss)
                rt.op("matmul", p_u0[par][:], up_v[:, par, 4096:4224],
                      rtld[:, 0:512], start=False, stop=True,
                      skip_group_check=True, inc=ts)
                lo0u_ts = ts.n
                rt.wait(vs, loU_free)
                rt.op("matmul", p_loU[:], up_v[:, par, 4096:4224],
                      rtld[:, 512:1024], start=True, stop=True, inc=ts)
                lo1u_ts = ts.n
                rt.wait(ss, gbcopy_ss)
                rt.op("matmul", p_g0[par][:], up_v[:, par, 4224:4352],
                      rtld[:, 1024:1536], start=False, stop=True,
                      skip_group_check=True, inc=ts)
                lo0g_ts = ts.n
                rt.wait(vs, loG_free)
                rt.op("matmul", p_loG[:], up_v[:, par, 4224:4352],
                      rtld[:, 1536:2048], start=True, stop=True, inc=ts)
                lo1g_ts = ts.n
                bundle_used_ts[par] = ts.n
                rs.wait(ts, lo0u_ts)
                rs.op("activation", su0_s, p_u0[par][:], AF.Silu, inc=ss)
                silu0_ss = ss.n
                u0_free[par] = (ss, silu0_ss)
                rv.wait(ts, lo1u_ts)
                rv.wait(ss, ubcopy_ss)
                rv.op("scalar_tensor_tensor", ub_s, p_loU[:], 1.0, ub_s,
                      ALU.mult, ALU.add, inc=vs)
                loU_free = vs.n
                u1_vs = vs.n
                rs.wait(vs, u1_vs)
                rs.op("activation", su1_s, ub_s, AF.Silu, inc=ss)
                silu1_ss = ss.n
                rv.wait(ss, silu0_ss)
                rv.wait(ts, lo0g_ts)
                rv.wait(ts, sraw_ts)
                rv.op("tensor_tensor", su0_s, su0_s, p_g0[par][:], ALU.mult, inc=vs)
                h0_vs = vs.n
                g0_free[par] = (vs, h0_vs)
                rv.wait(vs, h0_vs)
                rv.op("tensor_tensor", chb[:, 0:512], su0_s, wkb[:, 0:512],
                      ALU.mult, inc=vs)
                rv.wait(ts, lo1g_ts)
                rv.wait(ss, gbcopy_ss)
                rv.op("scalar_tensor_tensor", gb_s, p_loG[:], 1.0, gb_s,
                      ALU.mult, ALU.add, inc=vs)
                loG_free = vs.n
                rv.wait(ss, silu1_ss)
                rv.wait(vs)
                rv.op("tensor_tensor", su1_s, su1_s, gb_s, ALU.mult, inc=vs)
                rv.wait(vs)
                rv.op("tensor_tensor", chb[:, 512:1024], su1_s, wkb[:, 512:1024],
                      ALU.mult, inc=vs)
                rv.wait(vs)
                rv.op("tensor_tensor", mixed[:, hh * 512:(hh + 1) * 512],
                      chb[:, 0:512], chb[:, 512:1024], ALU.add, inc=vs)
                ch_vs = vs.n
            rt.wait(vs, ch_vs)
            rt.wait_tok(dnA_tok[(NHC - 1) % 3])
            for k in range(2):
                rt.op("matmul", p_sraw[k][:],
                      dnA_sb3[:, ((NHC - 1) % 3) * 128:((NHC - 1) % 3 + 1) * 128],
                      chb[:, k * 512:(k + 1) * 512],
                      start=False, stop=True, inc=(ts if k == 1 else None))
            sraw_done = ts.n
            for k in range(2):
                rv.wait(ts, sraw_done)
                rv.op("tensor_tensor", rtld[:, k * 512:(k + 1) * 512],
                      p_sraw[k][:], mt4[:, (half * 2 + k) * 512:(half * 2 + k + 1) * 512],
                      ALU.mult, inc=vs)
            stld_vs = vs.n
            barrier()
            stH.close()

            stE = ExitStack()
            p_d = [stE.enter_context(nc.psum_tensor(f"p_d{half}{i}", [128, 512], F32))
                   for i in range(8)]
            dbank_free = [0] * 8
            for tpair in range(2):
                for hh in range(NHC):
                    par = hh % 2
                    ry.wait(ts, dn_used_ts[par])
                    for tk in wconv_toks:
                        ry.wait_tok(tk)
                    tok_dn = Qy.dma(ry, wdn_v[:, par],
                                    w_downT_bf[hh * 128:(hh + 1) * 128, :])
                    rt.wait_tok(tok_dn)
                    if hh == 0:
                        for bi in range(8):
                            rt.wait(vs, dbank_free[bi])
                    for tq2 in range(2):
                        tloc = tpair * 2 + tq2
                        for qb in range(4):
                            rt.op("matmul", p_d[tq2 * 4 + qb][:],
                                  mixed[:, hh * 512 + tloc * 128:
                                        hh * 512 + (tloc + 1) * 128],
                                  wdn_v[:, par, qb * 512:(qb + 1) * 512],
                                  start=(hh == 0), stop=False,
                                  inc=(ts if (tq2 == 1 and qb == 3) else None))
                    dn_used_ts[par] = ts.n
                rt.wait(vs, stld_vs)
                rt.wait_tok(tok_dnB)
                for tq2 in range(2):
                    tloc = tpair * 2 + tq2
                    for qb in range(4):
                        for k in range(2):
                            rt.op("matmul", p_d[tq2 * 4 + qb][:],
                                  rtld[:, k * 512 + tloc * 128:
                                       k * 512 + (tloc + 1) * 128],
                                  dnB_sb[:, qb * 512:(qb + 1) * 512],
                                  start=False, stop=(k == 1),
                                  inc=(ts if (tq2 == 1 and qb == 3 and k == 1)
                                       else None))
                dn_done = ts.n
                for tq2 in range(2):
                    tch = half * 4 + tpair * 2 + tq2
                    rv.wait(ts, dn_done)
                    rv.wait(vs)
                    for qb in range(4):
                        rv.op("tensor_copy",
                              out=rowscrD[:, qb * 512:(qb + 1) * 512],
                              in_=p_d[tq2 * 4 + qb][:], inc=vs)
                        dbank_free[tq2 * 4 + qb] = vs.n
                    mean, rstd = ln_stats(rowscrD[:])
                    rv.wait(ss)
                    rv.op("tensor_scalar", rowscrD[:], rowscrD[:], mean, rstd,
                          ALU.subtract, op1=ALU.mult, inc=vs)
                    rv.wait(vs)
                    rv.op("tensor_tensor", x_res[:, tch * D:(tch + 1) * D],
                          x_res[:, tch * D:(tch + 1) * D], rowscrD[:],
                          ALU.add, inc=vs)
            stE.close()

        if "d_mixed" in dbg_out:
            rg.wait(vs)
            Qg.dma(rg, dbg_out["d_mixed"], mixed[:])

        barrier()
        stD.close()
        stL.close()

        rv_final = vs.n
        ry.wait(vs, rv_final)
        tok_out = Qy.dma(ry, out_own.rearrange("(j p) d -> p j d", p=128),
                         x_res[:].rearrange("p (j d) -> p j d", j=NTC))
        ry.wait_tok(tok_out)

        with nc.Block() as block:
            block.tensor(lambda e: rt.replay(e))
            block.vector(lambda e: rv.replay(e))
            block.scalar(lambda e: rs.replay(e))
            block.gpsimd(lambda e: rg.replay(e))
            block.sync(lambda e: ry.replay(e))

    return nc



def host_prep(inputs):
    x = np.asarray(inputs["L0_attn_x"])
    w_qkv = np.ascontiguousarray(inputs["L0_attn_w_qkv"], dtype=np.float32)
    w_out = np.ascontiguousarray(inputs["L0_attn_w_out"], dtype=np.float32)
    gate_w = np.asarray(inputs["L0_ffn_gate_w"])
    w_up = np.asarray(inputs["L0_ffn_w_up"])
    w_gate = np.asarray(inputs["L0_ffn_w_gate"])
    w_down = np.asarray(inputs["L0_ffn_w_down"])
    up_A = np.asarray(inputs["L0_ffn_up_A"])
    up_B = np.asarray(inputs["L0_ffn_up_B"])
    gate_A = np.asarray(inputs["L0_ffn_gate_A"])
    gate_B = np.asarray(inputs["L0_ffn_gate_B"])
    down_A = np.asarray(inputs["L0_ffn_down_A"])
    down_B = np.asarray(inputs["L0_ffn_down_B"])

    shared = {
        "w_qkv": w_qkv,
        "w_out": w_out,
        "gate_wT": np.ascontiguousarray(gate_w.T),
        "w_upT": np.ascontiguousarray(w_up.T),
        "w_gateT": np.ascontiguousarray(w_gate.T),
        "w_downT": np.ascontiguousarray(w_down.T),
        "upA_T": np.ascontiguousarray(up_A.reshape(ER, D).T),
        "gateA_T": np.ascontiguousarray(gate_A.reshape(ER, D).T),
        "dnA_T": np.ascontiguousarray(down_A.reshape(ER, HFF).T),
        "upB_T": np.ascontiguousarray(up_B.transpose(0, 2, 1).reshape(ER, HFF)),
        "gateB_T": np.ascontiguousarray(gate_B.transpose(0, 2, 1).reshape(ER, HFF)),
        "dnB_T": np.ascontiguousarray(down_B.transpose(0, 2, 1).reshape(ER, D)),
    }
    in_maps = []
    for core in range(8):
        b, r = core // 4, core % 4
        xb = x[b].astype(np.float32, copy=False)
        x_ownc = np.ascontiguousarray(
            xb.reshape(32, 128, D)[r::4].reshape(T, D))
        mval = NEGBIG / RSQ
        m = np.zeros((128, 512), dtype=np.float32)
        for rp in range(4):
            if rp > r:
                m[:, rp * 128:(rp + 1) * 128] = mval
            elif rp == r:
                tri = np.triu(np.full((128, 128), mval, dtype=np.float32), k=1)
                m[:, rp * 128:(rp + 1) * 128] = tri
        mT = np.concatenate([m[:, rp * 128:(rp + 1) * 128].T for rp in range(4)],
                            axis=1)
        z128 = np.zeros((128, 128), dtype=np.float32)
        mP = np.concatenate([mT[:, 0:128], z128, mT[:, 128:256], z128,
                             mT[:, 256:384], z128, mT[:, 384:512], z128], axis=1)
        m = np.ascontiguousarray(np.concatenate([mP, mT], axis=1))
        d = dict(shared)
        d["x_own"] = x_ownc
        d["mask512"] = m
        in_maps.append(d)
    return in_maps


def kernel(**inputs):
    in_maps = host_prep(inputs)
    nc = build(debug=DEBUG)
    res = run_bass_kernel_spmd(nc, in_maps, core_ids=list(range(8)))
    out = np.empty((2, S, D), dtype=np.float32)
    for core in range(8):
        b, r = core // 4, core % 4
        o = res.results[core]["out_own"].reshape(NTC, 128, D)
        out[b].reshape(32, 128, D)[r::4] = o
    return out
